# version 33
# baseline (speedup 1.0000x reference)
import os
import numpy as np
import ml_dtypes
from contextlib import ExitStack, contextmanager
import concourse.bass as bass
import concourse.mybir as mybir
from concourse.bass_utils import run_bass_kernel_spmd

F32 = mybir.dt.float32
BF16 = mybir.dt.bfloat16
AF = mybir.ActivationFunctionType
ALU = mybir.AluOpType
AX = mybir.AxisListType

D = 1024
S = 2048
BPC = 2
NCORES = 8
NSC = S // 128
NDC = D // 128

H0 = 16
HD0 = 64
H1 = 64
HD1 = 16
NE = 8
NT = 512
ET = NE * NT
HID = 4 * D


class Bld:
    ENG_SEM = {"tensor": "pe", "vector": "dve", "scalar": "act", "gpsimd": "pool"}
    DMA_SEM = {"sync": "qsp", "gpsimd": "qpool", "scalar": "qact", "vector": "qvec"}

    def __init__(self, nc, ctx):
        self.nc = nc
        self.ctx = ctx
        self.cur_ctx = ctx
        self.prefix = ""
        self.idents = None
        self.sem = {}
        names = ["pe", "dve", "act", "pool"]
        for q in ["qsp", "qpool", "qact", "qvec"]:
            names += [q + str(i) for i in range(3)]
        for name in names:
            self.sem[name] = ctx.enter_context(nc.semaphore(name))
        self.cnt = {k: 0 for k in self.sem}
        self.wm = {}

    def init_idents(self):
        self.idents = emit_identities(self)

    @contextmanager
    def stage(self, tag, barrier=True):
        if barrier:
            self.stage_barrier()
        old_prefix, old_ctx = self.prefix, self.cur_ctx
        self.prefix = tag + "_"
        self._cc = {}
        with ExitStack() as sctx, self.nc.named_scope(tag):
            self.cur_ctx = sctx
            yield sctx
        self.cur_ctx = old_ctx
        self.prefix = old_prefix
        self._cc = {}

    def stage_barrier(self):
        for eng in ("tensor", "vector", "scalar", "gpsimd", "sync"):
            for semname, cnt in list(self.cnt.items()):
                if cnt:
                    self._wait(eng, (semname, cnt))

    def done(self, outs):
        pass

    def finish_all(self):
        for semname, cnt in list(self.cnt.items()):
            if cnt:
                self._wait("sync", (semname, cnt))

    def sb(self, name, shape, dt=F32):
        return self.cur_ctx.enter_context(self.nc.sbuf_tensor(self.prefix + name, shape, dt))

    def ps(self, name, shape, dt=F32):
        return self.cur_ctx.enter_context(self.nc.psum_tensor(self.prefix + name, shape, dt))

    def _wait(self, eng, tok):
        if tok is None:
            return
        semname, val = tok
        key = (eng, semname)
        if self.wm.get(key, 0) >= val:
            return
        getattr(self.nc, eng).wait_ge(self.sem[semname], val)
        self.wm[key] = val

    def op(self, eng, method, *args, deps=(), **kw):
        for t in deps:
            self._wait(eng, t)
        inst = getattr(getattr(self.nc, eng), method)(*args, **kw)
        semname = self.ENG_SEM[eng]
        self.cnt[semname] += 1
        inst.then_inc(self.sem[semname], 1)
        return (semname, self.cnt[semname])

    def mm(self, out, lhsT, rhs, start, stop, deps=(), inc=True, **kw):
        for t in deps:
            self._wait("tensor", t)
        inst = self.nc.tensor.matmul(out, lhsT, rhs, start=start, stop=stop, **kw)
        if inc:
            self.cnt["pe"] += 1
            inst.then_inc(self.sem["pe"], 1)
            return ("pe", self.cnt["pe"])
        return None

    def dma(self, queue, out, in_, deps=(), lane=0, **kw):
        semname = self.DMA_SEM[queue] + str(lane % 3)
        self._wait(queue, (semname, self.cnt[semname]))
        for t in deps:
            self._wait(queue, t)
        inst = getattr(self.nc, queue).dma_start(out=out, in_=in_, **kw)
        self.cnt[semname] += 16
        inst.then_inc(self.sem[semname], 16)
        return (semname, self.cnt[semname])

    def constcol(self, value, dt=F32):
        key = (value, dt)
        if not hasattr(self, "_cc"):
            self._cc = {}
        if key not in self._cc:
            t = self.sb(f"cc{len(self._cc)}", [128, 1], dt)
            tok = self.op("vector", "memset", t[:], value)
            self._cc[key] = (t, tok)
        return self._cc[key]

    def barrier(self, eng, toks):
        for t in toks:
            self._wait(eng, t)

    def finish(self, toks):
        for t in toks:
            self._wait("sync", t)
        for q in ("qsp", "qpool", "qact", "qvec"):
            if self.cnt[q]:
                self._wait("sync", (q, self.cnt[q]))


def emit_identities(b):
    identf = b.sb("identf", [128, 128], F32)
    identb = b.sb("identb", [128, 128], BF16)
    t1 = b.op("gpsimd", "memset", identf[:], 0.0)
    t1 = b.op("gpsimd", "affine_select", out=identf[:], in_=identf[:],
              compare_op=ALU.not_equal, fill=1.0, base=0,
              pattern=[[-1, 128]], channel_multiplier=1, deps=[t1])
    t2 = b.op("gpsimd", "memset", identb[:], 0.0)
    t2 = b.op("gpsimd", "affine_select", out=identb[:], in_=identb[:],
              compare_op=ALU.not_equal, fill=1.0, base=0,
              pattern=[[-1, 128]], channel_multiplier=1, deps=[t2])
    return identf, t1, identb, t2


def tp_b16(b, psum_out, in_slice, identb, deps, base=0):
    p = in_slice.shape[0]
    for t in deps:
        b._wait("tensor", t)
    inst = b.nc.tensor.transpose(psum_out, in_slice, identb[base:base + p, base:base + p])
    b.cnt["pe"] += 1
    inst.then_inc(b.sem["pe"], 1)
    return ("pe", b.cnt["pe"])


def tp_f32(b, psum_out, in_slice, identf, deps):
    p = in_slice.shape[0]
    return b.mm(psum_out, in_slice, identf[0:p, 0:p], True, True, deps=deps)


def emit_ln_chunk(b, xt, xnb, st, ld, prev):
    t_sq = b.op("scalar", "activation", st["dump"][:], xt[:], AF.Square,
                accum_out=st["ssq"][:], deps=[ld, prev.get("t_v")])
    t_sm = b.op("vector", "reduce_sum", st["sm"][:], xt[:], axis=AX.X,
                deps=[ld, prev.get("t_m")])
    t_m = b.op("vector", "tensor_scalar_mul", st["mcol"][:], st["sm"][:], 1.0 / D,
               deps=[t_sm, prev.get("t_mm"), prev.get("t_mb")])
    t_mm = b.op("vector", "scalar_tensor_tensor", st["var"][:], st["mcol"][:], -1.0,
                st["mcol"][:], ALU.mult, ALU.mult, deps=[t_m, prev.get("t_ln")])
    t_v = b.op("vector", "scalar_tensor_tensor", st["var"][:], st["ssq"][:], 1.0 / D,
               st["var"][:], ALU.mult, ALU.add, deps=[t_mm, t_sq])
    epscol, epstok = b.constcol(1e-5)
    t_ln = b.op("scalar", "activation", st["rstd"][:], st["var"][:], AF.Ln, bias=epscol[:],
                deps=[t_v, prev.get("t_mb"), epstok])
    t_rs = b.op("scalar", "activation", st["rstd"][:], st["rstd"][:], AF.Exp, scale=-0.5,
                deps=[t_ln])
    t_mb = b.op("vector", "scalar_tensor_tensor", st["mb"][:], st["mcol"][:], -1.0,
                st["rstd"][:], ALU.mult, ALU.mult, deps=[t_m, t_rs, prev.get("t_xn")])
    t_xn = b.op("scalar", "activation", xnb[:], xt[:], AF.Identity,
                bias=st["mb"][:], scale=st["rstd"][:],
                deps=[t_rs, t_mb, ld] + prev.get("xnb_readers", []))
    return {"t_sq": t_sq, "t_sm": t_sm, "t_m": t_m, "t_mm": t_mm, "t_v": t_v,
            "t_ln": t_ln, "t_rs": t_rs, "t_mb": t_mb, "t_xn": t_xn}


def mk_stats(b, tag):
    return {"dump": b.sb(f"dump{tag}", [128, D]),
            "ssq": b.sb(f"ssq{tag}", [128, 1]),
            "sm": b.sb(f"sm{tag}", [128, 1]),
            "mcol": b.sb(f"mcol{tag}", [128, 1]),
            "var": b.sb(f"var{tag}", [128, 1]),
            "rstd": b.sb(f"rstd{tag}", [128, 1]),
            "mb": b.sb(f"mb{tag}", [128, 1])}


def emit_ln(b, x, xn_tm, xn_fm, tag="ln"):
    with b.stage(tag) as ctx:
        identf, tif, identb, tib = b.idents
        NPAR = 3
        xt = [b.sb(f"xt{i}", [128, D], F32) for i in range(NPAR)]
        xnb = [b.sb(f"xnb{i}", [128, D], BF16) for i in range(NPAR)]
        st = [mk_stats(b, i) for i in range(NPAR)]
        fmb = [b.sb(f"fmb{i}", [128, 8, 128], BF16) for i in range(NPAR)]
        ptp = [b.ps(f"ptp{i}", [128, 512], BF16) for i in range(4)]
        prev = [{} for _ in range(NPAR)]
        fm_dma = {}
        tev_tok = {}
        out_toks = []
        for bi in range(BPC):
            for sc in range(NSC):
                k = sc % NPAR
                ld = b.dma("sync", xt[k][:], x[bi, sc * 128:(sc + 1) * 128, :],
                           deps=[prev[k].get("t_xn"), prev[k].get("t_sm")], lane=k)
                toks = emit_ln_chunk(b, xt[k], xnb[k], st[k], ld, prev[k])
                t_xn = toks["t_xn"]
                st_dma = b.dma("sync", xn_tm[bi, sc * 128:(sc + 1) * 128, :], xnb[k][:],
                               deps=[t_xn], lane=k)
                out_toks.append(st_dma)
                readers = [st_dma]
                evs = []
                for dc in range(NDC):
                    kk = dc % 4
                    t_tp = tp_b16(b, ptp[kk][:, 0:128], xnb[k][:, dc * 128:(dc + 1) * 128],
                                  identb, deps=[t_xn, tib, tev_tok.get(kk)])
                    t_ev = b.op("vector", "tensor_copy", fmb[k][:, dc, :], ptp[kk][:, 0:128],
                                deps=[t_tp, fm_dma.get(k)])
                    tev_tok[kk] = t_ev
                    readers.append(t_tp)
                    evs.append(t_ev)
                fm_dma[k] = b.dma("gpsimd",
                                  xn_fm[bi, :, sc * 128:(sc + 1) * 128].rearrange(
                                      "(a p) f -> p a f", p=128),
                                  fmb[k][:], deps=evs, lane=k)
                out_toks.append(fm_dma[k])
                toks["xnb_readers"] = readers
                prev[k] = toks
        b.done(out_toks)


class TpMachine:

    def __init__(self, b, identb, tib, n=2, evac_eng="vector"):
        self.b = b
        self.n = n
        self.identb = identb
        self.tib = tib
        self.evac_eng = evac_eng
        self.ptp = [b.ps(f"tpm_p{i}", [128, 512], F32) for i in range(n)]
        self.ev = [b.sb(f"tpm_e{i}", [128, 128], BF16) for i in range(n)]
        self.tp_tok = {}
        self.ev_dma = {}
        self.i = 0

    def to_dram(self, src_slice, dram_slice, deps):
        b = self.b
        kk = self.i % self.n
        self.i += 1
        pt = self.ptp[kk][:].bitcast(BF16)[:, 0:128]
        t_tp = tp_b16(b, pt, src_slice, self.identb,
                      deps=list(deps) + [self.tib, self.tp_tok.get(kk)])
        t_ev = (b.op("vector", "tensor_copy", self.ev[kk][:], pt,
                     deps=[t_tp, self.ev_dma.get(kk)])
                if self.evac_eng == "vector" else
                b.op("scalar", "activation", self.ev[kk][:], pt, AF.Identity,
                     deps=[t_tp, self.ev_dma.get(kk)]))
        self.tp_tok[kk] = t_ev
        self.ev_dma[kk] = b.dma("sync" if kk % 2 else "gpsimd", dram_slice,
                                self.ev[kk][:], deps=[t_ev], lane=kk % 3)
        return t_tp, self.ev_dma[kk]

    def to_sbuf(self, src_slice, sbuf_slice, deps, extra_ev_deps=()):
        b = self.b
        kk = self.i % self.n
        self.i += 1
        pt = self.ptp[kk][:].bitcast(BF16)[:, 0:128]
        t_tp = tp_b16(b, pt, src_slice, self.identb,
                      deps=list(deps) + [self.tib, self.tp_tok.get(kk)])
        t_ev = (b.op("vector", "tensor_copy", sbuf_slice, pt,
                     deps=[t_tp] + [d for d in extra_ev_deps if d])
                if self.evac_eng == "vector" else
                b.op("scalar", "activation", sbuf_slice, pt, AF.Identity,
                     deps=[t_tp] + [d for d in extra_ev_deps if d]))
        self.tp_tok[kk] = t_ev
        return t_tp, t_ev


def emit_wtrans(b, tpm, wtile, fmb, wt_dram, out_dram, E, prev_reads, fm_dma):
    nec = E // 128
    outs = []
    for ec in range(nec):
        k = ec % 3
        ld = b.dma("gpsimd", wtile[k][:, 0:D], wt_dram[ec * 128:(ec + 1) * 128, :],
                   deps=[prev_reads[k]], lane=k)
        last_tp = None
        evs = []
        for dc in range(NDC):
            t_tp, t_ev = tpm.to_sbuf(wtile[k][:, dc * 128:(dc + 1) * 128],
                                     fmb[k][:, dc, :], deps=[ld],
                                     extra_ev_deps=[fm_dma.get(k)])
            evs.append(t_ev)
            last_tp = t_tp
        fm_dma[k] = b.dma("sync",
                          out_dram[:, ec * 128:(ec + 1) * 128].rearrange(
                              "(a p) f -> p a f", p=128),
                          fmb[k][:], deps=evs, lane=k)
        outs.append(fm_dma[k])
        prev_reads[k] = last_tp
    return outs


def emit_wprep(b, io):
    (qkv_w, proj_w, W0, lora_down, lora_up, down_aux, up_aux, Wq, Wk, Wv, Wo,
     slot_embeds, gamma_slot, qkvT, projT, WeffT, WqT, WkT, WvT, WoT, seT) = (
        io[k] for k in ("qkv_w", "proj_w", "W0", "lora_down", "lora_up", "down_aux",
                        "up_aux", "Wq", "Wk", "Wv", "Wo", "slot_embeds", "gamma_slot",
                        "qkvT", "projT", "WeffT", "WqT", "WkT", "WvT", "WoT", "seT"))
    with b.stage("wp") as ctx:
        identf, tif, identb, tib = b.idents
        tpm = TpMachine(b, identb, tib, n=4)
        wtile = [b.sb(f"wtile{i}", [128, D], BF16) for i in range(3)]
        fmb = [b.sb(f"wfmb{i}", [128, 8, 128], BF16) for i in range(3)]
        prev_reads = [None, None, None]
        fm_dma = {}
        outs = []
        outs += emit_wtrans(b, tpm, wtile, fmb, qkv_w, qkvT, 3 * D, prev_reads, fm_dma)
        outs += emit_wtrans(b, tpm, wtile, fmb, proj_w, projT, D, prev_reads, fm_dma)
        outs += emit_wtrans(b, tpm, wtile, fmb, Wq, WqT, D, prev_reads, fm_dma)
        outs += emit_wtrans(b, tpm, wtile, fmb, Wk, WkT, D, prev_reads, fm_dma)
        outs += emit_wtrans(b, tpm, wtile, fmb, Wv, WvT, D, prev_reads, fm_dma)
        outs += emit_wtrans(b, tpm, wtile, fmb, Wo, WoT, D, prev_reads, fm_dma)

        lds = b.sb("lds", [4, 100])
        lus = b.sb("lus", [50, 4])
        das = b.sb("das", [100, D])
        t_ld = b.dma("sync", lds[:], lora_down[:])
        t_lu = b.dma("sync", lus[:], lora_up[:])
        t_da = b.dma("sync", das[:], down_aux[:])
        p_sm = b.ps("p_sm", [128, 1024], F32)
        ldT = b.sb("ldT", [100, 4])
        t1 = b.sb("t1", [4, D])
        luT = b.sb("luT", [4, 50])
        t2 = b.sb("t2", [50, D])
        uaT = b.sb("uaT", [50, D])
        tk = tp_f32(b, p_sm[0:100, 0:4], lds[:], identf, deps=[t_ld, tif])
        tk = b.op("vector", "tensor_copy", ldT[:], p_sm[0:100, 0:4], deps=[tk])
        m1 = b.mm(p_sm[0:4, 0:512], ldT[:], das[:, 0:512], True, True, deps=[tk, t_da])
        m1b = b.mm(p_sm[0:4, 512:1024], ldT[:], das[:, 512:1024], True, True, deps=[tk, t_da])
        tk1 = b.op("vector", "tensor_copy", t1[:], p_sm[0:4, :], deps=[m1, m1b])
        tk2 = tp_f32(b, p_sm[0:4, 0:50], lus[:], identf, deps=[t_lu, tif, tk1])
        tk2 = b.op("vector", "tensor_copy", luT[:], p_sm[0:4, 0:50], deps=[tk2])
        m2 = b.mm(p_sm[0:50, 0:512], luT[:], t1[:, 0:512], True, True, deps=[tk2, tk1])
        m2b = b.mm(p_sm[0:50, 512:1024], luT[:], t1[:, 512:1024], True, True, deps=[tk2, tk1])
        tk3 = b.op("vector", "tensor_copy", t2[:], p_sm[0:50, :], deps=[m2, m2b])
        uas = b.sb("uas", [128, 50])
        t_ua_prev = None
        for dc in range(NDC):
            t_ua = b.dma("sync", uas[:], up_aux[dc * 128:(dc + 1) * 128, :],
                         deps=[t_ua_prev])
            tk4 = tp_f32(b, p_sm[0:50, 0:128], uas[:], identf, deps=[t_ua, tk3])
            t_ua_prev = b.op("vector", "tensor_copy", uaT[:, dc * 128:(dc + 1) * 128],
                             p_sm[0:50, 0:128], deps=[tk4])
        tk_uaT = t_ua_prev
        w0t = b.sb("w0t", [128, D])
        weff_b = b.sb("weff_b", [128, D], BF16)
        p_li = b.ps("p_li", [128, 1024], F32)
        prev_cast = None
        last_wtp = None
        for oc in range(NDC):
            mli = [b.mm(p_li[:, nb * 512:(nb + 1) * 512],
                        uaT[:, oc * 128:(oc + 1) * 128],
                        t2[:, nb * 512:(nb + 1) * 512], True, True,
                        deps=[tk_uaT, tk3, prev_cast]) for nb in range(2)]
            t_w0 = b.dma("sync", w0t[:], W0[oc * 128:(oc + 1) * 128, :], deps=[prev_cast])
            t_add = b.op("vector", "tensor_add", w0t[:], w0t[:], p_li[:], deps=mli + [t_w0])
            prev_cast = b.op("vector", "tensor_copy", weff_b[:], w0t[:],
                             deps=[t_add, last_wtp])
            evs = []
            kE = oc % 3
            for dc in range(NDC):
                last_wtp, t_ev = tpm.to_sbuf(weff_b[:, dc * 128:(dc + 1) * 128],
                                             fmb[kE][:, dc, :], deps=[prev_cast],
                                             extra_ev_deps=[fm_dma.get(kE)])
                evs.append(t_ev)
            fm_dma[kE] = b.dma("sync",
                               WeffT[:, oc * 128:(oc + 1) * 128].rearrange(
                                   "(a p) f -> p a f", p=128),
                               fmb[kE][:], deps=evs, lane=kE)
            outs.append(fm_dma[kE])

        ones1f = b.sb("ones1f", [1, 128])
        t_o1 = b.op("vector", "memset", ones1f[:], 1.0)
        g_row = b.sb("g_row", [1, D])
        t_gr = b.dma("sync", g_row[:], gamma_slot[:])
        gb = b.sb("gb", [128, D])
        mgb = [b.mm(p_sm[:, nb * 512:(nb + 1) * 512], ones1f[:],
                    g_row[:, nb * 512:(nb + 1) * 512], True, True,
                    deps=[t_o1, t_gr, tk_uaT]) for nb in range(2)]
        t_gb = b.op("vector", "tensor_copy", gb[:], p_sm[:], deps=mgb)
        set_ = [b.sb(f"set{i}", [128, D]) for i in range(2)]
        sed = b.sb("sed", [128, D])
        sen = [b.sb(f"sen{i}", [128, D], BF16) for i in range(2)]
        snq = b.sb("snq", [128, 1])
        srl = b.sb("srl", [128, 1])
        prevS = [{}, {}]
        for tc in range(ET // 128):
            k = tc % 2
            ld = b.dma("sync", set_[k][:], slot_embeds[tc * 128:(tc + 1) * 128, :],
                       deps=[prevS[k].get("rd")])
            t_sq = b.op("scalar", "activation", sed[:], set_[k][:], AF.Square,
                        accum_out=snq[:], deps=[ld, prevS[k].get("t_n2"), prevS[1 - k].get("t_n2")])
            t_n2 = b.op("vector", "tensor_scalar_max", snq[:], snq[:], 1e-24, deps=[t_sq])
            t_ln = b.op("scalar", "activation", srl[:], snq[:], AF.Ln,
                        deps=[t_n2, prevS[1 - k].get("t_sn")])
            t_ex = b.op("scalar", "activation", srl[:], srl[:], AF.Exp, scale=-0.5, deps=[t_ln])
            t_rl = b.op("vector", "tensor_scalar_mul", srl[:], srl[:], 32.0, deps=[t_ex])
            t_sn = b.op("vector", "scalar_tensor_tensor", sen[k][:], set_[k][:], srl[:],
                        gb[:], ALU.mult, ALU.mult,
                        deps=[t_rl, t_gb, ld] + prevS[k].get("senr", []))
            senr = []
            evs = []
            kE = tc % 3
            for dc in range(NDC):
                t_tp, t_ev = tpm.to_sbuf(sen[k][:, dc * 128:(dc + 1) * 128],
                                         fmb[kE][:, dc, :], deps=[t_sn],
                                         extra_ev_deps=[fm_dma.get(kE)])
                senr.append(t_tp)
                evs.append(t_ev)
            fm_dma[kE] = b.dma("sync",
                               seT[:, tc * 128:(tc + 1) * 128].rearrange(
                                   "(a p) f -> p a f", p=128),
                               fmb[kE][:], deps=evs, lane=kE)
            outs.append(fm_dma[kE])
            prevS[k] = {"rd": t_sn, "t_n2": t_n2, "senr": senr, "t_sn": t_sn}
        b.done(outs)


def emit_proj_tm(b, a_fm, WT, bias, o_tm, act="identity", nout=D, tag="pj"):
    nnb = nout // 512
    with b.stage(tag) as ctx:
        wt = [b.sb(f"wtile{i}", [128, nout], BF16) for i in range(NDC)]
        wtoks = [b.dma("sync", wt[i][:], WT[i * 128:(i + 1) * 128, :]) for i in range(NDC)]
        brow = b.sb("brow", [1, nout])
        browb = b.sb("browb", [1, nout], BF16)
        t_bias = b.dma("sync", brow[:], bias[:])
        t_bb = b.op("vector", "tensor_copy", browb[:], brow[:], deps=[t_bias])
        ones1b = b.sb("ones1b", [1, 128], BF16)
        t_o1 = b.op("vector", "memset", ones1b[:], 1.0)
        at = [b.sb(f"at{i}", [128, S], BF16) for i in range(NDC)]
        ot = [b.sb(f"ot{i}", [128, nout]) for i in range(2)]
        pso = [b.ps(f"pso{i}", [128, nout], F32) for i in range(2)]
        afm_toks = [None] * NDC
        prev_ev = [None, None]
        prev_st = [None, None]
        outs = []
        for bi in range(BPC):
            for i in range(NDC):
                afm_toks[i] = b.dma("gpsimd", at[i][:], a_fm[bi, i * 128:(i + 1) * 128, :],
                                    deps=[afm_toks[i], prev_st[0], prev_st[1]])
            for sc in range(NSC):
                k = sc % 2
                mms = []
                for nb in range(nnb):
                    dst = pso[k][:, nb * 512:(nb + 1) * 512]
                    for ic in range(NDC):
                        dep = [afm_toks[ic], wtoks[ic]] if nb == 0 else []
                        b.mm(dst, at[ic][:, sc * 128:(sc + 1) * 128],
                             wt[ic][:, nb * 512:(nb + 1) * 512],
                             ic == 0, False, deps=dep + [prev_ev[k]], inc=False)
                    mms.append(b.mm(dst, ones1b[:],
                                    browb[:, nb * 512:(nb + 1) * 512],
                                    False, True, deps=[t_o1, t_bb]))
                fn = AF.Silu if act == "silu" else AF.Identity
                t_ev = b.op("scalar", "activation", ot[k][:], pso[k][:], fn,
                            deps=mms + [prev_st[k]])
                prev_ev[k] = t_ev
                prev_st[k] = b.dma("sync", o_tm[bi, sc * 128:(sc + 1) * 128, :], ot[k][:],
                                   deps=[t_ev])
                outs.append(prev_st[k])
        b.done(outs)


def emit_l0qkv(b, xn_fm, qkvT, qkv_b, qkv_fm):
    NOC = 3 * D // 128
    with b.stage("q0") as ctx:
        identf, tif, identb, tib = b.idents
        wt = [b.sb(f"wtile{i}", [128, 3 * D], BF16) for i in range(NDC)]
        wtoks = [b.dma("sync", wt[i][:], qkvT[i * 128:(i + 1) * 128, :]) for i in range(NDC)]
        brow = b.sb("brow", [1, 3 * D])
        t_bias = b.dma("sync", brow[:], qkv_b[:])
        one11 = b.sb("one11", [1, 1])
        t_o11 = b.op("vector", "memset", one11[:], 1.0)
        at = [b.sb(f"at{i}", [128, S], BF16) for i in range(NDC)]
        ot = [b.sb(f"ot{i}", [128, S], BF16) for i in range(2)]
        pso = [b.ps(f"pso{i}", [128, 2048], F32) for i in range(2)]
        bcols = b.sb("bcols", [128, NOC])
        t_bc = []
        for oc in range(NOC):
            tm = b.mm(pso[0][:, oc:oc + 1], brow[:, oc * 128:(oc + 1) * 128], one11[:],
                      True, True, deps=[t_bias, t_o11])
            t_bc.append(tm)
        t_bcols = b.op("vector", "tensor_copy", bcols[:], pso[0][:, 0:NOC], deps=t_bc)
        afm_toks = [None] * NDC
        prev_ev = [t_bcols, None]
        prev_st = [None, None]
        outs = []
        for bi in range(BPC):
            for i in range(NDC):
                afm_toks[i] = b.dma("gpsimd", at[i][:], xn_fm[bi, i * 128:(i + 1) * 128, :],
                                    deps=[afm_toks[i], prev_st[0], prev_st[1]] if bi else [afm_toks[i]])
            for oc in range(NOC):
                k = oc % 2
                mms = []
                for nb in range(4):
                    dst = pso[k][:, nb * 512:(nb + 1) * 512]
                    for ic in range(NDC):
                        last = ic == NDC - 1
                        tm = b.mm(dst, wt[ic][:, oc * 128:(oc + 1) * 128],
                                  at[ic][:, nb * 512:(nb + 1) * 512],
                                  ic == 0, last,
                                  deps=[afm_toks[ic], wtoks[ic], prev_ev[k]],
                                  inc=last)
                    mms.append(tm)
                t_ev = b.op("scalar", "activation", ot[k][:], pso[k][:], AF.Identity,
                            bias=bcols[:, oc:oc + 1], deps=mms + [t_bcols, prev_st[k]])
                prev_ev[k] = t_ev
                prev_st[k] = b.dma("sync", qkv_fm[bi, oc * 128:(oc + 1) * 128, :], ot[k][:],
                                   deps=[t_ev])
                outs.append(prev_st[k])
        b.done(outs)


def emit_l0attn(b, qkv_fm, attn_fm):
    with b.stage("a0") as ctx:
        identf, tif, identb, tib = b.idents
        qkv = [b.sb(f"qkv{i}", [128, S], BF16) for i in range(24)]
        ao = [b.sb(f"ao{i}", [128, S], BF16) for i in range(NDC)]
        vp = [b.sb(f"vp{i}", [128, 16, 80], BF16) for i in range(2)]
        et = [b.sb(f"et{i}", [128, 1024], BF16) for i in range(4)]
        kaug = [b.sb(f"kaug{i}", [128, S], BF16) for i in range(2)]
        qaug = [b.sb(f"qaug{i}", [128, S], BF16) for i in range(2)]
        nums = [b.sb(f"nums{i}", [128, 1024]) for i in range(2)]
        denr = [b.sb(f"denr{i}", [1, 1024]) for i in range(2)]
        rden = [b.sb(f"rden{i}", [1, 1024]) for i in range(2)]
        onesc = b.sb("onesc", [1, 64])
        onescol = b.sb("onescol", [128, 1], BF16)
        qsq = b.sb("qsq", [64, S], BF16)
        ksq = b.sb("ksq", [64, S], BF16)
        km2 = b.sb("km2", [1, 2])
        t_on = b.op("vector", "memset", onesc[:], 1.0)
        t_oc = b.op("vector", "memset", onescol[:], 1.0)
        t_vone = [b.op("vector", "memset", vp[i][:, :, 64:80], 1.0) for i in range(2)]
        t_kone = [b.op("vector", "memset", kaug[i][64:65, :], -1.0) for i in range(2)]
        psS = [b.ps(f"psS{i}", [128, 1024], F32) for i in range(2)]
        psO = b.ps("psO", [128, 1024], F32)
        psB = b.ps("psB", [128, 1024], F32)
        qkv_toks = [None] * 24
        outs = []
        st_prev = {}
        prev = {}
        pending = []
        pending_pre = []

        def emit_preamble(bi, h):
            j = h % 2
            qt = qkv[h // 2]
            kt = qkv[8 + h // 2]
            base = 64 * (h % 2)
            t_ka = b.op("vector", "tensor_copy", kaug[j][0:64, :], kt[base:base + 64, :],
                        deps=[qkv_toks[8 + h // 2], t_kone[j], prev.get(("ka", j))])
            t_qa = b.op("vector", "tensor_copy", qaug[j][0:64, :], qt[base:base + 64, :],
                        deps=[qkv_toks[h // 2], prev.get(("qa", j))])
            t_ks = b.op("vector", "tensor_mul", ksq[:], kt[base:base + 64, :],
                        kt[base:base + 64, :], deps=[qkv_toks[8 + h // 2],
                                                     prev.get("ksq_rd")])
            t_qs = b.op("vector", "tensor_mul", qsq[:], qt[base:base + 64, :],
                        qt[base:base + 64, :], deps=[qkv_toks[h // 2],
                                                     prev.get("qsq_rd")])
            kn_mms = {0: [], 1: []}
            for half, row in ((0, 0), (1, 32)):
                for nb in range(2):
                    kn_mms[half].append(
                        b.mm(psB[row:row + 1, nb * 512:(nb + 1) * 512], onescol[0:64],
                             ksq[:, half * 1024 + nb * 512:half * 1024 + (nb + 1) * 512],
                             True, True,
                             deps=[t_ks, t_oc, prev.get("c1_rd"), prev.get("c2_rd"),
                                   prev.get("psB_mul")]))
            t_k1 = b.op("vector", "reduce_max", km2[:, 0:1], psB[0:1, :], axis=AX.X,
                        deps=kn_mms[0] + [prev.get("km2_rd")])
            t_k2 = b.op("vector", "reduce_max", km2[:, 1:2], psB[32:33, :], axis=AX.X,
                        deps=kn_mms[1])
            t_k3 = b.op("vector", "reduce_max", km2[:, 0:1], km2[:, 0:2], axis=AX.X,
                        deps=[t_k1, t_k2])
            qn_mms = {0: [], 1: []}
            for half, row in ((0, 0), (1, 32)):
                for nb in range(2):
                    qn_mms[half].append(
                        b.mm(psB[row:row + 1, nb * 512:(nb + 1) * 512], onescol[0:64],
                             qsq[:, half * 1024 + nb * 512:half * 1024 + (nb + 1) * 512],
                             True, True,
                             deps=[t_qs, t_oc, t_k1 if half == 0 else t_k2]))
            t_c1 = b.op("scalar", "activation", qaug[j][64:65, 0:1024], psB[0:1, :],
                        AF.Sqrt, scale=km2[:, 0:1],
                        deps=qn_mms[0] + [t_k3, t_qa, prev.get("c1_rd")])
            t_c2 = b.op("scalar", "activation", qaug[j][64:65, 1024:2048], psB[32:33, :],
                        AF.Sqrt, scale=km2[:, 0:1],
                        deps=qn_mms[1] + [t_k3, t_qa, prev.get("c2_rd")])
            prev["qsq_rd"] = qn_mms[1][-1]
            prev["ksq_rd"] = kn_mms[1][-1]
            prev["km2_rd"] = t_c2
            prev["c1_rd"] = t_c1
            prev["c2_rd"] = t_c2
            return {"t_ka": t_ka, "t_qa": t_qa, "t_c1": t_c1, "t_c2": t_c2}

        def emit_vps(bi2, h2):
            j2 = h2 % 2
            vt2 = qkv[16 + h2 // 2]
            t_vps2 = []
            for kc in range(NSC):
                t_vps2.append(b.dma("sync", vp[j2][:, kc, 0:64],
                                    vt2[64 * (h2 % 2):64 * (h2 % 2) + 64,
                                        kc * 128:(kc + 1) * 128],
                                    transpose=True,
                                    deps=[qkv_toks[16 + h2 // 2], t_vone[j2],
                                          prev.get(("vp", j2))], lane=kc % 3))
            return t_vps2

        def emit_next_loads(nbi, i, vtoks):
            if vtoks is None:
                pre2 = prev["_pre"]
                qkv_toks[i] = b.dma("gpsimd" if i % 2 else "sync", qkv[i][:],
                                    qkv_fm[nbi, i * 128:(i + 1) * 128, :],
                                    deps=[qkv_toks[i], prev.get("qsq_rd"),
                                          pre2["t_qa"]])
                kI = 8 + i
                qkv_toks[kI] = b.dma("gpsimd" if kI % 2 else "sync", qkv[kI][:],
                                    qkv_fm[nbi, kI * 128:(kI + 1) * 128, :],
                                    deps=[qkv_toks[kI], prev.get("ksq_rd"),
                                          pre2["t_ka"]])
            else:
                vI = 16 + i
                qkv_toks[vI] = b.dma("gpsimd" if vI % 2 else "sync", qkv[vI][:],
                                     qkv_fm[nbi, vI * 128:(vI + 1) * 128, :],
                                     deps=[qkv_toks[vI]] + vtoks)

        for bi in range(BPC):
            ao_w = {}
            if bi == 0:
                for i in [c + 8 * k for c in range(8) for k in range(3)]:
                    qkv_toks[i] = b.dma("gpsimd" if i % 2 else "sync", qkv[i][:],
                                        qkv_fm[bi, i * 128:(i + 1) * 128, :],
                                        deps=[qkv_toks[i]], lane=i % 3)
            pre = emit_preamble(bi, 0)
            pre_vps = emit_vps(bi, 0)
            for h in range(H0):
                j = h % 2
                t_vps = pre_vps
                t_ka, t_qa = pre["t_ka"], pre["t_qa"]
                t_c1, t_c2 = pre["t_c1"], pre["t_c2"]
                pending_pre.clear()
                if h + 1 < H0:
                    pending_pre.append(lambda bi=bi, h=h: prev.__setitem__(
                        "_pre", emit_preamble(bi, h + 1)))
                    pending_pre.append(lambda bi=bi, h=h: prev.__setitem__(
                        "_vps", emit_vps(bi, h + 1)))
                if bi + 1 < BPC:
                    if h % 2 == 0:
                        pending_pre.append(lambda bi=bi, h=h: emit_next_loads(
                            bi + 1, h // 2, None))
                    else:
                        pending_pre.append(lambda bi=bi, h=h, tv=t_vps: emit_next_loads(
                            bi + 1, h // 2, tv))
                for qh in range(2):
                    p = qh % 2
                    qsl = slice(qh * 1024, (qh + 1) * 1024)
                    t_avs = []
                    smm = {}

                    def emit_S(kc):
                        kk = kc % 2
                        tms = []
                        for nb in range(2):
                            tms.append(b.mm(psS[kk][:, nb * 512:(nb + 1) * 512],
                                            kaug[j][0:65, kc * 128:(kc + 1) * 128],
                                            qaug[j][0:65, qh * 1024 + nb * 512:qh * 1024 + (nb + 1) * 512],
                                            True, True,
                                            deps=[t_ka, t_qa, t_c2,
                                                  prev.get(("exp", kk, nb))],
                                            inc=True))
                        smm[kc] = tms

                    emit_S(0)
                    emit_S(1)
                    for kc in range(NSC):
                        kk = kc % 2
                        ke = kc % 4
                        if pending and qh == 0 and kc == 3:
                            for fn in pending:
                                fn()
                            pending.clear()
                        if pending_pre and qh == 1 and kc == 6:
                            for fn in pending_pre:
                                fn()
                            pending_pre.clear()
                        t_exp = {}
                        for nb in range(2):
                            t_exp[nb] = b.op("scalar", "activation",
                                             et[ke][:, nb * 512:(nb + 1) * 512],
                                             psS[kk][:, nb * 512:(nb + 1) * 512],
                                             AF.Exp, scale=0.125,
                                             deps=[smm[kc][nb],
                                                   prev.get(("av", ke, nb))])
                            prev[("exp", kk, nb)] = t_exp[nb]
                        if kc + 2 < NSC:
                            emit_S(kc + 2)
                        for nb in range(2):
                            tav = b.mm(psO[0:65, nb * 512:(nb + 1) * 512],
                                       vp[j][:, kc, 0:65],
                                       et[ke][:, nb * 512:(nb + 1) * 512],
                                       kc == 0, kc == NSC - 1,
                                       deps=[t_exp[nb]] + (t_vps +
                                                           [prev.get(("numden", nb))]
                                                           if kc == 0 else []),
                                       inc=True)
                            prev[("av", ke, nb)] = tav
                        t_avs.append(tav)
                    prev[("vp", j)] = t_avs[-1]
                    t_num, t_den = [], []
                    for nb in range(2):
                        csl = slice(nb * 512, (nb + 1) * 512)
                        tn = b.op("vector", "tensor_copy", nums[p][0:64, csl],
                                  psO[0:64, csl],
                                  deps=t_avs + [prev.get(("num_rd", p))])
                        td = b.op("vector", "tensor_copy", denr[p][:, csl],
                                  psO[64:65, csl], deps=[tn])
                        t_num.append(tn)
                        t_den.append(td)
                        prev[("numden", nb)] = td

                    def tail(p=p, h=h, qh=qh, qsl=qsl, t_num=t_num, t_den=t_den,
                             st_tok=st_prev.get(h // 2)):
                        t_l = b.op("scalar", "activation", rden[p][:], denr[p][:], AF.Ln,
                                   deps=t_den + [prev.get(("rden_rd", p))])
                        t_r = b.op("scalar", "activation", rden[p][:], rden[p][:], AF.Exp,
                                   scale=-1.0, deps=[t_l])
                        mb = []
                        for nb in range(2):
                            mb.append(b.mm(psB[64:128, nb * 512:(nb + 1) * 512], onesc[:],
                                           rden[p][:, nb * 512:(nb + 1) * 512], True, True,
                                           deps=[t_r, t_on, prev.get("psB_mul"),
                                                 prev.get("c1_rd"),
                                                 prev.get("c2_rd")]))
                        t_mul = b.op("vector", "tensor_tensor",
                                     ao[h // 2][64 * (h % 2):64 * (h % 2) + 64, qsl],
                                     nums[p][0:64, :], psB[64:128, :], ALU.mult,
                                     deps=mb + t_num + [st_tok])
                        prev["psB_mul"] = t_mul
                        prev[("num_rd", p)] = t_mul
                        prev[("den_rd", p)] = t_mul
                        prev[("rden_rd", p)] = t_mul
                        ao_w[h // 2] = t_mul
                    pending.append(tail)
                for fn in pending_pre:
                    fn()
                pending_pre.clear()
                prev[("ka", j)] = t_avs[-1]
                prev[("qa", j)] = t_avs[-1]
                pre = prev.pop("_pre", None)
                pre_vps = prev.pop("_vps", None)
            for fn in pending:
                fn()
            pending.clear()
            st_toks = {}
            for dc in range(NDC):
                if ao_w.get(dc) is None:
                    continue
                st_toks[dc] = b.dma("sync", attn_fm[bi, dc * 128:(dc + 1) * 128, :],
                                    ao[dc][:], deps=[ao_w[dc]])
                outs.append(st_toks[dc])
            st_prev = st_toks
        b.done(outs)


def emit_resid(b, x, s_tm, g_tm, x1, tag="rs"):
    with b.stage(tag) as ctx:
        W = 4 * D
        xt = [b.sb(f"xt{i}", [128, W]) for i in range(2)]
        st_ = [b.sb(f"st{i}", [128, W]) for i in range(2)]
        gt = [b.sb(f"gt{i}", [128, W]) for i in range(2)]
        prev = [None, None]
        outs = []
        for bi in range(BPC):
            for blk in range(4):
                k = blk % 2
                sl = slice(blk * 512, (blk + 1) * 512)
                vw = lambda t: t[bi, sl, :].rearrange("(a p) f -> p a f", p=128)
                v3 = lambda t: t[:].rearrange("p (a f) -> p a f", f=D)
                l1 = b.dma("sync", v3(xt[k]), vw(x), deps=[prev[k]], lane=k)
                l2 = b.dma("gpsimd", v3(st_[k]), vw(s_tm), deps=[prev[k]], lane=k)
                l3 = b.dma("sync", v3(gt[k]), vw(g_tm), deps=[prev[k]], lane=2)
                t1 = b.op("vector", "tensor_mul", st_[k][:], st_[k][:], gt[k][:],
                          deps=[l2, l3])
                t2 = b.op("vector", "tensor_add", xt[k][:], xt[k][:], st_[k][:],
                          deps=[t1, l1])
                prev[k] = b.dma("sync", vw(x1), v3(xt[k]), deps=[t2], lane=k)
                outs.append(prev[k])
        b.done(outs)


def emit_l0epi(b, xn_fm, WeffT, b0, attn_fm, projT, proj_b, x, x1):
    with b.stage("e0") as ctx:
        wg = [b.sb(f"wg{i}", [128, D], BF16) for i in range(NDC)]
        wp_ = [b.sb(f"wp{i}", [128, D], BF16) for i in range(NDC)]
        wg_t = [b.dma("sync", wg[i][:], WeffT[i * 128:(i + 1) * 128, :], lane=i % 3)
                for i in range(NDC)]
        wp_t = [b.dma("gpsimd", wp_[i][:], projT[i * 128:(i + 1) * 128, :], lane=i % 3)
                for i in range(NDC)]
        brow = b.sb("brow", [1, 2 * D])
        browb = b.sb("browb", [1, 2 * D], BF16)
        t_b1 = b.dma("sync", brow[:, 0:D], b0[:])
        t_b2 = b.dma("sync", brow[:, D:2 * D], proj_b[:])
        t_bb = b.op("vector", "tensor_copy", browb[:], brow[:], deps=[t_b1, t_b2])
        ones1b = b.sb("ones1b", [1, 128], BF16)
        t_o1 = b.op("vector", "memset", ones1b[:], 1.0)
        ag = [b.sb(f"ag{i}", [128, S], BF16) for i in range(NDC)]
        as_ = [b.sb(f"as{i}", [128, S], BF16) for i in range(NDC)]
        ag_t = [None] * NDC
        as_t = [None] * NDC
        gbuf = [b.sb(f"gbuf{i}", [128, D]) for i in range(NSC)]
        xb = [b.sb(f"xb{i}", [128, D]) for i in range(2)]
        yb = [b.sb(f"yb{i}", [128, D]) for i in range(2)]
        psg = [b.ps(f"psg{i}", [128, 1024], F32) for i in range(2)]
        pss = [b.ps(f"pss{i}", [128, 1024], F32) for i in range(2)]
        outs = []
        prev = {}
        for bi in range(BPC):
            for i in range(NDC):
                ag_t[i] = b.dma("gpsimd", ag[i][:], xn_fm[bi, i * 128:(i + 1) * 128, :],
                                deps=[ag_t[i], prev.get("glast")], lane=i % 3)
                as_t[i] = b.dma("sync", as_[i][:], attn_fm[bi, i * 128:(i + 1) * 128, :],
                                deps=[as_t[i], prev.get("slast")], lane=i % 3)
            for sc in range(NSC):
                k = sc % 2
                mms = []
                for nb in range(2):
                    dst = psg[k][:, nb * 512:(nb + 1) * 512]
                    for ic in range(NDC):
                        b.mm(dst, ag[ic][:, sc * 128:(sc + 1) * 128],
                             wg[ic][:, nb * 512:(nb + 1) * 512], ic == 0, False,
                             deps=[ag_t[ic], wg_t[ic], prev.get(f"gev{k}")], inc=False)
                    mms.append(b.mm(dst, ones1b[:], browb[:, nb * 512:(nb + 1) * 512],
                                    False, True, deps=[t_o1, t_bb]))
                t_g = b.op("scalar", "activation", gbuf[sc][:], psg[k][:], AF.Silu,
                           deps=mms + [prev.get(("g", sc))])
                prev[f"gev{k}"] = t_g
                prev[("g", sc)] = t_g
            prev["glast"] = prev[("g", NSC - 1)]
            for sc in range(NSC):
                k = sc % 2
                ssl = slice(sc * 128, (sc + 1) * 128)
                l_x = b.dma("sync", xb[k][:], x[bi, ssl, :],
                            deps=[prev.get(f"xadd{k}")], lane=k)
                mms = []
                for nb in range(2):
                    dst = pss[k][:, nb * 512:(nb + 1) * 512]
                    for ic in range(NDC):
                        b.mm(dst, as_[ic][:, sc * 128:(sc + 1) * 128],
                             wp_[ic][:, nb * 512:(nb + 1) * 512], ic == 0, False,
                             deps=[as_t[ic], wp_t[ic], prev.get(f"smul{k}")], inc=False)
                    mms.append(b.mm(dst, ones1b[:],
                                    browb[:, D + nb * 512:D + (nb + 1) * 512],
                                    False, True, deps=[t_o1, t_bb]))
                t_sg = b.op("vector", "tensor_mul", yb[k][:], pss[k][:], gbuf[sc][:],
                            deps=mms + [prev.get(("g", sc)), prev.get(f"yst{k}")])
                prev[f"smul{k}"] = t_sg
                prev[("g", sc)] = t_sg
                t_x = b.op("vector", "tensor_add", yb[k][:], yb[k][:], xb[k][:],
                           deps=[t_sg, l_x])
                d_o = b.dma("sync", x1[bi, ssl, :], yb[k][:], deps=[t_x], lane=k)
                prev[f"yst{k}"] = d_o
                prev[f"xadd{k}"] = d_o
                outs.append(d_o)
            prev["slast"] = prev["smul0"]
        b.done(outs)


def emit_l1qkv(b, xn_fm, WqT, WkT, WvT, bq, bk, bv, beta, Qt_fm, Kt_fm, Kt_tm, Vp_tm):
    with b.stage("q1") as ctx:
        identf, tif, identb, tib = b.idents
        wq = [b.sb(f"wq{i}", [128, D], BF16) for i in range(NDC)]
        wk = [b.sb(f"wk{i}", [128, D], BF16) for i in range(NDC)]
        wv = [b.sb(f"wv{i}", [128, D], BF16) for i in range(NDC)]
        wtok = {}
        for i in range(NDC):
            wtok[("q", i)] = b.dma("sync", wq[i][:], WqT[i * 128:(i + 1) * 128, :])
            wtok[("k", i)] = b.dma("gpsimd", wk[i][:], WkT[i * 128:(i + 1) * 128, :])
            wtok[("v", i)] = b.dma("sync", wv[i][:], WvT[i * 128:(i + 1) * 128, :])
        brows = b.sb("brows", [1, 3 * D])
        browsb = b.sb("browsb", [1, 3 * D], BF16)
        t_b1 = b.dma("sync", brows[:, 0:D], bq[:])
        t_b2 = b.dma("sync", brows[:, D:2 * D], bk[:])
        t_b3 = b.dma("sync", brows[:, 2 * D:3 * D], bv[:])
        t_bb = b.op("vector", "tensor_copy", browsb[:], brows[:], deps=[t_b1, t_b2, t_b3])
        ones1b = b.sb("ones1b", [1, 128], BF16)
        t_o1 = b.op("vector", "memset", ones1b[:], 1.0)
        ones1f = b.sb("ones1f", [1, 128])
        t_o1f = b.op("vector", "memset", ones1f[:], 1.0)
        berow = b.sb("berow", [1, 1])
        becol = b.sb("becol", [128, 1])
        t_be = b.dma("sync", berow[:], beta[:])
        t_be = b.op("scalar", "activation", berow[:], berow[:], AF.Exp, deps=[t_be])
        psq = [b.ps(f"psq{i}", [128, 1024], F32) for i in range(3)]
        mbe = b.mm(psq[0][:, 0:1], ones1f[:], berow[:], True, True, deps=[t_be, t_o1f])
        t_bec = b.op("vector", "tensor_copy", becol[:], psq[0][:, 0:1], deps=[mbe])

        xn = [b.sb(f"xn{i}", [128, S], BF16) for i in range(NDC)]
        gmax = [b.sb(f"gmax{i}", [128, 64]) for i in range(4)]
        sub = [b.sb(f"sub{i}", [128, 1024]) for i in range(4)]
        ex = [b.sb(f"ex{i}", [128, 1024]) for i in range(2)]
        s16 = [b.sb(f"s16_{i}", [128, 64]) for i in range(2)]
        rcp = [b.sb(f"rcp{i}", [128, 64]) for i in range(2)]
        qn = [b.sb(f"qn{i}", [128, 1024], BF16) for i in range(2)]
        kn = [b.sb(f"kn{i}", [128, 1024], BF16) for i in range(2)]
        vpb = [b.sb(f"vpb{i}", [128, 64, 17], BF16) for i in range(2)]
        tpm = TpMachine(b, identb, tib, evac_eng="scalar")
        xtok = [None] * NDC
        outs = []
        prev = {}
        vone = [b.op("vector", "memset", vpb[i][:, :, 16:17], 1.0) for i in range(2)]
        for bi in range(BPC):
            for i in range(NDC):
                xtok[i] = b.dma("gpsimd", xn[i][:], xn_fm[bi, i * 128:(i + 1) * 128, :],
                                deps=[xtok[i], prev.get("last"), prev.get("vev0"),
                                      prev.get("vev1")], lane=i % 3)
            pend = []
            for sc in range(NSC):
                ssl = slice(sc * 128, (sc + 1) * 128)
                kk = sc % 2
                allmms = {}
                for ti, (wt, wkey, boff) in enumerate(
                        [(wq, "q", 0), (wk, "k", D), (wv, "v", 2 * D)]):
                    ps = psq[ti]
                    mms = []
                    for nb in range(2):
                        dst = ps[:, nb * 512:(nb + 1) * 512]
                        for ic in range(NDC):
                            b.mm(dst, xn[ic][:, ssl], wt[ic][:, nb * 512:(nb + 1) * 512],
                                 ic == 0, False,
                                 deps=[xtok[ic], wtok[(wkey, ic)], t_bec,
                                       prev.get(f"ps_rd{ti}")],
                                 inc=False)
                        mms.append(b.mm(dst, ones1b[:],
                                        browsb[:, boff + nb * 512:boff + (nb + 1) * 512],
                                        False, True, deps=[t_o1, t_bb]))
                    allmms[wkey] = mms
                for fn in pend:
                    fn()
                pend = []
                for ti, (wkey, nbuf, fm_dram, tm_dram) in enumerate(
                        [("q", qn, Qt_fm, None), ("k", kn, Kt_fm, Kt_tm)]):
                    ps = psq[ti]
                    pid = ti * 2 + kk
                    mms = allmms[wkey]
                    ps3 = ps[:].rearrange("p (g k) -> p g k", k=16)
                    t_gm = b.op("vector", "reduce_max",
                                gmax[pid][:].rearrange("p (g o) -> p g o", o=1), ps3,
                                axis=AX.X, deps=mms + [prev.get(f"gmax_rd{pid}")])
                    t_sub = b.op("vector", "tensor_tensor",
                                 sub[pid][:].rearrange("p (g k) -> p g k", k=16), ps3,
                                 gmax[pid][:].rearrange("p (g o) -> p g o", o=1).broadcast_to(
                                     [128, 64, 16]), ALU.subtract,
                                 deps=[t_gm, prev.get(f"sub_rd{pid}")])
                    prev[f"ps_rd{ti}"] = t_sub
                    prev[f"gmax_rd{pid}"] = t_sub
                    if wkey == "q":
                        t_n = b.op("scalar", "activation", nbuf[kk][:], sub[pid][:],
                                   AF.Exp, scale=becol[:],
                                   deps=[t_sub, t_bec, prev.get(f"{wkey}n{kk}")])
                        prev[f"sub_rd{pid}"] = t_n
                    else:
                        t_exp = b.op("scalar", "activation", ex[kk][:], sub[pid][:], AF.Exp,
                                     scale=becol[:], deps=[t_sub, t_bec,
                                                           prev.get(f"ex_rd{kk}"),
                                                           prev.get(f"ex_rd2{kk}")])
                        prev[f"sub_rd{pid}"] = t_exp
                        t_ss = b.op("vector", "reduce_sum",
                                    s16[kk][:].rearrange("p (g o) -> p g o", o=1),
                                    ex[kk][:].rearrange("p (g k) -> p g k", k=16), axis=AX.X,
                                    deps=[t_exp, prev.get(f"s16_rd{kk}")])
                        t_rcp = b.op("vector", "reciprocal", rcp[kk][:], s16[kk][:],
                                     deps=[t_ss, prev.get(f"rcp_rd{kk}")])
                        prev[f"s16_rd{kk}"] = t_rcp
                        t_n = b.op("vector", "tensor_tensor",
                                   nbuf[kk][:].rearrange("p (g k) -> p g k", k=16),
                                   ex[kk][:].rearrange("p (g k) -> p g k", k=16),
                                   rcp[kk][:].rearrange("p (g o) -> p g o", o=1).broadcast_to(
                                       [128, 64, 16]), ALU.mult,
                                   deps=[t_rcp, t_exp, prev.get(f"{wkey}n{kk}")])
                        prev[f"rcp_rd{kk}"] = t_n
                        prev[f"ex_rd{kk}"] = t_ss
                        prev[f"ex_rd2{kk}"] = t_n

                    def flush(wkey=wkey, nbuf=nbuf, fm_dram=fm_dram, tm_dram=tm_dram,
                              kk=kk, ssl=ssl, t_n=t_n, bi=bi):
                        rd = []
                        if tm_dram is not None:
                            rd.append(b.dma("sync", tm_dram[bi, ssl, :], nbuf[kk][:],
                                            deps=[t_n], lane=2))
                            outs.append(rd[-1])
                        for dc in range(NDC):
                            t_tp, t_dma = tpm.to_dram(nbuf[kk][:, dc * 128:(dc + 1) * 128],
                                                      fm_dram[bi, dc * 128:(dc + 1) * 128, ssl],
                                                      deps=[t_n])
                            rd.append(t_tp)
                            outs.append(t_dma)
                        prev[f"{wkey}n{kk}"] = rd[-1]
                        prev["last"] = rd[-1]
                    pend.append(flush)
                t_vev = b.op("scalar", "activation", vpb[kk][:, :, 0:16],
                             psq[2][:].rearrange("p (h e) -> p h e", e=16), AF.Identity,
                             deps=allmms["v"] + [prev.get(f"vst{kk}"), vone[kk]])
                prev["ps_rd2"] = t_vev
                t_vst = b.dma("sync", Vp_tm[bi, ssl, :],
                              vpb[kk][:].rearrange("p h e -> p (h e)"), deps=[t_vev],
                              lane=kk)
                prev[f"vev{kk}"] = t_vev
                prev[f"vst{kk}"] = t_vst
                outs.append(t_vst)
            for fn in pend:
                fn()
        b.done(outs)


def emit_l1attn(b, Qt_fm, Kt_fm, Kt_tm, Vp_tm, attn_fm, ng=8):
    NG = ng
    GW = 8 * 17
    with b.stage("a1") as ctx:
        identf, tif, identb, tib = b.idents
        ind = b.sb("ind", [128, 8])
        t_i = b.op("gpsimd", "memset", ind[:], 1.0)
        t_i = b.op("gpsimd", "affine_select", out=ind[:], in_=ind[:], compare_op=ALU.is_ge,
                   fill=0.0, base=0, pattern=[[-16, 8]], channel_multiplier=1, deps=[t_i])
        t_i = b.op("gpsimd", "affine_select", out=ind[:], in_=ind[:], compare_op=ALU.is_ge,
                   fill=0.0, base=15, pattern=[[16, 8]], channel_multiplier=-1, deps=[t_i])
        indb = b.sb("indb", [128, 8], BF16)
        t_i = b.op("vector", "tensor_copy", indb[:], ind[:], deps=[t_i])
        mask8 = b.sb("mask8", [128, 1024], BF16)
        t_m8 = b.op("gpsimd", "memset", mask8[:], 1.0)
        t_m8 = b.op("gpsimd", "affine_select", out=mask8[:], in_=mask8[:],
                    compare_op=ALU.is_ge, fill=0.0, base=0,
                    pattern=[[0, 8], [1, 128]], channel_multiplier=-1, deps=[t_m8])
        bmask = b.sb("bmask", [128, GW], BF16)
        t_bm = b.op("gpsimd", "memset", bmask[:], 1.0)
        t_bm = b.op("gpsimd", "affine_select", out=bmask[:], in_=bmask[:],
                    compare_op=ALU.is_ge, fill=0.0, base=0,
                    pattern=[[-16, 8], [0, 17]], channel_multiplier=1, deps=[t_bm])
        t_bm = b.op("gpsimd", "affine_select", out=bmask[:], in_=bmask[:],
                    compare_op=ALU.is_ge, fill=0.0, base=15,
                    pattern=[[16, 8], [0, 17]], channel_multiplier=-1, deps=[t_bm])
        qgb = [[b.sb(f"qg{p}_{u}", [128, S], BF16) for u in range(2)] for p in range(2)]
        kgb = [[b.sb(f"kg{p}_{u}", [128, S], BF16) for u in range(2)] for p in range(2)]
        ktg = [[b.sb(f"ktg{u}_{i}", [128, 128], BF16) for i in range(NSC)] for u in range(2)]
        vg = [[b.sb(f"vg{u}_{i}", [128, GW], BF16) for i in range(NSC)] for u in range(2)]
        qmb = [[b.sb(f"qmb{u}_{p}", [128, 1024], BF16) for p in range(2)]
               for u in range(2)]
        smt = [b.sb(f"smt{u}", [128, 1024], BF16) for u in range(2)]
        sraw = [b.sb(f"sraw{u}", [128, 1024], BF16) for u in range(2)]
        state = [b.sb(f"state{u}", [128, GW]) for u in range(2)]
        state_bd = [b.sb(f"state_bd{u}", [128, GW], BF16) for u in range(2)]
        dens = [b.sb(f"dens{u}", [128, 8]) for u in range(2)]
        rdens = [b.sb(f"rdens{u}", [128, 8]) for u in range(2)]
        at_tm = [b.sb(f"at{i}", [128, D], BF16) for i in range(NSC)]
        psS = [b.ps(f"psS{u}", [128, 1024], F32) for u in range(2)]
        psO = [b.ps(f"psO{u}", [128, 512], F32) for u in range(2)]
        psT = [b.ps(f"psT{u}", [128, 512], F32) for u in range(2)]
        fmbt = [b.sb(f"fmbt{i}", [128, 8, 128], BF16) for i in range(2)]
        outs = []
        ld_prev = {}
        prev = {}
        qtoks = {}
        for bi in range(BPC):
            for gp in range(NG // 2):
                par = gp % 2
                toks = [{}, {}]
                for u in range(2):
                    g = gp * 2 + u
                    if (bi, gp) == (0, 0):
                        qtoks[(par, u)] = b.dma("sync", qgb[par][u][:],
                                                Qt_fm[bi, g * 128:(g + 1) * 128, :],
                                                lane=u)
                        qtoks[("k", par, u)] = b.dma("gpsimd", kgb[par][u][:],
                                                     Kt_fm[bi, g * 128:(g + 1) * 128, :],
                                                     lane=u)
                    t_kt, t_v = [], []
                    for c in range(NSC):
                        t_kt.append(b.dma("sync", ktg[u][c][:],
                                          Kt_tm[bi, c * 128:(c + 1) * 128,
                                                g * 128:(g + 1) * 128],
                                          deps=[ld_prev.get(("kt", u, c))],
                                          lane=2 - c % 2))
                        t_v.append(b.dma("gpsimd", vg[u][c][:],
                                         Vp_tm[bi, c * 128:(c + 1) * 128,
                                               g * GW:(g + 1) * GW],
                                         deps=[ld_prev.get(("v", u, c))],
                                         lane=2 - c % 2))
                    t_st = b.op("vector", "memset", state[u][:], 0.0,
                                deps=[prev.get(("st_rd", u))])
                    toks[u] = {"t_q": qtoks[(par, u)], "t_k": qtoks[("k", par, u)],
                               "t_kt": t_kt, "t_v": t_v, "t_st": t_st}

                def qm_jit(c, u):
                    tk = toks[u]
                    t = b.op("vector", "tensor_tensor",
                             qmb[u][c % 2][:].rearrange("p (g f) -> p g f", f=128),
                             qgb[par][u][:, c * 128:(c + 1) * 128].rearrange(
                                 "p (o f) -> p o f", o=1).broadcast_to([128, 8, 128]),
                             indb[:].rearrange("p (g o) -> p g o", o=1).broadcast_to(
                                 [128, 8, 128]), ALU.mult,
                             deps=[tk["t_q"], t_i, prev.get(("qmrd", u, c % 2))])
                    prev[("qm", u, c % 2)] = t

                for u in range(2):
                    qm_jit(0, u)
                    qm_jit(1, u)
                for c in range(NSC):
                    t_sms = {}
                    t_sus = {}
                    for u in range(2):
                        tk = toks[u]
                        t_sm = []
                        for h2 in range(8):
                            t_sm.append(b.mm(psS[u][:, h2 * 128:(h2 + 1) * 128],
                                             kgb[par][u][:, c * 128:(c + 1) * 128],
                                             qmb[u][c % 2][:, h2 * 128:(h2 + 1) * 128],
                                             True, True,
                                             deps=[tk["t_k"], prev.get(("qm", u, c % 2)),
                                                   prev.get(("sev", u))]))
                        t_sms[u] = t_sm
                        prev[("qmrd", u, c % 2)] = t_sm[-1]
                        t_sus[u] = b.mm(psT[u][0:128, 0:GW], ktg[u][c][:], vg[u][c][:],
                                        True, True,
                                        deps=[tk["t_kt"][c], tk["t_v"][c],
                                              prev.get(("st_add", u))])
                    if c == 4 and not (bi == BPC - 1 and gp == NG // 2 - 1):
                        nbi, ngp = (bi, gp + 1) if gp + 1 < NG // 2 else (bi + 1, 0)
                        npar = 1 - par
                        for u in range(2):
                            g2 = ngp * 2 + u
                            qtoks[(npar, u)] = b.dma(
                                "sync", qgb[npar][u][:],
                                Qt_fm[nbi, g2 * 128:(g2 + 1) * 128, :],
                                deps=[qtoks.get((npar, u)),
                                      ld_prev.get(("qg", npar, u))], lane=u)
                            qtoks[("k", npar, u)] = b.dma(
                                "gpsimd", kgb[npar][u][:],
                                Kt_fm[nbi, g2 * 128:(g2 + 1) * 128, :],
                                deps=[qtoks.get(("k", npar, u)),
                                      ld_prev.get(("kg", npar, u))], lane=u)
                    for u in range(2):
                        g = gp * 2 + u
                        tk = toks[u]
                        t_sm = t_sms[u]
                        t_cp = b.op("scalar", "activation", sraw[u][:], psS[u][:],
                                    AF.Identity, deps=t_sm + [prev.get(("sraw", u))])
                        prev[("sev", u)] = t_cp
                        t_sev = b.op("vector", "tensor_tensor", smt[u][:], sraw[u][:],
                                     mask8[:], ALU.mult,
                                     deps=[t_cp, t_m8, prev.get(("intra", u))])
                        prev[("sraw", u)] = t_sev
                        first = None
                        if c > 0:
                            t_bd = b.op("vector", "tensor_tensor", state_bd[u][:],
                                        state[u][:], bmask[:], ALU.mult,
                                        deps=[t_bm, prev[("st_add", u)],
                                              prev.get(("inter", u))])
                            first = b.mm(psO[u][0:128, 0:GW],
                                         qgb[par][u][:, c * 128:(c + 1) * 128],
                                         state_bd[u][:], True, False,
                                         deps=[tk["t_q"], t_bd, prev.get(("oev", u))],
                                         inc=True)
                            prev[("inter", u)] = first
                        t_im = None
                        for h2 in range(8):
                            t_im = b.mm(psO[u][0:128, h2 * 17:(h2 + 1) * 17],
                                        smt[u][:, h2 * 128:(h2 + 1) * 128],
                                        vg[u][c][:, h2 * 17:(h2 + 1) * 17],
                                        first is None and h2 == 0, h2 == 7,
                                        deps=[t_sev, tk["t_v"][c]] +
                                             ([prev.get(("oev", u))] if first is None else []),
                                        inc=h2 == 7)
                        prev[("intra", u)] = t_im
                        t_sa = b.op("vector", "tensor_add", state[u][:], state[u][:],
                                    psT[u][0:128, 0:GW], deps=[t_sus[u], tk["t_st"]])
                        prev[("st_add", u)] = t_sa
                        t_cpd = b.op("vector", "tensor_copy", dens[u][:],
                                     psO[u][0:128, 0:GW].rearrange(
                                         "p (h e) -> p h e", e=17)[:, :, 16],
                                     deps=[t_im, first, prev.get(("rd_rd", u))])
                        t_r = b.op("vector", "reciprocal", rdens[u][:], dens[u][:],
                                   deps=[t_cpd])
                        t_o = b.op("vector", "tensor_tensor",
                                   at_tm[c][:, g * 128:(g + 1) * 128].rearrange(
                                       "p (h e) -> p h e", e=16),
                                   psO[u][0:128, 0:GW].rearrange(
                                       "p (h e) -> p h e", e=17)[:, :, 0:16],
                                   rdens[u][:].rearrange("p (h o) -> p h o", o=1).broadcast_to(
                                       [128, 8, 16]), ALU.mult,
                                   deps=[t_r, t_im, prev.get(("at", c))])
                        prev[("oev", u)] = t_o
                        prev[("rd_rd", u)] = t_o
                        prev[("at", c)] = t_o
                        if c + 2 < NSC:
                            qm_jit(c + 2, u)
                for u in range(2):
                    prev[("st_rd", u)] = prev[("oev", u)]
                    ld_prev[("qg", par, u)] = prev[("oev", u)]
                    ld_prev[("kg", par, u)] = prev[("oev", u)]
                    for c in range(NSC):
                        ld_prev[("kt", u, c)] = prev[("st_add", u)]
                        ld_prev[("v", u, c)] = prev[("oev", u)]
            fmba = [psS[0], psS[1]]
            for c in range(NSC):
                kE = c % 2
                pt = fmba[kE][:].bitcast(BF16)
                evs = []
                last = None
                for dc in range(NDC if NG == 8 else NG):
                    t_tp = tp_b16(b, pt[:, dc * 128:(dc + 1) * 128],
                                  at_tm[c][:, dc * 128:(dc + 1) * 128], identb,
                                  deps=[prev[("at", c)], tib, prev.get(("fmev", kE)),
                                        prev.get(("sev", kE)) if c < 2 else None])
                    t_ev = b.op("vector", "tensor_copy", fmbt[kE][:, dc, :],
                                pt[:, dc * 128:(dc + 1) * 128],
                                deps=[t_tp, prev.get(("fmdma", kE))])
                    evs.append(t_ev)
                    last = t_ev
                prev[("fmev", kE)] = last
                prev[("fmdma", kE)] = b.dma("sync",
                                            attn_fm[bi, :, c * 128:(c + 1) * 128].rearrange(
                                                "(a p) f -> p a f", p=128),
                                            fmbt[kE][:], deps=evs, lane=c % 3)
                prev[("at", c)] = last
                prev[("sev", kE)] = last
                outs.append(prev[("fmdma", kE)])
        b.done(outs)


def emit_l1rms(b, xn_tm, gamma_x, xnn_tm, xnn_fm, barrier=True):
    with b.stage("rm", barrier=barrier) as ctx:
        identf, tif, identb, tib = b.idents
        tpm = TpMachine(b, identb, tib, n=2)
        fmbr = [b.sb(f"fmbr{i}", [128, 8, 128], BF16) for i in range(2)]
        fm_dma = {}
        ones1f = b.sb("ones1f", [1, 128])
        t_o1 = b.op("vector", "memset", ones1f[:], 1.0)
        g_row = b.sb("g_row", [1, D])
        t_gr = b.dma("sync", g_row[:], gamma_x[:])
        gb = b.sb("gb", [128, D], BF16)
        psg = b.ps("psg", [128, 1024], F32)
        mgb = [b.mm(psg[:, nb * 512:(nb + 1) * 512], ones1f[:],
                    g_row[:, nb * 512:(nb + 1) * 512], True, True,
                    deps=[t_o1, t_gr]) for nb in range(2)]
        t_gb = b.op("vector", "tensor_copy", gb[:], psg[:], deps=mgb)
        xt = [b.sb(f"xt{i}", [128, D], BF16) for i in range(2)]
        xo = [b.sb(f"xo{i}", [128, D], BF16) for i in range(2)]
        sed = b.sb("sed", [128, D])
        snq = b.sb("snq", [128, 1])
        srl = b.sb("srl", [128, 1])
        prevS = [{}, {}]
        outs = []
        for bi in range(BPC):
            for sc in range(NSC):
                k = sc % 2
                ld = b.dma("sync", xt[k][:], xn_tm[bi, sc * 128:(sc + 1) * 128, :],
                           deps=[prevS[k].get("rd")])
                t_sq = b.op("scalar", "activation", sed[:], xt[k][:], AF.Square,
                            accum_out=snq[:], deps=[ld, prevS[1 - k].get("t_n2")])
                t_n2 = b.op("vector", "tensor_scalar_max", snq[:], snq[:], 1e-24,
                            deps=[t_sq])
                t_ln = b.op("scalar", "activation", srl[:], snq[:], AF.Ln,
                            deps=[t_n2, prevS[1 - k].get("t_sn")])
                t_ex = b.op("scalar", "activation", srl[:], srl[:], AF.Exp, scale=-0.5,
                            deps=[t_ln])
                t_rl = b.op("vector", "tensor_scalar_mul", srl[:], srl[:], 32.0,
                            deps=[t_ex])
                t_sn = b.op("vector", "scalar_tensor_tensor", xo[k][:], xt[k][:], srl[:],
                            gb[:], ALU.mult, ALU.mult,
                            deps=[t_rl, t_gb, ld] + prevS[k].get("senr", []))
                st_dma = b.dma("sync", xnn_tm[bi, sc * 128:(sc + 1) * 128, :], xo[k][:],
                               deps=[t_sn], lane=k)
                outs.append(st_dma)
                senr = [st_dma]
                evs = []
                for dc in range(NDC):
                    t_tp, t_ev = tpm.to_sbuf(xo[k][:, dc * 128:(dc + 1) * 128],
                                             fmbr[k][:, dc, :], deps=[t_sn],
                                             extra_ev_deps=[fm_dma.get(k)])
                    senr.append(t_tp)
                    evs.append(t_ev)
                fm_dma[k] = b.dma("gpsimd",
                                  xnn_fm[bi, :, sc * 128:(sc + 1) * 128].rearrange(
                                      "(a p) f -> p a f", p=128),
                                  fmbr[k][:], deps=evs, lane=k)
                outs.append(fm_dma[k])
                prevS[k] = {"rd": t_sn, "t_n2": t_n2, "senr": senr, "t_sn": t_sn}
        b.done(outs)


def emit_l1ffnA(b, xnn_fm, seT, ncm, rmrow, logitsT):
    NEC = ET // 128
    with b.stage("fa") as ctx:
        identf, tif, identb, tib = b.idents
        se = [b.sb(f"se{i}", [128, ET], BF16) for i in range(NDC)]
        setok = [b.dma("gpsimd" if i % 2 else "sync", se[i][:],
                       seT[i * 128:(i + 1) * 128, :]) for i in range(NDC)]
        xn = [b.sb(f"xn{i}", [128, S], BF16) for i in range(NDC)]
        psL = [b.ps(f"psL{i}", [128, 2048], F32) for i in range(2)]
        t64 = b.sb("t64", [64, S])
        t32tmp = b.sb("t32tmp", [32, S])
        rm32 = b.sb("rm32", [32, S])
        ncm_s = b.sb("ncm_s", [128, 32])
        nctmp = b.sb("nctmp", [128, 2])
        rmc = b.sb("rmc", [128, 16])
        rmr = b.sb("rmr", [1, S])
        lbuf = [b.sb(f"lbuf{i}", [128, S]) for i in range(2)]
        xtok = [None] * NDC
        outs = []
        prev = {}
        for bi in range(BPC):
            for i in range(NDC):
                xtok[i] = b.dma("gpsimd", xn[i][:],
                                xnn_fm[bi, i * 128:(i + 1) * 128, :],
                                deps=[xtok[i], prev.get("lastmm")])
            for etc in range(NEC):
                kk = etc % 2
                mms = []
                for nb in range(4):
                    dst = psL[kk][:, nb * 512:(nb + 1) * 512]
                    for ic in range(NDC):
                        tmm = b.mm(dst, se[ic][:, etc * 128:(etc + 1) * 128],
                                   xn[ic][:, nb * 512:(nb + 1) * 512],
                                   ic == 0, ic == NDC - 1,
                                   deps=[setok[ic], xtok[ic], prev.get(f"rd{kk}"),
                                         prev.get(("lb", kk))],
                                   inc=ic == NDC - 1)
                    mms.append(tmm)
                t_nc1 = b.op("vector", "reduce_max", nctmp[:, 0:1],
                             psL[kk][:, 0:1024], axis=AX.X,
                             deps=mms[0:2] + [prev.get("nctmp_rd")])
                t_h0a = b.op("vector", "tensor_copy", t64[:, 0:1024],
                             psL[kk][64:128, 0:1024],
                             deps=mms[0:2] + [prev.get("t32")])
                t_h1a = b.op("vector", "tensor_max", t64[:, 0:1024], t64[:, 0:1024],
                             psL[kk][0:64, 0:1024], deps=[t_h0a])
                t_nc2 = b.op("vector", "reduce_max", nctmp[:, 1:2],
                             psL[kk][:, 1024:2048], axis=AX.X, deps=mms[2:4])
                t_h0b = b.op("vector", "tensor_copy", t64[:, 1024:2048],
                             psL[kk][64:128, 1024:2048],
                             deps=mms[2:4] + [prev.get("t32")])
                t_h1b = b.op("vector", "tensor_max", t64[:, 1024:2048],
                             t64[:, 1024:2048], psL[kk][0:64, 1024:2048],
                             deps=[t_h0b])
                t_nc = b.op("vector", "reduce_max", ncm_s[:, etc:etc + 1],
                            nctmp[:, 0:2], axis=AX.X, negate=True,
                            deps=[t_nc1, t_nc2, prev.get("ncm_rd")])
                prev["nctmp_rd"] = t_nc
                t_h1 = t_h1b
                b.op("scalar", "activation", lbuf[kk][:, 0:1024],
                     psL[kk][:, 0:1024], AF.Identity,
                     deps=[t_h1a, t_nc1, prev.get(("lb_dma", kk))])
                t_lb = b.op("scalar", "activation", lbuf[kk][:, 1024:2048],
                            psL[kk][:, 1024:2048], AF.Identity,
                            deps=[t_h1b, t_nc2])
                prev[("lb_dma", kk)] = b.dma("gpsimd", logitsT[bi, etc],
                                             lbuf[kk][:], deps=[t_lb], lane=kk)
                outs.append(prev[("lb_dma", kk)])
                prev[("lb", kk)] = t_lb
                t_cp = b.op("vector", "tensor_copy", t32tmp[:], t64[32:64, :],
                            deps=[t_h1, prev.get("t32tmp_rd")])
                t_h2 = b.op("vector", "tensor_max", t64[0:32, :], t64[0:32, :],
                            t32tmp[:], deps=[t_cp])
                prev["t32tmp_rd"] = t_h2
                if etc == 0:
                    t_32 = b.op("vector", "tensor_copy", rm32[:], t64[0:32, :],
                                deps=[t_h2, prev.get("rm_rd")])
                else:
                    t_32 = b.op("vector", "tensor_max", rm32[:], rm32[:], t64[0:32, :],
                                deps=[t_h2])
                prev["t32"] = t_32
                prev[f"rd{kk}"] = t_h1
                prev["ncm_rd"] = t_nc
                prev["lastmm"] = t_nc
            t_prev_r = None
            t_rms = []
            for j in range(NSC):
                tmm = b.mm(psL[0][0:128, 0:32], rm32[:, j * 128:(j + 1) * 128],
                           identf[0:32, 0:32], True, True,
                           deps=[t_32, tif, t_prev_r] +
                                ([prev.get("rd0"), prev.get(("lb", 0))]
                                 if j == 0 else []))
                t_r = b.op("vector", "reduce_max", rmc[:, j:j + 1], psL[0][0:128, 0:32],
                           axis=AX.X, deps=[tmm, prev.get("rmc_rd")])
                t_prev_r = t_r
                t_rms.append(t_r)
            rr_mms = []
            for j in range(NSC):
                tmm = b.mm(psL[1][0:1, j * 128:(j + 1) * 128], rmc[:, j:j + 1],
                           identf[:], True, True,
                           deps=[t_rms[j], tif] +
                                ([prev.get("rd1"), prev.get("rmr_rd"),
                                  prev.get(("lb", 1))] if j == 0 else []))
                rr_mms.append(tmm)
            prev["rmc_rd"] = rr_mms[-1]
            t_rr = b.op("scalar", "activation", rmr[:], psL[1][0:1, :], AF.Identity,
                        deps=rr_mms)
            prev["rd1"] = t_rr
            prev["rd0"] = t_rms[-1]
            d1 = b.dma("sync", rmrow[bi], rmr[:], deps=[t_rr])
            d2 = b.dma("sync", ncm[bi], ncm_s[:], deps=[prev["ncm_rd"]])
            prev["rmr_rd"] = d1
            prev["ncm_rd"] = d2
            prev["rm_rd"] = t_rms[-1]
            outs += [d1, d2]
        b.done(outs)


def emit_l1ffnB(b, xnn_fm, xnn_tm, seT, ncm, rmrow, slots_fm, E2T, logitsT):
    NEC = ET // 128
    with b.stage("fb") as ctx:
        identf, tif, identb, tib = b.idents
        tpm = TpMachine(b, identb, tib, n=2)
        ones1f = b.sb("ones1f", [1, 128])
        t_o1 = b.op("vector", "memset", ones1f[:], 1.0)
        lL = [b.sb(f"lL{i}", [128, S]) for i in range(2)]
        ltok = [None, None]
        xnt = [b.sb(f"xnt{i}", [128, D], BF16) for i in range(NSC)]
        ncm_s = b.sb("ncm_s", [128, 32])
        rmr = b.sb("rmr", [1, S])
        rmb = b.sb("rmb", [128, S])
        ebuf = [[b.sb(f"ebuf{p}_{i}", [128, 1024], BF16) for i in range(2)]
                for p in range(2)]
        zb = [b.sb(f"zb{p}", [128, 1024]) for p in range(2)]
        e2b = [[b.sb(f"e2b{p}_{i}", [128, 1024], BF16) for i in range(2)]
               for p in range(2)]
        eblk = [b.sb(f"eblk{i}", [128, 16, 256], BF16) for i in range(2)]
        sev = b.sb("sev", [128, 1024], BF16)
        cs = [b.sb(f"cs{p}", [128, 2]) for p in range(2)]
        rcs = [b.sb(f"rcs{p}", [128, 1]) for p in range(2)]
        rcsc = [b.sb(f"rcsc{p}", [128, 1], BF16) for p in range(2)]
        rrow = b.sb("rrow", [1, 256], BF16)
        rcsb = [b.sb(f"rcsb{p}", [128, 256], BF16) for p in range(2)]
        ones1b = b.sb("ones1b", [1, 128], BF16)
        t_o1b = b.op("vector", "memset", ones1b[:], 1.0)
        psL = [b.ps(f"psL{i}", [128, 1024], F32) for i in range(2)]
        psSl = b.ps("psSl", [128, 1024], F32)
        xttok = [None] * NSC
        outs = []
        prev = {}
        smms = []

        def l_load(bi2, etc2):
            lk2 = etc2 % 2
            ltok[lk2] = b.dma("gpsimd", lL[lk2][:], logitsT[bi2, etc2],
                              deps=[ltok[lk2], prev.get(("lL_rd", lk2))], lane=0)

        for bi in range(BPC):
            for i in range(NSC):
                xttok[i] = b.dma("sync", xnt[i][:],
                                 xnn_tm[bi, i * 128:(i + 1) * 128, :],
                                 deps=[xttok[i], prev.get("lastA")])
            l_load(bi, 0)
            l_load(bi, 1)
            t_ncm = b.dma("sync", ncm_s[:], ncm[bi],
                          deps=prev.get("ncm_rd") or [None])
            t_rmr = b.dma("sync", rmr[:], rmrow[bi], deps=[prev.get("rmr_rd")])
            t_rmb = []
            for half in range(2):
                mmsb = [b.mm(psL[half][:, nb * 512:(nb + 1) * 512], ones1f[:],
                             rmr[:, half * 1024 + nb * 512:half * 1024 + (nb + 1) * 512],
                             True, True,
                             deps=[t_rmr, t_o1, prev.get(f"psL_rd{half}"),
                                   prev.get("rmb_ex")])
                        for nb in range(2)]
                t_rmb.append(b.op("scalar", "activation",
                                  rmb[:, half * 1024:(half + 1) * 1024],
                                  psL[half][:], AF.Identity,
                                  deps=mmsb + (prev.get("rmb_rd") or [None])))
                prev[f"psL_rd{half}"] = t_rmb[half]
            for blk in range(16):
                tp_toks = {}
                sub_eh = {}
                for sub in range(2):
                    etc = blk * 2 + sub
                    lk = etc % 2
                    eh_toks = []
                    for half in range(2):
                        lsl = lL[lk][:, half * 1024:(half + 1) * 1024]
                        t_e = b.op("scalar", "activation", ebuf[lk][half][:], lsl,
                                   AF.Exp, bias=ncm_s[:, etc:etc + 1],
                                   accum_out=cs[lk][:, half:half + 1],
                                   deps=[ltok[lk], t_ncm,
                                         prev.get(("cs_rd", lk))] +
                                        (prev.get(("ebuf_rd", lk, half)) or []))
                        eh_toks.append(t_e)
                    sub_eh[sub] = eh_toks
                    t_cs = b.op("vector", "tensor_add", cs[lk][:, 0:1], cs[lk][:, 0:1],
                                cs[lk][:, 1:2], deps=eh_toks + [prev.get(("rcs_rd", lk))])
                    t_rcs = b.op("vector", "reciprocal", rcs[lk][:], cs[lk][:, 0:1],
                                 deps=[t_cs])
                    prev[("cs_rd", lk)] = t_rcs
                    t_rcc = b.op("vector", "tensor_copy", rcsc[lk][:], rcs[lk][:],
                                 deps=[t_rcs, prev.get(("rcc_rd", lk))])
                    prev[("rcs_rd", lk)] = t_rcc
                    prev[("rcc", sub)] = t_rcc
                    for half in range(2):
                        lsl = lL[lk][:, half * 1024:(half + 1) * 1024]
                        t_z = b.op("vector", "scalar_tensor_tensor", zb[lk][:], lsl,
                                   1.0, rmb[:, half * 1024:(half + 1) * 1024],
                                   ALU.mult, ALU.subtract,
                                   deps=[ltok[lk]] + t_rmb +
                                        [prev.get(("zb_rd", lk)), eh_toks[half]])
                        prev[("lL_rd", lk)] = t_z
                        t_e2 = b.op("scalar", "activation", e2b[lk][half][:], zb[lk][:],
                                    AF.Exp, deps=[t_z, prev.get(("e2dma", lk, half))])
                        prev[("zb_rd", lk)] = t_e2
                        d_e2 = b.dma("gpsimd", E2T[bi, etc * 128:(etc + 1) * 128,
                                                   half * 1024:(half + 1) * 1024],
                                     e2b[lk][half][:], deps=[t_e2], lane=half + 1)
                        prev[("e2dma", lk, half)] = d_e2
                        outs.append(d_e2)
                    tps = []
                    hp = {0: [], 1: []}
                    for j in range(NSC):
                        half = j // 8
                        src = ebuf[lk][half][:, (j % 8) * 128:(j % 8 + 1) * 128]
                        if j % 4 == 0:
                            t = b.dma("sync",
                                      eblk[blk % 2][:, j, sub * 128:(sub + 1) * 128],
                                      src, transpose=True,
                                      deps=[eh_toks[half],
                                            prev.get(("eblk_rd", blk % 2))],
                                      lane=j % 3)
                        elif j % 4 == 2:
                            t = b.dma("scalar",
                                      eblk[blk % 2][:, j, sub * 128:(sub + 1) * 128],
                                      src, transpose=True,
                                      deps=[eh_toks[half],
                                            prev.get(("eblk_rd", blk % 2))],
                                      lane=j % 3)
                        else:
                            t_tp, t = tpm.to_sbuf(
                                src, eblk[blk % 2][:, j, sub * 128:(sub + 1) * 128],
                                deps=[eh_toks[half]],
                                extra_ev_deps=[prev.get(("eblk_rd", blk % 2))])
                        tps.append(t)
                        hp[half].append(t)
                    prev[("ebuf_rd", lk, 0)] = list(hp[0])
                    prev[("ebuf_rd", lk, 1)] = list(hp[1])
                    tp_toks[sub] = tps
                    if etc + 2 < 2 * 16:
                        l_load(bi, etc + 2)
                for sub in range(2):
                    t_rcm = b.mm(psL[0][0:1, sub * 128:(sub + 1) * 128],
                                 rcsc[(blk * 2 + sub) % 2][:], identb[:], True, True,
                                 deps=[prev[("rcc", sub)], tib, prev.get("psL_rd0")])
                    prev[("rcc_rd", (blk * 2 + sub) % 2)] = t_rcm
                    t_rre = b.op("scalar", "activation",
                                 rrow[:, sub * 128:(sub + 1) * 128],
                                 psL[0][0:1, sub * 128:(sub + 1) * 128], AF.Identity,
                                 deps=[t_rcm, prev.get("rrow_rd")])
                    prev["psL_rd0"] = t_rre
                    prev[("rre", sub)] = t_rre
                t_bcm = [b.mm(psL[1][0:128, e * 128:(e + 1) * 128], ones1b[:],
                              rrow[:, e * 128:(e + 1) * 128], True, True,
                              deps=[t_o1b, prev[("rre", e)], prev.get("psL_rd1")])
                         for e in range(2)]
                t_rcb = b.op("scalar", "activation", rcsb[blk % 2][:],
                             psL[1][0:128, 0:256], AF.Identity,
                             deps=t_bcm + [prev.get(("rcsb_rd", blk % 2))])
                prev["psL_rd1"] = t_rcb
                prev["rrow_rd"] = t_bcm[-1]
                ev_sl = prev.get("sl_ev", [])
                smms = []
                for dhalf in range(2):
                    grp = []
                    for dcq in range(4):
                        dc = dhalf * 4 + dcq
                        for sc in range(NSC):
                            tmm = b.mm(psSl[:, dcq * 256:(dcq + 1) * 256],
                                       xnt[sc][:, dc * 128:(dc + 1) * 128],
                                       eblk[blk % 2][:, sc, :],
                                       sc == 0, sc == NSC - 1,
                                       deps=[xttok[sc]] + tp_toks[0] + tp_toks[1] +
                                            (ev_sl if sc == 0 else []),
                                       inc=sc == NSC - 1)
                            if sc == NSC - 1:
                                grp.append(tmm)
                    t_se = b.op("vector", "tensor_tensor",
                                sev[:].rearrange("p (g e) -> p g e", e=256),
                                psSl[:].rearrange("p (g e) -> p g e", e=256),
                                rcsb[blk % 2][:].rearrange(
                                    "p (o e) -> p o e", o=1).broadcast_to([128, 4, 256]),
                                ALU.mult,
                                deps=grp + [t_rcb] + (prev.get("sev_dma") or []))
                    ev_sl = [t_se]
                    prev["sl_ev"] = ev_sl
                    dmas = []
                    for dcq in range(4):
                        dc = dhalf * 4 + dcq
                        dmas.append(b.dma("gpsimd",
                                          slots_fm[bi, dc * 128:(dc + 1) * 128,
                                                   blk * 256:(blk + 1) * 256],
                                          sev[:, dcq * 256:(dcq + 1) * 256],
                                          deps=[t_se], lane=1 + dcq % 2))
                    prev["sev_dma"] = dmas[-2:]
                    outs += dmas
                    smms += grp
                prev[("eblk_rd", blk % 2)] = smms[-1]
                prev[("rcsb_rd", blk % 2)] = prev["sl_ev"][0]
            prev["lastA"] = prev["sl_ev"][0]
            prev["rmb_ex"] = prev.get("psL_rd1")
            prev["ncm_rd"] = [prev[("cs_rd", 0)], prev[("cs_rd", 1)]]
            prev["rmr_rd"] = t_rmb[-1]
            prev["rmb_rd"] = [prev[("zb_rd", 0)], prev[("zb_rd", 1)]]
        b.done(outs)


def emit_l1ffnC(b, slots_fm, w1, w2, processed):
    NHC = HID // 128
    BW = 512
    with b.stage("fc") as ctx:
        w1t = [b.sb(f"w1t{i}", [128, HID], BF16) for i in range(NDC)]
        w2t = [b.sb(f"w2t{i}", [128, D], BF16) for i in range(NHC)]
        w1tok = [b.dma("gpsimd", w1t[i][:], w1[i * 128:(i + 1) * 128, :], lane=i % 3)
                 for i in range(NDC)]
        w2tok = [b.dma("gpsimd", w2t[i][:], w2[i * 128:(i + 1) * 128, :], lane=i % 3)
                 for i in range(NHC)]
        sl = [b.sb(f"sl{i}", [128, 8, BW], BF16) for i in range(2)]
        hfm = b.sb("hfm", [128, NHC, BW], BF16)
        pev = [b.sb(f"pev{i}", [128, 1024], BF16) for i in range(2)]
        psH = [b.ps(f"psH{i}", [128, 512], F32) for i in range(2)]
        psP = [b.ps(f"psP{i}", [128, 1024], F32) for i in range(2)]
        outs = []
        prev = {}
        sltok = [None, None]
        pmm_last = None
        NBLK = ET // BW

        def sl_load(gblk):
            bi2, blk2 = divmod(gblk, NBLK)
            kb2 = gblk % 2
            sltok[kb2] = b.dma("sync", sl[kb2][:],
                               slots_fm[bi2, :, blk2 * BW:(blk2 + 1) * BW].rearrange(
                                   "(a p) f -> p a f", p=128),
                               deps=[sltok[kb2], prev.get(f"sl_rd{kb2}")], lane=kb2)

        sl_load(0)
        sl_load(1)
        for bi in range(BPC):
            for blk in range(NBLK):
                kb = blk % 2
                gblk = bi * NBLK + blk
                h_toks = []
                for hc in range(NHC):
                    kk = hc % 2
                    for ic in range(NDC):
                        tmm = b.mm(psH[kk][:, 0:BW], w1t[ic][:, hc * 128:(hc + 1) * 128],
                                   sl[kb][:, ic, :], ic == 0, ic == NDC - 1,
                                   deps=[w1tok[ic], sltok[kb], prev.get(f"hev{kk}")],
                                   inc=ic == NDC - 1)
                    t_h = b.op("scalar", "activation", hfm[:, hc, :], psH[kk][:, 0:BW],
                               AF.Gelu, deps=[tmm, prev.get(("hfm_rd", hc))])
                    prev[f"hev{kk}"] = t_h
                    h_toks.append(t_h)
                for sub in range(BW // 128):
                    kk = sub % 2
                    etc = blk * (BW // 128) + sub
                    for hc in range(NHC):
                        for nb in range(2):
                            tmm = b.mm(psP[kk][:, nb * 512:(nb + 1) * 512],
                                       hfm[:, hc, sub * 128:(sub + 1) * 128],
                                       w2t[hc][:, nb * 512:(nb + 1) * 512],
                                       hc == 0, hc == NHC - 1,
                                       deps=[w2tok[hc], h_toks[hc]] +
                                            ([prev.get(f"pev{kk}")] if hc == 0 else []),
                                       inc=hc == NHC - 1 and nb == 1)
                    pmm_last = tmm
                    t_p = b.op("scalar", "activation", pev[kk][:], psP[kk][:],
                               AF.Identity, deps=[tmm, prev.get(f"pdma{kk}")])
                    prev[f"pev{kk}"] = t_p
                    d_p = b.dma("sync", processed[bi, etc * 128:(etc + 1) * 128, :],
                                pev[kk][:], deps=[t_p], lane=kk)
                    prev[f"pdma{kk}"] = d_p
                    outs.append(d_p)
                for hc in range(NHC):
                    prev[("hfm_rd", hc)] = pmm_last
                prev[f"sl_rd{kb}"] = pmm_last
                if gblk + 2 < BPC * NBLK:
                    sl_load(gblk + 2)
        b.done(outs)


def emit_l1ffnD(b, processed, E2T, s1_tm, x1, out):
    NEC = ET // 128
    with b.stage("fd") as ctx:
        onescol = b.sb("onescol", [128, 1], BF16)
        t_oc = b.op("vector", "memset", onescol[:], 1.0)
        proc = [b.sb(f"proc{i}", [128, D], BF16) for i in range(NEC)]
        ptok = [None] * NEC
        e2s = [[b.sb(f"e2s{p}_{i}", [128, 4, 128], BF16) for i in range(8)]
               for p in range(2)]
        s1b = [b.sb(f"s1b{i}", [128, D]) for i in range(2)]
        x1b = [b.sb(f"x1b{i}", [128, D]) for i in range(2)]
        tb = [b.sb(f"tb{i}", [128, D]) for i in range(2)]
        gb_ = [b.sb(f"gb{i}", [128, D]) for i in range(2)]
        rden = [b.sb(f"rden{i}", [128, 1]) for i in range(2)]
        psU = [b.ps(f"psU{i}", [128, 1024], F32) for i in range(2)]
        psD = [b.ps(f"psD{i}", [128, 512], F32) for i in range(2)]
        outs = []
        prev = {}
        e2tok = {}
        for bi in range(BPC):
            for i in range(NEC):
                ptok[i] = b.dma("gpsimd" if i % 2 else "sync", proc[i][:],
                                processed[bi, i * 128:(i + 1) * 128, :],
                                deps=[ptok[i], prev.get("mm_last0"),
                                      prev.get("mm_last1")], lane=i % 3)
            def load_strip(sc2):
                kk2 = sc2 % 2
                sl2 = slice(sc2 * 128, (sc2 + 1) * 128)
                for jj in range(8):
                    e2tok[(kk2, jj)] = b.dma(
                        "gpsimd" if jj % 2 else "sync", e2s[kk2][jj][:],
                        E2T[bi, jj * 512:(jj + 1) * 512, sl2].rearrange(
                            "(a p) f -> p a f", p=128),
                        deps=[e2tok.get((kk2, jj)), prev.get(f"mm_last{kk2}")],
                        lane=kk2)

            load_strip(0)
            for sc in range(NSC):
                kk = sc % 2
                ssl = slice(sc * 128, (sc + 1) * 128)
                if sc + 1 < NSC:
                    load_strip(sc + 1)
                l_s1 = b.dma("sync", s1b[kk][:], s1_tm[bi, ssl, :],
                             deps=[prev.get(f"smul{kk}")])
                l_x1 = b.dma("gpsimd", x1b[kk][:], x1[bi, ssl, :],
                             deps=[prev.get(f"xadd{kk}")])
                umms = []
                dmms = []
                for etc in range(NEC):
                    jj, aa = divmod(etc, 4)
                    lhs = e2s[kk][jj][:, aa, :]
                    for nb in range(2):
                        tmm = b.mm(psU[kk][:, nb * 512:(nb + 1) * 512], lhs,
                                   proc[etc][:, nb * 512:(nb + 1) * 512],
                                   etc == 0, etc == NEC - 1,
                                   deps=[e2tok[(kk, jj)], ptok[etc],
                                         prev.get(f"uev{kk}")],
                                   inc=etc == NEC - 1 and nb == 1)
                    dmms.append(b.mm(psD[kk][:, 0:1], lhs, onescol[:],
                                     etc == 0, etc == NEC - 1,
                                     deps=([t_oc, prev.get(f"rden_rd{kk}")]
                                           if etc == 0 else []),
                                     inc=etc == NEC - 1))
                    umms.append(tmm)
                prev[f"mm_last{kk}"] = dmms[-1]
                t_r = b.op("vector", "reciprocal", rden[kk][:], psD[kk][:, 0:1],
                           deps=[dmms[-1], prev.get(f"rden_u{kk}")])
                prev[f"rden_rd{kk}"] = t_r
                t_u = b.op("vector", "tensor_scalar", tb[kk][:], psU[kk][:], rden[kk][:],
                           None, ALU.mult,
                           deps=[t_r, umms[-1], prev.get(f"gsil{kk}")])
                prev[f"rden_u{kk}"] = t_u
                prev[f"uev{kk}"] = t_u
                t_g = b.op("scalar", "activation", gb_[kk][:], tb[kk][:], AF.Silu,
                           deps=[t_u, prev.get(f"smul{kk}")])
                prev[f"gsil{kk}"] = t_g
                t_sg = b.op("vector", "tensor_mul", s1b[kk][:], s1b[kk][:], gb_[kk][:],
                            deps=[t_g, l_s1])
                prev[f"smul{kk}"] = t_sg
                t_x = b.op("vector", "tensor_add", x1b[kk][:], x1b[kk][:], s1b[kk][:],
                           deps=[t_sg, l_x1])
                d_o = b.dma("sync", out[bi, ssl, :], x1b[kk][:], deps=[t_x])
                prev[f"xadd{kk}"] = d_o
                outs.append(d_o)
        b.done(outs)


def _shard_batch(arr):
    return [np.ascontiguousarray(arr[2 * k:2 * k + 2]) for k in range(NCORES)]


def run_stage(nc, in_maps):
    res = run_bass_kernel_spmd(nc, in_maps, core_ids=list(range(NCORES)))
    return res.results


_NCS = {}


def _nc(name, builder):
    if name not in _NCS:
        _NCS[name] = builder()
    return _NCS[name]


def build_mega():
    nc = bass.Bass("TRN2", target_bir_lowering=False)

    def EI(n, sh, dt=F32):
        return nc.dram_tensor(n, sh, dt, kind="ExternalInput").ap()

    def IT(n, sh, dt):
        return nc.dram_tensor(n, sh, dt, kind="Internal").ap()

    x = EI("x", [BPC, S, D])
    qkv_w = EI("qkv_w", [3 * D, D]); qkv_b = EI("qkv_b", [1, 3 * D])
    proj_w = EI("proj_w", [D, D]); proj_b = EI("proj_b", [1, D])
    W0 = EI("W0", [D, D]); b0 = EI("b0", [1, D])
    lora_down = EI("lora_down", [4, 100]); lora_up = EI("lora_up", [50, 4])
    down_aux = EI("down_aux", [100, D]); up_aux = EI("up_aux", [D, 50])
    Wq = EI("Wq", [D, D]); Wk = EI("Wk", [D, D]); Wv = EI("Wv", [D, D]); Wo = EI("Wo", [D, D])
    bq = EI("bq", [1, D]); bk = EI("bk", [1, D]); bv = EI("bv", [1, D]); bo = EI("bo", [1, D])
    beta = EI("beta", [1, 1])
    gamma_x = EI("gamma_x", [1, D]); gamma_slot = EI("gamma_slot", [1, D])
    slot_embeds = EI("slot_embeds", [ET, D])
    w1 = EI("w1", [D, HID]); w2 = EI("w2", [HID, D])

    qkvT = IT("qkvT", [D, 3 * D], BF16)
    projT = IT("projT", [D, D], BF16); WeffT = IT("WeffT", [D, D], BF16)
    WqT = IT("WqT", [D, D], BF16); WkT = IT("WkT", [D, D], BF16)
    WvT = IT("WvT", [D, D], BF16); WoT = IT("WoT", [D, D], BF16)
    seT = IT("seT", [D, ET], BF16)
    xn_tm = IT("xn_tm", [BPC, S, D], BF16); xn_fm = IT("xn_fm", [BPC, D, S], BF16)
    qkv_fm = IT("qkv_fm", [BPC, 3 * D, S], BF16)
    attn_fm = IT("attn_fm", [BPC, D, S], BF16)
    x1 = IT("x1", [BPC, S, D], F32)
    xn1_tm = IT("xn1_tm", [BPC, S, D], BF16); xn1_fm = IT("xn1_fm", [BPC, D, S], BF16)
    Qt_fm = IT("Qt_fm", [BPC, D, S], BF16); Kt_fm = IT("Kt_fm", [BPC, D, S], BF16)
    Kt_tm = IT("Kt_tm", [BPC, S, D], BF16); Vp_tm = IT("Vp_tm", [BPC, S, H1 * 17], BF16)
    attn1_fm = IT("attn1_fm", [BPC, D, S], BF16)
    s1_tm = IT("s1_tm", [BPC, S, D], F32)
    xnn_tm = IT("xnn_tm", [BPC, S, D], BF16); xnn_fm = IT("xnn_fm", [BPC, D, S], BF16)
    ncm = IT("ncm", [BPC, 128, 32], F32); rmrow = IT("rmrow", [BPC, 1, S], F32)
    slots_fm = IT("slots_fm", [BPC, D, ET], BF16)
    logitsT = IT("logitsT", [BPC, ET // 128, 128, S], F32)
    E2T = IT("E2T", [BPC, ET, S], BF16)
    processed = IT("processed", [BPC, ET, D], BF16)
    out = nc.dram_tensor("out", [BPC, S, D], F32, kind="ExternalOutput").ap()

    wio = {"qkv_w": qkv_w, "proj_w": proj_w, "W0": W0, "lora_down": lora_down,
           "lora_up": lora_up, "down_aux": down_aux, "up_aux": up_aux,
           "Wq": Wq, "Wk": Wk, "Wv": Wv, "Wo": Wo, "slot_embeds": slot_embeds,
           "gamma_slot": gamma_slot, "qkvT": qkvT, "projT": projT, "WeffT": WeffT,
           "WqT": WqT, "WkT": WkT, "WvT": WvT, "WoT": WoT, "seT": seT}

    import os
    nstages = int(os.environ.get("MEGA_STAGES", "99"))
    with ExitStack() as ctx:
        b = Bld(nc, ctx)
        b.init_idents()
        stages = [
            lambda: emit_wprep(b, wio),
            lambda: emit_ln(b, x, xn_tm, xn_fm, tag="ln0"),
            lambda: emit_l0qkv(b, xn_fm, qkvT, qkv_b, qkv_fm),
            lambda: emit_l0attn(b, qkv_fm, attn_fm),
            lambda: emit_l0epi(b, xn_fm, WeffT, b0, attn_fm, projT, proj_b, x, x1),
            lambda: emit_ln(b, x1, xn1_tm, xn1_fm, tag="ln1"),
            lambda: emit_l1qkv(b, xn1_fm, WqT, WkT, WvT, bq, bk, bv, beta,
                               Qt_fm, Kt_fm, Kt_tm, Vp_tm),
            lambda: emit_l1attn(b, Qt_fm, Kt_fm, Kt_tm, Vp_tm, attn1_fm),
            lambda: emit_proj_tm(b, attn1_fm, WoT, bo, s1_tm, act="identity", tag="s1"),
            lambda: emit_l1rms(b, xn1_tm, gamma_x, xnn_tm, xnn_fm),
            lambda: emit_l1ffnA(b, xnn_fm, seT, ncm, rmrow, logitsT),
            lambda: emit_l1ffnB(b, xnn_fm, xnn_tm, seT, ncm, rmrow, slots_fm, E2T,
                                logitsT),
            lambda: emit_l1ffnC(b, slots_fm, w1, w2, processed),
            lambda: emit_l1ffnD(b, processed, E2T, s1_tm, x1, out),
        ]
        for f in stages[:nstages]:
            f()
        if nstages < len(stages):
            z = b.sb("zz", [128, D])
            tz = b.op("vector", "memset", z[:], 0.0)
            for bi in range(BPC):
                for sc_ in range(NSC):
                    b.dma("sync", out[bi, sc_ * 128:(sc_ + 1) * 128, :], z[:], deps=[tz])
        b.finish_all()
    return nc


def kernel(**inputs):
    inp = {k: np.ascontiguousarray(np.asarray(v)) for k, v in inputs.items()}
    xs = _shard_batch(inp["L0_attn_x"])
    base = {
        "qkv_w": inp["L0_attn_qkv_w"], "qkv_b": inp["L0_attn_qkv_b"].reshape(1, -1),
        "proj_w": inp["L0_attn_proj_w"], "proj_b": inp["L0_attn_proj_b"].reshape(1, -1),
        "W0": inp["L0_ffn_W"], "b0": inp["L0_ffn_b"].reshape(1, -1),
        "lora_down": inp["L0_ffn_lora_down"], "lora_up": inp["L0_ffn_lora_up"],
        "down_aux": inp["L0_ffn_down_aux"], "up_aux": inp["L0_ffn_up_aux"],
        "Wq": inp["L1_attn_Wq"], "Wk": inp["L1_attn_Wk"],
        "Wv": inp["L1_attn_Wv"], "Wo": inp["L1_attn_Wo"],
        "bq": inp["L1_attn_bq"].reshape(1, -1), "bk": inp["L1_attn_bk"].reshape(1, -1),
        "bv": inp["L1_attn_bv"].reshape(1, -1), "bo": inp["L1_attn_bo"].reshape(1, -1),
        "beta": inp["L1_attn_beta"].reshape(1, 1),
        "gamma_x": inp["L1_ffn_gamma_x"].reshape(1, -1),
        "gamma_slot": inp["L1_ffn_gamma_slot"].reshape(1, -1),
        "slot_embeds": inp["L1_ffn_slot_embeds"].reshape(ET, D),
        "w1": inp["L1_ffn_w1"], "w2": inp["L1_ffn_w2"],
    }
    in_maps = [{**base, "x": xs[k]} for k in range(NCORES)]
    res = run_bass_kernel_spmd(_nc("mega", build_mega), in_maps,
                               core_ids=list(range(NCORES)))
    return np.concatenate([np.asarray(r["out"]) for r in res.results], axis=0)


# version 34
# speedup vs baseline: 1.1607x; 1.1607x over previous
import os
import numpy as np
import ml_dtypes
from contextlib import ExitStack, contextmanager
import concourse.bass as bass
import concourse.mybir as mybir
from concourse.bass_utils import run_bass_kernel_spmd

F32 = mybir.dt.float32
BF16 = mybir.dt.bfloat16
AF = mybir.ActivationFunctionType
ALU = mybir.AluOpType
AX = mybir.AxisListType

D = 1024
S = 2048
BPC = 2
NCORES = 8
NSC = S // 128
NDC = D // 128

H0 = 16
HD0 = 64
H1 = 64
HD1 = 16
NE = 8
NT = 512
ET = NE * NT
HID = 4 * D


class Bld:
    ENG_SEM = {"tensor": "pe", "vector": "dve", "scalar": "act", "gpsimd": "pool"}
    DMA_SEM = {"sync": "qsp", "gpsimd": "qpool", "scalar": "qact", "vector": "qvec"}

    def __init__(self, nc, ctx):
        self.nc = nc
        self.ctx = ctx
        self.cur_ctx = ctx
        self.prefix = ""
        self.idents = None
        self.sem = {}
        names = ["pe", "dve", "act", "pool"]
        for q in ["qsp", "qpool", "qact", "qvec"]:
            names += [q + str(i) for i in range(3)]
        for name in names:
            self.sem[name] = ctx.enter_context(nc.semaphore(name))
        self.cnt = {k: 0 for k in self.sem}
        self.wm = {}

    def init_idents(self):
        self.idents = emit_identities(self)

    @contextmanager
    def stage(self, tag, barrier=True):
        if barrier:
            self.stage_barrier()
        old_prefix, old_ctx = self.prefix, self.cur_ctx
        self.prefix = tag + "_"
        self._cc = {}
        with ExitStack() as sctx, self.nc.named_scope(tag):
            self.cur_ctx = sctx
            yield sctx
        self.cur_ctx = old_ctx
        self.prefix = old_prefix
        self._cc = {}

    def stage_barrier(self):
        for eng in ("tensor", "vector", "scalar", "gpsimd", "sync"):
            for semname, cnt in list(self.cnt.items()):
                if cnt:
                    self._wait(eng, (semname, cnt))

    def done(self, outs):
        pass

    def finish_all(self):
        for semname, cnt in list(self.cnt.items()):
            if cnt:
                self._wait("sync", (semname, cnt))

    def sb(self, name, shape, dt=F32):
        return self.cur_ctx.enter_context(self.nc.sbuf_tensor(self.prefix + name, shape, dt))

    def ps(self, name, shape, dt=F32):
        return self.cur_ctx.enter_context(self.nc.psum_tensor(self.prefix + name, shape, dt))

    def _wait(self, eng, tok):
        if tok is None:
            return
        semname, val = tok
        key = (eng, semname)
        if self.wm.get(key, 0) >= val:
            return
        getattr(self.nc, eng).wait_ge(self.sem[semname], val)
        self.wm[key] = val

    def op(self, eng, method, *args, deps=(), **kw):
        for t in deps:
            self._wait(eng, t)
        inst = getattr(getattr(self.nc, eng), method)(*args, **kw)
        semname = self.ENG_SEM[eng]
        self.cnt[semname] += 1
        inst.then_inc(self.sem[semname], 1)
        return (semname, self.cnt[semname])

    def mm(self, out, lhsT, rhs, start, stop, deps=(), inc=True, **kw):
        for t in deps:
            self._wait("tensor", t)
        inst = self.nc.tensor.matmul(out, lhsT, rhs, start=start, stop=stop, **kw)
        if inc:
            self.cnt["pe"] += 1
            inst.then_inc(self.sem["pe"], 1)
            return ("pe", self.cnt["pe"])
        return None

    def dma(self, queue, out, in_, deps=(), lane=0, **kw):
        semname = self.DMA_SEM[queue] + str(lane % 3)
        self._wait(queue, (semname, self.cnt[semname]))
        for t in deps:
            self._wait(queue, t)
        inst = getattr(self.nc, queue).dma_start(out=out, in_=in_, **kw)
        self.cnt[semname] += 16
        inst.then_inc(self.sem[semname], 16)
        return (semname, self.cnt[semname])

    def constcol(self, value, dt=F32):
        key = (value, dt)
        if not hasattr(self, "_cc"):
            self._cc = {}
        if key not in self._cc:
            t = self.sb(f"cc{len(self._cc)}", [128, 1], dt)
            tok = self.op("vector", "memset", t[:], value)
            self._cc[key] = (t, tok)
        return self._cc[key]

    def barrier(self, eng, toks):
        for t in toks:
            self._wait(eng, t)

    def finish(self, toks):
        for t in toks:
            self._wait("sync", t)
        for q in ("qsp", "qpool", "qact", "qvec"):
            if self.cnt[q]:
                self._wait("sync", (q, self.cnt[q]))


def emit_identities(b):
    identf = b.sb("identf", [128, 128], F32)
    identb = b.sb("identb", [128, 128], BF16)
    t1 = b.op("gpsimd", "memset", identf[:], 0.0)
    t1 = b.op("gpsimd", "affine_select", out=identf[:], in_=identf[:],
              compare_op=ALU.not_equal, fill=1.0, base=0,
              pattern=[[-1, 128]], channel_multiplier=1, deps=[t1])
    t2 = b.op("gpsimd", "memset", identb[:], 0.0)
    t2 = b.op("gpsimd", "affine_select", out=identb[:], in_=identb[:],
              compare_op=ALU.not_equal, fill=1.0, base=0,
              pattern=[[-1, 128]], channel_multiplier=1, deps=[t2])
    return identf, t1, identb, t2


def tp_b16(b, psum_out, in_slice, identb, deps, base=0):
    p = in_slice.shape[0]
    for t in deps:
        b._wait("tensor", t)
    inst = b.nc.tensor.transpose(psum_out, in_slice, identb[base:base + p, base:base + p])
    b.cnt["pe"] += 1
    inst.then_inc(b.sem["pe"], 1)
    return ("pe", b.cnt["pe"])


def tp_f32(b, psum_out, in_slice, identf, deps):
    p = in_slice.shape[0]
    return b.mm(psum_out, in_slice, identf[0:p, 0:p], True, True, deps=deps)


def emit_ln_chunk(b, xt, xnb, st, ld, prev):
    t_sq = b.op("scalar", "activation", st["dump"][:], xt[:], AF.Square,
                accum_out=st["ssq"][:], deps=[ld, prev.get("t_v")])
    t_sm = b.op("vector", "reduce_sum", st["sm"][:], xt[:], axis=AX.X,
                deps=[ld, prev.get("t_m")])
    t_m = b.op("vector", "tensor_scalar_mul", st["mcol"][:], st["sm"][:], 1.0 / D,
               deps=[t_sm, prev.get("t_mm"), prev.get("t_mb")])
    t_mm = b.op("vector", "scalar_tensor_tensor", st["var"][:], st["mcol"][:], -1.0,
                st["mcol"][:], ALU.mult, ALU.mult, deps=[t_m, prev.get("t_ln")])
    t_v = b.op("vector", "scalar_tensor_tensor", st["var"][:], st["ssq"][:], 1.0 / D,
               st["var"][:], ALU.mult, ALU.add, deps=[t_mm, t_sq])
    epscol, epstok = b.constcol(1e-5)
    t_ln = b.op("scalar", "activation", st["rstd"][:], st["var"][:], AF.Ln, bias=epscol[:],
                deps=[t_v, prev.get("t_mb"), epstok])
    t_rs = b.op("scalar", "activation", st["rstd"][:], st["rstd"][:], AF.Exp, scale=-0.5,
                deps=[t_ln])
    t_mb = b.op("vector", "scalar_tensor_tensor", st["mb"][:], st["mcol"][:], -1.0,
                st["rstd"][:], ALU.mult, ALU.mult, deps=[t_m, t_rs, prev.get("t_xn")])
    t_xn = b.op("scalar", "activation", xnb[:], xt[:], AF.Identity,
                bias=st["mb"][:], scale=st["rstd"][:],
                deps=[t_rs, t_mb, ld] + prev.get("xnb_readers", []))
    return {"t_sq": t_sq, "t_sm": t_sm, "t_m": t_m, "t_mm": t_mm, "t_v": t_v,
            "t_ln": t_ln, "t_rs": t_rs, "t_mb": t_mb, "t_xn": t_xn}


def mk_stats(b, tag):
    return {"dump": b.sb(f"dump{tag}", [128, D]),
            "ssq": b.sb(f"ssq{tag}", [128, 1]),
            "sm": b.sb(f"sm{tag}", [128, 1]),
            "mcol": b.sb(f"mcol{tag}", [128, 1]),
            "var": b.sb(f"var{tag}", [128, 1]),
            "rstd": b.sb(f"rstd{tag}", [128, 1]),
            "mb": b.sb(f"mb{tag}", [128, 1])}


def emit_ln(b, x, xn_tm, xn_fm, tag="ln"):
    with b.stage(tag) as ctx:
        identf, tif, identb, tib = b.idents
        NPAR = 3
        xt = [b.sb(f"xt{i}", [128, D], F32) for i in range(NPAR)]
        xnb = [b.sb(f"xnb{i}", [128, D], BF16) for i in range(NPAR)]
        st = [mk_stats(b, i) for i in range(NPAR)]
        fmb = [b.sb(f"fmb{i}", [128, 8, 128], BF16) for i in range(NPAR)]
        ptp = [b.ps(f"ptp{i}", [128, 512], BF16) for i in range(4)]
        prev = [{} for _ in range(NPAR)]
        fm_dma = {}
        tev_tok = {}
        out_toks = []
        for bi in range(BPC):
            for sc in range(NSC):
                k = sc % NPAR
                ld = b.dma("sync", xt[k][:], x[bi, sc * 128:(sc + 1) * 128, :],
                           deps=[prev[k].get("t_xn"), prev[k].get("t_sm")], lane=k)
                toks = emit_ln_chunk(b, xt[k], xnb[k], st[k], ld, prev[k])
                t_xn = toks["t_xn"]
                st_dma = b.dma("sync", xn_tm[bi, sc * 128:(sc + 1) * 128, :], xnb[k][:],
                               deps=[t_xn], lane=k)
                out_toks.append(st_dma)
                readers = [st_dma]
                evs = []
                for dc in range(NDC):
                    kk = dc % 4
                    t_tp = tp_b16(b, ptp[kk][:, 0:128], xnb[k][:, dc * 128:(dc + 1) * 128],
                                  identb, deps=[t_xn, tib, tev_tok.get(kk)])
                    t_ev = b.op("vector", "tensor_copy", fmb[k][:, dc, :], ptp[kk][:, 0:128],
                                deps=[t_tp, fm_dma.get(k)])
                    tev_tok[kk] = t_ev
                    readers.append(t_tp)
                    evs.append(t_ev)
                fm_dma[k] = b.dma("gpsimd",
                                  xn_fm[bi, :, sc * 128:(sc + 1) * 128].rearrange(
                                      "(a p) f -> p a f", p=128),
                                  fmb[k][:], deps=evs, lane=k)
                out_toks.append(fm_dma[k])
                toks["xnb_readers"] = readers
                prev[k] = toks
        b.done(out_toks)


class TpMachine:

    def __init__(self, b, identb, tib, n=2, evac_eng="vector"):
        self.b = b
        self.n = n
        self.identb = identb
        self.tib = tib
        self.evac_eng = evac_eng
        self.ptp = [b.ps(f"tpm_p{i}", [128, 512], F32) for i in range(n)]
        self.ev = [b.sb(f"tpm_e{i}", [128, 128], BF16) for i in range(n)]
        self.tp_tok = {}
        self.ev_dma = {}
        self.i = 0

    def to_dram(self, src_slice, dram_slice, deps):
        b = self.b
        kk = self.i % self.n
        self.i += 1
        pt = self.ptp[kk][:].bitcast(BF16)[:, 0:128]
        t_tp = tp_b16(b, pt, src_slice, self.identb,
                      deps=list(deps) + [self.tib, self.tp_tok.get(kk)])
        t_ev = (b.op("vector", "tensor_copy", self.ev[kk][:], pt,
                     deps=[t_tp, self.ev_dma.get(kk)])
                if self.evac_eng == "vector" else
                b.op("scalar", "activation", self.ev[kk][:], pt, AF.Identity,
                     deps=[t_tp, self.ev_dma.get(kk)]))
        self.tp_tok[kk] = t_ev
        self.ev_dma[kk] = b.dma("sync" if kk % 2 else "gpsimd", dram_slice,
                                self.ev[kk][:], deps=[t_ev], lane=kk % 3)
        return t_tp, self.ev_dma[kk]

    def to_sbuf(self, src_slice, sbuf_slice, deps, extra_ev_deps=()):
        b = self.b
        kk = self.i % self.n
        self.i += 1
        pt = self.ptp[kk][:].bitcast(BF16)[:, 0:128]
        t_tp = tp_b16(b, pt, src_slice, self.identb,
                      deps=list(deps) + [self.tib, self.tp_tok.get(kk)])
        t_ev = (b.op("vector", "tensor_copy", sbuf_slice, pt,
                     deps=[t_tp] + [d for d in extra_ev_deps if d])
                if self.evac_eng == "vector" else
                b.op("scalar", "activation", sbuf_slice, pt, AF.Identity,
                     deps=[t_tp] + [d for d in extra_ev_deps if d]))
        self.tp_tok[kk] = t_ev
        return t_tp, t_ev


def emit_wtrans(b, tpm, wtile, fmb, wt_dram, out_dram, E, prev_reads, fm_dma):
    nec = E // 128
    outs = []
    for ec in range(nec):
        k = ec % 3
        ld = b.dma("gpsimd", wtile[k][:, 0:D], wt_dram[ec * 128:(ec + 1) * 128, :],
                   deps=[prev_reads[k]], lane=k)
        last_tp = None
        evs = []
        for dc in range(NDC):
            t_tp, t_ev = tpm.to_sbuf(wtile[k][:, dc * 128:(dc + 1) * 128],
                                     fmb[k][:, dc, :], deps=[ld],
                                     extra_ev_deps=[fm_dma.get(k)])
            evs.append(t_ev)
            last_tp = t_tp
        fm_dma[k] = b.dma("sync",
                          out_dram[:, ec * 128:(ec + 1) * 128].rearrange(
                              "(a p) f -> p a f", p=128),
                          fmb[k][:], deps=evs, lane=k)
        outs.append(fm_dma[k])
        prev_reads[k] = last_tp
    return outs


def emit_wprep(b, io):
    (qkv_w, proj_w, W0, lora_down, lora_up, down_aux, up_aux, Wq, Wk, Wv, Wo,
     slot_embeds, gamma_slot, qkvT, projT, WeffT, WqT, WkT, WvT, WoT, seT) = (
        io[k] for k in ("qkv_w", "proj_w", "W0", "lora_down", "lora_up", "down_aux",
                        "up_aux", "Wq", "Wk", "Wv", "Wo", "slot_embeds", "gamma_slot",
                        "qkvT", "projT", "WeffT", "WqT", "WkT", "WvT", "WoT", "seT"))
    with b.stage("wp") as ctx:
        identf, tif, identb, tib = b.idents
        tpm = TpMachine(b, identb, tib, n=4)
        wtile = [b.sb(f"wtile{i}", [128, D], BF16) for i in range(3)]
        fmb = [b.sb(f"wfmb{i}", [128, 8, 128], BF16) for i in range(3)]
        prev_reads = [None, None, None]
        fm_dma = {}
        outs = []
        outs += emit_wtrans(b, tpm, wtile, fmb, qkv_w, qkvT, 3 * D, prev_reads, fm_dma)
        outs += emit_wtrans(b, tpm, wtile, fmb, proj_w, projT, D, prev_reads, fm_dma)
        outs += emit_wtrans(b, tpm, wtile, fmb, Wq, WqT, D, prev_reads, fm_dma)
        outs += emit_wtrans(b, tpm, wtile, fmb, Wk, WkT, D, prev_reads, fm_dma)
        outs += emit_wtrans(b, tpm, wtile, fmb, Wv, WvT, D, prev_reads, fm_dma)
        outs += emit_wtrans(b, tpm, wtile, fmb, Wo, WoT, D, prev_reads, fm_dma)

        lds = b.sb("lds", [4, 100])
        lus = b.sb("lus", [50, 4])
        das = b.sb("das", [100, D])
        t_ld = b.dma("sync", lds[:], lora_down[:])
        t_lu = b.dma("sync", lus[:], lora_up[:])
        t_da = b.dma("sync", das[:], down_aux[:])
        p_sm = b.ps("p_sm", [128, 1024], F32)
        ldT = b.sb("ldT", [100, 4])
        t1 = b.sb("t1", [4, D])
        luT = b.sb("luT", [4, 50])
        t2 = b.sb("t2", [50, D])
        uaT = b.sb("uaT", [50, D])
        tk = tp_f32(b, p_sm[0:100, 0:4], lds[:], identf, deps=[t_ld, tif])
        tk = b.op("vector", "tensor_copy", ldT[:], p_sm[0:100, 0:4], deps=[tk])
        m1 = b.mm(p_sm[0:4, 0:512], ldT[:], das[:, 0:512], True, True, deps=[tk, t_da])
        m1b = b.mm(p_sm[0:4, 512:1024], ldT[:], das[:, 512:1024], True, True, deps=[tk, t_da])
        tk1 = b.op("vector", "tensor_copy", t1[:], p_sm[0:4, :], deps=[m1, m1b])
        tk2 = tp_f32(b, p_sm[0:4, 0:50], lus[:], identf, deps=[t_lu, tif, tk1])
        tk2 = b.op("vector", "tensor_copy", luT[:], p_sm[0:4, 0:50], deps=[tk2])
        m2 = b.mm(p_sm[0:50, 0:512], luT[:], t1[:, 0:512], True, True, deps=[tk2, tk1])
        m2b = b.mm(p_sm[0:50, 512:1024], luT[:], t1[:, 512:1024], True, True, deps=[tk2, tk1])
        tk3 = b.op("vector", "tensor_copy", t2[:], p_sm[0:50, :], deps=[m2, m2b])
        uas = b.sb("uas", [128, 50])
        t_ua_prev = None
        for dc in range(NDC):
            t_ua = b.dma("sync", uas[:], up_aux[dc * 128:(dc + 1) * 128, :],
                         deps=[t_ua_prev])
            tk4 = tp_f32(b, p_sm[0:50, 0:128], uas[:], identf, deps=[t_ua, tk3])
            t_ua_prev = b.op("vector", "tensor_copy", uaT[:, dc * 128:(dc + 1) * 128],
                             p_sm[0:50, 0:128], deps=[tk4])
        tk_uaT = t_ua_prev
        w0t = b.sb("w0t", [128, D])
        weff_b = b.sb("weff_b", [128, D], BF16)
        p_li = b.ps("p_li", [128, 1024], F32)
        prev_cast = None
        last_wtp = None
        for oc in range(NDC):
            mli = [b.mm(p_li[:, nb * 512:(nb + 1) * 512],
                        uaT[:, oc * 128:(oc + 1) * 128],
                        t2[:, nb * 512:(nb + 1) * 512], True, True,
                        deps=[tk_uaT, tk3, prev_cast]) for nb in range(2)]
            t_w0 = b.dma("sync", w0t[:], W0[oc * 128:(oc + 1) * 128, :], deps=[prev_cast])
            t_add = b.op("vector", "tensor_add", w0t[:], w0t[:], p_li[:], deps=mli + [t_w0])
            prev_cast = b.op("vector", "tensor_copy", weff_b[:], w0t[:],
                             deps=[t_add, last_wtp])
            evs = []
            kE = oc % 3
            for dc in range(NDC):
                last_wtp, t_ev = tpm.to_sbuf(weff_b[:, dc * 128:(dc + 1) * 128],
                                             fmb[kE][:, dc, :], deps=[prev_cast],
                                             extra_ev_deps=[fm_dma.get(kE)])
                evs.append(t_ev)
            fm_dma[kE] = b.dma("sync",
                               WeffT[:, oc * 128:(oc + 1) * 128].rearrange(
                                   "(a p) f -> p a f", p=128),
                               fmb[kE][:], deps=evs, lane=kE)
            outs.append(fm_dma[kE])

        ones1f = b.sb("ones1f", [1, 128])
        t_o1 = b.op("vector", "memset", ones1f[:], 1.0)
        g_row = b.sb("g_row", [1, D])
        t_gr = b.dma("sync", g_row[:], gamma_slot[:])
        gb = b.sb("gb", [128, D])
        mgb = [b.mm(p_sm[:, nb * 512:(nb + 1) * 512], ones1f[:],
                    g_row[:, nb * 512:(nb + 1) * 512], True, True,
                    deps=[t_o1, t_gr, tk_uaT]) for nb in range(2)]
        t_gb = b.op("vector", "tensor_copy", gb[:], p_sm[:], deps=mgb)
        set_ = [b.sb(f"set{i}", [128, D]) for i in range(2)]
        sed = b.sb("sed", [128, D])
        sen = [b.sb(f"sen{i}", [128, D], BF16) for i in range(2)]
        snq = b.sb("snq", [128, 1])
        srl = b.sb("srl", [128, 1])
        prevS = [{}, {}]
        for tc in range(ET // 128):
            k = tc % 2
            ld = b.dma("sync", set_[k][:], slot_embeds[tc * 128:(tc + 1) * 128, :],
                       deps=[prevS[k].get("rd")])
            t_sq = b.op("scalar", "activation", sed[:], set_[k][:], AF.Square,
                        accum_out=snq[:], deps=[ld, prevS[k].get("t_n2"), prevS[1 - k].get("t_n2")])
            t_n2 = b.op("vector", "tensor_scalar_max", snq[:], snq[:], 1e-24, deps=[t_sq])
            t_ln = b.op("scalar", "activation", srl[:], snq[:], AF.Ln,
                        deps=[t_n2, prevS[1 - k].get("t_sn")])
            t_ex = b.op("scalar", "activation", srl[:], srl[:], AF.Exp, scale=-0.5, deps=[t_ln])
            t_rl = b.op("vector", "tensor_scalar_mul", srl[:], srl[:], 32.0, deps=[t_ex])
            t_sn = b.op("vector", "scalar_tensor_tensor", sen[k][:], set_[k][:], srl[:],
                        gb[:], ALU.mult, ALU.mult,
                        deps=[t_rl, t_gb, ld] + prevS[k].get("senr", []))
            senr = []
            evs = []
            kE = tc % 3
            for dc in range(NDC):
                t_tp, t_ev = tpm.to_sbuf(sen[k][:, dc * 128:(dc + 1) * 128],
                                         fmb[kE][:, dc, :], deps=[t_sn],
                                         extra_ev_deps=[fm_dma.get(kE)])
                senr.append(t_tp)
                evs.append(t_ev)
            fm_dma[kE] = b.dma("sync",
                               seT[:, tc * 128:(tc + 1) * 128].rearrange(
                                   "(a p) f -> p a f", p=128),
                               fmb[kE][:], deps=evs, lane=kE)
            outs.append(fm_dma[kE])
            prevS[k] = {"rd": t_sn, "t_n2": t_n2, "senr": senr, "t_sn": t_sn}
        b.done(outs)


def emit_proj_tm(b, a_fm, WT, bias, o_tm, act="identity", nout=D, tag="pj"):
    nnb = nout // 512
    with b.stage(tag) as ctx:
        wt = [b.sb(f"wtile{i}", [128, nout], BF16) for i in range(NDC)]
        wtoks = [b.dma("sync", wt[i][:], WT[i * 128:(i + 1) * 128, :], lane=i % 3)
                 for i in range(NDC)]
        brow = b.sb("brow", [1, nout])
        browb = b.sb("browb", [1, nout], BF16)
        t_bias = b.dma("sync", brow[:], bias[:])
        t_bb = b.op("vector", "tensor_copy", browb[:], brow[:], deps=[t_bias])
        ones1b = b.sb("ones1b", [1, 128], BF16)
        t_o1 = b.op("vector", "memset", ones1b[:], 1.0)
        at = [b.sb(f"at{i}", [128, S], BF16) for i in range(NDC)]
        ot = [b.sb(f"ot{i}", [128, nout]) for i in range(2)]
        pso = [b.ps(f"pso{i}", [128, nout], F32) for i in range(2)]
        afm_toks = [None] * NDC
        prev_ev = [None, None]
        prev_st = [None, None]
        outs = []
        for bi in range(BPC):
            for i in range(NDC):
                afm_toks[i] = b.dma("gpsimd", at[i][:], a_fm[bi, i * 128:(i + 1) * 128, :],
                                    deps=[afm_toks[i], prev_st[0], prev_st[1]], lane=i % 3)
            for sc in range(NSC):
                k = sc % 2
                mms = []
                for nb in range(nnb):
                    dst = pso[k][:, nb * 512:(nb + 1) * 512]
                    for ic in range(NDC):
                        dep = [afm_toks[ic], wtoks[ic]] if nb == 0 else []
                        b.mm(dst, at[ic][:, sc * 128:(sc + 1) * 128],
                             wt[ic][:, nb * 512:(nb + 1) * 512],
                             ic == 0, False, deps=dep + [prev_ev[k]], inc=False)
                    mms.append(b.mm(dst, ones1b[:],
                                    browb[:, nb * 512:(nb + 1) * 512],
                                    False, True, deps=[t_o1, t_bb]))
                fn = AF.Silu if act == "silu" else AF.Identity
                t_ev = b.op("scalar", "activation", ot[k][:], pso[k][:], fn,
                            deps=mms + [prev_st[k]])
                prev_ev[k] = t_ev
                prev_st[k] = b.dma("sync", o_tm[bi, sc * 128:(sc + 1) * 128, :], ot[k][:],
                                   deps=[t_ev], lane=sc % 3)
                outs.append(prev_st[k])
        b.done(outs)


def emit_l0qkv(b, xn_fm, qkvT, qkv_b, qkv_fm):
    NOC = 3 * D // 128
    with b.stage("q0") as ctx:
        identf, tif, identb, tib = b.idents
        wt = [b.sb(f"wtile{i}", [128, 3 * D], BF16) for i in range(NDC)]
        wtoks = [b.dma("sync", wt[i][:], qkvT[i * 128:(i + 1) * 128, :], lane=i % 3)
                 for i in range(NDC)]
        brow = b.sb("brow", [1, 3 * D])
        t_bias = b.dma("sync", brow[:], qkv_b[:])
        one11 = b.sb("one11", [1, 1])
        t_o11 = b.op("vector", "memset", one11[:], 1.0)
        at = [b.sb(f"at{i}", [128, S], BF16) for i in range(NDC)]
        ot = [b.sb(f"ot{i}", [128, S], BF16) for i in range(2)]
        pso = [b.ps(f"pso{i}", [128, 2048], F32) for i in range(2)]
        bcols = b.sb("bcols", [128, NOC])
        t_bc = []
        for oc in range(NOC):
            tm = b.mm(pso[0][:, oc:oc + 1], brow[:, oc * 128:(oc + 1) * 128], one11[:],
                      True, True, deps=[t_bias, t_o11])
            t_bc.append(tm)
        t_bcols = b.op("vector", "tensor_copy", bcols[:], pso[0][:, 0:NOC], deps=t_bc)
        afm_toks = [None] * NDC
        prev_ev = [t_bcols, None]
        prev_st = [None, None]
        outs = []
        for bi in range(BPC):
            for i in range(NDC):
                afm_toks[i] = b.dma("gpsimd", at[i][:], xn_fm[bi, i * 128:(i + 1) * 128, :],
                                    deps=[afm_toks[i], prev_st[0], prev_st[1]] if bi else [afm_toks[i]],
                                    lane=i % 3)
            for oc in range(NOC):
                k = oc % 2
                mms = []
                for nb in range(4):
                    dst = pso[k][:, nb * 512:(nb + 1) * 512]
                    for ic in range(NDC):
                        last = ic == NDC - 1
                        tm = b.mm(dst, wt[ic][:, oc * 128:(oc + 1) * 128],
                                  at[ic][:, nb * 512:(nb + 1) * 512],
                                  ic == 0, last,
                                  deps=[afm_toks[ic], wtoks[ic], prev_ev[k]],
                                  inc=last)
                    mms.append(tm)
                t_ev = b.op("scalar", "activation", ot[k][:], pso[k][:], AF.Identity,
                            bias=bcols[:, oc:oc + 1], deps=mms + [t_bcols, prev_st[k]])
                prev_ev[k] = t_ev
                prev_st[k] = b.dma("sync", qkv_fm[bi, oc * 128:(oc + 1) * 128, :], ot[k][:],
                                   deps=[t_ev], lane=oc % 3)
                outs.append(prev_st[k])
        b.done(outs)


def emit_l0attn(b, qkv_fm, attn_fm):
    with b.stage("a0") as ctx:
        identf, tif, identb, tib = b.idents
        qkv = [b.sb(f"qkv{i}", [128, S], BF16) for i in range(24)]
        ao = [b.sb(f"ao{i}", [128, S], BF16) for i in range(NDC)]
        vp = [b.sb(f"vp{i}", [128, 16, 80], BF16) for i in range(2)]
        et = [b.sb(f"et{i}", [128, 1024], BF16) for i in range(4)]
        kaug = [b.sb(f"kaug{i}", [128, S], BF16) for i in range(2)]
        qaug = [b.sb(f"qaug{i}", [128, S], BF16) for i in range(2)]
        nums = [b.sb(f"nums{i}", [128, 1024]) for i in range(2)]
        denr = [b.sb(f"denr{i}", [1, 1024]) for i in range(2)]
        rden = [b.sb(f"rden{i}", [1, 1024]) for i in range(2)]
        onesc = b.sb("onesc", [1, 64])
        onescol = b.sb("onescol", [128, 1], BF16)
        qsq = b.sb("qsq", [64, S], BF16)
        ksq = b.sb("ksq", [64, S], BF16)
        km2 = b.sb("km2", [1, 2])
        t_on = b.op("vector", "memset", onesc[:], 1.0)
        t_oc = b.op("vector", "memset", onescol[:], 1.0)
        t_vone = [b.op("vector", "memset", vp[i][:, :, 64:80], 1.0) for i in range(2)]
        t_kone = [b.op("vector", "memset", kaug[i][64:65, :], -1.0) for i in range(2)]
        psS = [b.ps(f"psS{i}", [128, 1024], F32) for i in range(2)]
        psO = b.ps("psO", [128, 1024], F32)
        psB = b.ps("psB", [128, 1024], F32)
        qkv_toks = [None] * 24
        outs = []
        st_prev = {}
        prev = {}
        pending = []
        pending_pre = []

        def emit_preamble(bi, h):
            j = h % 2
            qt = qkv[h // 2]
            kt = qkv[8 + h // 2]
            base = 64 * (h % 2)
            t_ka = b.op("vector", "tensor_copy", kaug[j][0:64, :], kt[base:base + 64, :],
                        deps=[qkv_toks[8 + h // 2], t_kone[j], prev.get(("ka", j))])
            t_qa = b.op("vector", "tensor_copy", qaug[j][0:64, :], qt[base:base + 64, :],
                        deps=[qkv_toks[h // 2], prev.get(("qa", j))])
            t_ks = b.op("vector", "tensor_mul", ksq[:], kt[base:base + 64, :],
                        kt[base:base + 64, :], deps=[qkv_toks[8 + h // 2],
                                                     prev.get("ksq_rd")])
            t_qs = b.op("vector", "tensor_mul", qsq[:], qt[base:base + 64, :],
                        qt[base:base + 64, :], deps=[qkv_toks[h // 2],
                                                     prev.get("qsq_rd")])
            kn_mms = {0: [], 1: []}
            for half, row in ((0, 0), (1, 32)):
                for nb in range(2):
                    kn_mms[half].append(
                        b.mm(psB[row:row + 1, nb * 512:(nb + 1) * 512], onescol[0:64],
                             ksq[:, half * 1024 + nb * 512:half * 1024 + (nb + 1) * 512],
                             True, True,
                             deps=[t_ks, t_oc, prev.get("c1_rd"), prev.get("c2_rd"),
                                   prev.get("psB_mul")]))
            t_k1 = b.op("vector", "reduce_max", km2[:, 0:1], psB[0:1, :], axis=AX.X,
                        deps=kn_mms[0] + [prev.get("km2_rd")])
            t_k2 = b.op("vector", "reduce_max", km2[:, 1:2], psB[32:33, :], axis=AX.X,
                        deps=kn_mms[1])
            t_k3 = b.op("vector", "reduce_max", km2[:, 0:1], km2[:, 0:2], axis=AX.X,
                        deps=[t_k1, t_k2])
            qn_mms = {0: [], 1: []}
            for half, row in ((0, 0), (1, 32)):
                for nb in range(2):
                    qn_mms[half].append(
                        b.mm(psB[row:row + 1, nb * 512:(nb + 1) * 512], onescol[0:64],
                             qsq[:, half * 1024 + nb * 512:half * 1024 + (nb + 1) * 512],
                             True, True,
                             deps=[t_qs, t_oc, t_k1 if half == 0 else t_k2]))
            t_c1 = b.op("scalar", "activation", qaug[j][64:65, 0:1024], psB[0:1, :],
                        AF.Sqrt, scale=km2[:, 0:1],
                        deps=qn_mms[0] + [t_k3, t_qa, prev.get("c1_rd")])
            t_c2 = b.op("scalar", "activation", qaug[j][64:65, 1024:2048], psB[32:33, :],
                        AF.Sqrt, scale=km2[:, 0:1],
                        deps=qn_mms[1] + [t_k3, t_qa, prev.get("c2_rd")])
            prev["qsq_rd"] = qn_mms[1][-1]
            prev["ksq_rd"] = kn_mms[1][-1]
            prev["km2_rd"] = t_c2
            prev["c1_rd"] = t_c1
            prev["c2_rd"] = t_c2
            return {"t_ka": t_ka, "t_qa": t_qa, "t_c1": t_c1, "t_c2": t_c2}

        def emit_vps(bi2, h2):
            j2 = h2 % 2
            vt2 = qkv[16 + h2 // 2]
            t_vps2 = []
            for kc in range(NSC):
                t_vps2.append(b.dma("sync", vp[j2][:, kc, 0:64],
                                    vt2[64 * (h2 % 2):64 * (h2 % 2) + 64,
                                        kc * 128:(kc + 1) * 128],
                                    transpose=True,
                                    deps=[qkv_toks[16 + h2 // 2], t_vone[j2],
                                          prev.get(("vp", j2))], lane=kc % 3))
            return t_vps2

        def emit_next_loads(nbi, i, vtoks):
            if vtoks is None:
                pre2 = prev["_pre"]
                qkv_toks[i] = b.dma("gpsimd" if i % 2 else "sync", qkv[i][:],
                                    qkv_fm[nbi, i * 128:(i + 1) * 128, :],
                                    deps=[qkv_toks[i], prev.get("qsq_rd"),
                                          pre2["t_qa"]])
                kI = 8 + i
                qkv_toks[kI] = b.dma("gpsimd" if kI % 2 else "sync", qkv[kI][:],
                                    qkv_fm[nbi, kI * 128:(kI + 1) * 128, :],
                                    deps=[qkv_toks[kI], prev.get("ksq_rd"),
                                          pre2["t_ka"]])
            else:
                vI = 16 + i
                qkv_toks[vI] = b.dma("gpsimd" if vI % 2 else "sync", qkv[vI][:],
                                     qkv_fm[nbi, vI * 128:(vI + 1) * 128, :],
                                     deps=[qkv_toks[vI]] + vtoks)

        for bi in range(BPC):
            ao_w = {}
            if bi == 0:
                for i in [c + 8 * k for c in range(8) for k in range(3)]:
                    qkv_toks[i] = b.dma("gpsimd" if i % 2 else "sync", qkv[i][:],
                                        qkv_fm[bi, i * 128:(i + 1) * 128, :],
                                        deps=[qkv_toks[i]], lane=i % 3)
            pre = emit_preamble(bi, 0)
            pre_vps = emit_vps(bi, 0)
            for h in range(H0):
                j = h % 2
                t_vps = pre_vps
                t_ka, t_qa = pre["t_ka"], pre["t_qa"]
                t_c1, t_c2 = pre["t_c1"], pre["t_c2"]
                pending_pre.clear()
                if h + 1 < H0:
                    pending_pre.append(lambda bi=bi, h=h: prev.__setitem__(
                        "_pre", emit_preamble(bi, h + 1)))
                    pending_pre.append(lambda bi=bi, h=h: prev.__setitem__(
                        "_vps", emit_vps(bi, h + 1)))
                if bi + 1 < BPC:
                    if h % 2 == 0:
                        pending_pre.append(lambda bi=bi, h=h: emit_next_loads(
                            bi + 1, h // 2, None))
                    else:
                        pending_pre.append(lambda bi=bi, h=h, tv=t_vps: emit_next_loads(
                            bi + 1, h // 2, tv))
                for qh in range(2):
                    p = qh % 2
                    qsl = slice(qh * 1024, (qh + 1) * 1024)
                    t_avs = []
                    smm = {}

                    def emit_S(kc):
                        kk = kc % 2
                        tms = []
                        for nb in range(2):
                            tms.append(b.mm(psS[kk][:, nb * 512:(nb + 1) * 512],
                                            kaug[j][0:65, kc * 128:(kc + 1) * 128],
                                            qaug[j][0:65, qh * 1024 + nb * 512:qh * 1024 + (nb + 1) * 512],
                                            True, True,
                                            deps=[t_ka, t_qa, t_c2,
                                                  prev.get(("exp", kk, nb))],
                                            inc=True))
                        smm[kc] = tms

                    emit_S(0)
                    emit_S(1)
                    for kc in range(NSC):
                        kk = kc % 2
                        ke = kc % 4
                        if pending and qh == 0 and kc == 3:
                            for fn in pending:
                                fn()
                            pending.clear()
                        if pending_pre and qh == 1 and kc == 6:
                            for fn in pending_pre:
                                fn()
                            pending_pre.clear()
                        t_exp = {}
                        for nb in range(2):
                            t_exp[nb] = b.op("scalar", "activation",
                                             et[ke][:, nb * 512:(nb + 1) * 512],
                                             psS[kk][:, nb * 512:(nb + 1) * 512],
                                             AF.Exp, scale=0.125,
                                             deps=[smm[kc][nb],
                                                   prev.get(("av", ke, nb))])
                            prev[("exp", kk, nb)] = t_exp[nb]
                        if kc + 2 < NSC:
                            emit_S(kc + 2)
                        for nb in range(2):
                            tav = b.mm(psO[0:65, nb * 512:(nb + 1) * 512],
                                       vp[j][:, kc, 0:65],
                                       et[ke][:, nb * 512:(nb + 1) * 512],
                                       kc == 0, kc == NSC - 1,
                                       deps=[t_exp[nb]] + (t_vps +
                                                           [prev.get(("numden", nb))]
                                                           if kc == 0 else []),
                                       inc=True)
                            prev[("av", ke, nb)] = tav
                        t_avs.append(tav)
                    prev[("vp", j)] = t_avs[-1]
                    t_num, t_den = [], []
                    for nb in range(2):
                        csl = slice(nb * 512, (nb + 1) * 512)
                        tn = b.op("vector", "tensor_copy", nums[p][0:64, csl],
                                  psO[0:64, csl],
                                  deps=t_avs + [prev.get(("num_rd", p))])
                        td = b.op("vector", "tensor_copy", denr[p][:, csl],
                                  psO[64:65, csl], deps=[tn])
                        t_num.append(tn)
                        t_den.append(td)
                        prev[("numden", nb)] = td

                    def tail(p=p, h=h, qh=qh, qsl=qsl, t_num=t_num, t_den=t_den,
                             st_tok=st_prev.get(h // 2)):
                        t_l = b.op("scalar", "activation", rden[p][:], denr[p][:], AF.Ln,
                                   deps=t_den + [prev.get(("rden_rd", p))])
                        t_r = b.op("scalar", "activation", rden[p][:], rden[p][:], AF.Exp,
                                   scale=-1.0, deps=[t_l])
                        mb = []
                        for nb in range(2):
                            mb.append(b.mm(psB[64:128, nb * 512:(nb + 1) * 512], onesc[:],
                                           rden[p][:, nb * 512:(nb + 1) * 512], True, True,
                                           deps=[t_r, t_on, prev.get("psB_mul"),
                                                 prev.get("c1_rd"),
                                                 prev.get("c2_rd")]))
                        t_mul = b.op("vector", "tensor_tensor",
                                     ao[h // 2][64 * (h % 2):64 * (h % 2) + 64, qsl],
                                     nums[p][0:64, :], psB[64:128, :], ALU.mult,
                                     deps=mb + t_num + [st_tok])
                        prev["psB_mul"] = t_mul
                        prev[("num_rd", p)] = t_mul
                        prev[("den_rd", p)] = t_mul
                        prev[("rden_rd", p)] = t_mul
                        ao_w[h // 2] = t_mul
                    pending.append(tail)
                for fn in pending_pre:
                    fn()
                pending_pre.clear()
                prev[("ka", j)] = t_avs[-1]
                prev[("qa", j)] = t_avs[-1]
                pre = prev.pop("_pre", None)
                pre_vps = prev.pop("_vps", None)
            for fn in pending:
                fn()
            pending.clear()
            st_toks = {}
            for dc in range(NDC):
                if ao_w.get(dc) is None:
                    continue
                st_toks[dc] = b.dma("sync", attn_fm[bi, dc * 128:(dc + 1) * 128, :],
                                    ao[dc][:], deps=[ao_w[dc]], lane=dc % 3)
                outs.append(st_toks[dc])
            st_prev = st_toks
        b.done(outs)


def emit_resid(b, x, s_tm, g_tm, x1, tag="rs"):
    with b.stage(tag) as ctx:
        W = 4 * D
        xt = [b.sb(f"xt{i}", [128, W]) for i in range(2)]
        st_ = [b.sb(f"st{i}", [128, W]) for i in range(2)]
        gt = [b.sb(f"gt{i}", [128, W]) for i in range(2)]
        prev = [None, None]
        outs = []
        for bi in range(BPC):
            for blk in range(4):
                k = blk % 2
                sl = slice(blk * 512, (blk + 1) * 512)
                vw = lambda t: t[bi, sl, :].rearrange("(a p) f -> p a f", p=128)
                v3 = lambda t: t[:].rearrange("p (a f) -> p a f", f=D)
                l1 = b.dma("sync", v3(xt[k]), vw(x), deps=[prev[k]], lane=k)
                l2 = b.dma("gpsimd", v3(st_[k]), vw(s_tm), deps=[prev[k]], lane=k)
                l3 = b.dma("sync", v3(gt[k]), vw(g_tm), deps=[prev[k]], lane=2)
                t1 = b.op("vector", "tensor_mul", st_[k][:], st_[k][:], gt[k][:],
                          deps=[l2, l3])
                t2 = b.op("vector", "tensor_add", xt[k][:], xt[k][:], st_[k][:],
                          deps=[t1, l1])
                prev[k] = b.dma("sync", vw(x1), v3(xt[k]), deps=[t2], lane=k)
                outs.append(prev[k])
        b.done(outs)


def emit_l0epi(b, xn_fm, WeffT, b0, attn_fm, projT, proj_b, x, x1):
    with b.stage("e0") as ctx:
        wg = [b.sb(f"wg{i}", [128, D], BF16) for i in range(NDC)]
        wp_ = [b.sb(f"wp{i}", [128, D], BF16) for i in range(NDC)]
        wg_t = [b.dma("sync", wg[i][:], WeffT[i * 128:(i + 1) * 128, :], lane=i % 3)
                for i in range(NDC)]
        wp_t = [b.dma("gpsimd", wp_[i][:], projT[i * 128:(i + 1) * 128, :], lane=i % 3)
                for i in range(NDC)]
        brow = b.sb("brow", [1, 2 * D])
        browb = b.sb("browb", [1, 2 * D], BF16)
        t_b1 = b.dma("sync", brow[:, 0:D], b0[:])
        t_b2 = b.dma("sync", brow[:, D:2 * D], proj_b[:])
        t_bb = b.op("vector", "tensor_copy", browb[:], brow[:], deps=[t_b1, t_b2])
        ones1b = b.sb("ones1b", [1, 128], BF16)
        t_o1 = b.op("vector", "memset", ones1b[:], 1.0)
        ag = [b.sb(f"ag{i}", [128, S], BF16) for i in range(NDC)]
        as_ = [b.sb(f"as{i}", [128, S], BF16) for i in range(NDC)]
        ag_t = [None] * NDC
        as_t = [None] * NDC
        gbuf = [b.sb(f"gbuf{i}", [128, D]) for i in range(NSC)]
        xb = [b.sb(f"xb{i}", [128, D]) for i in range(2)]
        yb = [b.sb(f"yb{i}", [128, D]) for i in range(2)]
        psg = [b.ps(f"psg{i}", [128, 1024], F32) for i in range(2)]
        pss = [b.ps(f"pss{i}", [128, 1024], F32) for i in range(2)]
        outs = []
        prev = {}
        for bi in range(BPC):
            for i in range(NDC):
                ag_t[i] = b.dma("gpsimd", ag[i][:], xn_fm[bi, i * 128:(i + 1) * 128, :],
                                deps=[ag_t[i], prev.get("glast")], lane=i % 3)
                as_t[i] = b.dma("sync", as_[i][:], attn_fm[bi, i * 128:(i + 1) * 128, :],
                                deps=[as_t[i], prev.get("slast")], lane=i % 3)
            for sc in range(NSC):
                k = sc % 2
                mms = []
                for nb in range(2):
                    dst = psg[k][:, nb * 512:(nb + 1) * 512]
                    for ic in range(NDC):
                        b.mm(dst, ag[ic][:, sc * 128:(sc + 1) * 128],
                             wg[ic][:, nb * 512:(nb + 1) * 512], ic == 0, False,
                             deps=[ag_t[ic], wg_t[ic], prev.get(f"gev{k}")], inc=False)
                    mms.append(b.mm(dst, ones1b[:], browb[:, nb * 512:(nb + 1) * 512],
                                    False, True, deps=[t_o1, t_bb]))
                t_g = b.op("scalar", "activation", gbuf[sc][:], psg[k][:], AF.Silu,
                           deps=mms + [prev.get(("g", sc))])
                prev[f"gev{k}"] = t_g
                prev[("g", sc)] = t_g
            prev["glast"] = prev[("g", NSC - 1)]
            for sc in range(NSC):
                k = sc % 2
                ssl = slice(sc * 128, (sc + 1) * 128)
                l_x = b.dma("sync", xb[k][:], x[bi, ssl, :],
                            deps=[prev.get(f"xadd{k}")], lane=k)
                mms = []
                for nb in range(2):
                    dst = pss[k][:, nb * 512:(nb + 1) * 512]
                    for ic in range(NDC):
                        b.mm(dst, as_[ic][:, sc * 128:(sc + 1) * 128],
                             wp_[ic][:, nb * 512:(nb + 1) * 512], ic == 0, False,
                             deps=[as_t[ic], wp_t[ic], prev.get(f"smul{k}")], inc=False)
                    mms.append(b.mm(dst, ones1b[:],
                                    browb[:, D + nb * 512:D + (nb + 1) * 512],
                                    False, True, deps=[t_o1, t_bb]))
                t_sg = b.op("vector", "tensor_mul", yb[k][:], pss[k][:], gbuf[sc][:],
                            deps=mms + [prev.get(("g", sc)), prev.get(f"yst{k}")])
                prev[f"smul{k}"] = t_sg
                prev[("g", sc)] = t_sg
                t_x = b.op("vector", "tensor_add", yb[k][:], yb[k][:], xb[k][:],
                           deps=[t_sg, l_x])
                d_o = b.dma("sync", x1[bi, ssl, :], yb[k][:], deps=[t_x], lane=k)
                prev[f"yst{k}"] = d_o
                prev[f"xadd{k}"] = d_o
                outs.append(d_o)
            prev["slast"] = prev["smul0"]
        b.done(outs)


def emit_l1qkv(b, xn_fm, WqT, WkT, WvT, bq, bk, bv, beta, Qt_fm, Kt_fm, Kt_tm, Vp_tm):
    with b.stage("q1") as ctx:
        identf, tif, identb, tib = b.idents
        wq = [b.sb(f"wq{i}", [128, D], BF16) for i in range(NDC)]
        wk = [b.sb(f"wk{i}", [128, D], BF16) for i in range(NDC)]
        wv = [b.sb(f"wv{i}", [128, D], BF16) for i in range(NDC)]
        wtok = {}
        for i in range(NDC):
            wtok[("q", i)] = b.dma("sync", wq[i][:], WqT[i * 128:(i + 1) * 128, :],
                                   lane=i % 3)
            wtok[("k", i)] = b.dma("gpsimd", wk[i][:], WkT[i * 128:(i + 1) * 128, :],
                                   lane=i % 3)
            wtok[("v", i)] = b.dma("sync", wv[i][:], WvT[i * 128:(i + 1) * 128, :],
                                   lane=(i + 1) % 3)
        brows = b.sb("brows", [1, 3 * D])
        browsb = b.sb("browsb", [1, 3 * D], BF16)
        t_b1 = b.dma("sync", brows[:, 0:D], bq[:])
        t_b2 = b.dma("sync", brows[:, D:2 * D], bk[:])
        t_b3 = b.dma("sync", brows[:, 2 * D:3 * D], bv[:])
        t_bb = b.op("vector", "tensor_copy", browsb[:], brows[:], deps=[t_b1, t_b2, t_b3])
        ones1b = b.sb("ones1b", [1, 128], BF16)
        t_o1 = b.op("vector", "memset", ones1b[:], 1.0)
        ones1f = b.sb("ones1f", [1, 128])
        t_o1f = b.op("vector", "memset", ones1f[:], 1.0)
        berow = b.sb("berow", [1, 1])
        becol = b.sb("becol", [128, 1])
        t_be = b.dma("sync", berow[:], beta[:])
        t_be = b.op("scalar", "activation", berow[:], berow[:], AF.Exp, deps=[t_be])
        psq = [b.ps(f"psq{i}", [128, 1024], F32) for i in range(3)]
        mbe = b.mm(psq[0][:, 0:1], ones1f[:], berow[:], True, True, deps=[t_be, t_o1f])
        t_bec = b.op("vector", "tensor_copy", becol[:], psq[0][:, 0:1], deps=[mbe])

        xn = [b.sb(f"xn{i}", [128, S], BF16) for i in range(NDC)]
        gmax = [b.sb(f"gmax{i}", [128, 64]) for i in range(4)]
        sub = [b.sb(f"sub{i}", [128, 1024]) for i in range(4)]
        ex = [b.sb(f"ex{i}", [128, 1024]) for i in range(2)]
        s16 = [b.sb(f"s16_{i}", [128, 64]) for i in range(2)]
        rcp = [b.sb(f"rcp{i}", [128, 64]) for i in range(2)]
        qn = [b.sb(f"qn{i}", [128, 1024], BF16) for i in range(2)]
        kn = [b.sb(f"kn{i}", [128, 1024], BF16) for i in range(2)]
        vpb = [b.sb(f"vpb{i}", [128, 64, 17], BF16) for i in range(2)]
        tpm = TpMachine(b, identb, tib, evac_eng="scalar")
        xtok = [None] * NDC
        outs = []
        prev = {}
        vone = [b.op("vector", "memset", vpb[i][:, :, 16:17], 1.0) for i in range(2)]
        for bi in range(BPC):
            for i in range(NDC):
                xtok[i] = b.dma("gpsimd", xn[i][:], xn_fm[bi, i * 128:(i + 1) * 128, :],
                                deps=[xtok[i], prev.get("last"), prev.get("vev0"),
                                      prev.get("vev1")], lane=i % 3)
            pend = []
            for sc in range(NSC):
                ssl = slice(sc * 128, (sc + 1) * 128)
                kk = sc % 2
                allmms = {}
                for ti, (wt, wkey, boff) in enumerate(
                        [(wq, "q", 0), (wk, "k", D), (wv, "v", 2 * D)]):
                    ps = psq[ti]
                    mms = []
                    for nb in range(2):
                        dst = ps[:, nb * 512:(nb + 1) * 512]
                        for ic in range(NDC):
                            b.mm(dst, xn[ic][:, ssl], wt[ic][:, nb * 512:(nb + 1) * 512],
                                 ic == 0, False,
                                 deps=[xtok[ic], wtok[(wkey, ic)], t_bec,
                                       prev.get(f"ps_rd{ti}")],
                                 inc=False)
                        mms.append(b.mm(dst, ones1b[:],
                                        browsb[:, boff + nb * 512:boff + (nb + 1) * 512],
                                        False, True, deps=[t_o1, t_bb]))
                    allmms[wkey] = mms
                for fn in pend:
                    fn()
                pend = []
                for ti, (wkey, nbuf, fm_dram, tm_dram) in enumerate(
                        [("q", qn, Qt_fm, None), ("k", kn, Kt_fm, Kt_tm)]):
                    ps = psq[ti]
                    pid = ti * 2 + kk
                    mms = allmms[wkey]
                    ps3 = ps[:].rearrange("p (g k) -> p g k", k=16)
                    t_gm = b.op("vector", "reduce_max",
                                gmax[pid][:].rearrange("p (g o) -> p g o", o=1), ps3,
                                axis=AX.X, deps=mms + [prev.get(f"gmax_rd{pid}")])
                    t_sub = b.op("vector", "tensor_tensor",
                                 sub[pid][:].rearrange("p (g k) -> p g k", k=16), ps3,
                                 gmax[pid][:].rearrange("p (g o) -> p g o", o=1).broadcast_to(
                                     [128, 64, 16]), ALU.subtract,
                                 deps=[t_gm, prev.get(f"sub_rd{pid}")])
                    prev[f"ps_rd{ti}"] = t_sub
                    prev[f"gmax_rd{pid}"] = t_sub
                    if wkey == "q":
                        t_n = b.op("scalar", "activation", nbuf[kk][:], sub[pid][:],
                                   AF.Exp, scale=becol[:],
                                   deps=[t_sub, t_bec, prev.get(f"{wkey}n{kk}")])
                        prev[f"sub_rd{pid}"] = t_n
                    else:
                        t_exp = b.op("scalar", "activation", ex[kk][:], sub[pid][:], AF.Exp,
                                     scale=becol[:], deps=[t_sub, t_bec,
                                                           prev.get(f"ex_rd{kk}"),
                                                           prev.get(f"ex_rd2{kk}")])
                        prev[f"sub_rd{pid}"] = t_exp
                        t_ss = b.op("vector", "reduce_sum",
                                    s16[kk][:].rearrange("p (g o) -> p g o", o=1),
                                    ex[kk][:].rearrange("p (g k) -> p g k", k=16), axis=AX.X,
                                    deps=[t_exp, prev.get(f"s16_rd{kk}")])
                        t_rcp = b.op("vector", "reciprocal", rcp[kk][:], s16[kk][:],
                                     deps=[t_ss, prev.get(f"rcp_rd{kk}")])
                        prev[f"s16_rd{kk}"] = t_rcp
                        t_n = b.op("vector", "tensor_tensor",
                                   nbuf[kk][:].rearrange("p (g k) -> p g k", k=16),
                                   ex[kk][:].rearrange("p (g k) -> p g k", k=16),
                                   rcp[kk][:].rearrange("p (g o) -> p g o", o=1).broadcast_to(
                                       [128, 64, 16]), ALU.mult,
                                   deps=[t_rcp, t_exp, prev.get(f"{wkey}n{kk}")])
                        prev[f"rcp_rd{kk}"] = t_n
                        prev[f"ex_rd{kk}"] = t_ss
                        prev[f"ex_rd2{kk}"] = t_n

                    def flush(wkey=wkey, nbuf=nbuf, fm_dram=fm_dram, tm_dram=tm_dram,
                              kk=kk, ssl=ssl, t_n=t_n, bi=bi):
                        rd = []
                        if tm_dram is not None:
                            rd.append(b.dma("sync", tm_dram[bi, ssl, :], nbuf[kk][:],
                                            deps=[t_n], lane=2))
                            outs.append(rd[-1])
                        for dc in range(NDC):
                            t_tp, t_dma = tpm.to_dram(nbuf[kk][:, dc * 128:(dc + 1) * 128],
                                                      fm_dram[bi, dc * 128:(dc + 1) * 128, ssl],
                                                      deps=[t_n])
                            rd.append(t_tp)
                            outs.append(t_dma)
                        prev[f"{wkey}n{kk}"] = rd[-1]
                        prev["last"] = rd[-1]
                    pend.append(flush)
                t_vev = b.op("scalar", "activation", vpb[kk][:, :, 0:16],
                             psq[2][:].rearrange("p (h e) -> p h e", e=16), AF.Identity,
                             deps=allmms["v"] + [prev.get(f"vst{kk}"), vone[kk]])
                prev["ps_rd2"] = t_vev
                t_vst = b.dma("sync", Vp_tm[bi, ssl, :],
                              vpb[kk][:].rearrange("p h e -> p (h e)"), deps=[t_vev],
                              lane=kk)
                prev[f"vev{kk}"] = t_vev
                prev[f"vst{kk}"] = t_vst
                outs.append(t_vst)
            for fn in pend:
                fn()
        b.done(outs)


def emit_l1attn(b, Qt_fm, Kt_fm, Kt_tm, Vp_tm, attn_fm, ng=8):
    NG = ng
    GW = 8 * 17
    with b.stage("a1") as ctx:
        identf, tif, identb, tib = b.idents
        ind = b.sb("ind", [128, 8])
        t_i = b.op("gpsimd", "memset", ind[:], 1.0)
        t_i = b.op("gpsimd", "affine_select", out=ind[:], in_=ind[:], compare_op=ALU.is_ge,
                   fill=0.0, base=0, pattern=[[-16, 8]], channel_multiplier=1, deps=[t_i])
        t_i = b.op("gpsimd", "affine_select", out=ind[:], in_=ind[:], compare_op=ALU.is_ge,
                   fill=0.0, base=15, pattern=[[16, 8]], channel_multiplier=-1, deps=[t_i])
        indb = b.sb("indb", [128, 8], BF16)
        t_i = b.op("vector", "tensor_copy", indb[:], ind[:], deps=[t_i])
        mask8 = b.sb("mask8", [128, 1024], BF16)
        t_m8 = b.op("gpsimd", "memset", mask8[:], 1.0)
        t_m8 = b.op("gpsimd", "affine_select", out=mask8[:], in_=mask8[:],
                    compare_op=ALU.is_ge, fill=0.0, base=0,
                    pattern=[[0, 8], [1, 128]], channel_multiplier=-1, deps=[t_m8])
        bmask = b.sb("bmask", [128, GW], BF16)
        t_bm = b.op("gpsimd", "memset", bmask[:], 1.0)
        t_bm = b.op("gpsimd", "affine_select", out=bmask[:], in_=bmask[:],
                    compare_op=ALU.is_ge, fill=0.0, base=0,
                    pattern=[[-16, 8], [0, 17]], channel_multiplier=1, deps=[t_bm])
        t_bm = b.op("gpsimd", "affine_select", out=bmask[:], in_=bmask[:],
                    compare_op=ALU.is_ge, fill=0.0, base=15,
                    pattern=[[16, 8], [0, 17]], channel_multiplier=-1, deps=[t_bm])
        qgb = [[b.sb(f"qg{p}_{u}", [128, S], BF16) for u in range(2)] for p in range(2)]
        kgb = [[b.sb(f"kg{p}_{u}", [128, S], BF16) for u in range(2)] for p in range(2)]
        ktg = [[b.sb(f"ktg{u}_{i}", [128, 128], BF16) for i in range(NSC)] for u in range(2)]
        vg = [[b.sb(f"vg{u}_{i}", [128, GW], BF16) for i in range(NSC)] for u in range(2)]
        qmb = [[b.sb(f"qmb{u}_{p}", [128, 1024], BF16) for p in range(2)]
               for u in range(2)]
        smt = [b.sb(f"smt{u}", [128, 1024], BF16) for u in range(2)]
        sraw = [b.sb(f"sraw{u}", [128, 1024], BF16) for u in range(2)]
        state = [b.sb(f"state{u}", [128, GW]) for u in range(2)]
        state_bd = [b.sb(f"state_bd{u}", [128, GW], BF16) for u in range(2)]
        dens = [b.sb(f"dens{u}", [128, 8]) for u in range(2)]
        rdens = [b.sb(f"rdens{u}", [128, 8]) for u in range(2)]
        at_tm = [b.sb(f"at{i}", [128, D], BF16) for i in range(NSC)]
        psS = [b.ps(f"psS{u}", [128, 1024], F32) for u in range(2)]
        psO = [b.ps(f"psO{u}", [128, 512], F32) for u in range(2)]
        psT = [b.ps(f"psT{u}", [128, 512], F32) for u in range(2)]
        fmbt = [b.sb(f"fmbt{i}", [128, 8, 128], BF16) for i in range(2)]
        outs = []
        ld_prev = {}
        prev = {}
        qtoks = {}
        for bi in range(BPC):
            for gp in range(NG // 2):
                par = gp % 2
                toks = [{}, {}]
                for u in range(2):
                    g = gp * 2 + u
                    if (bi, gp) == (0, 0):
                        qtoks[(par, u)] = b.dma("sync", qgb[par][u][:],
                                                Qt_fm[bi, g * 128:(g + 1) * 128, :],
                                                lane=u)
                        qtoks[("k", par, u)] = b.dma("gpsimd", kgb[par][u][:],
                                                     Kt_fm[bi, g * 128:(g + 1) * 128, :],
                                                     lane=u)
                    t_kt, t_v = [], []
                    for c in range(NSC):
                        t_kt.append(b.dma("sync", ktg[u][c][:],
                                          Kt_tm[bi, c * 128:(c + 1) * 128,
                                                g * 128:(g + 1) * 128],
                                          deps=[ld_prev.get(("kt", u, c))],
                                          lane=2 - c % 2))
                        t_v.append(b.dma("gpsimd", vg[u][c][:],
                                         Vp_tm[bi, c * 128:(c + 1) * 128,
                                               g * GW:(g + 1) * GW],
                                         deps=[ld_prev.get(("v", u, c))],
                                         lane=2 - c % 2))
                    t_st = b.op("vector", "memset", state[u][:], 0.0,
                                deps=[prev.get(("st_rd", u))])
                    toks[u] = {"t_q": qtoks[(par, u)], "t_k": qtoks[("k", par, u)],
                               "t_kt": t_kt, "t_v": t_v, "t_st": t_st}

                def qm_jit(c, u):
                    tk = toks[u]
                    t = b.op("vector", "tensor_tensor",
                             qmb[u][c % 2][:].rearrange("p (g f) -> p g f", f=128),
                             qgb[par][u][:, c * 128:(c + 1) * 128].rearrange(
                                 "p (o f) -> p o f", o=1).broadcast_to([128, 8, 128]),
                             indb[:].rearrange("p (g o) -> p g o", o=1).broadcast_to(
                                 [128, 8, 128]), ALU.mult,
                             deps=[tk["t_q"], t_i, prev.get(("qmrd", u, c % 2))])
                    prev[("qm", u, c % 2)] = t

                for u in range(2):
                    qm_jit(0, u)
                    qm_jit(1, u)
                for c in range(NSC):
                    t_sms = {}
                    t_sus = {}
                    for u in range(2):
                        tk = toks[u]
                        t_sm = []
                        for h2 in range(8):
                            t_sm.append(b.mm(psS[u][:, h2 * 128:(h2 + 1) * 128],
                                             kgb[par][u][:, c * 128:(c + 1) * 128],
                                             qmb[u][c % 2][:, h2 * 128:(h2 + 1) * 128],
                                             True, True,
                                             deps=[tk["t_k"], prev.get(("qm", u, c % 2)),
                                                   prev.get(("sev", u))]))
                        t_sms[u] = t_sm
                        prev[("qmrd", u, c % 2)] = t_sm[-1]
                        t_sus[u] = b.mm(psT[u][0:128, 0:GW], ktg[u][c][:], vg[u][c][:],
                                        True, True,
                                        deps=[tk["t_kt"][c], tk["t_v"][c],
                                              prev.get(("st_add", u))])
                    if c == 4 and not (bi == BPC - 1 and gp == NG // 2 - 1):
                        nbi, ngp = (bi, gp + 1) if gp + 1 < NG // 2 else (bi + 1, 0)
                        npar = 1 - par
                        for u in range(2):
                            g2 = ngp * 2 + u
                            qtoks[(npar, u)] = b.dma(
                                "sync", qgb[npar][u][:],
                                Qt_fm[nbi, g2 * 128:(g2 + 1) * 128, :],
                                deps=[qtoks.get((npar, u)),
                                      ld_prev.get(("qg", npar, u))], lane=u)
                            qtoks[("k", npar, u)] = b.dma(
                                "gpsimd", kgb[npar][u][:],
                                Kt_fm[nbi, g2 * 128:(g2 + 1) * 128, :],
                                deps=[qtoks.get(("k", npar, u)),
                                      ld_prev.get(("kg", npar, u))], lane=u)
                    for u in range(2):
                        g = gp * 2 + u
                        tk = toks[u]
                        t_sm = t_sms[u]
                        t_cp = b.op("scalar", "activation", sraw[u][:], psS[u][:],
                                    AF.Identity, deps=t_sm + [prev.get(("sraw", u))])
                        prev[("sev", u)] = t_cp
                        t_sev = b.op("vector", "tensor_tensor", smt[u][:], sraw[u][:],
                                     mask8[:], ALU.mult,
                                     deps=[t_cp, t_m8, prev.get(("intra", u))])
                        prev[("sraw", u)] = t_sev
                        first = None
                        if c > 0:
                            t_bd = b.op("vector", "tensor_tensor", state_bd[u][:],
                                        state[u][:], bmask[:], ALU.mult,
                                        deps=[t_bm, prev[("st_add", u)],
                                              prev.get(("inter", u))])
                            first = b.mm(psO[u][0:128, 0:GW],
                                         qgb[par][u][:, c * 128:(c + 1) * 128],
                                         state_bd[u][:], True, False,
                                         deps=[tk["t_q"], t_bd, prev.get(("oev", u))],
                                         inc=True)
                            prev[("inter", u)] = first
                        t_im = None
                        for h2 in range(8):
                            t_im = b.mm(psO[u][0:128, h2 * 17:(h2 + 1) * 17],
                                        smt[u][:, h2 * 128:(h2 + 1) * 128],
                                        vg[u][c][:, h2 * 17:(h2 + 1) * 17],
                                        first is None and h2 == 0, h2 == 7,
                                        deps=[t_sev, tk["t_v"][c]] +
                                             ([prev.get(("oev", u))] if first is None else []),
                                        inc=h2 == 7)
                        prev[("intra", u)] = t_im
                        t_sa = b.op("vector", "tensor_add", state[u][:], state[u][:],
                                    psT[u][0:128, 0:GW], deps=[t_sus[u], tk["t_st"]])
                        prev[("st_add", u)] = t_sa
                        t_cpd = b.op("vector", "tensor_copy", dens[u][:],
                                     psO[u][0:128, 0:GW].rearrange(
                                         "p (h e) -> p h e", e=17)[:, :, 16],
                                     deps=[t_im, first, prev.get(("rd_rd", u))])
                        t_r = b.op("vector", "reciprocal", rdens[u][:], dens[u][:],
                                   deps=[t_cpd])
                        t_o = b.op("vector", "tensor_tensor",
                                   at_tm[c][:, g * 128:(g + 1) * 128].rearrange(
                                       "p (h e) -> p h e", e=16),
                                   psO[u][0:128, 0:GW].rearrange(
                                       "p (h e) -> p h e", e=17)[:, :, 0:16],
                                   rdens[u][:].rearrange("p (h o) -> p h o", o=1).broadcast_to(
                                       [128, 8, 16]), ALU.mult,
                                   deps=[t_r, t_im, prev.get(("at", c))])
                        prev[("oev", u)] = t_o
                        prev[("rd_rd", u)] = t_o
                        prev[("at", c)] = t_o
                        if c + 2 < NSC:
                            qm_jit(c + 2, u)
                for u in range(2):
                    prev[("st_rd", u)] = prev[("oev", u)]
                    ld_prev[("qg", par, u)] = prev[("oev", u)]
                    ld_prev[("kg", par, u)] = prev[("oev", u)]
                    for c in range(NSC):
                        ld_prev[("kt", u, c)] = prev[("st_add", u)]
                        ld_prev[("v", u, c)] = prev[("oev", u)]
            fmba = [psS[0], psS[1]]
            for c in range(NSC):
                kE = c % 2
                pt = fmba[kE][:].bitcast(BF16)
                evs = []
                last = None
                for dc in range(NDC if NG == 8 else NG):
                    t_tp = tp_b16(b, pt[:, dc * 128:(dc + 1) * 128],
                                  at_tm[c][:, dc * 128:(dc + 1) * 128], identb,
                                  deps=[prev[("at", c)], tib, prev.get(("fmev", kE)),
                                        prev.get(("sev", kE)) if c < 2 else None])
                    t_ev = b.op("vector", "tensor_copy", fmbt[kE][:, dc, :],
                                pt[:, dc * 128:(dc + 1) * 128],
                                deps=[t_tp, prev.get(("fmdma", kE))])
                    evs.append(t_ev)
                    last = t_ev
                prev[("fmev", kE)] = last
                prev[("fmdma", kE)] = b.dma("sync",
                                            attn_fm[bi, :, c * 128:(c + 1) * 128].rearrange(
                                                "(a p) f -> p a f", p=128),
                                            fmbt[kE][:], deps=evs, lane=c % 3)
                prev[("at", c)] = last
                prev[("sev", kE)] = last
                outs.append(prev[("fmdma", kE)])
        b.done(outs)


def emit_l1rms(b, xn_tm, gamma_x, xnn_tm, xnn_fm, barrier=True):
    with b.stage("rm", barrier=barrier) as ctx:
        identf, tif, identb, tib = b.idents
        tpm = TpMachine(b, identb, tib, n=2)
        fmbr = [b.sb(f"fmbr{i}", [128, 8, 128], BF16) for i in range(2)]
        fm_dma = {}
        ones1f = b.sb("ones1f", [1, 128])
        t_o1 = b.op("vector", "memset", ones1f[:], 1.0)
        g_row = b.sb("g_row", [1, D])
        t_gr = b.dma("sync", g_row[:], gamma_x[:])
        gb = b.sb("gb", [128, D], BF16)
        psg = b.ps("psg", [128, 1024], F32)
        mgb = [b.mm(psg[:, nb * 512:(nb + 1) * 512], ones1f[:],
                    g_row[:, nb * 512:(nb + 1) * 512], True, True,
                    deps=[t_o1, t_gr]) for nb in range(2)]
        t_gb = b.op("vector", "tensor_copy", gb[:], psg[:], deps=mgb)
        xt = [b.sb(f"xt{i}", [128, D], BF16) for i in range(2)]
        xo = [b.sb(f"xo{i}", [128, D], BF16) for i in range(2)]
        sed = b.sb("sed", [128, D])
        snq = b.sb("snq", [128, 1])
        srl = b.sb("srl", [128, 1])
        prevS = [{}, {}]
        outs = []
        for bi in range(BPC):
            for sc in range(NSC):
                k = sc % 2
                ld = b.dma("sync", xt[k][:], xn_tm[bi, sc * 128:(sc + 1) * 128, :],
                           deps=[prevS[k].get("rd")])
                t_sq = b.op("scalar", "activation", sed[:], xt[k][:], AF.Square,
                            accum_out=snq[:], deps=[ld, prevS[1 - k].get("t_n2")])
                t_n2 = b.op("vector", "tensor_scalar_max", snq[:], snq[:], 1e-24,
                            deps=[t_sq])
                t_ln = b.op("scalar", "activation", srl[:], snq[:], AF.Ln,
                            deps=[t_n2, prevS[1 - k].get("t_sn")])
                t_ex = b.op("scalar", "activation", srl[:], srl[:], AF.Exp, scale=-0.5,
                            deps=[t_ln])
                t_rl = b.op("vector", "tensor_scalar_mul", srl[:], srl[:], 32.0,
                            deps=[t_ex])
                t_sn = b.op("vector", "scalar_tensor_tensor", xo[k][:], xt[k][:], srl[:],
                            gb[:], ALU.mult, ALU.mult,
                            deps=[t_rl, t_gb, ld] + prevS[k].get("senr", []))
                st_dma = b.dma("sync", xnn_tm[bi, sc * 128:(sc + 1) * 128, :], xo[k][:],
                               deps=[t_sn], lane=k)
                outs.append(st_dma)
                senr = [st_dma]
                evs = []
                for dc in range(NDC):
                    t_tp, t_ev = tpm.to_sbuf(xo[k][:, dc * 128:(dc + 1) * 128],
                                             fmbr[k][:, dc, :], deps=[t_sn],
                                             extra_ev_deps=[fm_dma.get(k)])
                    senr.append(t_tp)
                    evs.append(t_ev)
                fm_dma[k] = b.dma("gpsimd",
                                  xnn_fm[bi, :, sc * 128:(sc + 1) * 128].rearrange(
                                      "(a p) f -> p a f", p=128),
                                  fmbr[k][:], deps=evs, lane=k)
                outs.append(fm_dma[k])
                prevS[k] = {"rd": t_sn, "t_n2": t_n2, "senr": senr, "t_sn": t_sn}
        b.done(outs)


def emit_l1ffnA(b, xnn_fm, seT, ncm, rmrow, logitsT):
    NEC = ET // 128
    with b.stage("fa") as ctx:
        identf, tif, identb, tib = b.idents
        se = [b.sb(f"se{i}", [128, ET], BF16) for i in range(NDC)]
        setok = [b.dma("gpsimd" if i % 2 else "sync", se[i][:],
                       seT[i * 128:(i + 1) * 128, :]) for i in range(NDC)]
        xn = [b.sb(f"xn{i}", [128, S], BF16) for i in range(NDC)]
        psL = [b.ps(f"psL{i}", [128, 2048], F32) for i in range(2)]
        t64 = b.sb("t64", [64, S])
        t32tmp = b.sb("t32tmp", [32, S])
        rm32 = b.sb("rm32", [32, S])
        ncm_s = b.sb("ncm_s", [128, 32])
        nctmp = b.sb("nctmp", [128, 2])
        rmc = b.sb("rmc", [128, 16])
        rmr = b.sb("rmr", [1, S])
        lbuf = [b.sb(f"lbuf{i}", [128, S]) for i in range(2)]
        xtok = [None] * NDC
        outs = []
        prev = {}
        for bi in range(BPC):
            for i in range(NDC):
                xtok[i] = b.dma("gpsimd", xn[i][:],
                                xnn_fm[bi, i * 128:(i + 1) * 128, :],
                                deps=[xtok[i], prev.get("lastmm")])
            for etc in range(NEC):
                kk = etc % 2
                mms = []
                for nb in range(4):
                    dst = psL[kk][:, nb * 512:(nb + 1) * 512]
                    for ic in range(NDC):
                        tmm = b.mm(dst, se[ic][:, etc * 128:(etc + 1) * 128],
                                   xn[ic][:, nb * 512:(nb + 1) * 512],
                                   ic == 0, ic == NDC - 1,
                                   deps=[setok[ic], xtok[ic], prev.get(f"rd{kk}"),
                                         prev.get(("lb", kk))],
                                   inc=ic == NDC - 1)
                    mms.append(tmm)
                t_nc1 = b.op("vector", "reduce_max", nctmp[:, 0:1],
                             psL[kk][:, 0:1024], axis=AX.X,
                             deps=mms[0:2] + [prev.get("nctmp_rd")])
                t_h0a = b.op("vector", "tensor_copy", t64[:, 0:1024],
                             psL[kk][64:128, 0:1024],
                             deps=mms[0:2] + [prev.get("t32")])
                t_h1a = b.op("vector", "tensor_max", t64[:, 0:1024], t64[:, 0:1024],
                             psL[kk][0:64, 0:1024], deps=[t_h0a])
                t_nc2 = b.op("vector", "reduce_max", nctmp[:, 1:2],
                             psL[kk][:, 1024:2048], axis=AX.X, deps=mms[2:4])
                t_h0b = b.op("vector", "tensor_copy", t64[:, 1024:2048],
                             psL[kk][64:128, 1024:2048],
                             deps=mms[2:4] + [prev.get("t32")])
                t_h1b = b.op("vector", "tensor_max", t64[:, 1024:2048],
                             t64[:, 1024:2048], psL[kk][0:64, 1024:2048],
                             deps=[t_h0b])
                t_nc = b.op("vector", "reduce_max", ncm_s[:, etc:etc + 1],
                            nctmp[:, 0:2], axis=AX.X, negate=True,
                            deps=[t_nc1, t_nc2, prev.get("ncm_rd")])
                prev["nctmp_rd"] = t_nc
                t_h1 = t_h1b
                b.op("scalar", "activation", lbuf[kk][:, 0:1024],
                     psL[kk][:, 0:1024], AF.Identity,
                     deps=[t_h1a, t_nc1, prev.get(("lb_dma", kk))])
                t_lb = b.op("scalar", "activation", lbuf[kk][:, 1024:2048],
                            psL[kk][:, 1024:2048], AF.Identity,
                            deps=[t_h1b, t_nc2])
                prev[("lb_dma", kk)] = b.dma("gpsimd", logitsT[bi, etc],
                                             lbuf[kk][:], deps=[t_lb], lane=kk)
                outs.append(prev[("lb_dma", kk)])
                prev[("lb", kk)] = t_lb
                t_cp = b.op("vector", "tensor_copy", t32tmp[:], t64[32:64, :],
                            deps=[t_h1, prev.get("t32tmp_rd")])
                t_h2 = b.op("vector", "tensor_max", t64[0:32, :], t64[0:32, :],
                            t32tmp[:], deps=[t_cp])
                prev["t32tmp_rd"] = t_h2
                if etc == 0:
                    t_32 = b.op("vector", "tensor_copy", rm32[:], t64[0:32, :],
                                deps=[t_h2, prev.get("rm_rd")])
                else:
                    t_32 = b.op("vector", "tensor_max", rm32[:], rm32[:], t64[0:32, :],
                                deps=[t_h2])
                prev["t32"] = t_32
                prev[f"rd{kk}"] = t_h1
                prev["ncm_rd"] = t_nc
                prev["lastmm"] = t_nc
            t_prev_r = None
            t_rms = []
            for j in range(NSC):
                tmm = b.mm(psL[0][0:128, 0:32], rm32[:, j * 128:(j + 1) * 128],
                           identf[0:32, 0:32], True, True,
                           deps=[t_32, tif, t_prev_r] +
                                ([prev.get("rd0"), prev.get(("lb", 0))]
                                 if j == 0 else []))
                t_r = b.op("vector", "reduce_max", rmc[:, j:j + 1], psL[0][0:128, 0:32],
                           axis=AX.X, deps=[tmm, prev.get("rmc_rd")])
                t_prev_r = t_r
                t_rms.append(t_r)
            rr_mms = []
            for j in range(NSC):
                tmm = b.mm(psL[1][0:1, j * 128:(j + 1) * 128], rmc[:, j:j + 1],
                           identf[:], True, True,
                           deps=[t_rms[j], tif] +
                                ([prev.get("rd1"), prev.get("rmr_rd"),
                                  prev.get(("lb", 1))] if j == 0 else []))
                rr_mms.append(tmm)
            prev["rmc_rd"] = rr_mms[-1]
            t_rr = b.op("scalar", "activation", rmr[:], psL[1][0:1, :], AF.Identity,
                        deps=rr_mms)
            prev["rd1"] = t_rr
            prev["rd0"] = t_rms[-1]
            d1 = b.dma("sync", rmrow[bi], rmr[:], deps=[t_rr])
            d2 = b.dma("sync", ncm[bi], ncm_s[:], deps=[prev["ncm_rd"]])
            prev["rmr_rd"] = d1
            prev["ncm_rd"] = d2
            prev["rm_rd"] = t_rms[-1]
            outs += [d1, d2]
        b.done(outs)


def emit_l1ffnB(b, xnn_fm, xnn_tm, seT, ncm, rmrow, slots_fm, E2T, logitsT):
    NEC = ET // 128
    with b.stage("fb") as ctx:
        identf, tif, identb, tib = b.idents
        tpm = TpMachine(b, identb, tib, n=2)
        ones1f = b.sb("ones1f", [1, 128])
        t_o1 = b.op("vector", "memset", ones1f[:], 1.0)
        lL = [b.sb(f"lL{i}", [128, S]) for i in range(2)]
        ltok = [None, None]
        xnt = [b.sb(f"xnt{i}", [128, D], BF16) for i in range(NSC)]
        ncm_s = b.sb("ncm_s", [128, 32])
        rmr = b.sb("rmr", [1, S])
        rmb = b.sb("rmb", [128, S])
        ebuf = [[b.sb(f"ebuf{p}_{i}", [128, 1024], BF16) for i in range(2)]
                for p in range(2)]
        zb = [b.sb(f"zb{p}", [128, 1024]) for p in range(2)]
        e2b = [[b.sb(f"e2b{p}_{i}", [128, 1024], BF16) for i in range(2)]
               for p in range(2)]
        eblk = [b.sb(f"eblk{i}", [128, 16, 256], BF16) for i in range(2)]
        sev = b.sb("sev", [128, 1024], BF16)
        cs = [b.sb(f"cs{p}", [128, 2]) for p in range(2)]
        rcs = [b.sb(f"rcs{p}", [128, 1]) for p in range(2)]
        rcsc = [b.sb(f"rcsc{p}", [128, 1], BF16) for p in range(2)]
        rrow = b.sb("rrow", [1, 256], BF16)
        rcsb = [b.sb(f"rcsb{p}", [128, 256], BF16) for p in range(2)]
        ones1b = b.sb("ones1b", [1, 128], BF16)
        t_o1b = b.op("vector", "memset", ones1b[:], 1.0)
        psL = [b.ps(f"psL{i}", [128, 1024], F32) for i in range(2)]
        psSl = b.ps("psSl", [128, 1024], F32)
        xttok = [None] * NSC
        outs = []
        prev = {}
        smms = []

        def l_load(bi2, etc2):
            lk2 = etc2 % 2
            ltok[lk2] = b.dma("gpsimd", lL[lk2][:], logitsT[bi2, etc2],
                              deps=[ltok[lk2], prev.get(("lL_rd", lk2))], lane=0)

        for bi in range(BPC):
            for i in range(NSC):
                xttok[i] = b.dma("sync", xnt[i][:],
                                 xnn_tm[bi, i * 128:(i + 1) * 128, :],
                                 deps=[xttok[i], prev.get("lastA")])
            l_load(bi, 0)
            l_load(bi, 1)
            t_ncm = b.dma("sync", ncm_s[:], ncm[bi],
                          deps=prev.get("ncm_rd") or [None])
            t_rmr = b.dma("sync", rmr[:], rmrow[bi], deps=[prev.get("rmr_rd")])
            t_rmb = []
            for half in range(2):
                mmsb = [b.mm(psL[half][:, nb * 512:(nb + 1) * 512], ones1f[:],
                             rmr[:, half * 1024 + nb * 512:half * 1024 + (nb + 1) * 512],
                             True, True,
                             deps=[t_rmr, t_o1, prev.get(f"psL_rd{half}"),
                                   prev.get("rmb_ex")])
                        for nb in range(2)]
                t_rmb.append(b.op("scalar", "activation",
                                  rmb[:, half * 1024:(half + 1) * 1024],
                                  psL[half][:], AF.Identity,
                                  deps=mmsb + (prev.get("rmb_rd") or [None])))
                prev[f"psL_rd{half}"] = t_rmb[half]
            for blk in range(16):
                tp_toks = {}
                sub_eh = {}
                for sub in range(2):
                    etc = blk * 2 + sub
                    lk = etc % 2
                    eh_toks = []
                    for half in range(2):
                        lsl = lL[lk][:, half * 1024:(half + 1) * 1024]
                        t_e = b.op("scalar", "activation", ebuf[lk][half][:], lsl,
                                   AF.Exp, bias=ncm_s[:, etc:etc + 1],
                                   accum_out=cs[lk][:, half:half + 1],
                                   deps=[ltok[lk], t_ncm,
                                         prev.get(("cs_rd", lk))] +
                                        (prev.get(("ebuf_rd", lk, half)) or []))
                        eh_toks.append(t_e)
                    sub_eh[sub] = eh_toks
                    t_cs = b.op("vector", "tensor_add", cs[lk][:, 0:1], cs[lk][:, 0:1],
                                cs[lk][:, 1:2], deps=eh_toks + [prev.get(("rcs_rd", lk))])
                    t_rcs = b.op("vector", "reciprocal", rcs[lk][:], cs[lk][:, 0:1],
                                 deps=[t_cs])
                    prev[("cs_rd", lk)] = t_rcs
                    t_rcc = b.op("vector", "tensor_copy", rcsc[lk][:], rcs[lk][:],
                                 deps=[t_rcs, prev.get(("rcc_rd", lk))])
                    prev[("rcs_rd", lk)] = t_rcc
                    prev[("rcc", sub)] = t_rcc
                    for half in range(2):
                        lsl = lL[lk][:, half * 1024:(half + 1) * 1024]
                        t_z = b.op("vector", "scalar_tensor_tensor", zb[lk][:], lsl,
                                   1.0, rmb[:, half * 1024:(half + 1) * 1024],
                                   ALU.mult, ALU.subtract,
                                   deps=[ltok[lk]] + t_rmb +
                                        [prev.get(("zb_rd", lk)), eh_toks[half]])
                        prev[("lL_rd", lk)] = t_z
                        t_e2 = b.op("scalar", "activation", e2b[lk][half][:], zb[lk][:],
                                    AF.Exp, deps=[t_z, prev.get(("e2dma", lk, half))])
                        prev[("zb_rd", lk)] = t_e2
                        d_e2 = b.dma("gpsimd", E2T[bi, etc * 128:(etc + 1) * 128,
                                                   half * 1024:(half + 1) * 1024],
                                     e2b[lk][half][:], deps=[t_e2], lane=half + 1)
                        prev[("e2dma", lk, half)] = d_e2
                        outs.append(d_e2)
                    tps = []
                    hp = {0: [], 1: []}
                    for j in range(NSC):
                        half = j // 8
                        src = ebuf[lk][half][:, (j % 8) * 128:(j % 8 + 1) * 128]
                        if j % 4 == 0:
                            t = b.dma("sync",
                                      eblk[blk % 2][:, j, sub * 128:(sub + 1) * 128],
                                      src, transpose=True,
                                      deps=[eh_toks[half],
                                            prev.get(("eblk_rd", blk % 2))],
                                      lane=j % 3)
                        elif j % 4 == 2:
                            t = b.dma("scalar",
                                      eblk[blk % 2][:, j, sub * 128:(sub + 1) * 128],
                                      src, transpose=True,
                                      deps=[eh_toks[half],
                                            prev.get(("eblk_rd", blk % 2))],
                                      lane=j % 3)
                        else:
                            t_tp, t = tpm.to_sbuf(
                                src, eblk[blk % 2][:, j, sub * 128:(sub + 1) * 128],
                                deps=[eh_toks[half]],
                                extra_ev_deps=[prev.get(("eblk_rd", blk % 2))])
                        tps.append(t)
                        hp[half].append(t)
                    prev[("ebuf_rd", lk, 0)] = list(hp[0])
                    prev[("ebuf_rd", lk, 1)] = list(hp[1])
                    tp_toks[sub] = tps
                    if etc + 2 < 2 * 16:
                        l_load(bi, etc + 2)
                for sub in range(2):
                    t_rcm = b.mm(psL[0][0:1, sub * 128:(sub + 1) * 128],
                                 rcsc[(blk * 2 + sub) % 2][:], identb[:], True, True,
                                 deps=[prev[("rcc", sub)], tib, prev.get("psL_rd0")])
                    prev[("rcc_rd", (blk * 2 + sub) % 2)] = t_rcm
                    t_rre = b.op("scalar", "activation",
                                 rrow[:, sub * 128:(sub + 1) * 128],
                                 psL[0][0:1, sub * 128:(sub + 1) * 128], AF.Identity,
                                 deps=[t_rcm, prev.get("rrow_rd")])
                    prev["psL_rd0"] = t_rre
                    prev[("rre", sub)] = t_rre
                t_bcm = [b.mm(psL[1][0:128, e * 128:(e + 1) * 128], ones1b[:],
                              rrow[:, e * 128:(e + 1) * 128], True, True,
                              deps=[t_o1b, prev[("rre", e)], prev.get("psL_rd1")])
                         for e in range(2)]
                t_rcb = b.op("scalar", "activation", rcsb[blk % 2][:],
                             psL[1][0:128, 0:256], AF.Identity,
                             deps=t_bcm + [prev.get(("rcsb_rd", blk % 2))])
                prev["psL_rd1"] = t_rcb
                prev["rrow_rd"] = t_bcm[-1]
                ev_sl = prev.get("sl_ev", [])
                smms = []
                for dhalf in range(2):
                    grp = []
                    for dcq in range(4):
                        dc = dhalf * 4 + dcq
                        for sc in range(NSC):
                            tmm = b.mm(psSl[:, dcq * 256:(dcq + 1) * 256],
                                       xnt[sc][:, dc * 128:(dc + 1) * 128],
                                       eblk[blk % 2][:, sc, :],
                                       sc == 0, sc == NSC - 1,
                                       deps=[xttok[sc]] + tp_toks[0] + tp_toks[1] +
                                            (ev_sl if sc == 0 else []),
                                       inc=sc == NSC - 1)
                            if sc == NSC - 1:
                                grp.append(tmm)
                    t_se = b.op("vector", "tensor_tensor",
                                sev[:].rearrange("p (g e) -> p g e", e=256),
                                psSl[:].rearrange("p (g e) -> p g e", e=256),
                                rcsb[blk % 2][:].rearrange(
                                    "p (o e) -> p o e", o=1).broadcast_to([128, 4, 256]),
                                ALU.mult,
                                deps=grp + [t_rcb] + (prev.get("sev_dma") or []))
                    ev_sl = [t_se]
                    prev["sl_ev"] = ev_sl
                    dmas = []
                    for dcq in range(4):
                        dc = dhalf * 4 + dcq
                        dmas.append(b.dma("gpsimd",
                                          slots_fm[bi, dc * 128:(dc + 1) * 128,
                                                   blk * 256:(blk + 1) * 256],
                                          sev[:, dcq * 256:(dcq + 1) * 256],
                                          deps=[t_se], lane=1 + dcq % 2))
                    prev["sev_dma"] = dmas[-2:]
                    outs += dmas
                    smms += grp
                prev[("eblk_rd", blk % 2)] = smms[-1]
                prev[("rcsb_rd", blk % 2)] = prev["sl_ev"][0]
            prev["lastA"] = prev["sl_ev"][0]
            prev["rmb_ex"] = prev.get("psL_rd1")
            prev["ncm_rd"] = [prev[("cs_rd", 0)], prev[("cs_rd", 1)]]
            prev["rmr_rd"] = t_rmb[-1]
            prev["rmb_rd"] = [prev[("zb_rd", 0)], prev[("zb_rd", 1)]]
        b.done(outs)


def emit_l1ffnC(b, slots_fm, w1, w2, processed):
    NHC = HID // 128
    BW = 512
    with b.stage("fc") as ctx:
        w1t = [b.sb(f"w1t{i}", [128, HID], BF16) for i in range(NDC)]
        w2t = [b.sb(f"w2t{i}", [128, D], BF16) for i in range(NHC)]
        w1tok = [b.dma("gpsimd", w1t[i][:], w1[i * 128:(i + 1) * 128, :], lane=i % 3)
                 for i in range(NDC)]
        w2tok = [b.dma("gpsimd", w2t[i][:], w2[i * 128:(i + 1) * 128, :], lane=i % 3)
                 for i in range(NHC)]
        sl = [b.sb(f"sl{i}", [128, 8, BW], BF16) for i in range(2)]
        hfm = b.sb("hfm", [128, NHC, BW], BF16)
        pev = [b.sb(f"pev{i}", [128, 1024], BF16) for i in range(2)]
        psH = [b.ps(f"psH{i}", [128, 512], F32) for i in range(2)]
        psP = [b.ps(f"psP{i}", [128, 1024], F32) for i in range(2)]
        outs = []
        prev = {}
        sltok = [None, None]
        pmm_last = None
        NBLK = ET // BW

        def sl_load(gblk):
            bi2, blk2 = divmod(gblk, NBLK)
            kb2 = gblk % 2
            sltok[kb2] = b.dma("sync", sl[kb2][:],
                               slots_fm[bi2, :, blk2 * BW:(blk2 + 1) * BW].rearrange(
                                   "(a p) f -> p a f", p=128),
                               deps=[sltok[kb2], prev.get(f"sl_rd{kb2}")], lane=kb2)

        sl_load(0)
        sl_load(1)
        for bi in range(BPC):
            for blk in range(NBLK):
                kb = blk % 2
                gblk = bi * NBLK + blk
                h_toks = []
                for hc in range(NHC):
                    kk = hc % 2
                    for ic in range(NDC):
                        tmm = b.mm(psH[kk][:, 0:BW], w1t[ic][:, hc * 128:(hc + 1) * 128],
                                   sl[kb][:, ic, :], ic == 0, ic == NDC - 1,
                                   deps=[w1tok[ic], sltok[kb], prev.get(f"hev{kk}")],
                                   inc=ic == NDC - 1)
                    t_h = b.op("scalar", "activation", hfm[:, hc, :], psH[kk][:, 0:BW],
                               AF.Gelu, deps=[tmm, prev.get(("hfm_rd", hc))])
                    prev[f"hev{kk}"] = t_h
                    h_toks.append(t_h)
                for sub in range(BW // 128):
                    kk = sub % 2
                    etc = blk * (BW // 128) + sub
                    for hc in range(NHC):
                        for nb in range(2):
                            tmm = b.mm(psP[kk][:, nb * 512:(nb + 1) * 512],
                                       hfm[:, hc, sub * 128:(sub + 1) * 128],
                                       w2t[hc][:, nb * 512:(nb + 1) * 512],
                                       hc == 0, hc == NHC - 1,
                                       deps=[w2tok[hc], h_toks[hc]] +
                                            ([prev.get(f"pev{kk}")] if hc == 0 else []),
                                       inc=hc == NHC - 1 and nb == 1)
                    pmm_last = tmm
                    t_p = b.op("scalar", "activation", pev[kk][:], psP[kk][:],
                               AF.Identity, deps=[tmm, prev.get(f"pdma{kk}")])
                    prev[f"pev{kk}"] = t_p
                    d_p = b.dma("sync", processed[bi, etc * 128:(etc + 1) * 128, :],
                                pev[kk][:], deps=[t_p], lane=kk)
                    prev[f"pdma{kk}"] = d_p
                    outs.append(d_p)
                for hc in range(NHC):
                    prev[("hfm_rd", hc)] = pmm_last
                prev[f"sl_rd{kb}"] = pmm_last
                if gblk + 2 < BPC * NBLK:
                    sl_load(gblk + 2)
        b.done(outs)


def emit_l1ffnD(b, processed, E2T, s1_tm, x1, out):
    NEC = ET // 128
    with b.stage("fd") as ctx:
        onescol = b.sb("onescol", [128, 1], BF16)
        t_oc = b.op("vector", "memset", onescol[:], 1.0)
        proc = [b.sb(f"proc{i}", [128, D], BF16) for i in range(NEC)]
        ptok = [None] * NEC
        e2s = [[b.sb(f"e2s{p}_{i}", [128, 4, 128], BF16) for i in range(8)]
               for p in range(2)]
        s1b = [b.sb(f"s1b{i}", [128, D]) for i in range(2)]
        x1b = [b.sb(f"x1b{i}", [128, D]) for i in range(2)]
        tb = [b.sb(f"tb{i}", [128, D]) for i in range(2)]
        gb_ = [b.sb(f"gb{i}", [128, D]) for i in range(2)]
        rden = [b.sb(f"rden{i}", [128, 1]) for i in range(2)]
        psU = [b.ps(f"psU{i}", [128, 1024], F32) for i in range(2)]
        psD = [b.ps(f"psD{i}", [128, 512], F32) for i in range(2)]
        outs = []
        prev = {}
        e2tok = {}
        for bi in range(BPC):
            for i in range(NEC):
                ptok[i] = b.dma("gpsimd" if i % 2 else "sync", proc[i][:],
                                processed[bi, i * 128:(i + 1) * 128, :],
                                deps=[ptok[i], prev.get("mm_last0"),
                                      prev.get("mm_last1")], lane=i % 3)
            def load_strip(sc2):
                kk2 = sc2 % 2
                sl2 = slice(sc2 * 128, (sc2 + 1) * 128)
                for jj in range(8):
                    e2tok[(kk2, jj)] = b.dma(
                        "gpsimd" if jj % 2 else "sync", e2s[kk2][jj][:],
                        E2T[bi, jj * 512:(jj + 1) * 512, sl2].rearrange(
                            "(a p) f -> p a f", p=128),
                        deps=[e2tok.get((kk2, jj)), prev.get(f"mm_last{kk2}")],
                        lane=kk2)

            load_strip(0)
            for sc in range(NSC):
                kk = sc % 2
                ssl = slice(sc * 128, (sc + 1) * 128)
                if sc + 1 < NSC:
                    load_strip(sc + 1)
                l_s1 = b.dma("sync", s1b[kk][:], s1_tm[bi, ssl, :],
                             deps=[prev.get(f"smul{kk}")])
                l_x1 = b.dma("gpsimd", x1b[kk][:], x1[bi, ssl, :],
                             deps=[prev.get(f"xadd{kk}")])
                umms = []
                dmms = []
                for etc in range(NEC):
                    jj, aa = divmod(etc, 4)
                    lhs = e2s[kk][jj][:, aa, :]
                    for nb in range(2):
                        tmm = b.mm(psU[kk][:, nb * 512:(nb + 1) * 512], lhs,
                                   proc[etc][:, nb * 512:(nb + 1) * 512],
                                   etc == 0, etc == NEC - 1,
                                   deps=[e2tok[(kk, jj)], ptok[etc],
                                         prev.get(f"uev{kk}")],
                                   inc=etc == NEC - 1 and nb == 1)
                    dmms.append(b.mm(psD[kk][:, 0:1], lhs, onescol[:],
                                     etc == 0, etc == NEC - 1,
                                     deps=([t_oc, prev.get(f"rden_rd{kk}")]
                                           if etc == 0 else []),
                                     inc=etc == NEC - 1))
                    umms.append(tmm)
                prev[f"mm_last{kk}"] = dmms[-1]
                t_r = b.op("vector", "reciprocal", rden[kk][:], psD[kk][:, 0:1],
                           deps=[dmms[-1], prev.get(f"rden_u{kk}")])
                prev[f"rden_rd{kk}"] = t_r
                t_u = b.op("vector", "tensor_scalar", tb[kk][:], psU[kk][:], rden[kk][:],
                           None, ALU.mult,
                           deps=[t_r, umms[-1], prev.get(f"gsil{kk}")])
                prev[f"rden_u{kk}"] = t_u
                prev[f"uev{kk}"] = t_u
                t_g = b.op("scalar", "activation", gb_[kk][:], tb[kk][:], AF.Silu,
                           deps=[t_u, prev.get(f"smul{kk}")])
                prev[f"gsil{kk}"] = t_g
                t_sg = b.op("vector", "tensor_mul", s1b[kk][:], s1b[kk][:], gb_[kk][:],
                            deps=[t_g, l_s1])
                prev[f"smul{kk}"] = t_sg
                t_x = b.op("vector", "tensor_add", x1b[kk][:], x1b[kk][:], s1b[kk][:],
                           deps=[t_sg, l_x1])
                d_o = b.dma("sync", out[bi, ssl, :], x1b[kk][:], deps=[t_x])
                prev[f"xadd{kk}"] = d_o
                outs.append(d_o)
        b.done(outs)


def _shard_batch(arr):
    return [np.ascontiguousarray(arr[2 * k:2 * k + 2]) for k in range(NCORES)]


def run_stage(nc, in_maps):
    res = run_bass_kernel_spmd(nc, in_maps, core_ids=list(range(NCORES)))
    return res.results


_NCS = {}


def _nc(name, builder):
    if name not in _NCS:
        _NCS[name] = builder()
    return _NCS[name]


def build_mega():
    nc = bass.Bass("TRN2", target_bir_lowering=False)

    def EI(n, sh, dt=F32):
        return nc.dram_tensor(n, sh, dt, kind="ExternalInput").ap()

    def IT(n, sh, dt):
        return nc.dram_tensor(n, sh, dt, kind="Internal").ap()

    x = EI("x", [BPC, S, D])
    qkv_w = EI("qkv_w", [3 * D, D]); qkv_b = EI("qkv_b", [1, 3 * D])
    proj_w = EI("proj_w", [D, D]); proj_b = EI("proj_b", [1, D])
    W0 = EI("W0", [D, D]); b0 = EI("b0", [1, D])
    lora_down = EI("lora_down", [4, 100]); lora_up = EI("lora_up", [50, 4])
    down_aux = EI("down_aux", [100, D]); up_aux = EI("up_aux", [D, 50])
    Wq = EI("Wq", [D, D]); Wk = EI("Wk", [D, D]); Wv = EI("Wv", [D, D]); Wo = EI("Wo", [D, D])
    bq = EI("bq", [1, D]); bk = EI("bk", [1, D]); bv = EI("bv", [1, D]); bo = EI("bo", [1, D])
    beta = EI("beta", [1, 1])
    gamma_x = EI("gamma_x", [1, D]); gamma_slot = EI("gamma_slot", [1, D])
    slot_embeds = EI("slot_embeds", [ET, D])
    w1 = EI("w1", [D, HID]); w2 = EI("w2", [HID, D])

    qkvT = IT("qkvT", [D, 3 * D], BF16)
    projT = IT("projT", [D, D], BF16); WeffT = IT("WeffT", [D, D], BF16)
    WqT = IT("WqT", [D, D], BF16); WkT = IT("WkT", [D, D], BF16)
    WvT = IT("WvT", [D, D], BF16); WoT = IT("WoT", [D, D], BF16)
    seT = IT("seT", [D, ET], BF16)
    xn_tm = IT("xn_tm", [BPC, S, D], BF16); xn_fm = IT("xn_fm", [BPC, D, S], BF16)
    qkv_fm = IT("qkv_fm", [BPC, 3 * D, S], BF16)
    attn_fm = IT("attn_fm", [BPC, D, S], BF16)
    x1 = IT("x1", [BPC, S, D], F32)
    xn1_tm = IT("xn1_tm", [BPC, S, D], BF16); xn1_fm = IT("xn1_fm", [BPC, D, S], BF16)
    Qt_fm = IT("Qt_fm", [BPC, D, S], BF16); Kt_fm = IT("Kt_fm", [BPC, D, S], BF16)
    Kt_tm = IT("Kt_tm", [BPC, S, D], BF16); Vp_tm = IT("Vp_tm", [BPC, S, H1 * 17], BF16)
    attn1_fm = IT("attn1_fm", [BPC, D, S], BF16)
    s1_tm = IT("s1_tm", [BPC, S, D], F32)
    xnn_tm = IT("xnn_tm", [BPC, S, D], BF16); xnn_fm = IT("xnn_fm", [BPC, D, S], BF16)
    ncm = IT("ncm", [BPC, 128, 32], F32); rmrow = IT("rmrow", [BPC, 1, S], F32)
    slots_fm = IT("slots_fm", [BPC, D, ET], BF16)
    logitsT = IT("logitsT", [BPC, ET // 128, 128, S], F32)
    E2T = IT("E2T", [BPC, ET, S], BF16)
    processed = IT("processed", [BPC, ET, D], BF16)
    out = nc.dram_tensor("out", [BPC, S, D], F32, kind="ExternalOutput").ap()

    wio = {"qkv_w": qkv_w, "proj_w": proj_w, "W0": W0, "lora_down": lora_down,
           "lora_up": lora_up, "down_aux": down_aux, "up_aux": up_aux,
           "Wq": Wq, "Wk": Wk, "Wv": Wv, "Wo": Wo, "slot_embeds": slot_embeds,
           "gamma_slot": gamma_slot, "qkvT": qkvT, "projT": projT, "WeffT": WeffT,
           "WqT": WqT, "WkT": WkT, "WvT": WvT, "WoT": WoT, "seT": seT}

    import os
    nstages = int(os.environ.get("MEGA_STAGES", "99"))
    with ExitStack() as ctx:
        b = Bld(nc, ctx)
        b.init_idents()
        stages = [
            lambda: emit_wprep(b, wio),
            lambda: emit_ln(b, x, xn_tm, xn_fm, tag="ln0"),
            lambda: emit_l0qkv(b, xn_fm, qkvT, qkv_b, qkv_fm),
            lambda: emit_l0attn(b, qkv_fm, attn_fm),
            lambda: emit_l0epi(b, xn_fm, WeffT, b0, attn_fm, projT, proj_b, x, x1),
            lambda: emit_ln(b, x1, xn1_tm, xn1_fm, tag="ln1"),
            lambda: emit_l1qkv(b, xn1_fm, WqT, WkT, WvT, bq, bk, bv, beta,
                               Qt_fm, Kt_fm, Kt_tm, Vp_tm),
            lambda: emit_l1attn(b, Qt_fm, Kt_fm, Kt_tm, Vp_tm, attn1_fm),
            lambda: emit_proj_tm(b, attn1_fm, WoT, bo, s1_tm, act="identity", tag="s1"),
            lambda: emit_l1rms(b, xn1_tm, gamma_x, xnn_tm, xnn_fm),
            lambda: emit_l1ffnA(b, xnn_fm, seT, ncm, rmrow, logitsT),
            lambda: emit_l1ffnB(b, xnn_fm, xnn_tm, seT, ncm, rmrow, slots_fm, E2T,
                                logitsT),
            lambda: emit_l1ffnC(b, slots_fm, w1, w2, processed),
            lambda: emit_l1ffnD(b, processed, E2T, s1_tm, x1, out),
        ]
        for f in stages[:nstages]:
            f()
        if nstages < len(stages):
            z = b.sb("zz", [128, D])
            tz = b.op("vector", "memset", z[:], 0.0)
            for bi in range(BPC):
                for sc_ in range(NSC):
                    b.dma("sync", out[bi, sc_ * 128:(sc_ + 1) * 128, :], z[:], deps=[tz])
        b.finish_all()
    return nc


def kernel(**inputs):
    inp = {k: np.ascontiguousarray(np.asarray(v)) for k, v in inputs.items()}
    xs = _shard_batch(inp["L0_attn_x"])
    base = {
        "qkv_w": inp["L0_attn_qkv_w"], "qkv_b": inp["L0_attn_qkv_b"].reshape(1, -1),
        "proj_w": inp["L0_attn_proj_w"], "proj_b": inp["L0_attn_proj_b"].reshape(1, -1),
        "W0": inp["L0_ffn_W"], "b0": inp["L0_ffn_b"].reshape(1, -1),
        "lora_down": inp["L0_ffn_lora_down"], "lora_up": inp["L0_ffn_lora_up"],
        "down_aux": inp["L0_ffn_down_aux"], "up_aux": inp["L0_ffn_up_aux"],
        "Wq": inp["L1_attn_Wq"], "Wk": inp["L1_attn_Wk"],
        "Wv": inp["L1_attn_Wv"], "Wo": inp["L1_attn_Wo"],
        "bq": inp["L1_attn_bq"].reshape(1, -1), "bk": inp["L1_attn_bk"].reshape(1, -1),
        "bv": inp["L1_attn_bv"].reshape(1, -1), "bo": inp["L1_attn_bo"].reshape(1, -1),
        "beta": inp["L1_attn_beta"].reshape(1, 1),
        "gamma_x": inp["L1_ffn_gamma_x"].reshape(1, -1),
        "gamma_slot": inp["L1_ffn_gamma_slot"].reshape(1, -1),
        "slot_embeds": inp["L1_ffn_slot_embeds"].reshape(ET, D),
        "w1": inp["L1_ffn_w1"], "w2": inp["L1_ffn_w2"],
    }
    in_maps = [{**base, "x": xs[k]} for k in range(NCORES)]
    res = run_bass_kernel_spmd(_nc("mega", build_mega), in_maps,
                               core_ids=list(range(NCORES)))
    return np.concatenate([np.asarray(r["out"]) for r in res.results], axis=0)


# version 35 (final)
# speedup vs baseline: 1.1723x; 1.0100x over previous
import os
import numpy as np
import ml_dtypes
from contextlib import ExitStack, contextmanager
import concourse.bass as bass
import concourse.mybir as mybir
from concourse.bass_utils import run_bass_kernel_spmd

F32 = mybir.dt.float32
BF16 = mybir.dt.bfloat16
AF = mybir.ActivationFunctionType
ALU = mybir.AluOpType
AX = mybir.AxisListType

D = 1024
S = 2048
BPC = 2
NCORES = 8
NSC = S // 128
NDC = D // 128

H0 = 16
HD0 = 64
H1 = 64
HD1 = 16
NE = 8
NT = 512
ET = NE * NT
HID = 4 * D


class Bld:
    ENG_SEM = {"tensor": "pe", "vector": "dve", "scalar": "act", "gpsimd": "pool"}
    DMA_SEM = {"sync": "qsp", "gpsimd": "qpool", "scalar": "qact", "vector": "qvec"}

    def __init__(self, nc, ctx):
        self.nc = nc
        self.ctx = ctx
        self.cur_ctx = ctx
        self.prefix = ""
        self.idents = None
        self.sem = {}
        names = ["pe", "dve", "act", "pool"]
        for q in ["qsp", "qpool", "qact", "qvec"]:
            names += [q + str(i) for i in range(3)]
        for name in names:
            self.sem[name] = ctx.enter_context(nc.semaphore(name))
        self.cnt = {k: 0 for k in self.sem}
        self.wm = {}

    def init_idents(self):
        self.idents = emit_identities(self)

    @contextmanager
    def stage(self, tag, barrier=True):
        if barrier:
            self.stage_barrier()
        old_prefix, old_ctx = self.prefix, self.cur_ctx
        self.prefix = tag + "_"
        self._cc = {}
        with ExitStack() as sctx, self.nc.named_scope(tag):
            self.cur_ctx = sctx
            yield sctx
        self.cur_ctx = old_ctx
        self.prefix = old_prefix
        self._cc = {}

    def stage_barrier(self):
        for eng in ("tensor", "vector", "scalar", "gpsimd", "sync"):
            for semname, cnt in list(self.cnt.items()):
                if cnt:
                    self._wait(eng, (semname, cnt))

    def done(self, outs):
        pass

    def finish_all(self):
        for semname, cnt in list(self.cnt.items()):
            if cnt:
                self._wait("sync", (semname, cnt))

    def sb(self, name, shape, dt=F32):
        return self.cur_ctx.enter_context(self.nc.sbuf_tensor(self.prefix + name, shape, dt))

    def ps(self, name, shape, dt=F32):
        return self.cur_ctx.enter_context(self.nc.psum_tensor(self.prefix + name, shape, dt))

    def _wait(self, eng, tok):
        if tok is None:
            return
        semname, val = tok
        key = (eng, semname)
        if self.wm.get(key, 0) >= val:
            return
        getattr(self.nc, eng).wait_ge(self.sem[semname], val)
        self.wm[key] = val

    def op(self, eng, method, *args, deps=(), **kw):
        for t in deps:
            self._wait(eng, t)
        inst = getattr(getattr(self.nc, eng), method)(*args, **kw)
        semname = self.ENG_SEM[eng]
        self.cnt[semname] += 1
        inst.then_inc(self.sem[semname], 1)
        return (semname, self.cnt[semname])

    def mm(self, out, lhsT, rhs, start, stop, deps=(), inc=True, **kw):
        for t in deps:
            self._wait("tensor", t)
        inst = self.nc.tensor.matmul(out, lhsT, rhs, start=start, stop=stop, **kw)
        if inc:
            self.cnt["pe"] += 1
            inst.then_inc(self.sem["pe"], 1)
            return ("pe", self.cnt["pe"])
        return None

    def dma(self, queue, out, in_, deps=(), lane=0, **kw):
        semname = self.DMA_SEM[queue] + str(lane % 3)
        self._wait(queue, (semname, self.cnt[semname]))
        for t in deps:
            self._wait(queue, t)
        inst = getattr(self.nc, queue).dma_start(out=out, in_=in_, **kw)
        self.cnt[semname] += 16
        inst.then_inc(self.sem[semname], 16)
        return (semname, self.cnt[semname])

    def constcol(self, value, dt=F32):
        key = (value, dt)
        if not hasattr(self, "_cc"):
            self._cc = {}
        if key not in self._cc:
            t = self.sb(f"cc{len(self._cc)}", [128, 1], dt)
            tok = self.op("vector", "memset", t[:], value)
            self._cc[key] = (t, tok)
        return self._cc[key]

    def barrier(self, eng, toks):
        for t in toks:
            self._wait(eng, t)

    def finish(self, toks):
        for t in toks:
            self._wait("sync", t)
        for q in ("qsp", "qpool", "qact", "qvec"):
            if self.cnt[q]:
                self._wait("sync", (q, self.cnt[q]))


def emit_identities(b):
    identf = b.sb("identf", [128, 128], F32)
    identb = b.sb("identb", [128, 128], BF16)
    t1 = b.op("gpsimd", "memset", identf[:], 0.0)
    t1 = b.op("gpsimd", "affine_select", out=identf[:], in_=identf[:],
              compare_op=ALU.not_equal, fill=1.0, base=0,
              pattern=[[-1, 128]], channel_multiplier=1, deps=[t1])
    t2 = b.op("gpsimd", "memset", identb[:], 0.0)
    t2 = b.op("gpsimd", "affine_select", out=identb[:], in_=identb[:],
              compare_op=ALU.not_equal, fill=1.0, base=0,
              pattern=[[-1, 128]], channel_multiplier=1, deps=[t2])
    return identf, t1, identb, t2


def tp_b16(b, psum_out, in_slice, identb, deps, base=0):
    p = in_slice.shape[0]
    for t in deps:
        b._wait("tensor", t)
    inst = b.nc.tensor.transpose(psum_out, in_slice, identb[base:base + p, base:base + p])
    b.cnt["pe"] += 1
    inst.then_inc(b.sem["pe"], 1)
    return ("pe", b.cnt["pe"])


def tp_f32(b, psum_out, in_slice, identf, deps):
    p = in_slice.shape[0]
    return b.mm(psum_out, in_slice, identf[0:p, 0:p], True, True, deps=deps)


def emit_ln_chunk(b, xt, xnb, st, ld, prev):
    t_sq = b.op("scalar", "activation", st["dump"][:], xt[:], AF.Square,
                accum_out=st["ssq"][:], deps=[ld, prev.get("t_v")])
    t_sm = b.op("vector", "reduce_sum", st["sm"][:], xt[:], axis=AX.X,
                deps=[ld, prev.get("t_m")])
    t_m = b.op("vector", "tensor_scalar_mul", st["mcol"][:], st["sm"][:], 1.0 / D,
               deps=[t_sm, prev.get("t_mm"), prev.get("t_mb")])
    t_mm = b.op("vector", "scalar_tensor_tensor", st["var"][:], st["mcol"][:], -1.0,
                st["mcol"][:], ALU.mult, ALU.mult, deps=[t_m, prev.get("t_ln")])
    t_v = b.op("vector", "scalar_tensor_tensor", st["var"][:], st["ssq"][:], 1.0 / D,
               st["var"][:], ALU.mult, ALU.add, deps=[t_mm, t_sq])
    epscol, epstok = b.constcol(1e-5)
    t_ln = b.op("scalar", "activation", st["rstd"][:], st["var"][:], AF.Ln, bias=epscol[:],
                deps=[t_v, prev.get("t_mb"), epstok])
    t_rs = b.op("scalar", "activation", st["rstd"][:], st["rstd"][:], AF.Exp, scale=-0.5,
                deps=[t_ln])
    t_mb = b.op("vector", "scalar_tensor_tensor", st["mb"][:], st["mcol"][:], -1.0,
                st["rstd"][:], ALU.mult, ALU.mult, deps=[t_m, t_rs, prev.get("t_xn")])
    t_xn = b.op("scalar", "activation", xnb[:], xt[:], AF.Identity,
                bias=st["mb"][:], scale=st["rstd"][:],
                deps=[t_rs, t_mb, ld] + prev.get("xnb_readers", []))
    return {"t_sq": t_sq, "t_sm": t_sm, "t_m": t_m, "t_mm": t_mm, "t_v": t_v,
            "t_ln": t_ln, "t_rs": t_rs, "t_mb": t_mb, "t_xn": t_xn}


def mk_stats(b, tag):
    return {"dump": b.sb(f"dump{tag}", [128, D]),
            "ssq": b.sb(f"ssq{tag}", [128, 1]),
            "sm": b.sb(f"sm{tag}", [128, 1]),
            "mcol": b.sb(f"mcol{tag}", [128, 1]),
            "var": b.sb(f"var{tag}", [128, 1]),
            "rstd": b.sb(f"rstd{tag}", [128, 1]),
            "mb": b.sb(f"mb{tag}", [128, 1])}


def emit_ln(b, x, xn_tm, xn_fm, tag="ln"):
    with b.stage(tag) as ctx:
        identf, tif, identb, tib = b.idents
        NPAR = 3
        xt = [b.sb(f"xt{i}", [128, D], F32) for i in range(NPAR)]
        xnb = [b.sb(f"xnb{i}", [128, D], BF16) for i in range(NPAR)]
        st = [mk_stats(b, i) for i in range(NPAR)]
        fmb = [b.sb(f"fmb{i}", [128, 8, 128], BF16) for i in range(NPAR)]
        ptp = [b.ps(f"ptp{i}", [128, 512], BF16) for i in range(4)]
        prev = [{} for _ in range(NPAR)]
        fm_dma = {}
        tev_tok = {}
        out_toks = []
        for bi in range(BPC):
            for sc in range(NSC):
                k = sc % NPAR
                ld = b.dma("sync", xt[k][:], x[bi, sc * 128:(sc + 1) * 128, :],
                           deps=[prev[k].get("t_xn"), prev[k].get("t_sm")], lane=k)
                toks = emit_ln_chunk(b, xt[k], xnb[k], st[k], ld, prev[k])
                t_xn = toks["t_xn"]
                st_dma = b.dma("sync", xn_tm[bi, sc * 128:(sc + 1) * 128, :], xnb[k][:],
                               deps=[t_xn], lane=k)
                out_toks.append(st_dma)
                readers = [st_dma]
                evs = []
                for dc in range(NDC):
                    kk = dc % 4
                    t_tp = tp_b16(b, ptp[kk][:, 0:128], xnb[k][:, dc * 128:(dc + 1) * 128],
                                  identb, deps=[t_xn, tib, tev_tok.get(kk)])
                    t_ev = b.op("vector", "tensor_copy", fmb[k][:, dc, :], ptp[kk][:, 0:128],
                                deps=[t_tp, fm_dma.get(k)])
                    tev_tok[kk] = t_ev
                    readers.append(t_tp)
                    evs.append(t_ev)
                fm_dma[k] = b.dma("gpsimd",
                                  xn_fm[bi, :, sc * 128:(sc + 1) * 128].rearrange(
                                      "(a p) f -> p a f", p=128),
                                  fmb[k][:], deps=evs, lane=k)
                out_toks.append(fm_dma[k])
                toks["xnb_readers"] = readers
                prev[k] = toks
        b.done(out_toks)


class TpMachine:

    def __init__(self, b, identb, tib, n=2, evac_eng="vector"):
        self.b = b
        self.n = n
        self.identb = identb
        self.tib = tib
        self.evac_eng = evac_eng
        self.ptp = [b.ps(f"tpm_p{i}", [128, 512], F32) for i in range(n)]
        self.ev = [b.sb(f"tpm_e{i}", [128, 128], BF16) for i in range(n)]
        self.tp_tok = {}
        self.ev_dma = {}
        self.i = 0

    def to_dram(self, src_slice, dram_slice, deps):
        b = self.b
        kk = self.i % self.n
        self.i += 1
        pt = self.ptp[kk][:].bitcast(BF16)[:, 0:128]
        t_tp = tp_b16(b, pt, src_slice, self.identb,
                      deps=list(deps) + [self.tib, self.tp_tok.get(kk)])
        t_ev = (b.op("vector", "tensor_copy", self.ev[kk][:], pt,
                     deps=[t_tp, self.ev_dma.get(kk)])
                if self.evac_eng == "vector" else
                b.op("scalar", "activation", self.ev[kk][:], pt, AF.Identity,
                     deps=[t_tp, self.ev_dma.get(kk)]))
        self.tp_tok[kk] = t_ev
        self.ev_dma[kk] = b.dma("sync" if kk % 2 else "gpsimd", dram_slice,
                                self.ev[kk][:], deps=[t_ev], lane=kk % 3)
        return t_tp, self.ev_dma[kk]

    def to_sbuf(self, src_slice, sbuf_slice, deps, extra_ev_deps=()):
        b = self.b
        kk = self.i % self.n
        self.i += 1
        pt = self.ptp[kk][:].bitcast(BF16)[:, 0:128]
        t_tp = tp_b16(b, pt, src_slice, self.identb,
                      deps=list(deps) + [self.tib, self.tp_tok.get(kk)])
        t_ev = (b.op("vector", "tensor_copy", sbuf_slice, pt,
                     deps=[t_tp] + [d for d in extra_ev_deps if d])
                if self.evac_eng == "vector" else
                b.op("scalar", "activation", sbuf_slice, pt, AF.Identity,
                     deps=[t_tp] + [d for d in extra_ev_deps if d]))
        self.tp_tok[kk] = t_ev
        return t_tp, t_ev


def emit_wtrans(b, tpm, wtile, fmb, wt_dram, out_dram, E, prev_reads, fm_dma):
    nec = E // 128
    outs = []
    for ec in range(nec):
        k = ec % 3
        ld = b.dma("gpsimd", wtile[k][:, 0:D], wt_dram[ec * 128:(ec + 1) * 128, :],
                   deps=[prev_reads[k]], lane=k)
        last_tp = None
        evs = []
        for dc in range(NDC):
            t_tp, t_ev = tpm.to_sbuf(wtile[k][:, dc * 128:(dc + 1) * 128],
                                     fmb[k][:, dc, :], deps=[ld],
                                     extra_ev_deps=[fm_dma.get(k)])
            evs.append(t_ev)
            last_tp = t_tp
        fm_dma[k] = b.dma("sync",
                          out_dram[:, ec * 128:(ec + 1) * 128].rearrange(
                              "(a p) f -> p a f", p=128),
                          fmb[k][:], deps=evs, lane=k)
        outs.append(fm_dma[k])
        prev_reads[k] = last_tp
    return outs


def emit_wprep(b, io):
    (qkv_w, proj_w, W0, lora_down, lora_up, down_aux, up_aux, Wq, Wk, Wv, Wo,
     slot_embeds, gamma_slot, qkvT, projT, WeffT, WqT, WkT, WvT, WoT, seT) = (
        io[k] for k in ("qkv_w", "proj_w", "W0", "lora_down", "lora_up", "down_aux",
                        "up_aux", "Wq", "Wk", "Wv", "Wo", "slot_embeds", "gamma_slot",
                        "qkvT", "projT", "WeffT", "WqT", "WkT", "WvT", "WoT", "seT"))
    with b.stage("wp") as ctx:
        identf, tif, identb, tib = b.idents
        tpm = TpMachine(b, identb, tib, n=4)
        wtile = [b.sb(f"wtile{i}", [128, D], BF16) for i in range(3)]
        fmb = [b.sb(f"wfmb{i}", [128, 8, 128], BF16) for i in range(3)]
        prev_reads = [None, None, None]
        fm_dma = {}
        outs = []
        outs += emit_wtrans(b, tpm, wtile, fmb, qkv_w, qkvT, 3 * D, prev_reads, fm_dma)
        outs += emit_wtrans(b, tpm, wtile, fmb, proj_w, projT, D, prev_reads, fm_dma)
        outs += emit_wtrans(b, tpm, wtile, fmb, Wq, WqT, D, prev_reads, fm_dma)
        outs += emit_wtrans(b, tpm, wtile, fmb, Wk, WkT, D, prev_reads, fm_dma)
        outs += emit_wtrans(b, tpm, wtile, fmb, Wv, WvT, D, prev_reads, fm_dma)
        outs += emit_wtrans(b, tpm, wtile, fmb, Wo, WoT, D, prev_reads, fm_dma)

        lds = b.sb("lds", [4, 100])
        lus = b.sb("lus", [50, 4])
        das = b.sb("das", [100, D])
        t_ld = b.dma("sync", lds[:], lora_down[:])
        t_lu = b.dma("sync", lus[:], lora_up[:])
        t_da = b.dma("sync", das[:], down_aux[:])
        p_sm = b.ps("p_sm", [128, 1024], F32)
        ldT = b.sb("ldT", [100, 4])
        t1 = b.sb("t1", [4, D])
        luT = b.sb("luT", [4, 50])
        t2 = b.sb("t2", [50, D])
        uaT = b.sb("uaT", [50, D])
        tk = tp_f32(b, p_sm[0:100, 0:4], lds[:], identf, deps=[t_ld, tif])
        tk = b.op("vector", "tensor_copy", ldT[:], p_sm[0:100, 0:4], deps=[tk])
        m1 = b.mm(p_sm[0:4, 0:512], ldT[:], das[:, 0:512], True, True, deps=[tk, t_da])
        m1b = b.mm(p_sm[0:4, 512:1024], ldT[:], das[:, 512:1024], True, True, deps=[tk, t_da])
        tk1 = b.op("vector", "tensor_copy", t1[:], p_sm[0:4, :], deps=[m1, m1b])
        tk2 = tp_f32(b, p_sm[0:4, 0:50], lus[:], identf, deps=[t_lu, tif, tk1])
        tk2 = b.op("vector", "tensor_copy", luT[:], p_sm[0:4, 0:50], deps=[tk2])
        m2 = b.mm(p_sm[0:50, 0:512], luT[:], t1[:, 0:512], True, True, deps=[tk2, tk1])
        m2b = b.mm(p_sm[0:50, 512:1024], luT[:], t1[:, 512:1024], True, True, deps=[tk2, tk1])
        tk3 = b.op("vector", "tensor_copy", t2[:], p_sm[0:50, :], deps=[m2, m2b])
        uas = b.sb("uas", [128, 50])
        t_ua_prev = None
        for dc in range(NDC):
            t_ua = b.dma("sync", uas[:], up_aux[dc * 128:(dc + 1) * 128, :],
                         deps=[t_ua_prev])
            tk4 = tp_f32(b, p_sm[0:50, 0:128], uas[:], identf, deps=[t_ua, tk3])
            t_ua_prev = b.op("vector", "tensor_copy", uaT[:, dc * 128:(dc + 1) * 128],
                             p_sm[0:50, 0:128], deps=[tk4])
        tk_uaT = t_ua_prev
        w0t = b.sb("w0t", [128, D])
        weff_b = b.sb("weff_b", [128, D], BF16)
        p_li = b.ps("p_li", [128, 1024], F32)
        prev_cast = None
        last_wtp = None
        for oc in range(NDC):
            mli = [b.mm(p_li[:, nb * 512:(nb + 1) * 512],
                        uaT[:, oc * 128:(oc + 1) * 128],
                        t2[:, nb * 512:(nb + 1) * 512], True, True,
                        deps=[tk_uaT, tk3, prev_cast]) for nb in range(2)]
            t_w0 = b.dma("sync", w0t[:], W0[oc * 128:(oc + 1) * 128, :], deps=[prev_cast],
                         lane=oc % 3)
            t_add = b.op("vector", "tensor_add", w0t[:], w0t[:], p_li[:], deps=mli + [t_w0])
            prev_cast = b.op("vector", "tensor_copy", weff_b[:], w0t[:],
                             deps=[t_add, last_wtp])
            evs = []
            kE = oc % 3
            for dc in range(NDC):
                last_wtp, t_ev = tpm.to_sbuf(weff_b[:, dc * 128:(dc + 1) * 128],
                                             fmb[kE][:, dc, :], deps=[prev_cast],
                                             extra_ev_deps=[fm_dma.get(kE)])
                evs.append(t_ev)
            fm_dma[kE] = b.dma("sync",
                               WeffT[:, oc * 128:(oc + 1) * 128].rearrange(
                                   "(a p) f -> p a f", p=128),
                               fmb[kE][:], deps=evs, lane=kE)
            outs.append(fm_dma[kE])

        ones1f = b.sb("ones1f", [1, 128])
        t_o1 = b.op("vector", "memset", ones1f[:], 1.0)
        g_row = b.sb("g_row", [1, D])
        t_gr = b.dma("sync", g_row[:], gamma_slot[:])
        gb = b.sb("gb", [128, D])
        mgb = [b.mm(p_sm[:, nb * 512:(nb + 1) * 512], ones1f[:],
                    g_row[:, nb * 512:(nb + 1) * 512], True, True,
                    deps=[t_o1, t_gr, tk_uaT]) for nb in range(2)]
        t_gb = b.op("vector", "tensor_copy", gb[:], p_sm[:], deps=mgb)
        set_ = [b.sb(f"set{i}", [128, D]) for i in range(2)]
        sed = b.sb("sed", [128, D])
        sen = [b.sb(f"sen{i}", [128, D], BF16) for i in range(2)]
        snq = b.sb("snq", [128, 1])
        srl = b.sb("srl", [128, 1])
        prevS = [{}, {}]
        for tc in range(ET // 128):
            k = tc % 2
            ld = b.dma("sync", set_[k][:], slot_embeds[tc * 128:(tc + 1) * 128, :],
                       deps=[prevS[k].get("rd")], lane=tc % 3)
            t_sq = b.op("scalar", "activation", sed[:], set_[k][:], AF.Square,
                        accum_out=snq[:], deps=[ld, prevS[k].get("t_n2"), prevS[1 - k].get("t_n2")])
            t_n2 = b.op("vector", "tensor_scalar_max", snq[:], snq[:], 1e-24, deps=[t_sq])
            t_ln = b.op("scalar", "activation", srl[:], snq[:], AF.Ln,
                        deps=[t_n2, prevS[1 - k].get("t_sn")])
            t_ex = b.op("scalar", "activation", srl[:], srl[:], AF.Exp, scale=-0.5, deps=[t_ln])
            t_rl = b.op("vector", "tensor_scalar_mul", srl[:], srl[:], 32.0, deps=[t_ex])
            t_sn = b.op("vector", "scalar_tensor_tensor", sen[k][:], set_[k][:], srl[:],
                        gb[:], ALU.mult, ALU.mult,
                        deps=[t_rl, t_gb, ld] + prevS[k].get("senr", []))
            senr = []
            evs = []
            kE = tc % 3
            for dc in range(NDC):
                t_tp, t_ev = tpm.to_sbuf(sen[k][:, dc * 128:(dc + 1) * 128],
                                         fmb[kE][:, dc, :], deps=[t_sn],
                                         extra_ev_deps=[fm_dma.get(kE)])
                senr.append(t_tp)
                evs.append(t_ev)
            fm_dma[kE] = b.dma("sync",
                               seT[:, tc * 128:(tc + 1) * 128].rearrange(
                                   "(a p) f -> p a f", p=128),
                               fmb[kE][:], deps=evs, lane=kE)
            outs.append(fm_dma[kE])
            prevS[k] = {"rd": t_sn, "t_n2": t_n2, "senr": senr, "t_sn": t_sn}
        b.done(outs)


def emit_proj_tm(b, a_fm, WT, bias, o_tm, act="identity", nout=D, tag="pj"):
    nnb = nout // 512
    with b.stage(tag) as ctx:
        wt = [b.sb(f"wtile{i}", [128, nout], BF16) for i in range(NDC)]
        wtoks = [b.dma("sync", wt[i][:], WT[i * 128:(i + 1) * 128, :], lane=i % 3)
                 for i in range(NDC)]
        brow = b.sb("brow", [1, nout])
        browb = b.sb("browb", [1, nout], BF16)
        t_bias = b.dma("sync", brow[:], bias[:])
        t_bb = b.op("vector", "tensor_copy", browb[:], brow[:], deps=[t_bias])
        ones1b = b.sb("ones1b", [1, 128], BF16)
        t_o1 = b.op("vector", "memset", ones1b[:], 1.0)
        at = [b.sb(f"at{i}", [128, S], BF16) for i in range(NDC)]
        ot = [b.sb(f"ot{i}", [128, nout]) for i in range(2)]
        pso = [b.ps(f"pso{i}", [128, nout], F32) for i in range(2)]
        afm_toks = [None] * NDC
        prev_ev = [None, None]
        prev_st = [None, None]
        outs = []
        for bi in range(BPC):
            for i in range(NDC):
                afm_toks[i] = b.dma("gpsimd", at[i][:], a_fm[bi, i * 128:(i + 1) * 128, :],
                                    deps=[afm_toks[i], prev_st[0], prev_st[1]], lane=i % 3)
            for sc in range(NSC):
                k = sc % 2
                mms = []
                for nb in range(nnb):
                    dst = pso[k][:, nb * 512:(nb + 1) * 512]
                    for ic in range(NDC):
                        dep = [afm_toks[ic], wtoks[ic]] if nb == 0 else []
                        b.mm(dst, at[ic][:, sc * 128:(sc + 1) * 128],
                             wt[ic][:, nb * 512:(nb + 1) * 512],
                             ic == 0, False, deps=dep + [prev_ev[k]], inc=False)
                    mms.append(b.mm(dst, ones1b[:],
                                    browb[:, nb * 512:(nb + 1) * 512],
                                    False, True, deps=[t_o1, t_bb]))
                fn = AF.Silu if act == "silu" else AF.Identity
                t_ev = b.op("scalar", "activation", ot[k][:], pso[k][:], fn,
                            deps=mms + [prev_st[k]])
                prev_ev[k] = t_ev
                prev_st[k] = b.dma("sync", o_tm[bi, sc * 128:(sc + 1) * 128, :], ot[k][:],
                                   deps=[t_ev], lane=sc % 3)
                outs.append(prev_st[k])
        b.done(outs)


def emit_l0qkv(b, xn_fm, qkvT, qkv_b, qkv_fm):
    NOC = 3 * D // 128
    with b.stage("q0") as ctx:
        identf, tif, identb, tib = b.idents
        wt = [b.sb(f"wtile{i}", [128, 3 * D], BF16) for i in range(NDC)]
        wtoks = [b.dma("sync", wt[i][:], qkvT[i * 128:(i + 1) * 128, :], lane=i % 3)
                 for i in range(NDC)]
        brow = b.sb("brow", [1, 3 * D])
        t_bias = b.dma("sync", brow[:], qkv_b[:])
        one11 = b.sb("one11", [1, 1])
        t_o11 = b.op("vector", "memset", one11[:], 1.0)
        at = [b.sb(f"at{i}", [128, S], BF16) for i in range(NDC)]
        ot = [b.sb(f"ot{i}", [128, S], BF16) for i in range(2)]
        pso = [b.ps(f"pso{i}", [128, 2048], F32) for i in range(2)]
        bcols = b.sb("bcols", [128, NOC])
        t_bc = []
        for oc in range(NOC):
            tm = b.mm(pso[0][:, oc:oc + 1], brow[:, oc * 128:(oc + 1) * 128], one11[:],
                      True, True, deps=[t_bias, t_o11])
            t_bc.append(tm)
        t_bcols = b.op("vector", "tensor_copy", bcols[:], pso[0][:, 0:NOC], deps=t_bc)
        afm_toks = [None] * NDC
        prev_ev = [t_bcols, None]
        prev_st = [None, None]
        outs = []
        for bi in range(BPC):
            for i in range(NDC):
                afm_toks[i] = b.dma("gpsimd", at[i][:], xn_fm[bi, i * 128:(i + 1) * 128, :],
                                    deps=[afm_toks[i], prev_st[0], prev_st[1]] if bi else [afm_toks[i]],
                                    lane=i % 3)
            for oc in range(NOC):
                k = oc % 2
                mms = []
                for nb in range(4):
                    dst = pso[k][:, nb * 512:(nb + 1) * 512]
                    for ic in range(NDC):
                        last = ic == NDC - 1
                        tm = b.mm(dst, wt[ic][:, oc * 128:(oc + 1) * 128],
                                  at[ic][:, nb * 512:(nb + 1) * 512],
                                  ic == 0, last,
                                  deps=[afm_toks[ic], wtoks[ic], prev_ev[k]],
                                  inc=last)
                    mms.append(tm)
                t_ev = b.op("scalar", "activation", ot[k][:], pso[k][:], AF.Identity,
                            bias=bcols[:, oc:oc + 1], deps=mms + [t_bcols, prev_st[k]])
                prev_ev[k] = t_ev
                prev_st[k] = b.dma("sync", qkv_fm[bi, oc * 128:(oc + 1) * 128, :], ot[k][:],
                                   deps=[t_ev], lane=oc % 3)
                outs.append(prev_st[k])
        b.done(outs)


def emit_l0attn(b, qkv_fm, attn_fm):
    with b.stage("a0") as ctx:
        identf, tif, identb, tib = b.idents
        qkv = [b.sb(f"qkv{i}", [128, S], BF16) for i in range(24)]
        ao = [b.sb(f"ao{i}", [128, S], BF16) for i in range(NDC)]
        vp = [b.sb(f"vp{i}", [128, 16, 80], BF16) for i in range(2)]
        et = [b.sb(f"et{i}", [128, 1024], BF16) for i in range(4)]
        kaug = [b.sb(f"kaug{i}", [128, S], BF16) for i in range(2)]
        qaug = [b.sb(f"qaug{i}", [128, S], BF16) for i in range(2)]
        nums = [b.sb(f"nums{i}", [128, 1024]) for i in range(2)]
        denr = [b.sb(f"denr{i}", [1, 1024]) for i in range(2)]
        rden = [b.sb(f"rden{i}", [1, 1024]) for i in range(2)]
        onesc = b.sb("onesc", [1, 64])
        onescol = b.sb("onescol", [128, 1], BF16)
        qsq = b.sb("qsq", [64, S], BF16)
        ksq = b.sb("ksq", [64, S], BF16)
        km2 = b.sb("km2", [1, 2])
        t_on = b.op("vector", "memset", onesc[:], 1.0)
        t_oc = b.op("vector", "memset", onescol[:], 1.0)
        t_vone = [b.op("vector", "memset", vp[i][:, :, 64:80], 1.0) for i in range(2)]
        t_kone = [b.op("vector", "memset", kaug[i][64:65, :], -1.0) for i in range(2)]
        psS = [b.ps(f"psS{i}", [128, 1024], F32) for i in range(2)]
        psO = b.ps("psO", [128, 1024], F32)
        psB = b.ps("psB", [128, 1024], F32)
        qkv_toks = [None] * 24
        outs = []
        st_prev = {}
        prev = {}
        pending = []
        pending_pre = []

        def emit_preamble(bi, h):
            j = h % 2
            qt = qkv[h // 2]
            kt = qkv[8 + h // 2]
            base = 64 * (h % 2)
            t_ka = b.op("vector", "tensor_copy", kaug[j][0:64, :], kt[base:base + 64, :],
                        deps=[qkv_toks[8 + h // 2], t_kone[j], prev.get(("ka", j))])
            t_qa = b.op("vector", "tensor_copy", qaug[j][0:64, :], qt[base:base + 64, :],
                        deps=[qkv_toks[h // 2], prev.get(("qa", j))])
            t_ks = b.op("vector", "tensor_mul", ksq[:], kt[base:base + 64, :],
                        kt[base:base + 64, :], deps=[qkv_toks[8 + h // 2],
                                                     prev.get("ksq_rd")])
            t_qs = b.op("vector", "tensor_mul", qsq[:], qt[base:base + 64, :],
                        qt[base:base + 64, :], deps=[qkv_toks[h // 2],
                                                     prev.get("qsq_rd")])
            kn_mms = {0: [], 1: []}
            for half, row in ((0, 0), (1, 32)):
                for nb in range(2):
                    kn_mms[half].append(
                        b.mm(psB[row:row + 1, nb * 512:(nb + 1) * 512], onescol[0:64],
                             ksq[:, half * 1024 + nb * 512:half * 1024 + (nb + 1) * 512],
                             True, True,
                             deps=[t_ks, t_oc, prev.get("c1_rd"), prev.get("c2_rd"),
                                   prev.get("psB_mul")]))
            t_k1 = b.op("vector", "reduce_max", km2[:, 0:1], psB[0:1, :], axis=AX.X,
                        deps=kn_mms[0] + [prev.get("km2_rd")])
            t_k2 = b.op("vector", "reduce_max", km2[:, 1:2], psB[32:33, :], axis=AX.X,
                        deps=kn_mms[1])
            t_k3 = b.op("vector", "reduce_max", km2[:, 0:1], km2[:, 0:2], axis=AX.X,
                        deps=[t_k1, t_k2])
            qn_mms = {0: [], 1: []}
            for half, row in ((0, 0), (1, 32)):
                for nb in range(2):
                    qn_mms[half].append(
                        b.mm(psB[row:row + 1, nb * 512:(nb + 1) * 512], onescol[0:64],
                             qsq[:, half * 1024 + nb * 512:half * 1024 + (nb + 1) * 512],
                             True, True,
                             deps=[t_qs, t_oc, t_k1 if half == 0 else t_k2]))
            t_c1 = b.op("scalar", "activation", qaug[j][64:65, 0:1024], psB[0:1, :],
                        AF.Sqrt, scale=km2[:, 0:1],
                        deps=qn_mms[0] + [t_k3, t_qa, prev.get("c1_rd")])
            t_c2 = b.op("scalar", "activation", qaug[j][64:65, 1024:2048], psB[32:33, :],
                        AF.Sqrt, scale=km2[:, 0:1],
                        deps=qn_mms[1] + [t_k3, t_qa, prev.get("c2_rd")])
            prev["qsq_rd"] = qn_mms[1][-1]
            prev["ksq_rd"] = kn_mms[1][-1]
            prev["km2_rd"] = t_c2
            prev["c1_rd"] = t_c1
            prev["c2_rd"] = t_c2
            return {"t_ka": t_ka, "t_qa": t_qa, "t_c1": t_c1, "t_c2": t_c2}

        def emit_vps(bi2, h2):
            j2 = h2 % 2
            vt2 = qkv[16 + h2 // 2]
            t_vps2 = []
            for kc in range(NSC):
                t_vps2.append(b.dma("sync", vp[j2][:, kc, 0:64],
                                    vt2[64 * (h2 % 2):64 * (h2 % 2) + 64,
                                        kc * 128:(kc + 1) * 128],
                                    transpose=True,
                                    deps=[qkv_toks[16 + h2 // 2], t_vone[j2],
                                          prev.get(("vp", j2))], lane=kc % 3))
            return t_vps2

        def emit_next_loads(nbi, i, vtoks):
            if vtoks is None:
                pre2 = prev["_pre"]
                qkv_toks[i] = b.dma("gpsimd" if i % 2 else "sync", qkv[i][:],
                                    qkv_fm[nbi, i * 128:(i + 1) * 128, :],
                                    deps=[qkv_toks[i], prev.get("qsq_rd"),
                                          pre2["t_qa"]])
                kI = 8 + i
                qkv_toks[kI] = b.dma("gpsimd" if kI % 2 else "sync", qkv[kI][:],
                                    qkv_fm[nbi, kI * 128:(kI + 1) * 128, :],
                                    deps=[qkv_toks[kI], prev.get("ksq_rd"),
                                          pre2["t_ka"]])
            else:
                vI = 16 + i
                qkv_toks[vI] = b.dma("gpsimd" if vI % 2 else "sync", qkv[vI][:],
                                     qkv_fm[nbi, vI * 128:(vI + 1) * 128, :],
                                     deps=[qkv_toks[vI]] + vtoks)

        for bi in range(BPC):
            ao_w = {}
            if bi == 0:
                for i in [c + 8 * k for c in range(8) for k in range(3)]:
                    qkv_toks[i] = b.dma("gpsimd" if i % 2 else "sync", qkv[i][:],
                                        qkv_fm[bi, i * 128:(i + 1) * 128, :],
                                        deps=[qkv_toks[i]], lane=i % 3)
            pre = emit_preamble(bi, 0)
            pre_vps = emit_vps(bi, 0)
            for h in range(H0):
                j = h % 2
                t_vps = pre_vps
                t_ka, t_qa = pre["t_ka"], pre["t_qa"]
                t_c1, t_c2 = pre["t_c1"], pre["t_c2"]
                pending_pre.clear()
                if h + 1 < H0:
                    pending_pre.append(lambda bi=bi, h=h: prev.__setitem__(
                        "_pre", emit_preamble(bi, h + 1)))
                    pending_pre.append(lambda bi=bi, h=h: prev.__setitem__(
                        "_vps", emit_vps(bi, h + 1)))
                if bi + 1 < BPC:
                    if h % 2 == 0:
                        pending_pre.append(lambda bi=bi, h=h: emit_next_loads(
                            bi + 1, h // 2, None))
                    else:
                        pending_pre.append(lambda bi=bi, h=h, tv=t_vps: emit_next_loads(
                            bi + 1, h // 2, tv))
                for qh in range(2):
                    p = qh % 2
                    qsl = slice(qh * 1024, (qh + 1) * 1024)
                    t_avs = []
                    smm = {}

                    def emit_S(kc):
                        kk = kc % 2
                        tms = []
                        for nb in range(2):
                            tms.append(b.mm(psS[kk][:, nb * 512:(nb + 1) * 512],
                                            kaug[j][0:65, kc * 128:(kc + 1) * 128],
                                            qaug[j][0:65, qh * 1024 + nb * 512:qh * 1024 + (nb + 1) * 512],
                                            True, True,
                                            deps=[t_ka, t_qa, t_c2,
                                                  prev.get(("exp", kk, nb))],
                                            inc=True))
                        smm[kc] = tms

                    emit_S(0)
                    emit_S(1)
                    for kc in range(NSC):
                        kk = kc % 2
                        ke = kc % 4
                        if pending and qh == 0 and kc == 3:
                            for fn in pending:
                                fn()
                            pending.clear()
                        if pending_pre and qh == 1 and kc == 6:
                            for fn in pending_pre:
                                fn()
                            pending_pre.clear()
                        t_exp = {}
                        for nb in range(2):
                            t_exp[nb] = b.op("scalar", "activation",
                                             et[ke][:, nb * 512:(nb + 1) * 512],
                                             psS[kk][:, nb * 512:(nb + 1) * 512],
                                             AF.Exp, scale=0.125,
                                             deps=[smm[kc][nb],
                                                   prev.get(("av", ke, nb))])
                            prev[("exp", kk, nb)] = t_exp[nb]
                        if kc + 2 < NSC:
                            emit_S(kc + 2)
                        for nb in range(2):
                            tav = b.mm(psO[0:65, nb * 512:(nb + 1) * 512],
                                       vp[j][:, kc, 0:65],
                                       et[ke][:, nb * 512:(nb + 1) * 512],
                                       kc == 0, kc == NSC - 1,
                                       deps=[t_exp[nb]] + (t_vps +
                                                           [prev.get(("numden", nb))]
                                                           if kc == 0 else []),
                                       inc=True)
                            prev[("av", ke, nb)] = tav
                        t_avs.append(tav)
                    prev[("vp", j)] = t_avs[-1]
                    t_num, t_den = [], []
                    for nb in range(2):
                        csl = slice(nb * 512, (nb + 1) * 512)
                        tn = b.op("vector", "tensor_copy", nums[p][0:64, csl],
                                  psO[0:64, csl],
                                  deps=t_avs + [prev.get(("num_rd", p))])
                        td = b.op("vector", "tensor_copy", denr[p][:, csl],
                                  psO[64:65, csl], deps=[tn])
                        t_num.append(tn)
                        t_den.append(td)
                        prev[("numden", nb)] = td

                    def tail(p=p, h=h, qh=qh, qsl=qsl, t_num=t_num, t_den=t_den,
                             st_tok=st_prev.get(h // 2)):
                        t_l = b.op("scalar", "activation", rden[p][:], denr[p][:], AF.Ln,
                                   deps=t_den + [prev.get(("rden_rd", p))])
                        t_r = b.op("scalar", "activation", rden[p][:], rden[p][:], AF.Exp,
                                   scale=-1.0, deps=[t_l])
                        mb = []
                        for nb in range(2):
                            mb.append(b.mm(psB[64:128, nb * 512:(nb + 1) * 512], onesc[:],
                                           rden[p][:, nb * 512:(nb + 1) * 512], True, True,
                                           deps=[t_r, t_on, prev.get("psB_mul"),
                                                 prev.get("c1_rd"),
                                                 prev.get("c2_rd")]))
                        t_mul = b.op("vector", "tensor_tensor",
                                     ao[h // 2][64 * (h % 2):64 * (h % 2) + 64, qsl],
                                     nums[p][0:64, :], psB[64:128, :], ALU.mult,
                                     deps=mb + t_num + [st_tok])
                        prev["psB_mul"] = t_mul
                        prev[("num_rd", p)] = t_mul
                        prev[("den_rd", p)] = t_mul
                        prev[("rden_rd", p)] = t_mul
                        ao_w[h // 2] = t_mul
                    pending.append(tail)
                for fn in pending_pre:
                    fn()
                pending_pre.clear()
                prev[("ka", j)] = t_avs[-1]
                prev[("qa", j)] = t_avs[-1]
                pre = prev.pop("_pre", None)
                pre_vps = prev.pop("_vps", None)
            for fn in pending:
                fn()
            pending.clear()
            st_toks = {}
            for dc in range(NDC):
                if ao_w.get(dc) is None:
                    continue
                st_toks[dc] = b.dma("sync", attn_fm[bi, dc * 128:(dc + 1) * 128, :],
                                    ao[dc][:], deps=[ao_w[dc]], lane=dc % 3)
                outs.append(st_toks[dc])
            st_prev = st_toks
        b.done(outs)


def emit_resid(b, x, s_tm, g_tm, x1, tag="rs"):
    with b.stage(tag) as ctx:
        W = 4 * D
        xt = [b.sb(f"xt{i}", [128, W]) for i in range(2)]
        st_ = [b.sb(f"st{i}", [128, W]) for i in range(2)]
        gt = [b.sb(f"gt{i}", [128, W]) for i in range(2)]
        prev = [None, None]
        outs = []
        for bi in range(BPC):
            for blk in range(4):
                k = blk % 2
                sl = slice(blk * 512, (blk + 1) * 512)
                vw = lambda t: t[bi, sl, :].rearrange("(a p) f -> p a f", p=128)
                v3 = lambda t: t[:].rearrange("p (a f) -> p a f", f=D)
                l1 = b.dma("sync", v3(xt[k]), vw(x), deps=[prev[k]], lane=k)
                l2 = b.dma("gpsimd", v3(st_[k]), vw(s_tm), deps=[prev[k]], lane=k)
                l3 = b.dma("sync", v3(gt[k]), vw(g_tm), deps=[prev[k]], lane=2)
                t1 = b.op("vector", "tensor_mul", st_[k][:], st_[k][:], gt[k][:],
                          deps=[l2, l3])
                t2 = b.op("vector", "tensor_add", xt[k][:], xt[k][:], st_[k][:],
                          deps=[t1, l1])
                prev[k] = b.dma("sync", vw(x1), v3(xt[k]), deps=[t2], lane=k)
                outs.append(prev[k])
        b.done(outs)


def emit_l0epi(b, xn_fm, WeffT, b0, attn_fm, projT, proj_b, x, x1):
    with b.stage("e0") as ctx:
        wg = [b.sb(f"wg{i}", [128, D], BF16) for i in range(NDC)]
        wp_ = [b.sb(f"wp{i}", [128, D], BF16) for i in range(NDC)]
        wg_t = [b.dma("sync", wg[i][:], WeffT[i * 128:(i + 1) * 128, :], lane=i % 3)
                for i in range(NDC)]
        wp_t = [b.dma("gpsimd", wp_[i][:], projT[i * 128:(i + 1) * 128, :], lane=i % 3)
                for i in range(NDC)]
        brow = b.sb("brow", [1, 2 * D])
        browb = b.sb("browb", [1, 2 * D], BF16)
        t_b1 = b.dma("sync", brow[:, 0:D], b0[:])
        t_b2 = b.dma("sync", brow[:, D:2 * D], proj_b[:])
        t_bb = b.op("vector", "tensor_copy", browb[:], brow[:], deps=[t_b1, t_b2])
        ones1b = b.sb("ones1b", [1, 128], BF16)
        t_o1 = b.op("vector", "memset", ones1b[:], 1.0)
        ag = [b.sb(f"ag{i}", [128, S], BF16) for i in range(NDC)]
        as_ = [b.sb(f"as{i}", [128, S], BF16) for i in range(NDC)]
        ag_t = [None] * NDC
        as_t = [None] * NDC
        gbuf = [b.sb(f"gbuf{i}", [128, D]) for i in range(NSC)]
        xb = [b.sb(f"xb{i}", [128, D]) for i in range(2)]
        yb = [b.sb(f"yb{i}", [128, D]) for i in range(2)]
        psg = [b.ps(f"psg{i}", [128, 1024], F32) for i in range(2)]
        pss = [b.ps(f"pss{i}", [128, 1024], F32) for i in range(2)]
        outs = []
        prev = {}
        for bi in range(BPC):
            for i in range(NDC):
                ag_t[i] = b.dma("gpsimd", ag[i][:], xn_fm[bi, i * 128:(i + 1) * 128, :],
                                deps=[ag_t[i], prev.get("glast")], lane=i % 3)
                as_t[i] = b.dma("sync", as_[i][:], attn_fm[bi, i * 128:(i + 1) * 128, :],
                                deps=[as_t[i], prev.get("slast")], lane=i % 3)
            for sc in range(NSC):
                k = sc % 2
                mms = []
                for nb in range(2):
                    dst = psg[k][:, nb * 512:(nb + 1) * 512]
                    for ic in range(NDC):
                        b.mm(dst, ag[ic][:, sc * 128:(sc + 1) * 128],
                             wg[ic][:, nb * 512:(nb + 1) * 512], ic == 0, False,
                             deps=[ag_t[ic], wg_t[ic], prev.get(f"gev{k}")], inc=False)
                    mms.append(b.mm(dst, ones1b[:], browb[:, nb * 512:(nb + 1) * 512],
                                    False, True, deps=[t_o1, t_bb]))
                t_g = b.op("scalar", "activation", gbuf[sc][:], psg[k][:], AF.Silu,
                           deps=mms + [prev.get(("g", sc))])
                prev[f"gev{k}"] = t_g
                prev[("g", sc)] = t_g
            prev["glast"] = prev[("g", NSC - 1)]
            for sc in range(NSC):
                k = sc % 2
                ssl = slice(sc * 128, (sc + 1) * 128)
                l_x = b.dma("sync", xb[k][:], x[bi, ssl, :],
                            deps=[prev.get(f"xadd{k}")], lane=k)
                mms = []
                for nb in range(2):
                    dst = pss[k][:, nb * 512:(nb + 1) * 512]
                    for ic in range(NDC):
                        b.mm(dst, as_[ic][:, sc * 128:(sc + 1) * 128],
                             wp_[ic][:, nb * 512:(nb + 1) * 512], ic == 0, False,
                             deps=[as_t[ic], wp_t[ic], prev.get(f"smul{k}")], inc=False)
                    mms.append(b.mm(dst, ones1b[:],
                                    browb[:, D + nb * 512:D + (nb + 1) * 512],
                                    False, True, deps=[t_o1, t_bb]))
                t_sg = b.op("vector", "tensor_mul", yb[k][:], pss[k][:], gbuf[sc][:],
                            deps=mms + [prev.get(("g", sc)), prev.get(f"yst{k}")])
                prev[f"smul{k}"] = t_sg
                prev[("g", sc)] = t_sg
                t_x = b.op("vector", "tensor_add", yb[k][:], yb[k][:], xb[k][:],
                           deps=[t_sg, l_x])
                d_o = b.dma("sync", x1[bi, ssl, :], yb[k][:], deps=[t_x], lane=k)
                prev[f"yst{k}"] = d_o
                prev[f"xadd{k}"] = d_o
                outs.append(d_o)
            prev["slast"] = prev["smul0"]
        b.done(outs)


def emit_l1qkv(b, xn_fm, WqT, WkT, WvT, bq, bk, bv, beta, Qt_fm, Kt_fm, Kt_tm, Vp_tm):
    with b.stage("q1") as ctx:
        identf, tif, identb, tib = b.idents
        wq = [b.sb(f"wq{i}", [128, D], BF16) for i in range(NDC)]
        wk = [b.sb(f"wk{i}", [128, D], BF16) for i in range(NDC)]
        wv = [b.sb(f"wv{i}", [128, D], BF16) for i in range(NDC)]
        wtok = {}
        for i in range(NDC):
            wtok[("q", i)] = b.dma("sync", wq[i][:], WqT[i * 128:(i + 1) * 128, :],
                                   lane=i % 3)
            wtok[("k", i)] = b.dma("gpsimd", wk[i][:], WkT[i * 128:(i + 1) * 128, :],
                                   lane=i % 3)
            wtok[("v", i)] = b.dma("sync", wv[i][:], WvT[i * 128:(i + 1) * 128, :],
                                   lane=(i + 1) % 3)
        brows = b.sb("brows", [1, 3 * D])
        browsb = b.sb("browsb", [1, 3 * D], BF16)
        t_b1 = b.dma("sync", brows[:, 0:D], bq[:])
        t_b2 = b.dma("sync", brows[:, D:2 * D], bk[:])
        t_b3 = b.dma("sync", brows[:, 2 * D:3 * D], bv[:])
        t_bb = b.op("vector", "tensor_copy", browsb[:], brows[:], deps=[t_b1, t_b2, t_b3])
        ones1b = b.sb("ones1b", [1, 128], BF16)
        t_o1 = b.op("vector", "memset", ones1b[:], 1.0)
        ones1f = b.sb("ones1f", [1, 128])
        t_o1f = b.op("vector", "memset", ones1f[:], 1.0)
        berow = b.sb("berow", [1, 1])
        becol = b.sb("becol", [128, 1])
        t_be = b.dma("sync", berow[:], beta[:])
        t_be = b.op("scalar", "activation", berow[:], berow[:], AF.Exp, deps=[t_be])
        psq = [b.ps(f"psq{i}", [128, 1024], F32) for i in range(3)]
        mbe = b.mm(psq[0][:, 0:1], ones1f[:], berow[:], True, True, deps=[t_be, t_o1f])
        t_bec = b.op("vector", "tensor_copy", becol[:], psq[0][:, 0:1], deps=[mbe])

        xn = [b.sb(f"xn{i}", [128, S], BF16) for i in range(NDC)]
        gmax = [b.sb(f"gmax{i}", [128, 64]) for i in range(4)]
        sub = [b.sb(f"sub{i}", [128, 1024]) for i in range(4)]
        ex = [b.sb(f"ex{i}", [128, 1024]) for i in range(2)]
        s16 = [b.sb(f"s16_{i}", [128, 64]) for i in range(2)]
        rcp = [b.sb(f"rcp{i}", [128, 64]) for i in range(2)]
        qn = [b.sb(f"qn{i}", [128, 1024], BF16) for i in range(2)]
        kn = [b.sb(f"kn{i}", [128, 1024], BF16) for i in range(2)]
        vpb = [b.sb(f"vpb{i}", [128, 64, 17], BF16) for i in range(2)]
        tpm = TpMachine(b, identb, tib, evac_eng="scalar")
        xtok = [None] * NDC
        outs = []
        prev = {}
        vone = [b.op("vector", "memset", vpb[i][:, :, 16:17], 1.0) for i in range(2)]
        for bi in range(BPC):
            for i in range(NDC):
                xtok[i] = b.dma("gpsimd", xn[i][:], xn_fm[bi, i * 128:(i + 1) * 128, :],
                                deps=[xtok[i], prev.get("last"), prev.get("vev0"),
                                      prev.get("vev1")], lane=i % 3)
            pend = []
            for sc in range(NSC):
                ssl = slice(sc * 128, (sc + 1) * 128)
                kk = sc % 2
                allmms = {}
                for ti, (wt, wkey, boff) in enumerate(
                        [(wq, "q", 0), (wk, "k", D), (wv, "v", 2 * D)]):
                    ps = psq[ti]
                    mms = []
                    for nb in range(2):
                        dst = ps[:, nb * 512:(nb + 1) * 512]
                        for ic in range(NDC):
                            b.mm(dst, xn[ic][:, ssl], wt[ic][:, nb * 512:(nb + 1) * 512],
                                 ic == 0, False,
                                 deps=[xtok[ic], wtok[(wkey, ic)], t_bec,
                                       prev.get(f"ps_rd{ti}")],
                                 inc=False)
                        mms.append(b.mm(dst, ones1b[:],
                                        browsb[:, boff + nb * 512:boff + (nb + 1) * 512],
                                        False, True, deps=[t_o1, t_bb]))
                    allmms[wkey] = mms
                for fn in pend:
                    fn()
                pend = []
                for ti, (wkey, nbuf, fm_dram, tm_dram) in enumerate(
                        [("q", qn, Qt_fm, None), ("k", kn, Kt_fm, Kt_tm)]):
                    ps = psq[ti]
                    pid = ti * 2 + kk
                    mms = allmms[wkey]
                    ps3 = ps[:].rearrange("p (g k) -> p g k", k=16)
                    t_gm = b.op("vector", "reduce_max",
                                gmax[pid][:].rearrange("p (g o) -> p g o", o=1), ps3,
                                axis=AX.X, deps=mms + [prev.get(f"gmax_rd{pid}")])
                    t_sub = b.op("vector", "tensor_tensor",
                                 sub[pid][:].rearrange("p (g k) -> p g k", k=16), ps3,
                                 gmax[pid][:].rearrange("p (g o) -> p g o", o=1).broadcast_to(
                                     [128, 64, 16]), ALU.subtract,
                                 deps=[t_gm, prev.get(f"sub_rd{pid}")])
                    prev[f"ps_rd{ti}"] = t_sub
                    prev[f"gmax_rd{pid}"] = t_sub
                    if wkey == "q":
                        t_n = b.op("scalar", "activation", nbuf[kk][:], sub[pid][:],
                                   AF.Exp, scale=becol[:],
                                   deps=[t_sub, t_bec, prev.get(f"{wkey}n{kk}")])
                        prev[f"sub_rd{pid}"] = t_n
                    else:
                        t_exp = b.op("scalar", "activation", ex[kk][:], sub[pid][:], AF.Exp,
                                     scale=becol[:], deps=[t_sub, t_bec,
                                                           prev.get(f"ex_rd{kk}"),
                                                           prev.get(f"ex_rd2{kk}")])
                        prev[f"sub_rd{pid}"] = t_exp
                        t_ss = b.op("vector", "reduce_sum",
                                    s16[kk][:].rearrange("p (g o) -> p g o", o=1),
                                    ex[kk][:].rearrange("p (g k) -> p g k", k=16), axis=AX.X,
                                    deps=[t_exp, prev.get(f"s16_rd{kk}")])
                        t_rcp = b.op("vector", "reciprocal", rcp[kk][:], s16[kk][:],
                                     deps=[t_ss, prev.get(f"rcp_rd{kk}")])
                        prev[f"s16_rd{kk}"] = t_rcp
                        t_n = b.op("vector", "tensor_tensor",
                                   nbuf[kk][:].rearrange("p (g k) -> p g k", k=16),
                                   ex[kk][:].rearrange("p (g k) -> p g k", k=16),
                                   rcp[kk][:].rearrange("p (g o) -> p g o", o=1).broadcast_to(
                                       [128, 64, 16]), ALU.mult,
                                   deps=[t_rcp, t_exp, prev.get(f"{wkey}n{kk}")])
                        prev[f"rcp_rd{kk}"] = t_n
                        prev[f"ex_rd{kk}"] = t_ss
                        prev[f"ex_rd2{kk}"] = t_n

                    def flush(wkey=wkey, nbuf=nbuf, fm_dram=fm_dram, tm_dram=tm_dram,
                              kk=kk, ssl=ssl, t_n=t_n, bi=bi):
                        rd = []
                        if tm_dram is not None:
                            rd.append(b.dma("sync", tm_dram[bi, ssl, :], nbuf[kk][:],
                                            deps=[t_n], lane=2))
                            outs.append(rd[-1])
                        for dc in range(NDC):
                            t_tp, t_dma = tpm.to_dram(nbuf[kk][:, dc * 128:(dc + 1) * 128],
                                                      fm_dram[bi, dc * 128:(dc + 1) * 128, ssl],
                                                      deps=[t_n])
                            rd.append(t_tp)
                            outs.append(t_dma)
                        prev[f"{wkey}n{kk}"] = rd[-1]
                        prev["last"] = rd[-1]
                    pend.append(flush)
                t_vev = b.op("scalar", "activation", vpb[kk][:, :, 0:16],
                             psq[2][:].rearrange("p (h e) -> p h e", e=16), AF.Identity,
                             deps=allmms["v"] + [prev.get(f"vst{kk}"), vone[kk]])
                prev["ps_rd2"] = t_vev
                t_vst = b.dma("sync", Vp_tm[bi, ssl, :],
                              vpb[kk][:].rearrange("p h e -> p (h e)"), deps=[t_vev],
                              lane=kk)
                prev[f"vev{kk}"] = t_vev
                prev[f"vst{kk}"] = t_vst
                outs.append(t_vst)
            for fn in pend:
                fn()
        b.done(outs)


def emit_l1attn(b, Qt_fm, Kt_fm, Kt_tm, Vp_tm, attn_fm, ng=8):
    NG = ng
    GW = 8 * 17
    with b.stage("a1") as ctx:
        identf, tif, identb, tib = b.idents
        ind = b.sb("ind", [128, 8])
        t_i = b.op("gpsimd", "memset", ind[:], 1.0)
        t_i = b.op("gpsimd", "affine_select", out=ind[:], in_=ind[:], compare_op=ALU.is_ge,
                   fill=0.0, base=0, pattern=[[-16, 8]], channel_multiplier=1, deps=[t_i])
        t_i = b.op("gpsimd", "affine_select", out=ind[:], in_=ind[:], compare_op=ALU.is_ge,
                   fill=0.0, base=15, pattern=[[16, 8]], channel_multiplier=-1, deps=[t_i])
        indb = b.sb("indb", [128, 8], BF16)
        t_i = b.op("vector", "tensor_copy", indb[:], ind[:], deps=[t_i])
        mask8 = b.sb("mask8", [128, 1024], BF16)
        t_m8 = b.op("gpsimd", "memset", mask8[:], 1.0)
        t_m8 = b.op("gpsimd", "affine_select", out=mask8[:], in_=mask8[:],
                    compare_op=ALU.is_ge, fill=0.0, base=0,
                    pattern=[[0, 8], [1, 128]], channel_multiplier=-1, deps=[t_m8])
        bmask = b.sb("bmask", [128, GW], BF16)
        t_bm = b.op("gpsimd", "memset", bmask[:], 1.0)
        t_bm = b.op("gpsimd", "affine_select", out=bmask[:], in_=bmask[:],
                    compare_op=ALU.is_ge, fill=0.0, base=0,
                    pattern=[[-16, 8], [0, 17]], channel_multiplier=1, deps=[t_bm])
        t_bm = b.op("gpsimd", "affine_select", out=bmask[:], in_=bmask[:],
                    compare_op=ALU.is_ge, fill=0.0, base=15,
                    pattern=[[16, 8], [0, 17]], channel_multiplier=-1, deps=[t_bm])
        qgb = [[b.sb(f"qg{p}_{u}", [128, S], BF16) for u in range(2)] for p in range(2)]
        kgb = [[b.sb(f"kg{p}_{u}", [128, S], BF16) for u in range(2)] for p in range(2)]
        ktg = [[b.sb(f"ktg{u}_{i}", [128, 128], BF16) for i in range(NSC)] for u in range(2)]
        vg = [[b.sb(f"vg{u}_{i}", [128, GW], BF16) for i in range(NSC)] for u in range(2)]
        qmb = [[b.sb(f"qmb{u}_{p}", [128, 1024], BF16) for p in range(2)]
               for u in range(2)]
        smt = [b.sb(f"smt{u}", [128, 1024], BF16) for u in range(2)]
        sraw = [b.sb(f"sraw{u}", [128, 1024], BF16) for u in range(2)]
        state = [b.sb(f"state{u}", [128, GW]) for u in range(2)]
        state_bd = [b.sb(f"state_bd{u}", [128, GW], BF16) for u in range(2)]
        dens = [b.sb(f"dens{u}", [128, 8]) for u in range(2)]
        rdens = [b.sb(f"rdens{u}", [128, 8]) for u in range(2)]
        at_tm = [b.sb(f"at{i}", [128, D], BF16) for i in range(NSC)]
        psS = [b.ps(f"psS{u}", [128, 1024], F32) for u in range(2)]
        psO = [b.ps(f"psO{u}", [128, 512], F32) for u in range(2)]
        psT = [b.ps(f"psT{u}", [128, 512], F32) for u in range(2)]
        fmbt = [b.sb(f"fmbt{i}", [128, 8, 128], BF16) for i in range(2)]
        outs = []
        ld_prev = {}
        prev = {}
        qtoks = {}
        for bi in range(BPC):
            for gp in range(NG // 2):
                par = gp % 2
                toks = [{}, {}]
                for u in range(2):
                    g = gp * 2 + u
                    if (bi, gp) == (0, 0):
                        qtoks[(par, u)] = b.dma("sync", qgb[par][u][:],
                                                Qt_fm[bi, g * 128:(g + 1) * 128, :],
                                                lane=u)
                        qtoks[("k", par, u)] = b.dma("gpsimd", kgb[par][u][:],
                                                     Kt_fm[bi, g * 128:(g + 1) * 128, :],
                                                     lane=u)
                    t_kt, t_v = [], []
                    for c in range(NSC):
                        t_kt.append(b.dma("sync", ktg[u][c][:],
                                          Kt_tm[bi, c * 128:(c + 1) * 128,
                                                g * 128:(g + 1) * 128],
                                          deps=[ld_prev.get(("kt", u, c))],
                                          lane=2 - c % 2))
                        t_v.append(b.dma("gpsimd", vg[u][c][:],
                                         Vp_tm[bi, c * 128:(c + 1) * 128,
                                               g * GW:(g + 1) * GW],
                                         deps=[ld_prev.get(("v", u, c))],
                                         lane=2 - c % 2))
                    t_st = b.op("vector", "memset", state[u][:], 0.0,
                                deps=[prev.get(("st_rd", u))])
                    toks[u] = {"t_q": qtoks[(par, u)], "t_k": qtoks[("k", par, u)],
                               "t_kt": t_kt, "t_v": t_v, "t_st": t_st}

                def qm_jit(c, u):
                    tk = toks[u]
                    t = b.op("vector", "tensor_tensor",
                             qmb[u][c % 2][:].rearrange("p (g f) -> p g f", f=128),
                             qgb[par][u][:, c * 128:(c + 1) * 128].rearrange(
                                 "p (o f) -> p o f", o=1).broadcast_to([128, 8, 128]),
                             indb[:].rearrange("p (g o) -> p g o", o=1).broadcast_to(
                                 [128, 8, 128]), ALU.mult,
                             deps=[tk["t_q"], t_i, prev.get(("qmrd", u, c % 2))])
                    prev[("qm", u, c % 2)] = t

                for u in range(2):
                    qm_jit(0, u)
                    qm_jit(1, u)
                for c in range(NSC):
                    t_sms = {}
                    t_sus = {}
                    for u in range(2):
                        tk = toks[u]
                        t_sm = []
                        for h2 in range(8):
                            t_sm.append(b.mm(psS[u][:, h2 * 128:(h2 + 1) * 128],
                                             kgb[par][u][:, c * 128:(c + 1) * 128],
                                             qmb[u][c % 2][:, h2 * 128:(h2 + 1) * 128],
                                             True, True,
                                             deps=[tk["t_k"], prev.get(("qm", u, c % 2)),
                                                   prev.get(("sev", u))]))
                        t_sms[u] = t_sm
                        prev[("qmrd", u, c % 2)] = t_sm[-1]
                        t_sus[u] = b.mm(psT[u][0:128, 0:GW], ktg[u][c][:], vg[u][c][:],
                                        True, True,
                                        deps=[tk["t_kt"][c], tk["t_v"][c],
                                              prev.get(("st_add", u))])
                    if c == 4 and not (bi == BPC - 1 and gp == NG // 2 - 1):
                        nbi, ngp = (bi, gp + 1) if gp + 1 < NG // 2 else (bi + 1, 0)
                        npar = 1 - par
                        for u in range(2):
                            g2 = ngp * 2 + u
                            qtoks[(npar, u)] = b.dma(
                                "sync", qgb[npar][u][:],
                                Qt_fm[nbi, g2 * 128:(g2 + 1) * 128, :],
                                deps=[qtoks.get((npar, u)),
                                      ld_prev.get(("qg", npar, u))], lane=u)
                            qtoks[("k", npar, u)] = b.dma(
                                "gpsimd", kgb[npar][u][:],
                                Kt_fm[nbi, g2 * 128:(g2 + 1) * 128, :],
                                deps=[qtoks.get(("k", npar, u)),
                                      ld_prev.get(("kg", npar, u))], lane=u)
                    for u in range(2):
                        g = gp * 2 + u
                        tk = toks[u]
                        t_sm = t_sms[u]
                        t_cp = b.op("scalar", "activation", sraw[u][:], psS[u][:],
                                    AF.Identity, deps=t_sm + [prev.get(("sraw", u))])
                        prev[("sev", u)] = t_cp
                        t_sev = b.op("vector", "tensor_tensor", smt[u][:], sraw[u][:],
                                     mask8[:], ALU.mult,
                                     deps=[t_cp, t_m8, prev.get(("intra", u))])
                        prev[("sraw", u)] = t_sev
                        first = None
                        if c > 0:
                            t_bd = b.op("vector", "tensor_tensor", state_bd[u][:],
                                        state[u][:], bmask[:], ALU.mult,
                                        deps=[t_bm, prev[("st_add", u)],
                                              prev.get(("inter", u))])
                            first = b.mm(psO[u][0:128, 0:GW],
                                         qgb[par][u][:, c * 128:(c + 1) * 128],
                                         state_bd[u][:], True, False,
                                         deps=[tk["t_q"], t_bd, prev.get(("oev", u))],
                                         inc=True)
                            prev[("inter", u)] = first
                        t_im = None
                        for h2 in range(8):
                            t_im = b.mm(psO[u][0:128, h2 * 17:(h2 + 1) * 17],
                                        smt[u][:, h2 * 128:(h2 + 1) * 128],
                                        vg[u][c][:, h2 * 17:(h2 + 1) * 17],
                                        first is None and h2 == 0, h2 == 7,
                                        deps=[t_sev, tk["t_v"][c]] +
                                             ([prev.get(("oev", u))] if first is None else []),
                                        inc=h2 == 7)
                        prev[("intra", u)] = t_im
                        t_sa = b.op("vector", "tensor_add", state[u][:], state[u][:],
                                    psT[u][0:128, 0:GW], deps=[t_sus[u], tk["t_st"]])
                        prev[("st_add", u)] = t_sa
                        t_cpd = b.op("vector", "tensor_copy", dens[u][:],
                                     psO[u][0:128, 0:GW].rearrange(
                                         "p (h e) -> p h e", e=17)[:, :, 16],
                                     deps=[t_im, first, prev.get(("rd_rd", u))])
                        t_r = b.op("vector", "reciprocal", rdens[u][:], dens[u][:],
                                   deps=[t_cpd])
                        t_o = b.op("vector", "tensor_tensor",
                                   at_tm[c][:, g * 128:(g + 1) * 128].rearrange(
                                       "p (h e) -> p h e", e=16),
                                   psO[u][0:128, 0:GW].rearrange(
                                       "p (h e) -> p h e", e=17)[:, :, 0:16],
                                   rdens[u][:].rearrange("p (h o) -> p h o", o=1).broadcast_to(
                                       [128, 8, 16]), ALU.mult,
                                   deps=[t_r, t_im, prev.get(("at", c))])
                        prev[("oev", u)] = t_o
                        prev[("rd_rd", u)] = t_o
                        prev[("at", c)] = t_o
                        if c + 2 < NSC:
                            qm_jit(c + 2, u)
                for u in range(2):
                    prev[("st_rd", u)] = prev[("oev", u)]
                    ld_prev[("qg", par, u)] = prev[("oev", u)]
                    ld_prev[("kg", par, u)] = prev[("oev", u)]
                    for c in range(NSC):
                        ld_prev[("kt", u, c)] = prev[("st_add", u)]
                        ld_prev[("v", u, c)] = prev[("oev", u)]
            fmba = [psS[0], psS[1]]
            for c in range(NSC):
                kE = c % 2
                pt = fmba[kE][:].bitcast(BF16)
                evs = []
                last = None
                for dc in range(NDC if NG == 8 else NG):
                    t_tp = tp_b16(b, pt[:, dc * 128:(dc + 1) * 128],
                                  at_tm[c][:, dc * 128:(dc + 1) * 128], identb,
                                  deps=[prev[("at", c)], tib, prev.get(("fmev", kE)),
                                        prev.get(("sev", kE)) if c < 2 else None])
                    t_ev = b.op("vector", "tensor_copy", fmbt[kE][:, dc, :],
                                pt[:, dc * 128:(dc + 1) * 128],
                                deps=[t_tp, prev.get(("fmdma", kE))])
                    evs.append(t_ev)
                    last = t_ev
                prev[("fmev", kE)] = last
                prev[("fmdma", kE)] = b.dma("sync",
                                            attn_fm[bi, :, c * 128:(c + 1) * 128].rearrange(
                                                "(a p) f -> p a f", p=128),
                                            fmbt[kE][:], deps=evs, lane=c % 3)
                prev[("at", c)] = last
                prev[("sev", kE)] = last
                outs.append(prev[("fmdma", kE)])
        b.done(outs)


def emit_l1rms(b, xn_tm, gamma_x, xnn_tm, xnn_fm, barrier=True):
    with b.stage("rm", barrier=barrier) as ctx:
        identf, tif, identb, tib = b.idents
        tpm = TpMachine(b, identb, tib, n=2)
        fmbr = [b.sb(f"fmbr{i}", [128, 8, 128], BF16) for i in range(2)]
        fm_dma = {}
        ones1f = b.sb("ones1f", [1, 128])
        t_o1 = b.op("vector", "memset", ones1f[:], 1.0)
        g_row = b.sb("g_row", [1, D])
        t_gr = b.dma("sync", g_row[:], gamma_x[:])
        gb = b.sb("gb", [128, D], BF16)
        psg = b.ps("psg", [128, 1024], F32)
        mgb = [b.mm(psg[:, nb * 512:(nb + 1) * 512], ones1f[:],
                    g_row[:, nb * 512:(nb + 1) * 512], True, True,
                    deps=[t_o1, t_gr]) for nb in range(2)]
        t_gb = b.op("vector", "tensor_copy", gb[:], psg[:], deps=mgb)
        xt = [b.sb(f"xt{i}", [128, D], BF16) for i in range(2)]
        xo = [b.sb(f"xo{i}", [128, D], BF16) for i in range(2)]
        sed = b.sb("sed", [128, D])
        snq = b.sb("snq", [128, 1])
        srl = b.sb("srl", [128, 1])
        prevS = [{}, {}]
        outs = []
        for bi in range(BPC):
            for sc in range(NSC):
                k = sc % 2
                ld = b.dma("sync", xt[k][:], xn_tm[bi, sc * 128:(sc + 1) * 128, :],
                           deps=[prevS[k].get("rd")])
                t_sq = b.op("scalar", "activation", sed[:], xt[k][:], AF.Square,
                            accum_out=snq[:], deps=[ld, prevS[1 - k].get("t_n2")])
                t_n2 = b.op("vector", "tensor_scalar_max", snq[:], snq[:], 1e-24,
                            deps=[t_sq])
                t_ln = b.op("scalar", "activation", srl[:], snq[:], AF.Ln,
                            deps=[t_n2, prevS[1 - k].get("t_sn")])
                t_ex = b.op("scalar", "activation", srl[:], srl[:], AF.Exp, scale=-0.5,
                            deps=[t_ln])
                t_rl = b.op("vector", "tensor_scalar_mul", srl[:], srl[:], 32.0,
                            deps=[t_ex])
                t_sn = b.op("vector", "scalar_tensor_tensor", xo[k][:], xt[k][:], srl[:],
                            gb[:], ALU.mult, ALU.mult,
                            deps=[t_rl, t_gb, ld] + prevS[k].get("senr", []))
                st_dma = b.dma("sync", xnn_tm[bi, sc * 128:(sc + 1) * 128, :], xo[k][:],
                               deps=[t_sn], lane=k)
                outs.append(st_dma)
                senr = [st_dma]
                evs = []
                for dc in range(NDC):
                    t_tp, t_ev = tpm.to_sbuf(xo[k][:, dc * 128:(dc + 1) * 128],
                                             fmbr[k][:, dc, :], deps=[t_sn],
                                             extra_ev_deps=[fm_dma.get(k)])
                    senr.append(t_tp)
                    evs.append(t_ev)
                fm_dma[k] = b.dma("gpsimd",
                                  xnn_fm[bi, :, sc * 128:(sc + 1) * 128].rearrange(
                                      "(a p) f -> p a f", p=128),
                                  fmbr[k][:], deps=evs, lane=k)
                outs.append(fm_dma[k])
                prevS[k] = {"rd": t_sn, "t_n2": t_n2, "senr": senr, "t_sn": t_sn}
        b.done(outs)


def emit_l1ffnA(b, xnn_fm, seT, ncm, rmrow, logitsT):
    NEC = ET // 128
    with b.stage("fa") as ctx:
        identf, tif, identb, tib = b.idents
        se = [b.sb(f"se{i}", [128, ET], BF16) for i in range(NDC)]
        setok = [b.dma("gpsimd" if i % 2 else "sync", se[i][:],
                       seT[i * 128:(i + 1) * 128, :]) for i in range(NDC)]
        xn = [b.sb(f"xn{i}", [128, S], BF16) for i in range(NDC)]
        psL = [b.ps(f"psL{i}", [128, 2048], F32) for i in range(2)]
        t64 = b.sb("t64", [64, S])
        t32tmp = b.sb("t32tmp", [32, S])
        rm32 = b.sb("rm32", [32, S])
        ncm_s = b.sb("ncm_s", [128, 32])
        nctmp = b.sb("nctmp", [128, 2])
        rmc = b.sb("rmc", [128, 16])
        rmr = b.sb("rmr", [1, S])
        lbuf = [b.sb(f"lbuf{i}", [128, S]) for i in range(2)]
        xtok = [None] * NDC
        outs = []
        prev = {}
        for bi in range(BPC):
            for i in range(NDC):
                xtok[i] = b.dma("gpsimd", xn[i][:],
                                xnn_fm[bi, i * 128:(i + 1) * 128, :],
                                deps=[xtok[i], prev.get("lastmm")])
            for etc in range(NEC):
                kk = etc % 2
                mms = []
                for nb in range(4):
                    dst = psL[kk][:, nb * 512:(nb + 1) * 512]
                    for ic in range(NDC):
                        tmm = b.mm(dst, se[ic][:, etc * 128:(etc + 1) * 128],
                                   xn[ic][:, nb * 512:(nb + 1) * 512],
                                   ic == 0, ic == NDC - 1,
                                   deps=[setok[ic], xtok[ic], prev.get(f"rd{kk}"),
                                         prev.get(("lb", kk))],
                                   inc=ic == NDC - 1)
                    mms.append(tmm)
                t_nc1 = b.op("vector", "reduce_max", nctmp[:, 0:1],
                             psL[kk][:, 0:1024], axis=AX.X,
                             deps=mms[0:2] + [prev.get("nctmp_rd")])
                t_h0a = b.op("vector", "tensor_copy", t64[:, 0:1024],
                             psL[kk][64:128, 0:1024],
                             deps=mms[0:2] + [prev.get("t32")])
                t_h1a = b.op("vector", "tensor_max", t64[:, 0:1024], t64[:, 0:1024],
                             psL[kk][0:64, 0:1024], deps=[t_h0a])
                t_nc2 = b.op("vector", "reduce_max", nctmp[:, 1:2],
                             psL[kk][:, 1024:2048], axis=AX.X, deps=mms[2:4])
                t_h0b = b.op("vector", "tensor_copy", t64[:, 1024:2048],
                             psL[kk][64:128, 1024:2048],
                             deps=mms[2:4] + [prev.get("t32")])
                t_h1b = b.op("vector", "tensor_max", t64[:, 1024:2048],
                             t64[:, 1024:2048], psL[kk][0:64, 1024:2048],
                             deps=[t_h0b])
                t_nc = b.op("vector", "reduce_max", ncm_s[:, etc:etc + 1],
                            nctmp[:, 0:2], axis=AX.X, negate=True,
                            deps=[t_nc1, t_nc2, prev.get("ncm_rd")])
                prev["nctmp_rd"] = t_nc
                t_h1 = t_h1b
                b.op("scalar", "activation", lbuf[kk][:, 0:1024],
                     psL[kk][:, 0:1024], AF.Identity,
                     deps=[t_h1a, t_nc1, prev.get(("lb_dma", kk))])
                t_lb = b.op("scalar", "activation", lbuf[kk][:, 1024:2048],
                            psL[kk][:, 1024:2048], AF.Identity,
                            deps=[t_h1b, t_nc2])
                prev[("lb_dma", kk)] = b.dma("gpsimd", logitsT[bi, etc],
                                             lbuf[kk][:], deps=[t_lb], lane=kk)
                outs.append(prev[("lb_dma", kk)])
                prev[("lb", kk)] = t_lb
                t_cp = b.op("vector", "tensor_copy", t32tmp[:], t64[32:64, :],
                            deps=[t_h1, prev.get("t32tmp_rd")])
                t_h2 = b.op("vector", "tensor_max", t64[0:32, :], t64[0:32, :],
                            t32tmp[:], deps=[t_cp])
                prev["t32tmp_rd"] = t_h2
                if etc == 0:
                    t_32 = b.op("vector", "tensor_copy", rm32[:], t64[0:32, :],
                                deps=[t_h2, prev.get("rm_rd")])
                else:
                    t_32 = b.op("vector", "tensor_max", rm32[:], rm32[:], t64[0:32, :],
                                deps=[t_h2])
                prev["t32"] = t_32
                prev[f"rd{kk}"] = t_h1
                prev["ncm_rd"] = t_nc
                prev["lastmm"] = t_nc
            t_prev_r = None
            t_rms = []
            for j in range(NSC):
                tmm = b.mm(psL[0][0:128, 0:32], rm32[:, j * 128:(j + 1) * 128],
                           identf[0:32, 0:32], True, True,
                           deps=[t_32, tif, t_prev_r] +
                                ([prev.get("rd0"), prev.get(("lb", 0))]
                                 if j == 0 else []))
                t_r = b.op("vector", "reduce_max", rmc[:, j:j + 1], psL[0][0:128, 0:32],
                           axis=AX.X, deps=[tmm, prev.get("rmc_rd")])
                t_prev_r = t_r
                t_rms.append(t_r)
            rr_mms = []
            for j in range(NSC):
                tmm = b.mm(psL[1][0:1, j * 128:(j + 1) * 128], rmc[:, j:j + 1],
                           identf[:], True, True,
                           deps=[t_rms[j], tif] +
                                ([prev.get("rd1"), prev.get("rmr_rd"),
                                  prev.get(("lb", 1))] if j == 0 else []))
                rr_mms.append(tmm)
            prev["rmc_rd"] = rr_mms[-1]
            t_rr = b.op("scalar", "activation", rmr[:], psL[1][0:1, :], AF.Identity,
                        deps=rr_mms)
            prev["rd1"] = t_rr
            prev["rd0"] = t_rms[-1]
            d1 = b.dma("sync", rmrow[bi], rmr[:], deps=[t_rr])
            d2 = b.dma("sync", ncm[bi], ncm_s[:], deps=[prev["ncm_rd"]])
            prev["rmr_rd"] = d1
            prev["ncm_rd"] = d2
            prev["rm_rd"] = t_rms[-1]
            outs += [d1, d2]
        b.done(outs)


def emit_l1ffnB(b, xnn_fm, xnn_tm, seT, ncm, rmrow, slots_fm, E2T, logitsT):
    NEC = ET // 128
    with b.stage("fb") as ctx:
        identf, tif, identb, tib = b.idents
        tpm = TpMachine(b, identb, tib, n=2)
        ones1f = b.sb("ones1f", [1, 128])
        t_o1 = b.op("vector", "memset", ones1f[:], 1.0)
        lL = [b.sb(f"lL{i}", [128, S]) for i in range(2)]
        ltok = [None, None]
        xnt = [b.sb(f"xnt{i}", [128, D], BF16) for i in range(NSC)]
        ncm_s = b.sb("ncm_s", [128, 32])
        rmr = b.sb("rmr", [1, S])
        rmb = b.sb("rmb", [128, S])
        ebuf = [[b.sb(f"ebuf{p}_{i}", [128, 1024], BF16) for i in range(2)]
                for p in range(2)]
        zb = [b.sb(f"zb{p}", [128, 1024]) for p in range(2)]
        e2b = [[b.sb(f"e2b{p}_{i}", [128, 1024], BF16) for i in range(2)]
               for p in range(2)]
        eblk = [b.sb(f"eblk{i}", [128, 16, 256], BF16) for i in range(2)]
        sev = b.sb("sev", [128, 1024], BF16)
        cs = [b.sb(f"cs{p}", [128, 2]) for p in range(2)]
        rcs = [b.sb(f"rcs{p}", [128, 1]) for p in range(2)]
        rcsc = [b.sb(f"rcsc{p}", [128, 1], BF16) for p in range(2)]
        rrow = b.sb("rrow", [1, 256], BF16)
        rcsb = [b.sb(f"rcsb{p}", [128, 256], BF16) for p in range(2)]
        ones1b = b.sb("ones1b", [1, 128], BF16)
        t_o1b = b.op("vector", "memset", ones1b[:], 1.0)
        psL = [b.ps(f"psL{i}", [128, 1024], F32) for i in range(2)]
        psSl = b.ps("psSl", [128, 1024], F32)
        xttok = [None] * NSC
        outs = []
        prev = {}
        smms = []

        def l_load(bi2, etc2):
            lk2 = etc2 % 2
            ltok[lk2] = b.dma("gpsimd", lL[lk2][:], logitsT[bi2, etc2],
                              deps=[ltok[lk2], prev.get(("lL_rd", lk2))], lane=0)

        for bi in range(BPC):
            for i in range(NSC):
                xttok[i] = b.dma("sync", xnt[i][:],
                                 xnn_tm[bi, i * 128:(i + 1) * 128, :],
                                 deps=[xttok[i], prev.get("lastA")])
            l_load(bi, 0)
            l_load(bi, 1)
            t_ncm = b.dma("sync", ncm_s[:], ncm[bi],
                          deps=prev.get("ncm_rd") or [None])
            t_rmr = b.dma("sync", rmr[:], rmrow[bi], deps=[prev.get("rmr_rd")])
            t_rmb = []
            for half in range(2):
                mmsb = [b.mm(psL[half][:, nb * 512:(nb + 1) * 512], ones1f[:],
                             rmr[:, half * 1024 + nb * 512:half * 1024 + (nb + 1) * 512],
                             True, True,
                             deps=[t_rmr, t_o1, prev.get(f"psL_rd{half}"),
                                   prev.get("rmb_ex")])
                        for nb in range(2)]
                t_rmb.append(b.op("scalar", "activation",
                                  rmb[:, half * 1024:(half + 1) * 1024],
                                  psL[half][:], AF.Identity,
                                  deps=mmsb + (prev.get("rmb_rd") or [None])))
                prev[f"psL_rd{half}"] = t_rmb[half]
            for blk in range(16):
                tp_toks = {}
                sub_eh = {}
                for sub in range(2):
                    etc = blk * 2 + sub
                    lk = etc % 2
                    eh_toks = []
                    for half in range(2):
                        lsl = lL[lk][:, half * 1024:(half + 1) * 1024]
                        t_e = b.op("scalar", "activation", ebuf[lk][half][:], lsl,
                                   AF.Exp, bias=ncm_s[:, etc:etc + 1],
                                   accum_out=cs[lk][:, half:half + 1],
                                   deps=[ltok[lk], t_ncm,
                                         prev.get(("cs_rd", lk))] +
                                        (prev.get(("ebuf_rd", lk, half)) or []))
                        eh_toks.append(t_e)
                    sub_eh[sub] = eh_toks
                    t_cs = b.op("vector", "tensor_add", cs[lk][:, 0:1], cs[lk][:, 0:1],
                                cs[lk][:, 1:2], deps=eh_toks + [prev.get(("rcs_rd", lk))])
                    t_rcs = b.op("vector", "reciprocal", rcs[lk][:], cs[lk][:, 0:1],
                                 deps=[t_cs])
                    prev[("cs_rd", lk)] = t_rcs
                    t_rcc = b.op("vector", "tensor_copy", rcsc[lk][:], rcs[lk][:],
                                 deps=[t_rcs, prev.get(("rcc_rd", lk))])
                    prev[("rcs_rd", lk)] = t_rcc
                    prev[("rcc", sub)] = t_rcc
                    for half in range(2):
                        lsl = lL[lk][:, half * 1024:(half + 1) * 1024]
                        t_z = b.op("vector", "scalar_tensor_tensor", zb[lk][:], lsl,
                                   1.0, rmb[:, half * 1024:(half + 1) * 1024],
                                   ALU.mult, ALU.subtract,
                                   deps=[ltok[lk]] + t_rmb +
                                        [prev.get(("zb_rd", lk)), eh_toks[half]])
                        prev[("lL_rd", lk)] = t_z
                        t_e2 = b.op("scalar", "activation", e2b[lk][half][:], zb[lk][:],
                                    AF.Exp, deps=[t_z, prev.get(("e2dma", lk, half))])
                        prev[("zb_rd", lk)] = t_e2
                        d_e2 = b.dma("gpsimd", E2T[bi, etc * 128:(etc + 1) * 128,
                                                   half * 1024:(half + 1) * 1024],
                                     e2b[lk][half][:], deps=[t_e2], lane=half + 1)
                        prev[("e2dma", lk, half)] = d_e2
                        outs.append(d_e2)
                    tps = []
                    hp = {0: [], 1: []}
                    for j in range(NSC):
                        half = j // 8
                        src = ebuf[lk][half][:, (j % 8) * 128:(j % 8 + 1) * 128]
                        if j % 4 == 0:
                            t = b.dma("sync",
                                      eblk[blk % 2][:, j, sub * 128:(sub + 1) * 128],
                                      src, transpose=True,
                                      deps=[eh_toks[half],
                                            prev.get(("eblk_rd", blk % 2))],
                                      lane=j % 3)
                        elif j % 4 == 2:
                            t = b.dma("scalar",
                                      eblk[blk % 2][:, j, sub * 128:(sub + 1) * 128],
                                      src, transpose=True,
                                      deps=[eh_toks[half],
                                            prev.get(("eblk_rd", blk % 2))],
                                      lane=j % 3)
                        else:
                            t_tp, t = tpm.to_sbuf(
                                src, eblk[blk % 2][:, j, sub * 128:(sub + 1) * 128],
                                deps=[eh_toks[half]],
                                extra_ev_deps=[prev.get(("eblk_rd", blk % 2))])
                        tps.append(t)
                        hp[half].append(t)
                    prev[("ebuf_rd", lk, 0)] = list(hp[0])
                    prev[("ebuf_rd", lk, 1)] = list(hp[1])
                    tp_toks[sub] = tps
                    if etc + 2 < 2 * 16:
                        l_load(bi, etc + 2)
                for sub in range(2):
                    t_rcm = b.mm(psL[0][0:1, sub * 128:(sub + 1) * 128],
                                 rcsc[(blk * 2 + sub) % 2][:], identb[:], True, True,
                                 deps=[prev[("rcc", sub)], tib, prev.get("psL_rd0")])
                    prev[("rcc_rd", (blk * 2 + sub) % 2)] = t_rcm
                    t_rre = b.op("scalar", "activation",
                                 rrow[:, sub * 128:(sub + 1) * 128],
                                 psL[0][0:1, sub * 128:(sub + 1) * 128], AF.Identity,
                                 deps=[t_rcm, prev.get("rrow_rd")])
                    prev["psL_rd0"] = t_rre
                    prev[("rre", sub)] = t_rre
                t_bcm = [b.mm(psL[1][0:128, e * 128:(e + 1) * 128], ones1b[:],
                              rrow[:, e * 128:(e + 1) * 128], True, True,
                              deps=[t_o1b, prev[("rre", e)], prev.get("psL_rd1")])
                         for e in range(2)]
                t_rcb = b.op("scalar", "activation", rcsb[blk % 2][:],
                             psL[1][0:128, 0:256], AF.Identity,
                             deps=t_bcm + [prev.get(("rcsb_rd", blk % 2))])
                prev["psL_rd1"] = t_rcb
                prev["rrow_rd"] = t_bcm[-1]
                ev_sl = prev.get("sl_ev", [])
                smms = []
                for dhalf in range(2):
                    grp = []
                    for dcq in range(4):
                        dc = dhalf * 4 + dcq
                        for sc in range(NSC):
                            tmm = b.mm(psSl[:, dcq * 256:(dcq + 1) * 256],
                                       xnt[sc][:, dc * 128:(dc + 1) * 128],
                                       eblk[blk % 2][:, sc, :],
                                       sc == 0, sc == NSC - 1,
                                       deps=[xttok[sc]] + tp_toks[0] + tp_toks[1] +
                                            (ev_sl if sc == 0 else []),
                                       inc=sc == NSC - 1)
                            if sc == NSC - 1:
                                grp.append(tmm)
                    t_se = b.op("vector", "tensor_tensor",
                                sev[:].rearrange("p (g e) -> p g e", e=256),
                                psSl[:].rearrange("p (g e) -> p g e", e=256),
                                rcsb[blk % 2][:].rearrange(
                                    "p (o e) -> p o e", o=1).broadcast_to([128, 4, 256]),
                                ALU.mult,
                                deps=grp + [t_rcb] + (prev.get("sev_dma") or []))
                    ev_sl = [t_se]
                    prev["sl_ev"] = ev_sl
                    dmas = []
                    for dcq in range(4):
                        dc = dhalf * 4 + dcq
                        dmas.append(b.dma("gpsimd",
                                          slots_fm[bi, dc * 128:(dc + 1) * 128,
                                                   blk * 256:(blk + 1) * 256],
                                          sev[:, dcq * 256:(dcq + 1) * 256],
                                          deps=[t_se], lane=1 + dcq % 2))
                    prev["sev_dma"] = dmas[-2:]
                    outs += dmas
                    smms += grp
                prev[("eblk_rd", blk % 2)] = smms[-1]
                prev[("rcsb_rd", blk % 2)] = prev["sl_ev"][0]
            prev["lastA"] = prev["sl_ev"][0]
            prev["rmb_ex"] = prev.get("psL_rd1")
            prev["ncm_rd"] = [prev[("cs_rd", 0)], prev[("cs_rd", 1)]]
            prev["rmr_rd"] = t_rmb[-1]
            prev["rmb_rd"] = [prev[("zb_rd", 0)], prev[("zb_rd", 1)]]
        b.done(outs)


def emit_l1ffnC(b, slots_fm, w1, w2, processed):
    NHC = HID // 128
    BW = 512
    with b.stage("fc") as ctx:
        w1t = [b.sb(f"w1t{i}", [128, HID], BF16) for i in range(NDC)]
        w2t = [b.sb(f"w2t{i}", [128, D], BF16) for i in range(NHC)]
        w1tok = [b.dma("gpsimd", w1t[i][:], w1[i * 128:(i + 1) * 128, :], lane=i % 3)
                 for i in range(NDC)]
        w2tok = [b.dma("gpsimd", w2t[i][:], w2[i * 128:(i + 1) * 128, :], lane=i % 3)
                 for i in range(NHC)]
        sl = [b.sb(f"sl{i}", [128, 8, BW], BF16) for i in range(2)]
        hfm = b.sb("hfm", [128, NHC, BW], BF16)
        pev = [b.sb(f"pev{i}", [128, 1024], BF16) for i in range(2)]
        psH = [b.ps(f"psH{i}", [128, 512], F32) for i in range(2)]
        psP = [b.ps(f"psP{i}", [128, 1024], F32) for i in range(2)]
        outs = []
        prev = {}
        sltok = [None, None]
        pmm_last = None
        NBLK = ET // BW

        def sl_load(gblk):
            bi2, blk2 = divmod(gblk, NBLK)
            kb2 = gblk % 2
            sltok[kb2] = b.dma("sync", sl[kb2][:],
                               slots_fm[bi2, :, blk2 * BW:(blk2 + 1) * BW].rearrange(
                                   "(a p) f -> p a f", p=128),
                               deps=[sltok[kb2], prev.get(f"sl_rd{kb2}")], lane=kb2)

        sl_load(0)
        sl_load(1)
        for bi in range(BPC):
            for blk in range(NBLK):
                kb = blk % 2
                gblk = bi * NBLK + blk
                h_toks = []
                for hc in range(NHC):
                    kk = hc % 2
                    for ic in range(NDC):
                        tmm = b.mm(psH[kk][:, 0:BW], w1t[ic][:, hc * 128:(hc + 1) * 128],
                                   sl[kb][:, ic, :], ic == 0, ic == NDC - 1,
                                   deps=[w1tok[ic], sltok[kb], prev.get(f"hev{kk}")],
                                   inc=ic == NDC - 1)
                    t_h = b.op("scalar", "activation", hfm[:, hc, :], psH[kk][:, 0:BW],
                               AF.Gelu, deps=[tmm, prev.get(("hfm_rd", hc))])
                    prev[f"hev{kk}"] = t_h
                    h_toks.append(t_h)
                for sub in range(BW // 128):
                    kk = sub % 2
                    etc = blk * (BW // 128) + sub
                    for hc in range(NHC):
                        for nb in range(2):
                            tmm = b.mm(psP[kk][:, nb * 512:(nb + 1) * 512],
                                       hfm[:, hc, sub * 128:(sub + 1) * 128],
                                       w2t[hc][:, nb * 512:(nb + 1) * 512],
                                       hc == 0, hc == NHC - 1,
                                       deps=[w2tok[hc], h_toks[hc]] +
                                            ([prev.get(f"pev{kk}")] if hc == 0 else []),
                                       inc=hc == NHC - 1 and nb == 1)
                    pmm_last = tmm
                    t_p = b.op("scalar", "activation", pev[kk][:], psP[kk][:],
                               AF.Identity, deps=[tmm, prev.get(f"pdma{kk}")])
                    prev[f"pev{kk}"] = t_p
                    d_p = b.dma("sync", processed[bi, etc * 128:(etc + 1) * 128, :],
                                pev[kk][:], deps=[t_p], lane=kk)
                    prev[f"pdma{kk}"] = d_p
                    outs.append(d_p)
                for hc in range(NHC):
                    prev[("hfm_rd", hc)] = pmm_last
                prev[f"sl_rd{kb}"] = pmm_last
                if gblk + 2 < BPC * NBLK:
                    sl_load(gblk + 2)
        b.done(outs)


def emit_l1ffnD(b, processed, E2T, s1_tm, x1, out):
    NEC = ET // 128
    with b.stage("fd") as ctx:
        onescol = b.sb("onescol", [128, 1], BF16)
        t_oc = b.op("vector", "memset", onescol[:], 1.0)
        proc = [b.sb(f"proc{i}", [128, D], BF16) for i in range(NEC)]
        ptok = [None] * NEC
        e2s = [[b.sb(f"e2s{p}_{i}", [128, 4, 128], BF16) for i in range(8)]
               for p in range(2)]
        s1b = [b.sb(f"s1b{i}", [128, D]) for i in range(2)]
        x1b = [b.sb(f"x1b{i}", [128, D]) for i in range(2)]
        tb = [b.sb(f"tb{i}", [128, D]) for i in range(2)]
        gb_ = [b.sb(f"gb{i}", [128, D]) for i in range(2)]
        rden = [b.sb(f"rden{i}", [128, 1]) for i in range(2)]
        psU = [b.ps(f"psU{i}", [128, 1024], F32) for i in range(2)]
        psD = [b.ps(f"psD{i}", [128, 512], F32) for i in range(2)]
        outs = []
        prev = {}
        e2tok = {}
        for bi in range(BPC):
            for i in range(NEC):
                ptok[i] = b.dma("gpsimd" if i % 2 else "sync", proc[i][:],
                                processed[bi, i * 128:(i + 1) * 128, :],
                                deps=[ptok[i], prev.get("mm_last0"),
                                      prev.get("mm_last1")], lane=i % 3)
            def load_strip(sc2):
                kk2 = sc2 % 2
                sl2 = slice(sc2 * 128, (sc2 + 1) * 128)
                for jj in range(8):
                    e2tok[(kk2, jj)] = b.dma(
                        "gpsimd" if jj % 2 else "sync", e2s[kk2][jj][:],
                        E2T[bi, jj * 512:(jj + 1) * 512, sl2].rearrange(
                            "(a p) f -> p a f", p=128),
                        deps=[e2tok.get((kk2, jj)), prev.get(f"mm_last{kk2}")],
                        lane=kk2)

            load_strip(0)
            for sc in range(NSC):
                kk = sc % 2
                ssl = slice(sc * 128, (sc + 1) * 128)
                if sc + 1 < NSC:
                    load_strip(sc + 1)
                l_s1 = b.dma("sync", s1b[kk][:], s1_tm[bi, ssl, :],
                             deps=[prev.get(f"smul{kk}")])
                l_x1 = b.dma("gpsimd", x1b[kk][:], x1[bi, ssl, :],
                             deps=[prev.get(f"xadd{kk}")])
                umms = []
                dmms = []
                for etc in range(NEC):
                    jj, aa = divmod(etc, 4)
                    lhs = e2s[kk][jj][:, aa, :]
                    for nb in range(2):
                        tmm = b.mm(psU[kk][:, nb * 512:(nb + 1) * 512], lhs,
                                   proc[etc][:, nb * 512:(nb + 1) * 512],
                                   etc == 0, etc == NEC - 1,
                                   deps=[e2tok[(kk, jj)], ptok[etc],
                                         prev.get(f"uev{kk}")],
                                   inc=etc == NEC - 1 and nb == 1)
                    dmms.append(b.mm(psD[kk][:, 0:1], lhs, onescol[:],
                                     etc == 0, etc == NEC - 1,
                                     deps=([t_oc, prev.get(f"rden_rd{kk}")]
                                           if etc == 0 else []),
                                     inc=etc == NEC - 1))
                    umms.append(tmm)
                prev[f"mm_last{kk}"] = dmms[-1]
                t_r = b.op("vector", "reciprocal", rden[kk][:], psD[kk][:, 0:1],
                           deps=[dmms[-1], prev.get(f"rden_u{kk}")])
                prev[f"rden_rd{kk}"] = t_r
                t_u = b.op("vector", "tensor_scalar", tb[kk][:], psU[kk][:], rden[kk][:],
                           None, ALU.mult,
                           deps=[t_r, umms[-1], prev.get(f"gsil{kk}")])
                prev[f"rden_u{kk}"] = t_u
                prev[f"uev{kk}"] = t_u
                t_g = b.op("scalar", "activation", gb_[kk][:], tb[kk][:], AF.Silu,
                           deps=[t_u, prev.get(f"smul{kk}")])
                prev[f"gsil{kk}"] = t_g
                t_sg = b.op("vector", "tensor_mul", s1b[kk][:], s1b[kk][:], gb_[kk][:],
                            deps=[t_g, l_s1])
                prev[f"smul{kk}"] = t_sg
                t_x = b.op("vector", "tensor_add", x1b[kk][:], x1b[kk][:], s1b[kk][:],
                           deps=[t_sg, l_x1])
                d_o = b.dma("sync", out[bi, ssl, :], x1b[kk][:], deps=[t_x])
                prev[f"xadd{kk}"] = d_o
                outs.append(d_o)
        b.done(outs)


def _shard_batch(arr):
    return [np.ascontiguousarray(arr[2 * k:2 * k + 2]) for k in range(NCORES)]


def run_stage(nc, in_maps):
    res = run_bass_kernel_spmd(nc, in_maps, core_ids=list(range(NCORES)))
    return res.results


_NCS = {}


def _nc(name, builder):
    if name not in _NCS:
        _NCS[name] = builder()
    return _NCS[name]


def build_mega():
    nc = bass.Bass("TRN2", target_bir_lowering=False)

    def EI(n, sh, dt=F32):
        return nc.dram_tensor(n, sh, dt, kind="ExternalInput").ap()

    def IT(n, sh, dt):
        return nc.dram_tensor(n, sh, dt, kind="Internal").ap()

    x = EI("x", [BPC, S, D])
    qkv_w = EI("qkv_w", [3 * D, D]); qkv_b = EI("qkv_b", [1, 3 * D])
    proj_w = EI("proj_w", [D, D]); proj_b = EI("proj_b", [1, D])
    W0 = EI("W0", [D, D]); b0 = EI("b0", [1, D])
    lora_down = EI("lora_down", [4, 100]); lora_up = EI("lora_up", [50, 4])
    down_aux = EI("down_aux", [100, D]); up_aux = EI("up_aux", [D, 50])
    Wq = EI("Wq", [D, D]); Wk = EI("Wk", [D, D]); Wv = EI("Wv", [D, D]); Wo = EI("Wo", [D, D])
    bq = EI("bq", [1, D]); bk = EI("bk", [1, D]); bv = EI("bv", [1, D]); bo = EI("bo", [1, D])
    beta = EI("beta", [1, 1])
    gamma_x = EI("gamma_x", [1, D]); gamma_slot = EI("gamma_slot", [1, D])
    slot_embeds = EI("slot_embeds", [ET, D])
    w1 = EI("w1", [D, HID]); w2 = EI("w2", [HID, D])

    qkvT = IT("qkvT", [D, 3 * D], BF16)
    projT = IT("projT", [D, D], BF16); WeffT = IT("WeffT", [D, D], BF16)
    WqT = IT("WqT", [D, D], BF16); WkT = IT("WkT", [D, D], BF16)
    WvT = IT("WvT", [D, D], BF16); WoT = IT("WoT", [D, D], BF16)
    seT = IT("seT", [D, ET], BF16)
    xn_tm = IT("xn_tm", [BPC, S, D], BF16); xn_fm = IT("xn_fm", [BPC, D, S], BF16)
    qkv_fm = IT("qkv_fm", [BPC, 3 * D, S], BF16)
    attn_fm = IT("attn_fm", [BPC, D, S], BF16)
    x1 = IT("x1", [BPC, S, D], F32)
    xn1_tm = IT("xn1_tm", [BPC, S, D], BF16); xn1_fm = IT("xn1_fm", [BPC, D, S], BF16)
    Qt_fm = IT("Qt_fm", [BPC, D, S], BF16); Kt_fm = IT("Kt_fm", [BPC, D, S], BF16)
    Kt_tm = IT("Kt_tm", [BPC, S, D], BF16); Vp_tm = IT("Vp_tm", [BPC, S, H1 * 17], BF16)
    attn1_fm = IT("attn1_fm", [BPC, D, S], BF16)
    s1_tm = IT("s1_tm", [BPC, S, D], F32)
    xnn_tm = IT("xnn_tm", [BPC, S, D], BF16); xnn_fm = IT("xnn_fm", [BPC, D, S], BF16)
    ncm = IT("ncm", [BPC, 128, 32], F32); rmrow = IT("rmrow", [BPC, 1, S], F32)
    slots_fm = IT("slots_fm", [BPC, D, ET], BF16)
    logitsT = IT("logitsT", [BPC, ET // 128, 128, S], F32)
    E2T = IT("E2T", [BPC, ET, S], BF16)
    processed = IT("processed", [BPC, ET, D], BF16)
    out = nc.dram_tensor("out", [BPC, S, D], F32, kind="ExternalOutput").ap()

    wio = {"qkv_w": qkv_w, "proj_w": proj_w, "W0": W0, "lora_down": lora_down,
           "lora_up": lora_up, "down_aux": down_aux, "up_aux": up_aux,
           "Wq": Wq, "Wk": Wk, "Wv": Wv, "Wo": Wo, "slot_embeds": slot_embeds,
           "gamma_slot": gamma_slot, "qkvT": qkvT, "projT": projT, "WeffT": WeffT,
           "WqT": WqT, "WkT": WkT, "WvT": WvT, "WoT": WoT, "seT": seT}

    import os
    nstages = int(os.environ.get("MEGA_STAGES", "99"))
    with ExitStack() as ctx:
        b = Bld(nc, ctx)
        b.init_idents()
        stages = [
            lambda: emit_wprep(b, wio),
            lambda: emit_ln(b, x, xn_tm, xn_fm, tag="ln0"),
            lambda: emit_l0qkv(b, xn_fm, qkvT, qkv_b, qkv_fm),
            lambda: emit_l0attn(b, qkv_fm, attn_fm),
            lambda: emit_l0epi(b, xn_fm, WeffT, b0, attn_fm, projT, proj_b, x, x1),
            lambda: emit_ln(b, x1, xn1_tm, xn1_fm, tag="ln1"),
            lambda: emit_l1qkv(b, xn1_fm, WqT, WkT, WvT, bq, bk, bv, beta,
                               Qt_fm, Kt_fm, Kt_tm, Vp_tm),
            lambda: emit_l1attn(b, Qt_fm, Kt_fm, Kt_tm, Vp_tm, attn1_fm),
            lambda: emit_proj_tm(b, attn1_fm, WoT, bo, s1_tm, act="identity", tag="s1"),
            lambda: emit_l1rms(b, xn1_tm, gamma_x, xnn_tm, xnn_fm),
            lambda: emit_l1ffnA(b, xnn_fm, seT, ncm, rmrow, logitsT),
            lambda: emit_l1ffnB(b, xnn_fm, xnn_tm, seT, ncm, rmrow, slots_fm, E2T,
                                logitsT),
            lambda: emit_l1ffnC(b, slots_fm, w1, w2, processed),
            lambda: emit_l1ffnD(b, processed, E2T, s1_tm, x1, out),
        ]
        for f in stages[:nstages]:
            f()
        if nstages < len(stages):
            z = b.sb("zz", [128, D])
            tz = b.op("vector", "memset", z[:], 0.0)
            for bi in range(BPC):
                for sc_ in range(NSC):
                    b.dma("sync", out[bi, sc_ * 128:(sc_ + 1) * 128, :], z[:], deps=[tz])
        b.finish_all()
    return nc


def kernel(**inputs):
    inp = {k: np.ascontiguousarray(np.asarray(v)) for k, v in inputs.items()}
    xs = _shard_batch(inp["L0_attn_x"])
    base = {
        "qkv_w": inp["L0_attn_qkv_w"], "qkv_b": inp["L0_attn_qkv_b"].reshape(1, -1),
        "proj_w": inp["L0_attn_proj_w"], "proj_b": inp["L0_attn_proj_b"].reshape(1, -1),
        "W0": inp["L0_ffn_W"], "b0": inp["L0_ffn_b"].reshape(1, -1),
        "lora_down": inp["L0_ffn_lora_down"], "lora_up": inp["L0_ffn_lora_up"],
        "down_aux": inp["L0_ffn_down_aux"], "up_aux": inp["L0_ffn_up_aux"],
        "Wq": inp["L1_attn_Wq"], "Wk": inp["L1_attn_Wk"],
        "Wv": inp["L1_attn_Wv"], "Wo": inp["L1_attn_Wo"],
        "bq": inp["L1_attn_bq"].reshape(1, -1), "bk": inp["L1_attn_bk"].reshape(1, -1),
        "bv": inp["L1_attn_bv"].reshape(1, -1), "bo": inp["L1_attn_bo"].reshape(1, -1),
        "beta": inp["L1_attn_beta"].reshape(1, 1),
        "gamma_x": inp["L1_ffn_gamma_x"].reshape(1, -1),
        "gamma_slot": inp["L1_ffn_gamma_slot"].reshape(1, -1),
        "slot_embeds": inp["L1_ffn_slot_embeds"].reshape(ET, D),
        "w1": inp["L1_ffn_w1"], "w2": inp["L1_ffn_w2"],
    }
    in_maps = [{**base, "x": xs[k]} for k in range(NCORES)]
    res = run_bass_kernel_spmd(_nc("mega", build_mega), in_maps,
                               core_ids=list(range(NCORES)))
    return np.concatenate([np.asarray(r["out"]) for r in res.results], axis=0)


# version 36
# speedup vs baseline: 1.1881x; 1.0135x over previous
import os
import numpy as np
import ml_dtypes
from contextlib import ExitStack, contextmanager
import concourse.bass as bass
import concourse.mybir as mybir
from concourse.bass_utils import run_bass_kernel_spmd

F32 = mybir.dt.float32
BF16 = mybir.dt.bfloat16
AF = mybir.ActivationFunctionType
ALU = mybir.AluOpType
AX = mybir.AxisListType

D = 1024
S = 2048
BPC = 2
NCORES = 8
NSC = S // 128
NDC = D // 128

H0 = 16
HD0 = 64
H1 = 64
HD1 = 16
NE = 8
NT = 512
ET = NE * NT
HID = 4 * D


class Bld:
    ENG_SEM = {"tensor": "pe", "vector": "dve", "scalar": "act", "gpsimd": "pool"}
    DMA_SEM = {"sync": "qsp", "gpsimd": "qpool", "scalar": "qact", "vector": "qvec"}

    def __init__(self, nc, ctx):
        self.nc = nc
        self.ctx = ctx
        self.cur_ctx = ctx
        self.prefix = ""
        self.idents = None
        self.sem = {}
        names = ["pe", "dve", "act", "pool"]
        for q in ["qsp", "qpool", "qact", "qvec"]:
            names += [q + str(i) for i in range(3)]
        for name in names:
            self.sem[name] = ctx.enter_context(nc.semaphore(name))
        self.cnt = {k: 0 for k in self.sem}
        self.wm = {}

    def init_idents(self):
        self.idents = emit_identities(self)

    @contextmanager
    def stage(self, tag, barrier=True):
        if barrier:
            self.stage_barrier()
        old_prefix, old_ctx = self.prefix, self.cur_ctx
        self.prefix = tag + "_"
        self._cc = {}
        with ExitStack() as sctx, self.nc.named_scope(tag):
            self.cur_ctx = sctx
            yield sctx
        self.cur_ctx = old_ctx
        self.prefix = old_prefix
        self._cc = {}

    def stage_barrier(self):
        for eng in ("tensor", "vector", "scalar", "gpsimd", "sync"):
            for semname, cnt in list(self.cnt.items()):
                if cnt:
                    self._wait(eng, (semname, cnt))

    def done(self, outs):
        pass

    def finish_all(self):
        for semname, cnt in list(self.cnt.items()):
            if cnt:
                self._wait("sync", (semname, cnt))

    def sb(self, name, shape, dt=F32):
        return self.cur_ctx.enter_context(self.nc.sbuf_tensor(self.prefix + name, shape, dt))

    def ps(self, name, shape, dt=F32):
        return self.cur_ctx.enter_context(self.nc.psum_tensor(self.prefix + name, shape, dt))

    def _wait(self, eng, tok):
        if tok is None:
            return
        semname, val = tok
        key = (eng, semname)
        if self.wm.get(key, 0) >= val:
            return
        getattr(self.nc, eng).wait_ge(self.sem[semname], val)
        self.wm[key] = val

    def op(self, eng, method, *args, deps=(), **kw):
        for t in deps:
            self._wait(eng, t)
        inst = getattr(getattr(self.nc, eng), method)(*args, **kw)
        semname = self.ENG_SEM[eng]
        self.cnt[semname] += 1
        inst.then_inc(self.sem[semname], 1)
        return (semname, self.cnt[semname])

    def mm(self, out, lhsT, rhs, start, stop, deps=(), inc=True, **kw):
        for t in deps:
            self._wait("tensor", t)
        inst = self.nc.tensor.matmul(out, lhsT, rhs, start=start, stop=stop, **kw)
        if inc:
            self.cnt["pe"] += 1
            inst.then_inc(self.sem["pe"], 1)
            return ("pe", self.cnt["pe"])
        return None

    def dma(self, queue, out, in_, deps=(), lane=0, **kw):
        semname = self.DMA_SEM[queue] + str(lane % 3)
        self._wait(queue, (semname, self.cnt[semname]))
        for t in deps:
            self._wait(queue, t)
        inst = getattr(self.nc, queue).dma_start(out=out, in_=in_, **kw)
        self.cnt[semname] += 16
        inst.then_inc(self.sem[semname], 16)
        return (semname, self.cnt[semname])

    def constcol(self, value, dt=F32):
        key = (value, dt)
        if not hasattr(self, "_cc"):
            self._cc = {}
        if key not in self._cc:
            t = self.sb(f"cc{len(self._cc)}", [128, 1], dt)
            tok = self.op("vector", "memset", t[:], value)
            self._cc[key] = (t, tok)
        return self._cc[key]

    def barrier(self, eng, toks):
        for t in toks:
            self._wait(eng, t)

    def finish(self, toks):
        for t in toks:
            self._wait("sync", t)
        for q in ("qsp", "qpool", "qact", "qvec"):
            if self.cnt[q]:
                self._wait("sync", (q, self.cnt[q]))


def emit_identities(b):
    identf = b.sb("identf", [128, 128], F32)
    identb = b.sb("identb", [128, 128], BF16)
    t1 = b.op("gpsimd", "memset", identf[:], 0.0)
    t1 = b.op("gpsimd", "affine_select", out=identf[:], in_=identf[:],
              compare_op=ALU.not_equal, fill=1.0, base=0,
              pattern=[[-1, 128]], channel_multiplier=1, deps=[t1])
    t2 = b.op("gpsimd", "memset", identb[:], 0.0)
    t2 = b.op("gpsimd", "affine_select", out=identb[:], in_=identb[:],
              compare_op=ALU.not_equal, fill=1.0, base=0,
              pattern=[[-1, 128]], channel_multiplier=1, deps=[t2])
    return identf, t1, identb, t2


def tp_b16(b, psum_out, in_slice, identb, deps, base=0):
    p = in_slice.shape[0]
    for t in deps:
        b._wait("tensor", t)
    inst = b.nc.tensor.transpose(psum_out, in_slice, identb[base:base + p, base:base + p])
    b.cnt["pe"] += 1
    inst.then_inc(b.sem["pe"], 1)
    return ("pe", b.cnt["pe"])


def tp_f32(b, psum_out, in_slice, identf, deps):
    p = in_slice.shape[0]
    return b.mm(psum_out, in_slice, identf[0:p, 0:p], True, True, deps=deps)


def emit_ln_chunk(b, xt, xnb, st, ld, prev):
    t_sq = b.op("scalar", "activation", st["dump"][:], xt[:], AF.Square,
                accum_out=st["ssq"][:], deps=[ld, prev.get("t_v")])
    t_sm = b.op("vector", "reduce_sum", st["sm"][:], xt[:], axis=AX.X,
                deps=[ld, prev.get("t_m")])
    t_m = b.op("vector", "tensor_scalar_mul", st["mcol"][:], st["sm"][:], 1.0 / D,
               deps=[t_sm, prev.get("t_mm"), prev.get("t_mb")])
    t_mm = b.op("vector", "scalar_tensor_tensor", st["var"][:], st["mcol"][:], -1.0,
                st["mcol"][:], ALU.mult, ALU.mult, deps=[t_m, prev.get("t_ln")])
    t_v = b.op("vector", "scalar_tensor_tensor", st["var"][:], st["ssq"][:], 1.0 / D,
               st["var"][:], ALU.mult, ALU.add, deps=[t_mm, t_sq])
    epscol, epstok = b.constcol(1e-5)
    t_ln = b.op("scalar", "activation", st["rstd"][:], st["var"][:], AF.Ln, bias=epscol[:],
                deps=[t_v, prev.get("t_mb"), epstok])
    t_rs = b.op("scalar", "activation", st["rstd"][:], st["rstd"][:], AF.Exp, scale=-0.5,
                deps=[t_ln])
    t_mb = b.op("vector", "scalar_tensor_tensor", st["mb"][:], st["mcol"][:], -1.0,
                st["rstd"][:], ALU.mult, ALU.mult, deps=[t_m, t_rs, prev.get("t_xn")])
    t_xn = b.op("scalar", "activation", xnb[:], xt[:], AF.Identity,
                bias=st["mb"][:], scale=st["rstd"][:],
                deps=[t_rs, t_mb, ld] + prev.get("xnb_readers", []))
    return {"t_sq": t_sq, "t_sm": t_sm, "t_m": t_m, "t_mm": t_mm, "t_v": t_v,
            "t_ln": t_ln, "t_rs": t_rs, "t_mb": t_mb, "t_xn": t_xn}


def mk_stats(b, tag):
    return {"dump": b.sb(f"dump{tag}", [128, D]),
            "ssq": b.sb(f"ssq{tag}", [128, 1]),
            "sm": b.sb(f"sm{tag}", [128, 1]),
            "mcol": b.sb(f"mcol{tag}", [128, 1]),
            "var": b.sb(f"var{tag}", [128, 1]),
            "rstd": b.sb(f"rstd{tag}", [128, 1]),
            "mb": b.sb(f"mb{tag}", [128, 1])}


def emit_ln(b, x, xn_tm, xn_fm, tag="ln"):
    with b.stage(tag) as ctx:
        identf, tif, identb, tib = b.idents
        NPAR = 3
        xt = [b.sb(f"xt{i}", [128, D], F32) for i in range(NPAR)]
        xnb = [b.sb(f"xnb{i}", [128, D], BF16) for i in range(NPAR)]
        st = [mk_stats(b, i) for i in range(NPAR)]
        fmb = [b.sb(f"fmb{i}", [128, 8, 128], BF16) for i in range(NPAR)]
        ptp = [b.ps(f"ptp{i}", [128, 512], BF16) for i in range(4)]
        prev = [{} for _ in range(NPAR)]
        fm_dma = {}
        tev_tok = {}
        out_toks = []
        for bi in range(BPC):
            for sc in range(NSC):
                k = sc % NPAR
                ld = b.dma("sync", xt[k][:], x[bi, sc * 128:(sc + 1) * 128, :],
                           deps=[prev[k].get("t_xn"), prev[k].get("t_sm")], lane=k)
                toks = emit_ln_chunk(b, xt[k], xnb[k], st[k], ld, prev[k])
                t_xn = toks["t_xn"]
                st_dma = b.dma("sync", xn_tm[bi, sc * 128:(sc + 1) * 128, :], xnb[k][:],
                               deps=[t_xn], lane=k)
                out_toks.append(st_dma)
                readers = [st_dma]
                evs = []
                for dc in range(NDC):
                    kk = dc % 4
                    t_tp = tp_b16(b, ptp[kk][:, 0:128], xnb[k][:, dc * 128:(dc + 1) * 128],
                                  identb, deps=[t_xn, tib, tev_tok.get(kk)])
                    t_ev = b.op("vector", "tensor_copy", fmb[k][:, dc, :], ptp[kk][:, 0:128],
                                deps=[t_tp, fm_dma.get(k)])
                    tev_tok[kk] = t_ev
                    readers.append(t_tp)
                    evs.append(t_ev)
                fm_dma[k] = b.dma("gpsimd",
                                  xn_fm[bi, :, sc * 128:(sc + 1) * 128].rearrange(
                                      "(a p) f -> p a f", p=128),
                                  fmb[k][:], deps=evs, lane=k)
                out_toks.append(fm_dma[k])
                toks["xnb_readers"] = readers
                prev[k] = toks
        b.done(out_toks)


class TpMachine:

    def __init__(self, b, identb, tib, n=2, evac_eng="vector"):
        self.b = b
        self.n = n
        self.identb = identb
        self.tib = tib
        self.evac_eng = evac_eng
        self.ptp = [b.ps(f"tpm_p{i}", [128, 512], F32) for i in range(n)]
        self.ev = [b.sb(f"tpm_e{i}", [128, 128], BF16) for i in range(n)]
        self.tp_tok = {}
        self.ev_dma = {}
        self.i = 0

    def to_dram(self, src_slice, dram_slice, deps):
        b = self.b
        kk = self.i % self.n
        self.i += 1
        pt = self.ptp[kk][:].bitcast(BF16)[:, 0:128]
        t_tp = tp_b16(b, pt, src_slice, self.identb,
                      deps=list(deps) + [self.tib, self.tp_tok.get(kk)])
        t_ev = (b.op("vector", "tensor_copy", self.ev[kk][:], pt,
                     deps=[t_tp, self.ev_dma.get(kk)])
                if self.evac_eng == "vector" else
                b.op("scalar", "activation", self.ev[kk][:], pt, AF.Identity,
                     deps=[t_tp, self.ev_dma.get(kk)]))
        self.tp_tok[kk] = t_ev
        self.ev_dma[kk] = b.dma("sync" if kk % 2 else "gpsimd", dram_slice,
                                self.ev[kk][:], deps=[t_ev], lane=kk % 3)
        return t_tp, self.ev_dma[kk]

    def to_sbuf(self, src_slice, sbuf_slice, deps, extra_ev_deps=()):
        b = self.b
        kk = self.i % self.n
        self.i += 1
        pt = self.ptp[kk][:].bitcast(BF16)[:, 0:128]
        t_tp = tp_b16(b, pt, src_slice, self.identb,
                      deps=list(deps) + [self.tib, self.tp_tok.get(kk)])
        t_ev = (b.op("vector", "tensor_copy", sbuf_slice, pt,
                     deps=[t_tp] + [d for d in extra_ev_deps if d])
                if self.evac_eng == "vector" else
                b.op("scalar", "activation", sbuf_slice, pt, AF.Identity,
                     deps=[t_tp] + [d for d in extra_ev_deps if d]))
        self.tp_tok[kk] = t_ev
        return t_tp, t_ev


def emit_wtrans(b, tpm, wtile, fmb, wt_dram, out_dram, E, prev_reads, fm_dma):
    nec = E // 128
    outs = []
    for ec in range(nec):
        k = ec % 3
        ld = b.dma("gpsimd", wtile[k][:, 0:D], wt_dram[ec * 128:(ec + 1) * 128, :],
                   deps=[prev_reads[k]], lane=k)
        last_tp = None
        evs = []
        for dc in range(NDC):
            t_tp, t_ev = tpm.to_sbuf(wtile[k][:, dc * 128:(dc + 1) * 128],
                                     fmb[k][:, dc, :], deps=[ld],
                                     extra_ev_deps=[fm_dma.get(k)])
            evs.append(t_ev)
            last_tp = t_tp
        fm_dma[k] = b.dma("sync",
                          out_dram[:, ec * 128:(ec + 1) * 128].rearrange(
                              "(a p) f -> p a f", p=128),
                          fmb[k][:], deps=evs, lane=k)
        outs.append(fm_dma[k])
        prev_reads[k] = last_tp
    return outs


def emit_wprep(b, io):
    (qkv_w, proj_w, W0, lora_down, lora_up, down_aux, up_aux, Wq, Wk, Wv, Wo,
     slot_embeds, gamma_slot, qkvT, projT, WeffT, WqT, WkT, WvT, WoT, seT) = (
        io[k] for k in ("qkv_w", "proj_w", "W0", "lora_down", "lora_up", "down_aux",
                        "up_aux", "Wq", "Wk", "Wv", "Wo", "slot_embeds", "gamma_slot",
                        "qkvT", "projT", "WeffT", "WqT", "WkT", "WvT", "WoT", "seT"))
    with b.stage("wp") as ctx:
        identf, tif, identb, tib = b.idents
        tpm = TpMachine(b, identb, tib, n=4)
        wtile = [b.sb(f"wtile{i}", [128, D], BF16) for i in range(3)]
        fmb = [b.sb(f"wfmb{i}", [128, 8, 128], BF16) for i in range(3)]
        prev_reads = [None, None, None]
        fm_dma = {}
        outs = []
        outs += emit_wtrans(b, tpm, wtile, fmb, qkv_w, qkvT, 3 * D, prev_reads, fm_dma)
        outs += emit_wtrans(b, tpm, wtile, fmb, proj_w, projT, D, prev_reads, fm_dma)
        outs += emit_wtrans(b, tpm, wtile, fmb, Wq, WqT, D, prev_reads, fm_dma)
        outs += emit_wtrans(b, tpm, wtile, fmb, Wk, WkT, D, prev_reads, fm_dma)
        outs += emit_wtrans(b, tpm, wtile, fmb, Wv, WvT, D, prev_reads, fm_dma)
        outs += emit_wtrans(b, tpm, wtile, fmb, Wo, WoT, D, prev_reads, fm_dma)

        lds = b.sb("lds", [4, 100])
        lus = b.sb("lus", [50, 4])
        das = b.sb("das", [100, D])
        t_ld = b.dma("sync", lds[:], lora_down[:])
        t_lu = b.dma("sync", lus[:], lora_up[:])
        t_da = b.dma("sync", das[:], down_aux[:])
        p_sm = b.ps("p_sm", [128, 1024], F32)
        ldT = b.sb("ldT", [100, 4])
        t1 = b.sb("t1", [4, D])
        luT = b.sb("luT", [4, 50])
        t2 = b.sb("t2", [50, D])
        uaT = b.sb("uaT", [50, D])
        tk = tp_f32(b, p_sm[0:100, 0:4], lds[:], identf, deps=[t_ld, tif])
        tk = b.op("vector", "tensor_copy", ldT[:], p_sm[0:100, 0:4], deps=[tk])
        m1 = b.mm(p_sm[0:4, 0:512], ldT[:], das[:, 0:512], True, True, deps=[tk, t_da])
        m1b = b.mm(p_sm[0:4, 512:1024], ldT[:], das[:, 512:1024], True, True, deps=[tk, t_da])
        tk1 = b.op("vector", "tensor_copy", t1[:], p_sm[0:4, :], deps=[m1, m1b])
        tk2 = tp_f32(b, p_sm[0:4, 0:50], lus[:], identf, deps=[t_lu, tif, tk1])
        tk2 = b.op("vector", "tensor_copy", luT[:], p_sm[0:4, 0:50], deps=[tk2])
        m2 = b.mm(p_sm[0:50, 0:512], luT[:], t1[:, 0:512], True, True, deps=[tk2, tk1])
        m2b = b.mm(p_sm[0:50, 512:1024], luT[:], t1[:, 512:1024], True, True, deps=[tk2, tk1])
        tk3 = b.op("vector", "tensor_copy", t2[:], p_sm[0:50, :], deps=[m2, m2b])
        uas = b.sb("uas", [128, 50])
        t_ua_prev = None
        for dc in range(NDC):
            t_ua = b.dma("sync", uas[:], up_aux[dc * 128:(dc + 1) * 128, :],
                         deps=[t_ua_prev])
            tk4 = tp_f32(b, p_sm[0:50, 0:128], uas[:], identf, deps=[t_ua, tk3])
            t_ua_prev = b.op("vector", "tensor_copy", uaT[:, dc * 128:(dc + 1) * 128],
                             p_sm[0:50, 0:128], deps=[tk4])
        tk_uaT = t_ua_prev
        w0t = b.sb("w0t", [128, D])
        weff_b = b.sb("weff_b", [128, D], BF16)
        p_li = b.ps("p_li", [128, 1024], F32)
        prev_cast = None
        last_wtp = None
        for oc in range(NDC):
            mli = [b.mm(p_li[:, nb * 512:(nb + 1) * 512],
                        uaT[:, oc * 128:(oc + 1) * 128],
                        t2[:, nb * 512:(nb + 1) * 512], True, True,
                        deps=[tk_uaT, tk3, prev_cast]) for nb in range(2)]
            t_w0 = b.dma("sync", w0t[:], W0[oc * 128:(oc + 1) * 128, :], deps=[prev_cast],
                         lane=oc % 3)
            t_add = b.op("vector", "tensor_add", w0t[:], w0t[:], p_li[:], deps=mli + [t_w0])
            prev_cast = b.op("vector", "tensor_copy", weff_b[:], w0t[:],
                             deps=[t_add, last_wtp])
            evs = []
            kE = oc % 3
            for dc in range(NDC):
                last_wtp, t_ev = tpm.to_sbuf(weff_b[:, dc * 128:(dc + 1) * 128],
                                             fmb[kE][:, dc, :], deps=[prev_cast],
                                             extra_ev_deps=[fm_dma.get(kE)])
                evs.append(t_ev)
            fm_dma[kE] = b.dma("sync",
                               WeffT[:, oc * 128:(oc + 1) * 128].rearrange(
                                   "(a p) f -> p a f", p=128),
                               fmb[kE][:], deps=evs, lane=kE)
            outs.append(fm_dma[kE])

        ones1f = b.sb("ones1f", [1, 128])
        t_o1 = b.op("vector", "memset", ones1f[:], 1.0)
        g_row = b.sb("g_row", [1, D])
        t_gr = b.dma("sync", g_row[:], gamma_slot[:])
        gb = b.sb("gb", [128, D])
        mgb = [b.mm(p_sm[:, nb * 512:(nb + 1) * 512], ones1f[:],
                    g_row[:, nb * 512:(nb + 1) * 512], True, True,
                    deps=[t_o1, t_gr, tk_uaT]) for nb in range(2)]
        t_gb = b.op("vector", "tensor_copy", gb[:], p_sm[:], deps=mgb)
        set_ = [b.sb(f"set{i}", [128, D]) for i in range(2)]
        sed = b.sb("sed", [128, D])
        sen = [b.sb(f"sen{i}", [128, D], BF16) for i in range(2)]
        snq = b.sb("snq", [128, 1])
        srl = b.sb("srl", [128, 1])
        prevS = [{}, {}]
        for tc in range(ET // 128):
            k = tc % 2
            ld = b.dma("sync", set_[k][:], slot_embeds[tc * 128:(tc + 1) * 128, :],
                       deps=[prevS[k].get("rd")], lane=tc % 3)
            t_sq = b.op("scalar", "activation", sed[:], set_[k][:], AF.Square,
                        accum_out=snq[:], deps=[ld, prevS[k].get("t_n2"), prevS[1 - k].get("t_n2")])
            t_n2 = b.op("vector", "tensor_scalar_max", snq[:], snq[:], 1e-24, deps=[t_sq])
            t_ln = b.op("scalar", "activation", srl[:], snq[:], AF.Ln,
                        deps=[t_n2, prevS[1 - k].get("t_sn")])
            t_ex = b.op("scalar", "activation", srl[:], srl[:], AF.Exp, scale=-0.5, deps=[t_ln])
            t_rl = b.op("vector", "tensor_scalar_mul", srl[:], srl[:], 32.0, deps=[t_ex])
            t_sn = b.op("vector", "scalar_tensor_tensor", sen[k][:], set_[k][:], srl[:],
                        gb[:], ALU.mult, ALU.mult,
                        deps=[t_rl, t_gb, ld] + prevS[k].get("senr", []))
            senr = []
            evs = []
            kE = tc % 3
            for dc in range(NDC):
                t_tp, t_ev = tpm.to_sbuf(sen[k][:, dc * 128:(dc + 1) * 128],
                                         fmb[kE][:, dc, :], deps=[t_sn],
                                         extra_ev_deps=[fm_dma.get(kE)])
                senr.append(t_tp)
                evs.append(t_ev)
            fm_dma[kE] = b.dma("sync",
                               seT[:, tc * 128:(tc + 1) * 128].rearrange(
                                   "(a p) f -> p a f", p=128),
                               fmb[kE][:], deps=evs, lane=kE)
            outs.append(fm_dma[kE])
            prevS[k] = {"rd": t_sn, "t_n2": t_n2, "senr": senr, "t_sn": t_sn}
        b.done(outs)


def emit_proj_tm(b, a_fm, WT, bias, o_tm, act="identity", nout=D, tag="pj"):
    nnb = nout // 512
    with b.stage(tag) as ctx:
        wt = [b.sb(f"wtile{i}", [128, nout], BF16) for i in range(NDC)]
        wtoks = [b.dma("sync", wt[i][:], WT[i * 128:(i + 1) * 128, :], lane=i % 3)
                 for i in range(NDC)]
        brow = b.sb("brow", [1, nout])
        browb = b.sb("browb", [1, nout], BF16)
        t_bias = b.dma("sync", brow[:], bias[:])
        t_bb = b.op("vector", "tensor_copy", browb[:], brow[:], deps=[t_bias])
        ones1b = b.sb("ones1b", [1, 128], BF16)
        t_o1 = b.op("vector", "memset", ones1b[:], 1.0)
        at = [b.sb(f"at{i}", [128, S], BF16) for i in range(NDC)]
        ot = [b.sb(f"ot{i}", [128, nout]) for i in range(2)]
        pso = [b.ps(f"pso{i}", [128, nout], F32) for i in range(2)]
        afm_toks = [None] * NDC
        prev_ev = [None, None]
        prev_st = [None, None]
        outs = []
        for bi in range(BPC):
            for i in range(NDC):
                afm_toks[i] = b.dma("gpsimd", at[i][:], a_fm[bi, i * 128:(i + 1) * 128, :],
                                    deps=[afm_toks[i], prev_st[0], prev_st[1]], lane=i % 3)
            for sc in range(NSC):
                k = sc % 2
                mms = []
                for nb in range(nnb):
                    dst = pso[k][:, nb * 512:(nb + 1) * 512]
                    for ic in range(NDC):
                        dep = [afm_toks[ic], wtoks[ic]] if nb == 0 else []
                        b.mm(dst, at[ic][:, sc * 128:(sc + 1) * 128],
                             wt[ic][:, nb * 512:(nb + 1) * 512],
                             ic == 0, False, deps=dep + [prev_ev[k]], inc=False)
                    mms.append(b.mm(dst, ones1b[:],
                                    browb[:, nb * 512:(nb + 1) * 512],
                                    False, True, deps=[t_o1, t_bb]))
                fn = AF.Silu if act == "silu" else AF.Identity
                t_ev = b.op("scalar", "activation", ot[k][:], pso[k][:], fn,
                            deps=mms + [prev_st[k]])
                prev_ev[k] = t_ev
                prev_st[k] = b.dma("sync", o_tm[bi, sc * 128:(sc + 1) * 128, :], ot[k][:],
                                   deps=[t_ev], lane=sc % 3)
                outs.append(prev_st[k])
        b.done(outs)


def emit_l0qkv(b, xn_fm, qkvT, qkv_b, qkv_fm):
    NOC = 3 * D // 128
    with b.stage("q0") as ctx:
        identf, tif, identb, tib = b.idents
        wt = [b.sb(f"wtile{i}", [128, 3 * D], BF16) for i in range(NDC)]
        wtoks = [b.dma("sync", wt[i][:], qkvT[i * 128:(i + 1) * 128, :], lane=i % 3)
                 for i in range(NDC)]
        brow = b.sb("brow", [1, 3 * D])
        t_bias = b.dma("sync", brow[:], qkv_b[:])
        one11 = b.sb("one11", [1, 1])
        t_o11 = b.op("vector", "memset", one11[:], 1.0)
        at = [b.sb(f"at{i}", [128, S], BF16) for i in range(NDC)]
        ot = [b.sb(f"ot{i}", [128, S], BF16) for i in range(2)]
        pso = [b.ps(f"pso{i}", [128, 2048], F32) for i in range(2)]
        bcols = b.sb("bcols", [128, NOC])
        t_bc = []
        for oc in range(NOC):
            tm = b.mm(pso[0][:, oc:oc + 1], brow[:, oc * 128:(oc + 1) * 128], one11[:],
                      True, True, deps=[t_bias, t_o11])
            t_bc.append(tm)
        t_bcols = b.op("vector", "tensor_copy", bcols[:], pso[0][:, 0:NOC], deps=t_bc)
        afm_toks = [None] * NDC
        prev_ev = [t_bcols, None]
        prev_st = [None, None]
        outs = []
        for bi in range(BPC):
            for i in range(NDC):
                afm_toks[i] = b.dma("gpsimd", at[i][:], xn_fm[bi, i * 128:(i + 1) * 128, :],
                                    deps=[afm_toks[i], prev_st[0], prev_st[1]] if bi else [afm_toks[i]],
                                    lane=i % 3)
            for oc in range(NOC):
                k = oc % 2
                mms = []
                for nb in range(4):
                    dst = pso[k][:, nb * 512:(nb + 1) * 512]
                    for ic in range(NDC):
                        last = ic == NDC - 1
                        tm = b.mm(dst, wt[ic][:, oc * 128:(oc + 1) * 128],
                                  at[ic][:, nb * 512:(nb + 1) * 512],
                                  ic == 0, last,
                                  deps=[afm_toks[ic], wtoks[ic], prev_ev[k]],
                                  inc=last)
                    mms.append(tm)
                t_ev = b.op("scalar", "activation", ot[k][:], pso[k][:], AF.Identity,
                            bias=bcols[:, oc:oc + 1], deps=mms + [t_bcols, prev_st[k]])
                prev_ev[k] = t_ev
                prev_st[k] = b.dma("sync", qkv_fm[bi, oc * 128:(oc + 1) * 128, :], ot[k][:],
                                   deps=[t_ev], lane=oc % 3)
                outs.append(prev_st[k])
        b.done(outs)


def emit_l0attn(b, qkv_fm, attn_fm):
    with b.stage("a0") as ctx:
        identf, tif, identb, tib = b.idents
        qkv = [b.sb(f"qkv{i}", [128, S], BF16) for i in range(24)]
        ao = [b.sb(f"ao{i}", [128, S], BF16) for i in range(NDC)]
        vp = [b.sb(f"vp{i}", [128, 16, 80], BF16) for i in range(2)]
        et = [b.sb(f"et{i}", [128, 1024], BF16) for i in range(4)]
        kaug = [b.sb(f"kaug{i}", [128, S], BF16) for i in range(2)]
        qaug = [b.sb(f"qaug{i}", [128, S], BF16) for i in range(2)]
        nums = [b.sb(f"nums{i}", [128, 1024]) for i in range(2)]
        denr = [b.sb(f"denr{i}", [1, 1024]) for i in range(2)]
        rden = [b.sb(f"rden{i}", [1, 1024]) for i in range(2)]
        onesc = b.sb("onesc", [1, 64])
        onescol = b.sb("onescol", [128, 1], BF16)
        qsq = b.sb("qsq", [64, S], BF16)
        ksq = b.sb("ksq", [64, S], BF16)
        km2 = b.sb("km2", [1, 2])
        t_on = b.op("vector", "memset", onesc[:], 1.0)
        t_oc = b.op("vector", "memset", onescol[:], 1.0)
        t_vone = [b.op("vector", "memset", vp[i][:, :, 64:80], 1.0) for i in range(2)]
        t_kone = [b.op("vector", "memset", kaug[i][64:65, :], -1.0) for i in range(2)]
        psS = [b.ps(f"psS{i}", [128, 1024], F32) for i in range(2)]
        psO = b.ps("psO", [128, 1024], F32)
        psB = b.ps("psB", [128, 1024], F32)
        qkv_toks = [None] * 24
        outs = []
        st_prev = {}
        prev = {}
        pending = []
        pending_pre = []

        def emit_preamble(bi, h):
            j = h % 2
            qt = qkv[h // 2]
            kt = qkv[8 + h // 2]
            base = 64 * (h % 2)
            t_ka = b.op("vector", "tensor_copy", kaug[j][0:64, :], kt[base:base + 64, :],
                        deps=[qkv_toks[8 + h // 2], t_kone[j], prev.get(("ka", j))])
            t_qa = b.op("vector", "tensor_copy", qaug[j][0:64, :], qt[base:base + 64, :],
                        deps=[qkv_toks[h // 2], prev.get(("qa", j))])
            t_ks = b.op("vector", "tensor_mul", ksq[:], kt[base:base + 64, :],
                        kt[base:base + 64, :], deps=[qkv_toks[8 + h // 2],
                                                     prev.get("ksq_rd")])
            t_qs = b.op("vector", "tensor_mul", qsq[:], qt[base:base + 64, :],
                        qt[base:base + 64, :], deps=[qkv_toks[h // 2],
                                                     prev.get("qsq_rd")])
            kn_mms = {0: [], 1: []}
            for half, row in ((0, 0), (1, 32)):
                for nb in range(2):
                    kn_mms[half].append(
                        b.mm(psB[row:row + 1, nb * 512:(nb + 1) * 512], onescol[0:64],
                             ksq[:, half * 1024 + nb * 512:half * 1024 + (nb + 1) * 512],
                             True, True,
                             deps=[t_ks, t_oc, prev.get("c1_rd"), prev.get("c2_rd"),
                                   prev.get("psB_mul")]))
            t_k1 = b.op("vector", "reduce_max", km2[:, 0:1], psB[0:1, :], axis=AX.X,
                        deps=kn_mms[0] + [prev.get("km2_rd")])
            t_k2 = b.op("vector", "reduce_max", km2[:, 1:2], psB[32:33, :], axis=AX.X,
                        deps=kn_mms[1])
            t_k3 = b.op("vector", "reduce_max", km2[:, 0:1], km2[:, 0:2], axis=AX.X,
                        deps=[t_k1, t_k2])
            qn_mms = {0: [], 1: []}
            for half, row in ((0, 0), (1, 32)):
                for nb in range(2):
                    qn_mms[half].append(
                        b.mm(psB[row:row + 1, nb * 512:(nb + 1) * 512], onescol[0:64],
                             qsq[:, half * 1024 + nb * 512:half * 1024 + (nb + 1) * 512],
                             True, True,
                             deps=[t_qs, t_oc, t_k1 if half == 0 else t_k2]))
            t_c1 = b.op("scalar", "activation", qaug[j][64:65, 0:1024], psB[0:1, :],
                        AF.Sqrt, scale=km2[:, 0:1],
                        deps=qn_mms[0] + [t_k3, t_qa, prev.get("c1_rd")])
            t_c2 = b.op("scalar", "activation", qaug[j][64:65, 1024:2048], psB[32:33, :],
                        AF.Sqrt, scale=km2[:, 0:1],
                        deps=qn_mms[1] + [t_k3, t_qa, prev.get("c2_rd")])
            prev["qsq_rd"] = qn_mms[1][-1]
            prev["ksq_rd"] = kn_mms[1][-1]
            prev["km2_rd"] = t_c2
            prev["c1_rd"] = t_c1
            prev["c2_rd"] = t_c2
            return {"t_ka": t_ka, "t_qa": t_qa, "t_c1": t_c1, "t_c2": t_c2}

        def emit_vps(bi2, h2):
            j2 = h2 % 2
            vt2 = qkv[16 + h2 // 2]
            t_vps2 = []
            for kc in range(NSC):
                t_vps2.append(b.dma("sync", vp[j2][:, kc, 0:64],
                                    vt2[64 * (h2 % 2):64 * (h2 % 2) + 64,
                                        kc * 128:(kc + 1) * 128],
                                    transpose=True,
                                    deps=[qkv_toks[16 + h2 // 2], t_vone[j2],
                                          prev.get(("vp", j2))], lane=kc % 3))
            return t_vps2

        def emit_next_loads(nbi, i, vtoks):
            if vtoks is None:
                pre2 = prev["_pre"]
                qkv_toks[i] = b.dma("gpsimd" if i % 2 else "sync", qkv[i][:],
                                    qkv_fm[nbi, i * 128:(i + 1) * 128, :],
                                    deps=[qkv_toks[i], prev.get("qsq_rd"),
                                          pre2["t_qa"]])
                kI = 8 + i
                qkv_toks[kI] = b.dma("gpsimd" if kI % 2 else "sync", qkv[kI][:],
                                    qkv_fm[nbi, kI * 128:(kI + 1) * 128, :],
                                    deps=[qkv_toks[kI], prev.get("ksq_rd"),
                                          pre2["t_ka"]])
            else:
                vI = 16 + i
                qkv_toks[vI] = b.dma("gpsimd" if vI % 2 else "sync", qkv[vI][:],
                                     qkv_fm[nbi, vI * 128:(vI + 1) * 128, :],
                                     deps=[qkv_toks[vI]] + vtoks)

        for bi in range(BPC):
            ao_w = {}
            if bi == 0:
                for i in [c + 8 * k for c in range(8) for k in range(3)]:
                    qkv_toks[i] = b.dma("gpsimd" if i % 2 else "sync", qkv[i][:],
                                        qkv_fm[bi, i * 128:(i + 1) * 128, :],
                                        deps=[qkv_toks[i]], lane=i % 3)
            pre = emit_preamble(bi, 0)
            pre_vps = emit_vps(bi, 0)
            for h in range(H0):
                j = h % 2
                t_vps = pre_vps
                t_ka, t_qa = pre["t_ka"], pre["t_qa"]
                t_c1, t_c2 = pre["t_c1"], pre["t_c2"]
                pending_pre.clear()
                if h + 1 < H0:
                    pending_pre.append(lambda bi=bi, h=h: prev.__setitem__(
                        "_pre", emit_preamble(bi, h + 1)))
                    pending_pre.append(lambda bi=bi, h=h: prev.__setitem__(
                        "_vps", emit_vps(bi, h + 1)))
                if bi + 1 < BPC:
                    if h % 2 == 0:
                        pending_pre.append(lambda bi=bi, h=h: emit_next_loads(
                            bi + 1, h // 2, None))
                    else:
                        pending_pre.append(lambda bi=bi, h=h, tv=t_vps: emit_next_loads(
                            bi + 1, h // 2, tv))
                for qh in range(2):
                    p = qh % 2
                    qsl = slice(qh * 1024, (qh + 1) * 1024)
                    t_avs = []
                    smm = {}

                    def emit_S(kc):
                        kk = kc % 2
                        tms = []
                        for nb in range(2):
                            tms.append(b.mm(psS[kk][:, nb * 512:(nb + 1) * 512],
                                            kaug[j][0:65, kc * 128:(kc + 1) * 128],
                                            qaug[j][0:65, qh * 1024 + nb * 512:qh * 1024 + (nb + 1) * 512],
                                            True, True,
                                            deps=[t_ka, t_qa, t_c2,
                                                  prev.get(("exp", kk, nb))],
                                            inc=True))
                        smm[kc] = tms

                    emit_S(0)
                    emit_S(1)
                    for kc in range(NSC):
                        kk = kc % 2
                        ke = kc % 4
                        if pending and qh == 0 and kc == 3:
                            for fn in pending:
                                fn()
                            pending.clear()
                        if pending_pre and qh == 1 and kc == 6:
                            for fn in pending_pre:
                                fn()
                            pending_pre.clear()
                        t_exp = {}
                        for nb in range(2):
                            t_exp[nb] = b.op("scalar", "activation",
                                             et[ke][:, nb * 512:(nb + 1) * 512],
                                             psS[kk][:, nb * 512:(nb + 1) * 512],
                                             AF.Exp, scale=0.125,
                                             deps=[smm[kc][nb],
                                                   prev.get(("av", ke, nb))])
                            prev[("exp", kk, nb)] = t_exp[nb]
                        if kc + 2 < NSC:
                            emit_S(kc + 2)
                        for nb in range(2):
                            tav = b.mm(psO[0:65, nb * 512:(nb + 1) * 512],
                                       vp[j][:, kc, 0:65],
                                       et[ke][:, nb * 512:(nb + 1) * 512],
                                       kc == 0, kc == NSC - 1,
                                       deps=[t_exp[nb]] + (t_vps +
                                                           [prev.get(("numden", nb))]
                                                           if kc == 0 else []),
                                       inc=True)
                            prev[("av", ke, nb)] = tav
                        t_avs.append(tav)
                    prev[("vp", j)] = t_avs[-1]
                    t_num, t_den = [], []
                    for nb in range(2):
                        csl = slice(nb * 512, (nb + 1) * 512)
                        tn = b.op("vector", "tensor_copy", nums[p][0:64, csl],
                                  psO[0:64, csl],
                                  deps=t_avs + [prev.get(("num_rd", p))])
                        td = b.op("vector", "tensor_copy", denr[p][:, csl],
                                  psO[64:65, csl], deps=[tn])
                        t_num.append(tn)
                        t_den.append(td)
                        prev[("numden", nb)] = td

                    def tail(p=p, h=h, qh=qh, qsl=qsl, t_num=t_num, t_den=t_den,
                             st_tok=st_prev.get(h // 2)):
                        t_l = b.op("scalar", "activation", rden[p][:], denr[p][:], AF.Ln,
                                   deps=t_den + [prev.get(("rden_rd", p))])
                        t_r = b.op("scalar", "activation", rden[p][:], rden[p][:], AF.Exp,
                                   scale=-1.0, deps=[t_l])
                        mb = []
                        for nb in range(2):
                            mb.append(b.mm(psB[64:128, nb * 512:(nb + 1) * 512], onesc[:],
                                           rden[p][:, nb * 512:(nb + 1) * 512], True, True,
                                           deps=[t_r, t_on, prev.get("psB_mul"),
                                                 prev.get("c1_rd"),
                                                 prev.get("c2_rd")]))
                        t_mul = b.op("vector", "tensor_tensor",
                                     ao[h // 2][64 * (h % 2):64 * (h % 2) + 64, qsl],
                                     nums[p][0:64, :], psB[64:128, :], ALU.mult,
                                     deps=mb + t_num + [st_tok])
                        prev["psB_mul"] = t_mul
                        prev[("num_rd", p)] = t_mul
                        prev[("den_rd", p)] = t_mul
                        prev[("rden_rd", p)] = t_mul
                        ao_w[h // 2] = t_mul
                    pending.append(tail)
                for fn in pending_pre:
                    fn()
                pending_pre.clear()
                prev[("ka", j)] = t_avs[-1]
                prev[("qa", j)] = t_avs[-1]
                pre = prev.pop("_pre", None)
                pre_vps = prev.pop("_vps", None)
            for fn in pending:
                fn()
            pending.clear()
            st_toks = {}
            for dc in range(NDC):
                if ao_w.get(dc) is None:
                    continue
                st_toks[dc] = b.dma("sync", attn_fm[bi, dc * 128:(dc + 1) * 128, :],
                                    ao[dc][:], deps=[ao_w[dc]], lane=dc % 3)
                outs.append(st_toks[dc])
            st_prev = st_toks
        b.done(outs)


def emit_resid(b, x, s_tm, g_tm, x1, tag="rs"):
    with b.stage(tag) as ctx:
        W = 4 * D
        xt = [b.sb(f"xt{i}", [128, W]) for i in range(2)]
        st_ = [b.sb(f"st{i}", [128, W]) for i in range(2)]
        gt = [b.sb(f"gt{i}", [128, W]) for i in range(2)]
        prev = [None, None]
        outs = []
        for bi in range(BPC):
            for blk in range(4):
                k = blk % 2
                sl = slice(blk * 512, (blk + 1) * 512)
                vw = lambda t: t[bi, sl, :].rearrange("(a p) f -> p a f", p=128)
                v3 = lambda t: t[:].rearrange("p (a f) -> p a f", f=D)
                l1 = b.dma("sync", v3(xt[k]), vw(x), deps=[prev[k]], lane=k)
                l2 = b.dma("gpsimd", v3(st_[k]), vw(s_tm), deps=[prev[k]], lane=k)
                l3 = b.dma("sync", v3(gt[k]), vw(g_tm), deps=[prev[k]], lane=2)
                t1 = b.op("vector", "tensor_mul", st_[k][:], st_[k][:], gt[k][:],
                          deps=[l2, l3])
                t2 = b.op("vector", "tensor_add", xt[k][:], xt[k][:], st_[k][:],
                          deps=[t1, l1])
                prev[k] = b.dma("sync", vw(x1), v3(xt[k]), deps=[t2], lane=k)
                outs.append(prev[k])
        b.done(outs)


def emit_l0epi(b, xn_fm, WeffT, b0, attn_fm, projT, proj_b, x, x1):
    with b.stage("e0") as ctx:
        wg = [b.sb(f"wg{i}", [128, D], BF16) for i in range(NDC)]
        wp_ = [b.sb(f"wp{i}", [128, D], BF16) for i in range(NDC)]
        wg_t = [b.dma("sync", wg[i][:], WeffT[i * 128:(i + 1) * 128, :], lane=i % 3)
                for i in range(NDC)]
        wp_t = [b.dma("gpsimd", wp_[i][:], projT[i * 128:(i + 1) * 128, :], lane=i % 3)
                for i in range(NDC)]
        brow = b.sb("brow", [1, 2 * D])
        browb = b.sb("browb", [1, 2 * D], BF16)
        t_b1 = b.dma("sync", brow[:, 0:D], b0[:])
        t_b2 = b.dma("sync", brow[:, D:2 * D], proj_b[:])
        t_bb = b.op("vector", "tensor_copy", browb[:], brow[:], deps=[t_b1, t_b2])
        ones1b = b.sb("ones1b", [1, 128], BF16)
        t_o1 = b.op("vector", "memset", ones1b[:], 1.0)
        ag = [b.sb(f"ag{i}", [128, S], BF16) for i in range(NDC)]
        as_ = [b.sb(f"as{i}", [128, S], BF16) for i in range(NDC)]
        ag_t = [None] * NDC
        as_t = [None] * NDC
        gbuf = [b.sb(f"gbuf{i}", [128, D]) for i in range(NSC)]
        xb = [b.sb(f"xb{i}", [128, D]) for i in range(2)]
        yb = [b.sb(f"yb{i}", [128, D]) for i in range(2)]
        psg = [b.ps(f"psg{i}", [128, 1024], F32) for i in range(2)]
        pss = [b.ps(f"pss{i}", [128, 1024], F32) for i in range(2)]
        outs = []
        prev = {}
        for bi in range(BPC):
            for i in range(NDC):
                ag_t[i] = b.dma("gpsimd", ag[i][:], xn_fm[bi, i * 128:(i + 1) * 128, :],
                                deps=[ag_t[i], prev.get("glast")], lane=i % 3)
                as_t[i] = b.dma("sync", as_[i][:], attn_fm[bi, i * 128:(i + 1) * 128, :],
                                deps=[as_t[i], prev.get("slast")], lane=i % 3)
            for sc in range(NSC):
                k = sc % 2
                mms = []
                for nb in range(2):
                    dst = psg[k][:, nb * 512:(nb + 1) * 512]
                    for ic in range(NDC):
                        b.mm(dst, ag[ic][:, sc * 128:(sc + 1) * 128],
                             wg[ic][:, nb * 512:(nb + 1) * 512], ic == 0, False,
                             deps=[ag_t[ic], wg_t[ic], prev.get(f"gev{k}")], inc=False)
                    mms.append(b.mm(dst, ones1b[:], browb[:, nb * 512:(nb + 1) * 512],
                                    False, True, deps=[t_o1, t_bb]))
                t_g = b.op("scalar", "activation", gbuf[sc][:], psg[k][:], AF.Silu,
                           deps=mms + [prev.get(("g", sc))])
                prev[f"gev{k}"] = t_g
                prev[("g", sc)] = t_g
            prev["glast"] = prev[("g", NSC - 1)]
            for sc in range(NSC):
                k = sc % 2
                ssl = slice(sc * 128, (sc + 1) * 128)
                l_x = b.dma("sync", xb[k][:], x[bi, ssl, :],
                            deps=[prev.get(f"xadd{k}")], lane=k)
                mms = []
                for nb in range(2):
                    dst = pss[k][:, nb * 512:(nb + 1) * 512]
                    for ic in range(NDC):
                        b.mm(dst, as_[ic][:, sc * 128:(sc + 1) * 128],
                             wp_[ic][:, nb * 512:(nb + 1) * 512], ic == 0, False,
                             deps=[as_t[ic], wp_t[ic], prev.get(f"smul{k}")], inc=False)
                    mms.append(b.mm(dst, ones1b[:],
                                    browb[:, D + nb * 512:D + (nb + 1) * 512],
                                    False, True, deps=[t_o1, t_bb]))
                t_sg = b.op("vector", "tensor_mul", yb[k][:], pss[k][:], gbuf[sc][:],
                            deps=mms + [prev.get(("g", sc)), prev.get(f"yst{k}")])
                prev[f"smul{k}"] = t_sg
                prev[("g", sc)] = t_sg
                t_x = b.op("vector", "tensor_add", yb[k][:], yb[k][:], xb[k][:],
                           deps=[t_sg, l_x])
                d_o = b.dma("sync", x1[bi, ssl, :], yb[k][:], deps=[t_x], lane=k)
                prev[f"yst{k}"] = d_o
                prev[f"xadd{k}"] = d_o
                outs.append(d_o)
            prev["slast"] = prev["smul0"]
        b.done(outs)


def emit_l1qkv(b, xn_fm, WqT, WkT, WvT, bq, bk, bv, beta, Qt_fm, Kt_fm, Kt_tm, Vp_tm):
    with b.stage("q1") as ctx:
        identf, tif, identb, tib = b.idents
        wq = [b.sb(f"wq{i}", [128, D], BF16) for i in range(NDC)]
        wk = [b.sb(f"wk{i}", [128, D], BF16) for i in range(NDC)]
        wv = [b.sb(f"wv{i}", [128, D], BF16) for i in range(NDC)]
        wtok = {}
        for i in range(NDC):
            wtok[("q", i)] = b.dma("sync", wq[i][:], WqT[i * 128:(i + 1) * 128, :],
                                   lane=i % 3)
            wtok[("k", i)] = b.dma("gpsimd", wk[i][:], WkT[i * 128:(i + 1) * 128, :],
                                   lane=i % 3)
            wtok[("v", i)] = b.dma("sync", wv[i][:], WvT[i * 128:(i + 1) * 128, :],
                                   lane=(i + 1) % 3)
        brows = b.sb("brows", [1, 3 * D])
        browsb = b.sb("browsb", [1, 3 * D], BF16)
        t_b1 = b.dma("sync", brows[:, 0:D], bq[:])
        t_b2 = b.dma("sync", brows[:, D:2 * D], bk[:])
        t_b3 = b.dma("sync", brows[:, 2 * D:3 * D], bv[:])
        t_bb = b.op("vector", "tensor_copy", browsb[:], brows[:], deps=[t_b1, t_b2, t_b3])
        ones1b = b.sb("ones1b", [1, 128], BF16)
        t_o1 = b.op("vector", "memset", ones1b[:], 1.0)
        ones1f = b.sb("ones1f", [1, 128])
        t_o1f = b.op("vector", "memset", ones1f[:], 1.0)
        berow = b.sb("berow", [1, 1])
        becol = b.sb("becol", [128, 1])
        t_be = b.dma("sync", berow[:], beta[:])
        t_be = b.op("scalar", "activation", berow[:], berow[:], AF.Exp, deps=[t_be])
        psq = [b.ps(f"psq{i}", [128, 1024], F32) for i in range(3)]
        mbe = b.mm(psq[0][:, 0:1], ones1f[:], berow[:], True, True, deps=[t_be, t_o1f])
        t_bec = b.op("vector", "tensor_copy", becol[:], psq[0][:, 0:1], deps=[mbe])

        xn = [b.sb(f"xn{i}", [128, S], BF16) for i in range(NDC)]
        gmax = [b.sb(f"gmax{i}", [128, 64]) for i in range(4)]
        sub = [b.sb(f"sub{i}", [128, 1024]) for i in range(4)]
        ex = [b.sb(f"ex{i}", [128, 1024]) for i in range(2)]
        s16 = [b.sb(f"s16_{i}", [128, 64]) for i in range(2)]
        rcp = [b.sb(f"rcp{i}", [128, 64]) for i in range(2)]
        qn = [b.sb(f"qn{i}", [128, 1024], BF16) for i in range(2)]
        kn = [b.sb(f"kn{i}", [128, 1024], BF16) for i in range(2)]
        vpb = [b.sb(f"vpb{i}", [128, 64, 17], BF16) for i in range(2)]
        tpm = TpMachine(b, identb, tib, evac_eng="scalar")
        xtok = [None] * NDC
        outs = []
        prev = {}
        vone = [b.op("vector", "memset", vpb[i][:, :, 16:17], 1.0) for i in range(2)]
        for bi in range(BPC):
            for i in range(NDC):
                xtok[i] = b.dma("gpsimd", xn[i][:], xn_fm[bi, i * 128:(i + 1) * 128, :],
                                deps=[xtok[i], prev.get("last"), prev.get("vev0"),
                                      prev.get("vev1")], lane=i % 3)
            pend = []
            for sc in range(NSC):
                ssl = slice(sc * 128, (sc + 1) * 128)
                kk = sc % 2
                allmms = {}
                for ti, (wt, wkey, boff) in enumerate(
                        [(wq, "q", 0), (wk, "k", D), (wv, "v", 2 * D)]):
                    ps = psq[ti]
                    mms = []
                    for nb in range(2):
                        dst = ps[:, nb * 512:(nb + 1) * 512]
                        for ic in range(NDC):
                            b.mm(dst, xn[ic][:, ssl], wt[ic][:, nb * 512:(nb + 1) * 512],
                                 ic == 0, False,
                                 deps=[xtok[ic], wtok[(wkey, ic)], t_bec,
                                       prev.get(f"ps_rd{ti}")],
                                 inc=False)
                        mms.append(b.mm(dst, ones1b[:],
                                        browsb[:, boff + nb * 512:boff + (nb + 1) * 512],
                                        False, True, deps=[t_o1, t_bb]))
                    allmms[wkey] = mms
                for fn in pend:
                    fn()
                pend = []
                for ti, (wkey, nbuf, fm_dram, tm_dram) in enumerate(
                        [("q", qn, Qt_fm, None), ("k", kn, Kt_fm, Kt_tm)]):
                    ps = psq[ti]
                    pid = ti * 2 + kk
                    mms = allmms[wkey]
                    ps3 = ps[:].rearrange("p (g k) -> p g k", k=16)
                    t_gm = b.op("vector", "reduce_max",
                                gmax[pid][:].rearrange("p (g o) -> p g o", o=1), ps3,
                                axis=AX.X, deps=mms + [prev.get(f"gmax_rd{pid}")])
                    t_sub = b.op("vector", "tensor_tensor",
                                 sub[pid][:].rearrange("p (g k) -> p g k", k=16), ps3,
                                 gmax[pid][:].rearrange("p (g o) -> p g o", o=1).broadcast_to(
                                     [128, 64, 16]), ALU.subtract,
                                 deps=[t_gm, prev.get(f"sub_rd{pid}")])
                    prev[f"ps_rd{ti}"] = t_sub
                    prev[f"gmax_rd{pid}"] = t_sub
                    if wkey == "q":
                        t_n = b.op("scalar", "activation", nbuf[kk][:], sub[pid][:],
                                   AF.Exp, scale=becol[:],
                                   deps=[t_sub, t_bec, prev.get(f"{wkey}n{kk}")])
                        prev[f"sub_rd{pid}"] = t_n
                    else:
                        t_exp = b.op("scalar", "activation", ex[kk][:], sub[pid][:], AF.Exp,
                                     scale=becol[:], deps=[t_sub, t_bec,
                                                           prev.get(f"ex_rd{kk}"),
                                                           prev.get(f"ex_rd2{kk}")])
                        prev[f"sub_rd{pid}"] = t_exp
                        t_ss = b.op("vector", "reduce_sum",
                                    s16[kk][:].rearrange("p (g o) -> p g o", o=1),
                                    ex[kk][:].rearrange("p (g k) -> p g k", k=16), axis=AX.X,
                                    deps=[t_exp, prev.get(f"s16_rd{kk}")])
                        t_rcp = b.op("vector", "reciprocal", rcp[kk][:], s16[kk][:],
                                     deps=[t_ss, prev.get(f"rcp_rd{kk}")])
                        prev[f"s16_rd{kk}"] = t_rcp
                        t_n = b.op("vector", "tensor_tensor",
                                   nbuf[kk][:].rearrange("p (g k) -> p g k", k=16),
                                   ex[kk][:].rearrange("p (g k) -> p g k", k=16),
                                   rcp[kk][:].rearrange("p (g o) -> p g o", o=1).broadcast_to(
                                       [128, 64, 16]), ALU.mult,
                                   deps=[t_rcp, t_exp, prev.get(f"{wkey}n{kk}")])
                        prev[f"rcp_rd{kk}"] = t_n
                        prev[f"ex_rd{kk}"] = t_ss
                        prev[f"ex_rd2{kk}"] = t_n

                    def flush(wkey=wkey, nbuf=nbuf, fm_dram=fm_dram, tm_dram=tm_dram,
                              kk=kk, ssl=ssl, t_n=t_n, bi=bi):
                        rd = []
                        if tm_dram is not None:
                            rd.append(b.dma("sync", tm_dram[bi, ssl, :], nbuf[kk][:],
                                            deps=[t_n], lane=2))
                            outs.append(rd[-1])
                        for dc in range(NDC):
                            t_tp, t_dma = tpm.to_dram(nbuf[kk][:, dc * 128:(dc + 1) * 128],
                                                      fm_dram[bi, dc * 128:(dc + 1) * 128, ssl],
                                                      deps=[t_n])
                            rd.append(t_tp)
                            outs.append(t_dma)
                        prev[f"{wkey}n{kk}"] = rd[-1]
                        prev["last"] = rd[-1]
                    pend.append(flush)
                t_vev = b.op("scalar", "activation", vpb[kk][:, :, 0:16],
                             psq[2][:].rearrange("p (h e) -> p h e", e=16), AF.Identity,
                             deps=allmms["v"] + [prev.get(f"vst{kk}"), vone[kk]])
                prev["ps_rd2"] = t_vev
                t_vst = b.dma("sync", Vp_tm[bi, ssl, :],
                              vpb[kk][:].rearrange("p h e -> p (h e)"), deps=[t_vev],
                              lane=kk)
                prev[f"vev{kk}"] = t_vev
                prev[f"vst{kk}"] = t_vst
                outs.append(t_vst)
            for fn in pend:
                fn()
        b.done(outs)


def emit_l1attn(b, Qt_fm, Kt_fm, Kt_tm, Vp_tm, attn_fm, ng=8):
    NG = ng
    GW = 8 * 17
    with b.stage("a1") as ctx:
        identf, tif, identb, tib = b.idents
        ind = b.sb("ind", [128, 8])
        t_i = b.op("gpsimd", "memset", ind[:], 1.0)
        t_i = b.op("gpsimd", "affine_select", out=ind[:], in_=ind[:], compare_op=ALU.is_ge,
                   fill=0.0, base=0, pattern=[[-16, 8]], channel_multiplier=1, deps=[t_i])
        t_i = b.op("gpsimd", "affine_select", out=ind[:], in_=ind[:], compare_op=ALU.is_ge,
                   fill=0.0, base=15, pattern=[[16, 8]], channel_multiplier=-1, deps=[t_i])
        indb = b.sb("indb", [128, 8], BF16)
        t_i = b.op("vector", "tensor_copy", indb[:], ind[:], deps=[t_i])
        mask8 = b.sb("mask8", [128, 1024], BF16)
        t_m8 = b.op("gpsimd", "memset", mask8[:], 1.0)
        t_m8 = b.op("gpsimd", "affine_select", out=mask8[:], in_=mask8[:],
                    compare_op=ALU.is_ge, fill=0.0, base=0,
                    pattern=[[0, 8], [1, 128]], channel_multiplier=-1, deps=[t_m8])
        bmask = b.sb("bmask", [128, GW], BF16)
        t_bm = b.op("gpsimd", "memset", bmask[:], 1.0)
        t_bm = b.op("gpsimd", "affine_select", out=bmask[:], in_=bmask[:],
                    compare_op=ALU.is_ge, fill=0.0, base=0,
                    pattern=[[-16, 8], [0, 17]], channel_multiplier=1, deps=[t_bm])
        t_bm = b.op("gpsimd", "affine_select", out=bmask[:], in_=bmask[:],
                    compare_op=ALU.is_ge, fill=0.0, base=15,
                    pattern=[[16, 8], [0, 17]], channel_multiplier=-1, deps=[t_bm])
        qgb = [[b.sb(f"qg{p}_{u}", [128, S], BF16) for u in range(2)] for p in range(2)]
        kgb = [[b.sb(f"kg{p}_{u}", [128, S], BF16) for u in range(2)] for p in range(2)]
        ktg = [[b.sb(f"ktg{u}_{i}", [128, 128], BF16) for i in range(NSC)] for u in range(2)]
        vg = [[b.sb(f"vg{u}_{i}", [128, GW], BF16) for i in range(NSC)] for u in range(2)]
        qmb = [[b.sb(f"qmb{u}_{p}", [128, 1024], BF16) for p in range(2)]
               for u in range(2)]
        smt = [b.sb(f"smt{u}", [128, 1024], BF16) for u in range(2)]
        sraw = [b.sb(f"sraw{u}", [128, 1024], BF16) for u in range(2)]
        state = [b.sb(f"state{u}", [128, GW]) for u in range(2)]
        state_bd = [b.sb(f"state_bd{u}", [128, GW], BF16) for u in range(2)]
        dens = [b.sb(f"dens{u}", [128, 8]) for u in range(2)]
        rdens = [b.sb(f"rdens{u}", [128, 8]) for u in range(2)]
        at_tm = [b.sb(f"at{i}", [128, D], BF16) for i in range(NSC)]
        psS = [b.ps(f"psS{u}", [128, 1024], F32) for u in range(2)]
        psO = [b.ps(f"psO{u}", [128, 512], F32) for u in range(2)]
        psT = [b.ps(f"psT{u}", [128, 512], F32) for u in range(2)]
        fmbt = [b.sb(f"fmbt{i}", [128, 8, 128], BF16) for i in range(2)]
        outs = []
        ld_prev = {}
        prev = {}
        qtoks = {}
        for bi in range(BPC):
            for gp in range(NG // 2):
                par = gp % 2
                toks = [{}, {}]
                for u in range(2):
                    g = gp * 2 + u
                    if (bi, gp) == (0, 0):
                        qtoks[(par, u)] = b.dma("sync", qgb[par][u][:],
                                                Qt_fm[bi, g * 128:(g + 1) * 128, :],
                                                lane=u)
                        qtoks[("k", par, u)] = b.dma("gpsimd", kgb[par][u][:],
                                                     Kt_fm[bi, g * 128:(g + 1) * 128, :],
                                                     lane=u)
                    t_kt, t_v = [], []
                    for c in range(NSC):
                        t_kt.append(b.dma("sync", ktg[u][c][:],
                                          Kt_tm[bi, c * 128:(c + 1) * 128,
                                                g * 128:(g + 1) * 128],
                                          deps=[ld_prev.get(("kt", u, c))],
                                          lane=2 - c % 2))
                        t_v.append(b.dma("gpsimd", vg[u][c][:],
                                         Vp_tm[bi, c * 128:(c + 1) * 128,
                                               g * GW:(g + 1) * GW],
                                         deps=[ld_prev.get(("v", u, c))],
                                         lane=2 - c % 2))
                    t_st = b.op("vector", "memset", state[u][:], 0.0,
                                deps=[prev.get(("st_rd", u))])
                    toks[u] = {"t_q": qtoks[(par, u)], "t_k": qtoks[("k", par, u)],
                               "t_kt": t_kt, "t_v": t_v, "t_st": t_st}

                def qm_jit(c, u):
                    tk = toks[u]
                    t = b.op("vector", "tensor_tensor",
                             qmb[u][c % 2][:].rearrange("p (g f) -> p g f", f=128),
                             qgb[par][u][:, c * 128:(c + 1) * 128].rearrange(
                                 "p (o f) -> p o f", o=1).broadcast_to([128, 8, 128]),
                             indb[:].rearrange("p (g o) -> p g o", o=1).broadcast_to(
                                 [128, 8, 128]), ALU.mult,
                             deps=[tk["t_q"], t_i, prev.get(("qmrd", u, c % 2))])
                    prev[("qm", u, c % 2)] = t

                for u in range(2):
                    qm_jit(0, u)
                    qm_jit(1, u)
                for c in range(NSC):
                    t_sms = {}
                    t_sus = {}
                    for u in range(2):
                        tk = toks[u]
                        t_sm = []
                        for h2 in range(8):
                            t_sm.append(b.mm(psS[u][:, h2 * 128:(h2 + 1) * 128],
                                             kgb[par][u][:, c * 128:(c + 1) * 128],
                                             qmb[u][c % 2][:, h2 * 128:(h2 + 1) * 128],
                                             True, True,
                                             deps=[tk["t_k"], prev.get(("qm", u, c % 2)),
                                                   prev.get(("sev", u))]))
                        t_sms[u] = t_sm
                        prev[("qmrd", u, c % 2)] = t_sm[-1]
                        t_sus[u] = b.mm(psT[u][0:128, 0:GW], ktg[u][c][:], vg[u][c][:],
                                        True, True,
                                        deps=[tk["t_kt"][c], tk["t_v"][c],
                                              prev.get(("st_add", u))])
                    if c == 4 and not (bi == BPC - 1 and gp == NG // 2 - 1):
                        nbi, ngp = (bi, gp + 1) if gp + 1 < NG // 2 else (bi + 1, 0)
                        npar = 1 - par
                        for u in range(2):
                            g2 = ngp * 2 + u
                            qtoks[(npar, u)] = b.dma(
                                "sync", qgb[npar][u][:],
                                Qt_fm[nbi, g2 * 128:(g2 + 1) * 128, :],
                                deps=[qtoks.get((npar, u)),
                                      ld_prev.get(("qg", npar, u))], lane=u)
                            qtoks[("k", npar, u)] = b.dma(
                                "gpsimd", kgb[npar][u][:],
                                Kt_fm[nbi, g2 * 128:(g2 + 1) * 128, :],
                                deps=[qtoks.get(("k", npar, u)),
                                      ld_prev.get(("kg", npar, u))], lane=u)
                    for u in range(2):
                        g = gp * 2 + u
                        tk = toks[u]
                        t_sm = t_sms[u]
                        t_cp = b.op("scalar", "activation", sraw[u][:], psS[u][:],
                                    AF.Identity, deps=t_sm + [prev.get(("sraw", u))])
                        prev[("sev", u)] = t_cp
                        t_sev = b.op("vector", "tensor_tensor", smt[u][:], sraw[u][:],
                                     mask8[:], ALU.mult,
                                     deps=[t_cp, t_m8, prev.get(("intra", u))])
                        prev[("sraw", u)] = t_sev
                        first = None
                        if c > 0:
                            t_bd = b.op("vector", "tensor_tensor", state_bd[u][:],
                                        state[u][:], bmask[:], ALU.mult,
                                        deps=[t_bm, prev[("st_add", u)],
                                              prev.get(("inter", u))])
                            first = b.mm(psO[u][0:128, 0:GW],
                                         qgb[par][u][:, c * 128:(c + 1) * 128],
                                         state_bd[u][:], True, False,
                                         deps=[tk["t_q"], t_bd, prev.get(("oev", u))],
                                         inc=True)
                            prev[("inter", u)] = first
                        t_im = None
                        for h2 in range(8):
                            t_im = b.mm(psO[u][0:128, h2 * 17:(h2 + 1) * 17],
                                        smt[u][:, h2 * 128:(h2 + 1) * 128],
                                        vg[u][c][:, h2 * 17:(h2 + 1) * 17],
                                        first is None and h2 == 0, h2 == 7,
                                        deps=[t_sev, tk["t_v"][c]] +
                                             ([prev.get(("oev", u))] if first is None else []),
                                        inc=h2 == 7)
                        prev[("intra", u)] = t_im
                        t_sa = b.op("vector", "tensor_add", state[u][:], state[u][:],
                                    psT[u][0:128, 0:GW], deps=[t_sus[u], tk["t_st"]])
                        prev[("st_add", u)] = t_sa
                        t_cpd = b.op("vector", "tensor_copy", dens[u][:],
                                     psO[u][0:128, 0:GW].rearrange(
                                         "p (h e) -> p h e", e=17)[:, :, 16],
                                     deps=[t_im, first, prev.get(("rd_rd", u))])
                        t_r = b.op("vector", "reciprocal", rdens[u][:], dens[u][:],
                                   deps=[t_cpd])
                        t_o = b.op("vector", "tensor_tensor",
                                   at_tm[c][:, g * 128:(g + 1) * 128].rearrange(
                                       "p (h e) -> p h e", e=16),
                                   psO[u][0:128, 0:GW].rearrange(
                                       "p (h e) -> p h e", e=17)[:, :, 0:16],
                                   rdens[u][:].rearrange("p (h o) -> p h o", o=1).broadcast_to(
                                       [128, 8, 16]), ALU.mult,
                                   deps=[t_r, t_im, prev.get(("at", c))])
                        prev[("oev", u)] = t_o
                        prev[("rd_rd", u)] = t_o
                        prev[("at", c)] = t_o
                        if c + 2 < NSC:
                            qm_jit(c + 2, u)
                for u in range(2):
                    prev[("st_rd", u)] = prev[("oev", u)]
                    ld_prev[("qg", par, u)] = prev[("oev", u)]
                    ld_prev[("kg", par, u)] = prev[("oev", u)]
                    for c in range(NSC):
                        ld_prev[("kt", u, c)] = prev[("st_add", u)]
                        ld_prev[("v", u, c)] = prev[("oev", u)]
            fmba = [psS[0], psS[1]]
            for c in range(NSC):
                kE = c % 2
                pt = fmba[kE][:].bitcast(BF16)
                evs = []
                last = None
                for dc in range(NDC if NG == 8 else NG):
                    t_tp = tp_b16(b, pt[:, dc * 128:(dc + 1) * 128],
                                  at_tm[c][:, dc * 128:(dc + 1) * 128], identb,
                                  deps=[prev[("at", c)], tib, prev.get(("fmev", kE)),
                                        prev.get(("sev", kE)) if c < 2 else None])
                    t_ev = b.op("vector", "tensor_copy", fmbt[kE][:, dc, :],
                                pt[:, dc * 128:(dc + 1) * 128],
                                deps=[t_tp, prev.get(("fmdma", kE))])
                    evs.append(t_ev)
                    last = t_ev
                prev[("fmev", kE)] = last
                prev[("fmdma", kE)] = b.dma("sync",
                                            attn_fm[bi, :, c * 128:(c + 1) * 128].rearrange(
                                                "(a p) f -> p a f", p=128),
                                            fmbt[kE][:], deps=evs, lane=c % 3)
                prev[("at", c)] = last
                prev[("sev", kE)] = last
                outs.append(prev[("fmdma", kE)])
        b.done(outs)


def emit_l1rms(b, xn_tm, gamma_x, xnn_tm, xnn_fm, barrier=True):
    with b.stage("rm", barrier=barrier) as ctx:
        identf, tif, identb, tib = b.idents
        tpm = TpMachine(b, identb, tib, n=2)
        fmbr = [b.sb(f"fmbr{i}", [128, 8, 128], BF16) for i in range(2)]
        fm_dma = {}
        ones1f = b.sb("ones1f", [1, 128])
        t_o1 = b.op("vector", "memset", ones1f[:], 1.0)
        g_row = b.sb("g_row", [1, D])
        t_gr = b.dma("sync", g_row[:], gamma_x[:])
        gb = b.sb("gb", [128, D], BF16)
        psg = b.ps("psg", [128, 1024], F32)
        mgb = [b.mm(psg[:, nb * 512:(nb + 1) * 512], ones1f[:],
                    g_row[:, nb * 512:(nb + 1) * 512], True, True,
                    deps=[t_o1, t_gr]) for nb in range(2)]
        t_gb = b.op("vector", "tensor_copy", gb[:], psg[:], deps=mgb)
        xt = [b.sb(f"xt{i}", [128, D], BF16) for i in range(2)]
        xo = [b.sb(f"xo{i}", [128, D], BF16) for i in range(2)]
        sed = b.sb("sed", [128, D])
        snq = b.sb("snq", [128, 1])
        srl = b.sb("srl", [128, 1])
        prevS = [{}, {}]
        outs = []
        for bi in range(BPC):
            for sc in range(NSC):
                k = sc % 2
                ld = b.dma("sync", xt[k][:], xn_tm[bi, sc * 128:(sc + 1) * 128, :],
                           deps=[prevS[k].get("rd")])
                t_sq = b.op("scalar", "activation", sed[:], xt[k][:], AF.Square,
                            accum_out=snq[:], deps=[ld, prevS[1 - k].get("t_n2")])
                t_n2 = b.op("vector", "tensor_scalar_max", snq[:], snq[:], 1e-24,
                            deps=[t_sq])
                t_ln = b.op("scalar", "activation", srl[:], snq[:], AF.Ln,
                            deps=[t_n2, prevS[1 - k].get("t_sn")])
                t_ex = b.op("scalar", "activation", srl[:], srl[:], AF.Exp, scale=-0.5,
                            deps=[t_ln])
                t_rl = b.op("vector", "tensor_scalar_mul", srl[:], srl[:], 32.0,
                            deps=[t_ex])
                t_sn = b.op("vector", "scalar_tensor_tensor", xo[k][:], xt[k][:], srl[:],
                            gb[:], ALU.mult, ALU.mult,
                            deps=[t_rl, t_gb, ld] + prevS[k].get("senr", []))
                st_dma = b.dma("sync", xnn_tm[bi, sc * 128:(sc + 1) * 128, :], xo[k][:],
                               deps=[t_sn], lane=k)
                outs.append(st_dma)
                senr = [st_dma]
                evs = []
                for dc in range(NDC):
                    t_tp, t_ev = tpm.to_sbuf(xo[k][:, dc * 128:(dc + 1) * 128],
                                             fmbr[k][:, dc, :], deps=[t_sn],
                                             extra_ev_deps=[fm_dma.get(k)])
                    senr.append(t_tp)
                    evs.append(t_ev)
                fm_dma[k] = b.dma("gpsimd",
                                  xnn_fm[bi, :, sc * 128:(sc + 1) * 128].rearrange(
                                      "(a p) f -> p a f", p=128),
                                  fmbr[k][:], deps=evs, lane=k)
                outs.append(fm_dma[k])
                prevS[k] = {"rd": t_sn, "t_n2": t_n2, "senr": senr, "t_sn": t_sn}
        b.done(outs)


def emit_l1ffnA(b, xnn_fm, seT, ncm, rmrow, logitsT):
    NEC = ET // 128
    with b.stage("fa") as ctx:
        identf, tif, identb, tib = b.idents
        se = [b.sb(f"se{i}", [128, ET], BF16) for i in range(NDC)]
        setok = [b.dma("gpsimd" if i % 2 else "sync", se[i][:],
                       seT[i * 128:(i + 1) * 128, :]) for i in range(NDC)]
        xn = [b.sb(f"xn{i}", [128, S], BF16) for i in range(NDC)]
        psL = [b.ps(f"psL{i}", [128, 2048], F32) for i in range(2)]
        t64 = b.sb("t64", [64, S])
        t32tmp = b.sb("t32tmp", [32, S])
        rm32 = b.sb("rm32", [32, S])
        ncm_s = b.sb("ncm_s", [128, 32])
        nctmp = b.sb("nctmp", [128, 2])
        rmc = b.sb("rmc", [128, 16])
        rmr = b.sb("rmr", [1, S])
        lbuf = [b.sb(f"lbuf{i}", [128, S]) for i in range(2)]
        xtok = [None] * NDC
        outs = []
        prev = {}
        for bi in range(BPC):
            for i in range(NDC):
                xtok[i] = b.dma("gpsimd", xn[i][:],
                                xnn_fm[bi, i * 128:(i + 1) * 128, :],
                                deps=[xtok[i], prev.get("lastmm")])
            for etc in range(NEC):
                kk = etc % 2
                mms = []
                for nb in range(4):
                    dst = psL[kk][:, nb * 512:(nb + 1) * 512]
                    for ic in range(NDC):
                        tmm = b.mm(dst, se[ic][:, etc * 128:(etc + 1) * 128],
                                   xn[ic][:, nb * 512:(nb + 1) * 512],
                                   ic == 0, ic == NDC - 1,
                                   deps=[setok[ic], xtok[ic], prev.get(f"rd{kk}"),
                                         prev.get(("lb", kk))],
                                   inc=ic == NDC - 1)
                    mms.append(tmm)
                t_nc1 = b.op("vector", "reduce_max", nctmp[:, 0:1],
                             psL[kk][:, 0:1024], axis=AX.X,
                             deps=mms[0:2] + [prev.get("nctmp_rd")])
                t_h0a = b.op("vector", "tensor_copy", t64[:, 0:1024],
                             psL[kk][64:128, 0:1024],
                             deps=mms[0:2] + [prev.get("t32")])
                t_h1a = b.op("vector", "tensor_max", t64[:, 0:1024], t64[:, 0:1024],
                             psL[kk][0:64, 0:1024], deps=[t_h0a])
                t_nc2 = b.op("vector", "reduce_max", nctmp[:, 1:2],
                             psL[kk][:, 1024:2048], axis=AX.X, deps=mms[2:4])
                t_h0b = b.op("vector", "tensor_copy", t64[:, 1024:2048],
                             psL[kk][64:128, 1024:2048],
                             deps=mms[2:4] + [prev.get("t32")])
                t_h1b = b.op("vector", "tensor_max", t64[:, 1024:2048],
                             t64[:, 1024:2048], psL[kk][0:64, 1024:2048],
                             deps=[t_h0b])
                t_nc = b.op("vector", "reduce_max", ncm_s[:, etc:etc + 1],
                            nctmp[:, 0:2], axis=AX.X, negate=True,
                            deps=[t_nc1, t_nc2, prev.get("ncm_rd")])
                prev["nctmp_rd"] = t_nc
                t_h1 = t_h1b
                b.op("scalar", "activation", lbuf[kk][:, 0:1024],
                     psL[kk][:, 0:1024], AF.Identity,
                     deps=[t_h1a, t_nc1, prev.get(("lb_dma", kk))])
                t_lb = b.op("scalar", "activation", lbuf[kk][:, 1024:2048],
                            psL[kk][:, 1024:2048], AF.Identity,
                            deps=[t_h1b, t_nc2])
                prev[("lb_dma", kk)] = b.dma("gpsimd", logitsT[bi, etc],
                                             lbuf[kk][:], deps=[t_lb], lane=kk)
                outs.append(prev[("lb_dma", kk)])
                prev[("lb", kk)] = t_lb
                t_cp = b.op("vector", "tensor_copy", t32tmp[:], t64[32:64, :],
                            deps=[t_h1, prev.get("t32tmp_rd")])
                t_h2 = b.op("vector", "tensor_max", t64[0:32, :], t64[0:32, :],
                            t32tmp[:], deps=[t_cp])
                prev["t32tmp_rd"] = t_h2
                if etc == 0:
                    t_32 = b.op("vector", "tensor_copy", rm32[:], t64[0:32, :],
                                deps=[t_h2, prev.get("rm_rd")])
                else:
                    t_32 = b.op("vector", "tensor_max", rm32[:], rm32[:], t64[0:32, :],
                                deps=[t_h2])
                prev["t32"] = t_32
                prev[f"rd{kk}"] = t_h1
                prev["ncm_rd"] = t_nc
                prev["lastmm"] = t_nc
            t_prev_r = None
            t_rms = []
            for j in range(NSC):
                tmm = b.mm(psL[0][0:128, 0:32], rm32[:, j * 128:(j + 1) * 128],
                           identf[0:32, 0:32], True, True,
                           deps=[t_32, tif, t_prev_r] +
                                ([prev.get("rd0"), prev.get(("lb", 0))]
                                 if j == 0 else []))
                t_r = b.op("vector", "reduce_max", rmc[:, j:j + 1], psL[0][0:128, 0:32],
                           axis=AX.X, deps=[tmm, prev.get("rmc_rd")])
                t_prev_r = t_r
                t_rms.append(t_r)
            rr_mms = []
            for j in range(NSC):
                tmm = b.mm(psL[1][0:1, j * 128:(j + 1) * 128], rmc[:, j:j + 1],
                           identf[:], True, True,
                           deps=[t_rms[j], tif] +
                                ([prev.get("rd1"), prev.get("rmr_rd"),
                                  prev.get(("lb", 1))] if j == 0 else []))
                rr_mms.append(tmm)
            prev["rmc_rd"] = rr_mms[-1]
            t_rr = b.op("scalar", "activation", rmr[:], psL[1][0:1, :], AF.Identity,
                        deps=rr_mms)
            prev["rd1"] = t_rr
            prev["rd0"] = t_rms[-1]
            d1 = b.dma("sync", rmrow[bi], rmr[:], deps=[t_rr])
            d2 = b.dma("sync", ncm[bi], ncm_s[:], deps=[prev["ncm_rd"]])
            prev["rmr_rd"] = d1
            prev["ncm_rd"] = d2
            prev["rm_rd"] = t_rms[-1]
            outs += [d1, d2]
        b.done(outs)


def emit_l1ffnB(b, xnn_fm, xnn_tm, seT, ncm, rmrow, slots_fm, E2T, logitsT):
    NEC = ET // 128
    with b.stage("fb") as ctx:
        identf, tif, identb, tib = b.idents
        tpm = TpMachine(b, identb, tib, n=2)
        ones1f = b.sb("ones1f", [1, 128])
        t_o1 = b.op("vector", "memset", ones1f[:], 1.0)
        lL = [b.sb(f"lL{i}", [128, S]) for i in range(2)]
        ltok = [None, None]
        xnt = [b.sb(f"xnt{i}", [128, D], BF16) for i in range(NSC)]
        ncm_s = b.sb("ncm_s", [128, 32])
        rmr = b.sb("rmr", [1, S])
        rmb = b.sb("rmb", [128, S])
        ebuf = [[b.sb(f"ebuf{p}_{i}", [128, 1024], BF16) for i in range(2)]
                for p in range(2)]
        zb = [b.sb(f"zb{p}", [128, 1024]) for p in range(2)]
        e2b = [[b.sb(f"e2b{p}_{i}", [128, 1024], BF16) for i in range(2)]
               for p in range(2)]
        eblk = [b.sb(f"eblk{i}", [128, 16, 256], BF16) for i in range(2)]
        sev = b.sb("sev", [128, 1024], BF16)
        cs = [b.sb(f"cs{p}", [128, 2]) for p in range(2)]
        rcs = [b.sb(f"rcs{p}", [128, 1]) for p in range(2)]
        rcsc = [b.sb(f"rcsc{p}", [128, 1], BF16) for p in range(2)]
        rrow = b.sb("rrow", [1, 256], BF16)
        rcsb = [b.sb(f"rcsb{p}", [128, 256], BF16) for p in range(2)]
        ones1b = b.sb("ones1b", [1, 128], BF16)
        t_o1b = b.op("vector", "memset", ones1b[:], 1.0)
        psL = [b.ps(f"psL{i}", [128, 1024], F32) for i in range(2)]
        psSl = b.ps("psSl", [128, 1024], F32)
        xttok = [None] * NSC
        outs = []
        prev = {}
        smms = []

        def l_load(bi2, etc2):
            lk2 = etc2 % 2
            ltok[lk2] = b.dma("gpsimd", lL[lk2][:], logitsT[bi2, etc2],
                              deps=[ltok[lk2], prev.get(("lL_rd", lk2))], lane=lk2)

        for bi in range(BPC):
            for i in range(NSC):
                xttok[i] = b.dma("sync", xnt[i][:],
                                 xnn_tm[bi, i * 128:(i + 1) * 128, :],
                                 deps=[xttok[i], prev.get("lastA")], lane=i % 3)
            l_load(bi, 0)
            l_load(bi, 1)
            t_ncm = b.dma("sync", ncm_s[:], ncm[bi],
                          deps=prev.get("ncm_rd") or [None])
            t_rmr = b.dma("sync", rmr[:], rmrow[bi], deps=[prev.get("rmr_rd")])
            t_rmb = []
            for half in range(2):
                mmsb = [b.mm(psL[half][:, nb * 512:(nb + 1) * 512], ones1f[:],
                             rmr[:, half * 1024 + nb * 512:half * 1024 + (nb + 1) * 512],
                             True, True,
                             deps=[t_rmr, t_o1, prev.get(f"psL_rd{half}"),
                                   prev.get("rmb_ex")])
                        for nb in range(2)]
                t_rmb.append(b.op("scalar", "activation",
                                  rmb[:, half * 1024:(half + 1) * 1024],
                                  psL[half][:], AF.Identity,
                                  deps=mmsb + (prev.get("rmb_rd") or [None])))
                prev[f"psL_rd{half}"] = t_rmb[half]
            for blk in range(16):
                tp_toks = {}
                sub_eh = {}
                for sub in range(2):
                    etc = blk * 2 + sub
                    lk = etc % 2
                    eh_toks = []
                    for half in range(2):
                        lsl = lL[lk][:, half * 1024:(half + 1) * 1024]
                        t_e = b.op("scalar", "activation", ebuf[lk][half][:], lsl,
                                   AF.Exp, bias=ncm_s[:, etc:etc + 1],
                                   accum_out=cs[lk][:, half:half + 1],
                                   deps=[ltok[lk], t_ncm,
                                         prev.get(("cs_rd", lk))] +
                                        (prev.get(("ebuf_rd", lk, half)) or []))
                        eh_toks.append(t_e)
                    sub_eh[sub] = eh_toks
                    t_cs = b.op("vector", "tensor_add", cs[lk][:, 0:1], cs[lk][:, 0:1],
                                cs[lk][:, 1:2], deps=eh_toks + [prev.get(("rcs_rd", lk))])
                    t_rcs = b.op("vector", "reciprocal", rcs[lk][:], cs[lk][:, 0:1],
                                 deps=[t_cs])
                    prev[("cs_rd", lk)] = t_rcs
                    t_rcc = b.op("vector", "tensor_copy", rcsc[lk][:], rcs[lk][:],
                                 deps=[t_rcs, prev.get(("rcc_rd", lk))])
                    prev[("rcs_rd", lk)] = t_rcc
                    prev[("rcc", sub)] = t_rcc
                    for half in range(2):
                        lsl = lL[lk][:, half * 1024:(half + 1) * 1024]
                        t_z = b.op("vector", "scalar_tensor_tensor", zb[lk][:], lsl,
                                   1.0, rmb[:, half * 1024:(half + 1) * 1024],
                                   ALU.mult, ALU.subtract,
                                   deps=[ltok[lk]] + t_rmb +
                                        [prev.get(("zb_rd", lk)), eh_toks[half]])
                        prev[("lL_rd", lk)] = t_z
                        t_e2 = b.op("scalar", "activation", e2b[lk][half][:], zb[lk][:],
                                    AF.Exp, deps=[t_z, prev.get(("e2dma", lk, half))])
                        prev[("zb_rd", lk)] = t_e2
                        d_e2 = b.dma("gpsimd", E2T[bi, etc * 128:(etc + 1) * 128,
                                                   half * 1024:(half + 1) * 1024],
                                     e2b[lk][half][:], deps=[t_e2], lane=half + 1)
                        prev[("e2dma", lk, half)] = d_e2
                        outs.append(d_e2)
                    tps = []
                    hp = {0: [], 1: []}
                    for j in range(NSC):
                        half = j // 8
                        src = ebuf[lk][half][:, (j % 8) * 128:(j % 8 + 1) * 128]
                        if j % 4 == 0:
                            t = b.dma("sync",
                                      eblk[blk % 2][:, j, sub * 128:(sub + 1) * 128],
                                      src, transpose=True,
                                      deps=[eh_toks[half],
                                            prev.get(("eblk_rd", blk % 2))],
                                      lane=j % 3)
                        elif j % 4 == 2:
                            t = b.dma("scalar",
                                      eblk[blk % 2][:, j, sub * 128:(sub + 1) * 128],
                                      src, transpose=True,
                                      deps=[eh_toks[half],
                                            prev.get(("eblk_rd", blk % 2))],
                                      lane=j % 3)
                        else:
                            t_tp, t = tpm.to_sbuf(
                                src, eblk[blk % 2][:, j, sub * 128:(sub + 1) * 128],
                                deps=[eh_toks[half]],
                                extra_ev_deps=[prev.get(("eblk_rd", blk % 2))])
                        tps.append(t)
                        hp[half].append(t)
                    prev[("ebuf_rd", lk, 0)] = list(hp[0])
                    prev[("ebuf_rd", lk, 1)] = list(hp[1])
                    tp_toks[sub] = tps
                    if etc + 2 < 2 * 16:
                        l_load(bi, etc + 2)
                for sub in range(2):
                    t_rcm = b.mm(psL[0][0:1, sub * 128:(sub + 1) * 128],
                                 rcsc[(blk * 2 + sub) % 2][:], identb[:], True, True,
                                 deps=[prev[("rcc", sub)], tib, prev.get("psL_rd0")])
                    prev[("rcc_rd", (blk * 2 + sub) % 2)] = t_rcm
                    t_rre = b.op("scalar", "activation",
                                 rrow[:, sub * 128:(sub + 1) * 128],
                                 psL[0][0:1, sub * 128:(sub + 1) * 128], AF.Identity,
                                 deps=[t_rcm, prev.get("rrow_rd")])
                    prev["psL_rd0"] = t_rre
                    prev[("rre", sub)] = t_rre
                t_bcm = [b.mm(psL[1][0:128, e * 128:(e + 1) * 128], ones1b[:],
                              rrow[:, e * 128:(e + 1) * 128], True, True,
                              deps=[t_o1b, prev[("rre", e)], prev.get("psL_rd1")])
                         for e in range(2)]
                t_rcb = b.op("scalar", "activation", rcsb[blk % 2][:],
                             psL[1][0:128, 0:256], AF.Identity,
                             deps=t_bcm + [prev.get(("rcsb_rd", blk % 2))])
                prev["psL_rd1"] = t_rcb
                prev["rrow_rd"] = t_bcm[-1]
                ev_sl = prev.get("sl_ev", [])
                smms = []
                for dhalf in range(2):
                    grp = []
                    for dcq in range(4):
                        dc = dhalf * 4 + dcq
                        for sc in range(NSC):
                            tmm = b.mm(psSl[:, dcq * 256:(dcq + 1) * 256],
                                       xnt[sc][:, dc * 128:(dc + 1) * 128],
                                       eblk[blk % 2][:, sc, :],
                                       sc == 0, sc == NSC - 1,
                                       deps=[xttok[sc]] + tp_toks[0] + tp_toks[1] +
                                            (ev_sl if sc == 0 else []),
                                       inc=sc == NSC - 1)
                            if sc == NSC - 1:
                                grp.append(tmm)
                    t_se = b.op("vector", "tensor_tensor",
                                sev[:].rearrange("p (g e) -> p g e", e=256),
                                psSl[:].rearrange("p (g e) -> p g e", e=256),
                                rcsb[blk % 2][:].rearrange(
                                    "p (o e) -> p o e", o=1).broadcast_to([128, 4, 256]),
                                ALU.mult,
                                deps=grp + [t_rcb] + (prev.get("sev_dma") or []))
                    ev_sl = [t_se]
                    prev["sl_ev"] = ev_sl
                    dmas = []
                    for dcq in range(4):
                        dc = dhalf * 4 + dcq
                        dmas.append(b.dma("gpsimd",
                                          slots_fm[bi, dc * 128:(dc + 1) * 128,
                                                   blk * 256:(blk + 1) * 256],
                                          sev[:, dcq * 256:(dcq + 1) * 256],
                                          deps=[t_se], lane=1 + dcq % 2))
                    prev["sev_dma"] = dmas[-2:]
                    outs += dmas
                    smms += grp
                prev[("eblk_rd", blk % 2)] = smms[-1]
                prev[("rcsb_rd", blk % 2)] = prev["sl_ev"][0]
            prev["lastA"] = prev["sl_ev"][0]
            prev["rmb_ex"] = prev.get("psL_rd1")
            prev["ncm_rd"] = [prev[("cs_rd", 0)], prev[("cs_rd", 1)]]
            prev["rmr_rd"] = t_rmb[-1]
            prev["rmb_rd"] = [prev[("zb_rd", 0)], prev[("zb_rd", 1)]]
        b.done(outs)


def emit_l1ffnC(b, slots_fm, w1, w2, processed):
    NHC = HID // 128
    BW = 512
    with b.stage("fc") as ctx:
        w1t = [b.sb(f"w1t{i}", [128, HID], BF16) for i in range(NDC)]
        w2t = [b.sb(f"w2t{i}", [128, D], BF16) for i in range(NHC)]
        w1tok = [b.dma("gpsimd", w1t[i][:], w1[i * 128:(i + 1) * 128, :], lane=i % 3)
                 for i in range(NDC)]
        w2tok = [b.dma("gpsimd", w2t[i][:], w2[i * 128:(i + 1) * 128, :], lane=i % 3)
                 for i in range(NHC)]
        sl = [b.sb(f"sl{i}", [128, 8, BW], BF16) for i in range(2)]
        hfm = b.sb("hfm", [128, NHC, BW], BF16)
        pev = [b.sb(f"pev{i}", [128, 1024], BF16) for i in range(2)]
        psH = [b.ps(f"psH{i}", [128, 512], F32) for i in range(2)]
        psP = [b.ps(f"psP{i}", [128, 1024], F32) for i in range(2)]
        outs = []
        prev = {}
        sltok = [None, None]
        pmm_last = None
        NBLK = ET // BW

        def sl_load(gblk):
            bi2, blk2 = divmod(gblk, NBLK)
            kb2 = gblk % 2
            sltok[kb2] = b.dma("sync", sl[kb2][:],
                               slots_fm[bi2, :, blk2 * BW:(blk2 + 1) * BW].rearrange(
                                   "(a p) f -> p a f", p=128),
                               deps=[sltok[kb2], prev.get(f"sl_rd{kb2}")],
                               lane=gblk % 3)

        sl_load(0)
        sl_load(1)
        for bi in range(BPC):
            for blk in range(NBLK):
                kb = blk % 2
                gblk = bi * NBLK + blk
                h_toks = []
                for hc in range(NHC):
                    kk = hc % 2
                    for ic in range(NDC):
                        tmm = b.mm(psH[kk][:, 0:BW], w1t[ic][:, hc * 128:(hc + 1) * 128],
                                   sl[kb][:, ic, :], ic == 0, ic == NDC - 1,
                                   deps=[w1tok[ic], sltok[kb], prev.get(f"hev{kk}")],
                                   inc=ic == NDC - 1)
                    t_h = b.op("scalar", "activation", hfm[:, hc, :], psH[kk][:, 0:BW],
                               AF.Gelu, deps=[tmm, prev.get(("hfm_rd", hc))])
                    prev[f"hev{kk}"] = t_h
                    h_toks.append(t_h)
                for sub in range(BW // 128):
                    kk = sub % 2
                    etc = blk * (BW // 128) + sub
                    for hc in range(NHC):
                        for nb in range(2):
                            tmm = b.mm(psP[kk][:, nb * 512:(nb + 1) * 512],
                                       hfm[:, hc, sub * 128:(sub + 1) * 128],
                                       w2t[hc][:, nb * 512:(nb + 1) * 512],
                                       hc == 0, hc == NHC - 1,
                                       deps=[w2tok[hc], h_toks[hc]] +
                                            ([prev.get(f"pev{kk}")] if hc == 0 else []),
                                       inc=hc == NHC - 1 and nb == 1)
                    pmm_last = tmm
                    t_p = b.op("scalar", "activation", pev[kk][:], psP[kk][:],
                               AF.Identity, deps=[tmm, prev.get(f"pdma{kk}")])
                    prev[f"pev{kk}"] = t_p
                    d_p = b.dma("sync", processed[bi, etc * 128:(etc + 1) * 128, :],
                                pev[kk][:], deps=[t_p], lane=kk)
                    prev[f"pdma{kk}"] = d_p
                    outs.append(d_p)
                for hc in range(NHC):
                    prev[("hfm_rd", hc)] = pmm_last
                prev[f"sl_rd{kb}"] = pmm_last
                if gblk + 2 < BPC * NBLK:
                    sl_load(gblk + 2)
        b.done(outs)


def emit_l1ffnD(b, processed, E2T, s1_tm, x1, out):
    NEC = ET // 128
    with b.stage("fd") as ctx:
        onescol = b.sb("onescol", [128, 1], BF16)
        t_oc = b.op("vector", "memset", onescol[:], 1.0)
        proc = [b.sb(f"proc{i}", [128, D], BF16) for i in range(NEC)]
        ptok = [None] * NEC
        e2s = [[b.sb(f"e2s{p}_{i}", [128, 4, 128], BF16) for i in range(8)]
               for p in range(2)]
        s1b = [b.sb(f"s1b{i}", [128, D]) for i in range(2)]
        x1b = [b.sb(f"x1b{i}", [128, D]) for i in range(2)]
        tb = [b.sb(f"tb{i}", [128, D]) for i in range(2)]
        gb_ = [b.sb(f"gb{i}", [128, D]) for i in range(2)]
        rden = [b.sb(f"rden{i}", [128, 1]) for i in range(2)]
        psU = [b.ps(f"psU{i}", [128, 1024], F32) for i in range(2)]
        psD = [b.ps(f"psD{i}", [128, 512], F32) for i in range(2)]
        outs = []
        prev = {}
        e2tok = {}
        for bi in range(BPC):
            for i in range(NEC):
                ptok[i] = b.dma("gpsimd" if i % 2 else "sync", proc[i][:],
                                processed[bi, i * 128:(i + 1) * 128, :],
                                deps=[ptok[i], prev.get("mm_last0"),
                                      prev.get("mm_last1")], lane=i % 3)
            def load_strip(sc2):
                kk2 = sc2 % 2
                sl2 = slice(sc2 * 128, (sc2 + 1) * 128)
                for jj in range(8):
                    e2tok[(kk2, jj)] = b.dma(
                        "gpsimd" if jj % 2 else "sync", e2s[kk2][jj][:],
                        E2T[bi, jj * 512:(jj + 1) * 512, sl2].rearrange(
                            "(a p) f -> p a f", p=128),
                        deps=[e2tok.get((kk2, jj)), prev.get(f"mm_last{kk2}")],
                        lane=(kk2 + jj) % 3)

            load_strip(0)
            for sc in range(NSC):
                kk = sc % 2
                ssl = slice(sc * 128, (sc + 1) * 128)
                if sc + 1 < NSC:
                    load_strip(sc + 1)
                l_s1 = b.dma("sync", s1b[kk][:], s1_tm[bi, ssl, :],
                             deps=[prev.get(f"smul{kk}")], lane=2)
                l_x1 = b.dma("gpsimd", x1b[kk][:], x1[bi, ssl, :],
                             deps=[prev.get(f"xadd{kk}")], lane=2)
                umms = []
                dmms = []
                for etc in range(NEC):
                    jj, aa = divmod(etc, 4)
                    lhs = e2s[kk][jj][:, aa, :]
                    for nb in range(2):
                        tmm = b.mm(psU[kk][:, nb * 512:(nb + 1) * 512], lhs,
                                   proc[etc][:, nb * 512:(nb + 1) * 512],
                                   etc == 0, etc == NEC - 1,
                                   deps=[e2tok[(kk, jj)], ptok[etc],
                                         prev.get(f"uev{kk}")],
                                   inc=etc == NEC - 1 and nb == 1)
                    dmms.append(b.mm(psD[kk][:, 0:1], lhs, onescol[:],
                                     etc == 0, etc == NEC - 1,
                                     deps=([t_oc, prev.get(f"rden_rd{kk}")]
                                           if etc == 0 else []),
                                     inc=etc == NEC - 1))
                    umms.append(tmm)
                prev[f"mm_last{kk}"] = dmms[-1]
                t_r = b.op("vector", "reciprocal", rden[kk][:], psD[kk][:, 0:1],
                           deps=[dmms[-1], prev.get(f"rden_u{kk}")])
                prev[f"rden_rd{kk}"] = t_r
                t_u = b.op("vector", "tensor_scalar", tb[kk][:], psU[kk][:], rden[kk][:],
                           None, ALU.mult,
                           deps=[t_r, umms[-1], prev.get(f"gsil{kk}")])
                prev[f"rden_u{kk}"] = t_u
                prev[f"uev{kk}"] = t_u
                t_g = b.op("scalar", "activation", gb_[kk][:], tb[kk][:], AF.Silu,
                           deps=[t_u, prev.get(f"smul{kk}")])
                prev[f"gsil{kk}"] = t_g
                t_sg = b.op("vector", "tensor_mul", s1b[kk][:], s1b[kk][:], gb_[kk][:],
                            deps=[t_g, l_s1])
                prev[f"smul{kk}"] = t_sg
                t_x = b.op("vector", "tensor_add", x1b[kk][:], x1b[kk][:], s1b[kk][:],
                           deps=[t_sg, l_x1])
                d_o = b.dma("sync", out[bi, ssl, :], x1b[kk][:], deps=[t_x])
                prev[f"xadd{kk}"] = d_o
                outs.append(d_o)
        b.done(outs)


def _shard_batch(arr):
    return [np.ascontiguousarray(arr[2 * k:2 * k + 2]) for k in range(NCORES)]


def run_stage(nc, in_maps):
    res = run_bass_kernel_spmd(nc, in_maps, core_ids=list(range(NCORES)))
    return res.results


_NCS = {}


def _nc(name, builder):
    if name not in _NCS:
        _NCS[name] = builder()
    return _NCS[name]


def build_mega():
    nc = bass.Bass("TRN2", target_bir_lowering=False)

    def EI(n, sh, dt=F32):
        return nc.dram_tensor(n, sh, dt, kind="ExternalInput").ap()

    def IT(n, sh, dt):
        return nc.dram_tensor(n, sh, dt, kind="Internal").ap()

    x = EI("x", [BPC, S, D])
    qkv_w = EI("qkv_w", [3 * D, D]); qkv_b = EI("qkv_b", [1, 3 * D])
    proj_w = EI("proj_w", [D, D]); proj_b = EI("proj_b", [1, D])
    W0 = EI("W0", [D, D]); b0 = EI("b0", [1, D])
    lora_down = EI("lora_down", [4, 100]); lora_up = EI("lora_up", [50, 4])
    down_aux = EI("down_aux", [100, D]); up_aux = EI("up_aux", [D, 50])
    Wq = EI("Wq", [D, D]); Wk = EI("Wk", [D, D]); Wv = EI("Wv", [D, D]); Wo = EI("Wo", [D, D])
    bq = EI("bq", [1, D]); bk = EI("bk", [1, D]); bv = EI("bv", [1, D]); bo = EI("bo", [1, D])
    beta = EI("beta", [1, 1])
    gamma_x = EI("gamma_x", [1, D]); gamma_slot = EI("gamma_slot", [1, D])
    slot_embeds = EI("slot_embeds", [ET, D])
    w1 = EI("w1", [D, HID]); w2 = EI("w2", [HID, D])

    qkvT = IT("qkvT", [D, 3 * D], BF16)
    projT = IT("projT", [D, D], BF16); WeffT = IT("WeffT", [D, D], BF16)
    WqT = IT("WqT", [D, D], BF16); WkT = IT("WkT", [D, D], BF16)
    WvT = IT("WvT", [D, D], BF16); WoT = IT("WoT", [D, D], BF16)
    seT = IT("seT", [D, ET], BF16)
    xn_tm = IT("xn_tm", [BPC, S, D], BF16); xn_fm = IT("xn_fm", [BPC, D, S], BF16)
    qkv_fm = IT("qkv_fm", [BPC, 3 * D, S], BF16)
    attn_fm = IT("attn_fm", [BPC, D, S], BF16)
    x1 = IT("x1", [BPC, S, D], F32)
    xn1_tm = IT("xn1_tm", [BPC, S, D], BF16); xn1_fm = IT("xn1_fm", [BPC, D, S], BF16)
    Qt_fm = IT("Qt_fm", [BPC, D, S], BF16); Kt_fm = IT("Kt_fm", [BPC, D, S], BF16)
    Kt_tm = IT("Kt_tm", [BPC, S, D], BF16); Vp_tm = IT("Vp_tm", [BPC, S, H1 * 17], BF16)
    attn1_fm = IT("attn1_fm", [BPC, D, S], BF16)
    s1_tm = IT("s1_tm", [BPC, S, D], F32)
    xnn_tm = IT("xnn_tm", [BPC, S, D], BF16); xnn_fm = IT("xnn_fm", [BPC, D, S], BF16)
    ncm = IT("ncm", [BPC, 128, 32], F32); rmrow = IT("rmrow", [BPC, 1, S], F32)
    slots_fm = IT("slots_fm", [BPC, D, ET], BF16)
    logitsT = IT("logitsT", [BPC, ET // 128, 128, S], F32)
    E2T = IT("E2T", [BPC, ET, S], BF16)
    processed = IT("processed", [BPC, ET, D], BF16)
    out = nc.dram_tensor("out", [BPC, S, D], F32, kind="ExternalOutput").ap()

    wio = {"qkv_w": qkv_w, "proj_w": proj_w, "W0": W0, "lora_down": lora_down,
           "lora_up": lora_up, "down_aux": down_aux, "up_aux": up_aux,
           "Wq": Wq, "Wk": Wk, "Wv": Wv, "Wo": Wo, "slot_embeds": slot_embeds,
           "gamma_slot": gamma_slot, "qkvT": qkvT, "projT": projT, "WeffT": WeffT,
           "WqT": WqT, "WkT": WkT, "WvT": WvT, "WoT": WoT, "seT": seT}

    import os
    nstages = int(os.environ.get("MEGA_STAGES", "99"))
    with ExitStack() as ctx:
        b = Bld(nc, ctx)
        b.init_idents()
        stages = [
            lambda: emit_wprep(b, wio),
            lambda: emit_ln(b, x, xn_tm, xn_fm, tag="ln0"),
            lambda: emit_l0qkv(b, xn_fm, qkvT, qkv_b, qkv_fm),
            lambda: emit_l0attn(b, qkv_fm, attn_fm),
            lambda: emit_l0epi(b, xn_fm, WeffT, b0, attn_fm, projT, proj_b, x, x1),
            lambda: emit_ln(b, x1, xn1_tm, xn1_fm, tag="ln1"),
            lambda: emit_l1qkv(b, xn1_fm, WqT, WkT, WvT, bq, bk, bv, beta,
                               Qt_fm, Kt_fm, Kt_tm, Vp_tm),
            lambda: emit_l1attn(b, Qt_fm, Kt_fm, Kt_tm, Vp_tm, attn1_fm),
            lambda: emit_proj_tm(b, attn1_fm, WoT, bo, s1_tm, act="identity", tag="s1"),
            lambda: emit_l1rms(b, xn1_tm, gamma_x, xnn_tm, xnn_fm),
            lambda: emit_l1ffnA(b, xnn_fm, seT, ncm, rmrow, logitsT),
            lambda: emit_l1ffnB(b, xnn_fm, xnn_tm, seT, ncm, rmrow, slots_fm, E2T,
                                logitsT),
            lambda: emit_l1ffnC(b, slots_fm, w1, w2, processed),
            lambda: emit_l1ffnD(b, processed, E2T, s1_tm, x1, out),
        ]
        for f in stages[:nstages]:
            f()
        if nstages < len(stages):
            z = b.sb("zz", [128, D])
            tz = b.op("vector", "memset", z[:], 0.0)
            for bi in range(BPC):
                for sc_ in range(NSC):
                    b.dma("sync", out[bi, sc_ * 128:(sc_ + 1) * 128, :], z[:], deps=[tz])
        b.finish_all()
    return nc


def kernel(**inputs):
    inp = {k: np.ascontiguousarray(np.asarray(v)) for k, v in inputs.items()}
    xs = _shard_batch(inp["L0_attn_x"])
    base = {
        "qkv_w": inp["L0_attn_qkv_w"], "qkv_b": inp["L0_attn_qkv_b"].reshape(1, -1),
        "proj_w": inp["L0_attn_proj_w"], "proj_b": inp["L0_attn_proj_b"].reshape(1, -1),
        "W0": inp["L0_ffn_W"], "b0": inp["L0_ffn_b"].reshape(1, -1),
        "lora_down": inp["L0_ffn_lora_down"], "lora_up": inp["L0_ffn_lora_up"],
        "down_aux": inp["L0_ffn_down_aux"], "up_aux": inp["L0_ffn_up_aux"],
        "Wq": inp["L1_attn_Wq"], "Wk": inp["L1_attn_Wk"],
        "Wv": inp["L1_attn_Wv"], "Wo": inp["L1_attn_Wo"],
        "bq": inp["L1_attn_bq"].reshape(1, -1), "bk": inp["L1_attn_bk"].reshape(1, -1),
        "bv": inp["L1_attn_bv"].reshape(1, -1), "bo": inp["L1_attn_bo"].reshape(1, -1),
        "beta": inp["L1_attn_beta"].reshape(1, 1),
        "gamma_x": inp["L1_ffn_gamma_x"].reshape(1, -1),
        "gamma_slot": inp["L1_ffn_gamma_slot"].reshape(1, -1),
        "slot_embeds": inp["L1_ffn_slot_embeds"].reshape(ET, D),
        "w1": inp["L1_ffn_w1"], "w2": inp["L1_ffn_w2"],
    }
    in_maps = [{**base, "x": xs[k]} for k in range(NCORES)]
    res = run_bass_kernel_spmd(_nc("mega", build_mega), in_maps,
                               core_ids=list(range(NCORES)))
    return np.concatenate([np.asarray(r["out"]) for r in res.results], axis=0)
